# version 04
# baseline (speedup 1.0000x reference)
import contextlib

import os
import numpy as np

import concourse.bass as bass
import concourse.mybir as mybir
from concourse.bass_utils import run_bass_kernel_spmd

F32 = mybir.dt.float32
BF16 = mybir.dt.bfloat16
AX = mybir.AxisListType.X
ALU = mybir.AluOpType
ACT = mybir.ActivationFunctionType

MAGIC = 12582912.0
NEGINF = -1e30
LN2PI = 6.283185307179586


class Cfg:
    def __init__(self, small=False):
        if small:
            self.T = 256
            self.D = 512
            self.H = 4
            self.KVH = 2
            self.WIN = 256
            self.F = 1024
            self.H1 = 8
            self.KVH1 = 4
        else:
            self.T = 1024
            self.D = 2048
            self.H = 16
            self.KVH = 4
            self.WIN = 512
            self.F = 8192
            self.H1 = 32
            self.KVH1 = 8
        self.HD = 128
        self.P1 = 64
        self.CH = 128
        self.NT = self.T // 128
        self.ND = self.D // 128
        self.NF = self.F // 128
        self.HT = self.WIN // 128
        self.NKT = self.NT + self.HT
        self.W5 = self.HT + 1
        self.CONVD = (self.H1 + 2 * self.KVH1) * self.P1
        self.NC1 = self.CONVD // 128
        self.ZGD = self.H1 * self.P1
        self.NZ1 = self.ZGD // 128
        self.PROJ = self.CONVD + self.ZGD + self.H1
        self.NPC = (self.PROJ + 127) // 128
        self.PROJ_PAD = self.NPC * 128
        self.NCH = self.T // self.CH
        self.G1 = self.H1 // self.KVH1
        self.EPS_LN = 1e-5
        self.EPS_QK = 1.1920928955078125e-07
        self.EPS_FFN0_AQ = 1e-5
        self.EPS_FFN0_RMS = 1e-6
        self.EPS_SSD = 1e-5
        self.EPS_FFN1_RMS = 1e-8
        self.ROPE_BASE = 10000.0


class Sem:

    def __init__(self, stack, nc, name):
        self.h = stack.enter_context(nc.semaphore(name))
        self.n = 0

    def inc(self, instr, k=1):
        instr.then_inc(self.h, k)
        self.n += k
        return instr

    def dma(self, instr):
        return self.inc(instr, 16)


class P:

    def __init__(self, nc, cfg, stack):
        self.nc = nc
        self.cfg = cfg
        self.stack = stack
        self.sT = Sem(stack, nc, "sT")
        self.sV = Sem(stack, nc, "sV")
        self.sS = Sem(stack, nc, "sS")
        self.sG = Sem(stack, nc, "sG")
        self.sCC = Sem(stack, nc, "sCC")
        self.sDW = Sem(stack, nc, "sDW")
        self.sDWb = Sem(stack, nc, "sDWb")
        self.sDG = Sem(stack, nc, "sDG")
        self.sDGb = Sem(stack, nc, "sDGb")
        self.sDS = Sem(stack, nc, "sDS")
        self.sDSb = Sem(stack, nc, "sDSb")
        self.sST = Sem(stack, nc, "sST")
        self.sSTG = Sem(stack, nc, "sSTG")
        self.sTf = Sem(stack, nc, "sTf")
        self.sVf = Sem(stack, nc, "sVf")
        self.all_sems = [self.sT, self.sV, self.sS, self.sG,
                         self.sDW, self.sDWb, self.sDG, self.sDGb,
                         self.sDS, self.sDSb, self.sST, self.sSTG,
                         self.sTf, self.sVf]

    def sb(self, name, shape, dtype, side="left"):
        return self.stack.enter_context(
            self.nc.sbuf_tensor(name, shape, dtype, side=side))

    def ps(self, name, shape, dtype=F32):
        return self.stack.enter_context(self.nc.psum_tensor(name, shape, dtype))


def _emit_barrier(p, block, engines=("tensor", "vector", "scalar", "gpsimd")):
    marks = {}
    for e in engines:
        def emit_mark(eng, e=e):
            sem = {"tensor": p.sT, "vector": p.sV, "scalar": p.sS,
                   "gpsimd": p.sG}[e]
            sem.inc(eng.drain())
            marks[e] = sem.n
        getattr(block, e)(emit_mark)
    for e in engines:
        def emit_wait(eng, e=e):
            for o in engines:
                if o == e:
                    continue
                sem = {"tensor": p.sT, "vector": p.sV, "scalar": p.sS,
                       "gpsimd": p.sG}[o]
                eng.wait_ge(sem.h, marks[o])
        getattr(block, e)(emit_wait)



def _tile_wT(w):
    return np.ascontiguousarray(w.T)


def _slab_stream(wt, chunk):
    din, dout = wt.shape
    assert dout % chunk == 0
    return np.ascontiguousarray(
        wt.reshape(din, dout // chunk, chunk).transpose(1, 0, 2))


def _pcol(v):
    n = v.shape[0]
    assert n % 128 == 0
    return np.ascontiguousarray(v.reshape(n // 128, 128).T)


def stage_inputs(cfg, inputs):
    c = cfg
    x = np.asarray(inputs["L0_attn_x"], np.float32)
    B, S, D = x.shape
    assert D == c.D and S == 2 * c.T

    wq = _slab_stream(_tile_wT(np.asarray(inputs["L0_attn_Wq"], np.float32)), 128)
    wk = np.ascontiguousarray(_tile_wT(np.asarray(inputs["L0_attn_Wk"], np.float32)))
    wv = np.ascontiguousarray(_tile_wT(np.asarray(inputs["L0_attn_Wv"], np.float32)))
    wo = _slab_stream(_tile_wT(np.asarray(inputs["L0_attn_Wo"], np.float32)), 128)
    qgain = np.asarray(inputs["L0_attn_q_gain"], np.float32).reshape(1, c.H)

    wg0 = _slab_stream(_tile_wT(np.asarray(inputs["L0_ffn_w_gate"], np.float32)), 128)
    wu0 = _slab_stream(_tile_wT(np.asarray(inputs["L0_ffn_w_up"], np.float32)), 128)
    wd0 = _slab_stream(_tile_wT(np.asarray(inputs["L0_ffn_w_down"], np.float32)), 128)
    lnw = _pcol(np.asarray(inputs["L0_ffn_ln_weight"], np.float32))

    wvkq = _tile_wT(np.asarray(inputs["L1_attn_W_vkqgdt"], np.float32))
    pad = np.zeros((c.D, c.PROJ_PAD - c.PROJ), np.float32)
    wvkq = _slab_stream(np.ascontiguousarray(np.concatenate([wvkq, pad], 1)), 128)
    convw = np.asarray(inputs["L1_attn_conv_w"], np.float32)
    convw0 = _pcol(convw[:, 0])
    convw1 = _pcol(convw[:, 1])
    convb = _pcol(np.asarray(inputs["L1_attn_conv_b"], np.float32))
    dtb = np.asarray(inputs["L1_attn_dt_bias"], np.float32).reshape(c.H1, 1)
    alog = np.asarray(inputs["L1_attn_A_log_bias"], np.float32).reshape(c.H1, 1)
    gnw = np.tile(np.asarray(inputs["L1_attn_gnorm_w"], np.float32), c.H1).reshape(1, c.ZGD)
    wo1 = _slab_stream(_tile_wT(np.asarray(inputs["L1_attn_W_o"], np.float32)), 128)

    wg1 = _slab_stream(_tile_wT(np.asarray(inputs["L1_ffn_W_g"], np.float32)), 128)
    wu1 = _slab_stream(_tile_wT(np.asarray(inputs["L1_ffn_W_u"], np.float32)), 128)
    wd1 = _slab_stream(_tile_wT(np.asarray(inputs["L1_ffn_W_d"], np.float32)), 128)

    shared = dict(
        wq=wq, wk=wk, wv=wv, wo=wo, qgain=qgain,
        wg0=wg0, wu0=wu0, wd0=wd0, lnw=lnw,
        wvkq=wvkq, convw0=convw0, convw1=convw1, convb=convb,
        dtb=dtb, alog=alog, gnw=gnw, wo1=wo1,
        wg1=wg1, wu1=wu1, wd1=wd1,
    )

    in_maps = []
    for core in range(2 * B):
        b, half = core // 2, core % 2
        t0 = half * c.T
        x_own = np.ascontiguousarray(x[b, t0:t0 + c.T].T)
        if half == 0:
            x_halo = np.zeros((c.D, c.WIN), np.float32)
        else:
            x_halo = np.ascontiguousarray(x[b, t0 - c.WIN:t0].T)
        params = np.zeros((1, 32), np.float32)
        params[0, 0] = float(t0)
        params[0, 1] = float(half)
        params[0, 2] = float(1 - half)
        for j in range(c.NKT):
            params[0, 3 + j] = NEGINF if (half == 0 and j < c.HT) else 0.0
        m = dict(shared)
        m.update(x_own=x_own, x_halo=x_halo, params=params)
        in_maps.append(m)
    return in_maps



def build_program(cfg, dbg=(), n_cores=8):
    c = cfg
    nc = bass.Bass("TRN2", target_bir_lowering=False)
    stack = contextlib.ExitStack()

    d = {}
    d["x_own"] = nc.dram_tensor("x_own", [c.D, c.T], F32, kind="ExternalInput").ap()
    d["x_halo"] = nc.dram_tensor("x_halo", [c.D, c.WIN], F32, kind="ExternalInput").ap()
    d["params"] = nc.dram_tensor("params", [1, 32], F32, kind="ExternalInput").ap()
    d["wq"] = nc.dram_tensor("wq", [c.H, c.D, 128], F32, kind="ExternalInput").ap()
    d["wk"] = nc.dram_tensor("wk", [c.D, c.KVH * c.HD], F32, kind="ExternalInput").ap()
    d["wv"] = nc.dram_tensor("wv", [c.D, c.KVH * c.HD], F32, kind="ExternalInput").ap()
    d["wo"] = nc.dram_tensor("wo", [c.ND, c.D, 128], F32, kind="ExternalInput").ap()
    d["qgain"] = nc.dram_tensor("qgain", [1, c.H], F32, kind="ExternalInput").ap()
    d["wg0"] = nc.dram_tensor("wg0", [c.NF, c.D, 128], F32, kind="ExternalInput").ap()
    d["wu0"] = nc.dram_tensor("wu0", [c.NF, c.D, 128], F32, kind="ExternalInput").ap()
    d["wd0"] = nc.dram_tensor("wd0", [c.ND, c.F, 128], F32, kind="ExternalInput").ap()
    d["lnw"] = nc.dram_tensor("lnw", [128, c.NF], F32, kind="ExternalInput").ap()
    d["wvkq"] = nc.dram_tensor("wvkq", [c.NPC, c.D, 128], F32, kind="ExternalInput").ap()
    d["convw0"] = nc.dram_tensor("convw0", [128, c.NC1], F32, kind="ExternalInput").ap()
    d["convw1"] = nc.dram_tensor("convw1", [128, c.NC1], F32, kind="ExternalInput").ap()
    d["convb"] = nc.dram_tensor("convb", [128, c.NC1], F32, kind="ExternalInput").ap()
    d["dtb"] = nc.dram_tensor("dtb", [c.H1, 1], F32, kind="ExternalInput").ap()
    d["alog"] = nc.dram_tensor("alog", [c.H1, 1], F32, kind="ExternalInput").ap()
    d["gnw"] = nc.dram_tensor("gnw", [1, c.ZGD], F32, kind="ExternalInput").ap()
    d["wo1"] = nc.dram_tensor("wo1", [c.ND, c.D, 128], F32, kind="ExternalInput").ap()
    d["wg1"] = nc.dram_tensor("wg1", [c.NF, c.D, 128], F32, kind="ExternalInput").ap()
    d["wu1"] = nc.dram_tensor("wu1", [c.NF, c.D, 128], F32, kind="ExternalInput").ap()
    d["wd1"] = nc.dram_tensor("wd1", [c.ND, c.F, 128], F32, kind="ExternalInput").ap()
    d["y_out"] = nc.dram_tensor("y_out", [c.D, c.T], F32, kind="ExternalOutput").ap()
    d["xT_sp"] = nc.dram_tensor("xT_sp", [c.D, c.T], F32, kind="Internal").ap()
    d["sT_sp"] = nc.dram_tensor("sT_sp", [c.D, c.T], BF16, kind="Internal").ap()
    d["zg_sp"] = nc.dram_tensor("zg_sp", [c.ZGD, c.T], BF16, kind="Internal").ap()
    d["cc1_in"] = nc.dram_tensor("cc1_in", [128, c.NC1], F32, kind="Internal").ap()
    d["cc1_out"] = nc.dram_tensor("cc1_out", [128, c.NC1], F32, kind="Internal").ap()
    d["cc2_in"] = nc.dram_tensor("cc2_in", [128, c.H1 * c.P1 // 2], F32, kind="Internal").ap()
    d["cc2_out"] = nc.dram_tensor("cc2_out", [128, c.H1 * c.P1 // 2], F32, kind="Internal").ap()
    for name, shape, dt_ in dbg:
        d[name] = nc.dram_tensor(name, shape, dt_, kind="ExternalOutput").ap()

    rg = [[2 * i, 2 * i + 1] for i in range(n_cores // 2)]

    p = P(nc, c, stack)
    dbg_out = {}

    with nc.Block() as block:
        build_body(p, block, d, rg, dbg_out, dbg)

    stack.close()
    nc.finalize()
    return nc


def kernel(**inputs):
    cfg = Cfg(small=False)
    in_maps = stage_inputs(cfg, inputs)
    nc = build_program(cfg)
    res = run_bass_kernel_spmd(nc, in_maps, core_ids=list(range(8)))
    out = np.empty((4, 2048, 2048), np.float32)
    for core in range(8):
        b, half = core // 2, core % 2
        out[b, half * cfg.T:(half + 1) * cfg.T] = res.results[core]["y_out"].T
    return out




class B:

    def __init__(self, p, block, d, rg, dbg):
        self.p, self.block, self.d, self.rg = p, block, d, rg
        self.nc = p.nc
        self.c = p.cfg
        self.dbg = set(n for (n, _, _) in dbg)
        self.qwaited = {}
        self.paranoid = False

    def on(self, eng, fn):
        getattr(self.block, eng)(fn)

    def barrier(self):
        _emit_barrier(self.p, self.block)

    def dbg_dump(self, name, src_ap):
        if name not in self.dbg:
            return
        p, d = self.p, self.d
        marks = {s: s.n for s in (p.sT, p.sV, p.sS, p.sG, p.sDW, p.sDWb, p.sDG,
                                  p.sDGb, p.sDS, p.sDSb, p.sST, p.sSTG, p.sCC)
                 if s.n > 0}

        def f(sync):
            for s, v in marks.items():
                sync.wait_ge(s.h, v)
            p.sDW.dma(sync.dma_start(out=d[name][:], in_=src_ap))
        self.on("sync", f)


    def _twait(self, te):
        p = self.p
        for s in (p.sV, p.sS, p.sG):
            if s.n:
                te.wait_ge(s.h, s.n)
        for s in (p.sDW, p.sDWb, p.sDG, p.sDGb, p.sDS, p.sDSb, p.sST, p.sSTG,
                  p.sCC):
            if s.n > self.qwaited.get(s, 0):
                te.wait_ge(s.h, s.n)
                self.qwaited[s] = s.n

    def step(self, eng, fn):
        p = self.p
        sems = {"tensor": p.sT, "vector": p.sV, "scalar": p.sS, "gpsimd": p.sG}
        own = sems.get(eng)
        marks = [(s, s.n) for nm, s in sems.items() if nm != eng and s.n > 0]
        own_q = {"sync": (p.sDW, p.sDWb, p.sST),
                 "gpsimd": (p.sDG, p.sDGb, p.sSTG, p.sCC),
                 "scalar": (p.sDS, p.sDSb)}.get(eng, ())
        for s in own_q:
            if s.n > 0:
                marks.append((s, s.n))
                self.qwaited[s] = max(self.qwaited.get(s, 0), s.n)
        for s in (p.sDW, p.sDWb, p.sDG, p.sDGb, p.sDS, p.sDSb, p.sST, p.sSTG,
                  p.sCC):
            if s not in own_q and s.n > self.qwaited.get(s, 0):
                marks.append((s, s.n))
                self.qwaited[s] = s.n

        def wrapped(e):
            for s, v in marks:
                e.wait_ge(s.h, v)
            if eng != "sync":
                e.drain()
            fn(e)
            if eng == "sync":
                pass
            else:
                own.inc(e.drain())
        self.on(eng, wrapped)
        if self.paranoid:
            self.barrier()

    def load_slab(self, eng, stg, sl):
        R = sl.shape[0]
        return eng.dma_start(
            out=stg[:, 0:R].rearrange("p (k c) -> p k c", c=128),
            in_=sl.rearrange("(k p) c -> p k c", p=128))

    def col2row(self, stat_ap, row_off, nt, psum):
        p = self.p

        def ft(te):
            self._twait(te)
            for t in range(nt):
                p.sT.inc(te.transpose(psum[0:1, t * 128:(t + 1) * 128],
                                      stat_ap[:, t:t + 1], self.idf[:]))
        self.on("tensor", ft)
        mark = p.sT.n

        def fv(v):
            v.wait_ge(p.sT.h, mark)
            p.sV.inc(v.tensor_copy(out=self.rows[0:1, row_off:row_off + 128 * nt],
                                   in_=psum[0:1, 0:128 * nt]))
        self.on("vector", fv)

    def bcast(self, row_ap, out_sb, n, psum):
        p = self.p

        def ft(te):
            self._twait(te)
            for j in range(0, n, 512):
                w = min(512, n - j)
                p.sT.inc(te.matmul(psum[:, j:j + w], self.ones_row[:, 0:128],
                                   row_ap[:, j:j + w], start=True, stop=True))
        self.on("tensor", ft)
        mark = p.sT.n

        def fv(v):
            v.wait_ge(p.sT.h, mark)
            p.sV.inc(v.tensor_copy(out=out_sb[:, 0:n], in_=psum[:, 0:n]))
        self.on("vector", fv)

    def stream_slabs(self, slabs, conv, use, extra_v=None):
        p = self.p
        loadm, convm, usem = {}, {}, {}
        for i, sl in enumerate(slabs):
            par = i % 2
            stg, wbf = self.WSTG[par], self.WBF[par]
            R = sl.shape[0]
            assert R % 128 == 0 and R <= stg.shape[1]

            def fl(sync, i=i, stg=stg, sl=sl, R=R):
                if i >= 2:
                    sync.wait_ge(p.sV.h, convm[i - 2])
                sem = p.sDW if par == 0 else p.sDWb
                trio = (p.sDW, p.sDWb, p.sST)
                if sem.n > 0:
                    sync.wait_ge(sem.h, sem.n)
                if i < 2:
                    for s__ in trio:
                        if s__ is not sem and s__.n > 0:
                            sync.wait_ge(s__.h, s__.n)
                    for s__ in (p.sV, p.sS):
                        if s__.n > 0:
                            sync.wait_ge(s__.h, s__.n)
                sem.dma(self.load_slab(sync, stg, sl))
            self.on("sync", fl)
            loadm[i] = ((p.sDW if par == 0 else p.sDWb), (p.sDW if par == 0 else p.sDWb).n)

            def fc(v, i=i, stg=stg, wbf=wbf, R=R):
                v.wait_ge(loadm[i][0].h, loadm[i][1])
                if i >= 2:
                    v.wait_ge(p.sT.h, usem[i - 2])
                conv(v, i, stg, wbf, R)
                if extra_v is not None:
                    extra_v(v, i, stg, R)
            self.on("vector", fc)
            convm[i] = p.sV.n

            def fu(te, i=i, wbf=wbf, R=R):
                te.wait_ge(p.sV.h, convm[i])
                use(te, i, wbf, R)
            self.on("tensor", fu)
            usem[i] = p.sT.n
        return usem

    def conv_cast(self, v, i, stg, wbf, R):
        self.p.sV.inc(v.tensor_copy(out=wbf[:, 0:R], in_=stg[:, 0:R]))

    def make_conv_tern_idx(self, m):
        p = self.p

        def conv(v, i, stg, wbf, R):
            v.tensor_scalar(out=stg[:, 0:R], in0=stg[:, 0:R],
                            scalar1=self.srep[:, m:m + 1], scalar2=1.0,
                            op0=ALU.mult, op1=ALU.min)
            v.drain()
            v.tensor_scalar(out=stg[:, 0:R], in0=stg[:, 0:R], scalar1=-1.0,
                            scalar2=MAGIC, op0=ALU.max, op1=ALU.add)
            v.drain()
            p.sV.inc(v.tensor_scalar(out=wbf[:, 0:R], in0=stg[:, 0:R],
                                     scalar1=MAGIC, scalar2=None,
                                     op0=ALU.subtract))
        return conv

    def conv_sign(self, v, i, stg, wbf, R):
        self.p.sV.inc(v.tensor_scalar(out=wbf[:, 0:R], in0=stg[:, 0:R],
                                      scalar1=0.0, scalar2=0.5,
                                      op0=ALU.is_gt, op1=ALU.subtract))


    def phase_init(self):
        p, c, d = self.p, self.c, self.d
        self.idf = p.sb("idf", [128, 128], F32)
        self.idb = p.sb("idb", [128, 128], BF16)
        self.ones_row = p.sb("ones_row", [1, 128], F32)
        self.ones_col = p.sb("ones_col", [128, 1], F32)
        self.ones_col_b = p.sb("ones_col_b", [128, 1], BF16)
        self.uones = p.sb("uones", [128, 128], F32)
        self.mk_diag = p.sb("mk_diag", [128, 128], BF16)
        self.mk_far = p.sb("mk_far", [128, 128], BF16)
        self.mk_lowT = p.sb("mk_lowT", [128, 128], F32)
        self.rows = p.sb("rows", [1, 2048], F32)
        self.params_b = p.sb("params_b", [128, 32], F32)
        self.qgain_sb = p.sb("qgain_sb", [1, c.H], F32)
        self.Mh_b = p.sb("Mh_b", [128, c.H], F32)
        self.lnw_sb = p.sb("lnw_sb", [128, c.NF], F32)
        self.cw0 = p.sb("cw0", [128, c.NC1], F32)
        self.cw1 = p.sb("cw1", [128, c.NC1], F32)
        self.cb_ = p.sb("cb_", [128, c.NC1], F32)
        self.dtb_sb = p.sb("dtb_sb", [c.H1, 1], F32)
        self.negA = p.sb("negA", [c.H1, 1], F32)
        self.stat = p.sb("stat", [128, 8 * c.NT + 32], F32)
        self.tscr = p.sb("tscr", [128, 1024], F32)
        self.scr2 = p.sb("scr2", [128, 1024], F32)
        self.bc1 = p.sb("bc1", [128, 1024], F32)
        self.bc2 = p.sb("bc2", [128, 1024], F32)
        self.maxA = p.sb("maxA", [128, 1024], F32)
        self.maxB = p.sb("maxB", [128, 1024], F32)
        self.WSTG = [p.sb(f"wstg{i}", [128, 2048], F32) for i in range(2)]
        self.WBF = [p.sb(f"wbf{i}", [128, 2048], BF16) for i in range(2)]
        self.srep = p.sb("srep", [128, 3], F32)
        self.XR = p.sb("XR", [128, 2048], F32)
        self.sR = p.sb("sR", [128, 1024], BF16)
        self.rowsE = p.sb("rowsE", [1, 768], F32)
        self.csJ = p.sb("csJ", [128, 2], F32)
        self.sgn = p.sb("sgn", [128, 1], F32)
        self.XN = None
        self.QX = None

        psA = self.psA = p.ps("psA", [128, 1024])
        self.psB = p.ps("psB", [128, 1024])
        self.psC = p.ps("psC", [128, 1024])
        self.psY = p.ps("psY", [128, 1024])

        def fg(g):
            g.memset(self.idf[:], 0.0)
            g.memset(self.uones[:], 0.0)
            g.memset(self.mk_diag[:], 1.0)
            g.memset(self.mk_far[:], 1.0)
            g.memset(self.mk_lowT[:], 0.0)
            g.memset(self.ones_row[:], 1.0)
            g.memset(self.ones_col[:], 1.0)
            g.memset(self.ones_col_b[:], 1.0)
            g.memset(self.sgn[:], 1.0)
            g.drain()
            g.affine_select(out=self.sgn[:], in_=self.sgn[:], pattern=[[0, 1]],
                            compare_op=ALU.is_ge, fill=-1.0, base=63,
                            channel_multiplier=-1)
            g.drain()
            g.affine_select(out=self.idf[:], in_=self.idf[:], pattern=[[-1, 128]],
                            compare_op=ALU.not_equal, fill=1.0, base=0,
                            channel_multiplier=1)
            g.affine_select(out=self.uones[:], in_=self.uones[:], pattern=[[1, 128]],
                            compare_op=ALU.is_ge, fill=-1.0, base=0,
                            channel_multiplier=-1)
            g.affine_select(out=self.mk_diag[:], in_=self.mk_diag[:],
                            pattern=[[1, 128]], compare_op=ALU.is_ge, fill=0.0,
                            base=0, channel_multiplier=-1)
            g.affine_select(out=self.mk_far[:], in_=self.mk_far[:],
                            pattern=[[-1, 128]], compare_op=ALU.is_ge, fill=0.0,
                            base=-1, channel_multiplier=1)
            g.affine_select(out=self.mk_lowT[:], in_=self.mk_lowT[:],
                            pattern=[[1, 128]], compare_op=ALU.is_ge, fill=NEGINF,
                            base=0, channel_multiplier=-1)
            p.sG.inc(g.drain())
        self.on("gpsimd", fg)
        gm = p.sG.n

        def fv(v):
            v.wait_ge(p.sG.h, gm)
            p.sV.inc(v.tensor_scalar(out=self.uones[:], in0=self.uones[:],
                                     scalar1=1.0, scalar2=None, op0=ALU.add))
            p.sV.inc(v.tensor_copy(out=self.idb[:], in_=self.idf[:]))
        self.on("vector", fv)

        def fl(sync):
            p.sDW.dma(sync.dma_start(out=self.rows[0:1, 0:32], in_=d["params"][:]))
            p.sDW.dma(sync.dma_start(out=self.qgain_sb[:], in_=d["qgain"][:]))
            p.sDW.dma(sync.dma_start(out=self.lnw_sb[:], in_=d["lnw"][:]))
            p.sDW.dma(sync.dma_start(out=self.cw0[:], in_=d["convw0"][:]))
            p.sDW.dma(sync.dma_start(out=self.cw1[:], in_=d["convw1"][:]))
            p.sDW.dma(sync.dma_start(out=self.cb_[:], in_=d["convb"][:]))
            p.sDW.dma(sync.dma_start(out=self.dtb_sb[:], in_=d["dtb"][:]))
            p.sDW.dma(sync.dma_start(out=self.negA[:], in_=d["alog"][:]))
        self.on("sync", fl)
        ldm = p.sDW.n

        def fwait(v):
            v.wait_ge(p.sDW.h, ldm)
            p.sV.inc(v.drain())
        self.on("vector", fwait)
        self.bcast(self.rows[0:1, 0:32], self.params_b, 32, self.psC)

        def fs(s):
            s.wait_ge(p.sDW.h, ldm)
            s.activation(self.negA[:], self.negA[:], ACT.Exp)
            s.drain()
            s.activation(self.rows[0:1, 64:64 + c.H], self.qgain_sb[:],
                         ACT.Abs, scale=float(np.sqrt(c.HD)))
            p.sS.inc(s.drain())
        self.on("scalar", fs)
        sm = p.sS.n

        def fv3(v):
            v.wait_ge(p.sS.h, sm)
            p.sV.inc(v.tensor_scalar(out=self.negA[:], in0=self.negA[:],
                                     scalar1=-1.0, scalar2=None, op0=ALU.mult))
        self.on("vector", fv3)
        self.bcast(self.rows[0:1, 64:64 + c.H], self.Mh_b, c.H, self.psC)
        self.barrier()


    def ln_pass(self, src, n, slot_m, slot_r, eps, xn_chunks=None, slot_c=None,
                aq_eps=None):
        p, c = self.p, self.c
        nd, nt = c.ND, n // 128
        S = c.NT

        def stat_ap(slot):
            return self.stat[:, slot * S:slot * S + nt]

        dm, sm = {}, {}
        for k in range(nd):
            par = k % 2

            def fl(g, k=k, par=par):
                if k >= 2:
                    g.wait_ge(p.sT.h, sm[k - 2])
                sem = p.sDG if par == 0 else p.sDGb
                trio = (p.sDG, p.sDGb, p.sSTG)
                if sem.n > 0:
                    g.wait_ge(sem.h, sem.n)
                if k < 2:
                    for s__ in trio:
                        if s__ is not sem and s__.n > 0:
                            g.wait_ge(s__.h, s__.n)
                    for s__ in (p.sV, p.sS):
                        if s__.n > 0:
                            g.wait_ge(s__.h, s__.n)
                sem.dma(g.dma_start(out=self.XR[:, par * 1024:par * 1024 + n],
                                      in_=src[k * 128:(k + 1) * 128, :]))
            self.on("gpsimd", fl)
            dm[k] = ((p.sDG if par == 0 else p.sDGb), (p.sDG if par == 0 else p.sDGb).n)

            def fs(s, k=k, par=par):
                s.wait_ge(dm[k][0].h, dm[k][1])
                sq = self.maxA if par == 0 else self.maxB
                p.sS.inc(s.activation(sq[:, 0:n],
                                      self.XR[:, par * 1024:par * 1024 + n],
                                      ACT.Square))
            self.on("scalar", fs)
            sqm = p.sS.n

            def ft(te, k=k, par=par, sqm=sqm):
                te.wait_ge(dm[k][0].h, dm[k][1])
                te.wait_ge(p.sS.h, sqm)
                sq = self.maxA if par == 0 else self.maxB
                for j in range(0, n, 512):
                    w = min(512, n - j)
                    te.matmul(self.psC[0:1, j:j + w], self.ones_col[:],
                              self.XR[:, par * 1024 + j:par * 1024 + j + w],
                              start=(k == 0), stop=(k == nd - 1))
                    p.sT.inc(te.matmul(
                        self.psA[0:1, j:j + w], self.ones_col[:],
                        sq[:, j:j + w],
                        start=(k == 0), stop=(k == nd - 1)))
            self.on("tensor", ft)
            sm[k] = p.sT.n
        self.barrier()

        def fv(v):
            v.tensor_copy(out=self.rows[0:1, 0:n], in_=self.psC[0:1, 0:n])
            p.sV.inc(v.tensor_copy(out=self.rows[0:1, 1024:1024 + n],
                                   in_=self.psA[0:1, 0:n]))
        self.on("vector", fv)
        self.barrier()

        def ft2(te):
            for t in range(nt):
                te.transpose(self.psC[:, t:t + 1],
                             self.rows[0:1, t * 128:(t + 1) * 128], self.idf[0:1, 0:1])
                p.sT.inc(te.transpose(
                    self.psB[:, t:t + 1],
                    self.rows[0:1, 1024 + t * 128:1024 + (t + 1) * 128],
                    self.idf[0:1, 0:1]))
        self.on("tensor", ft2)
        self.barrier()

        def fv2(v):
            v.tensor_scalar(out=stat_ap(slot_m), in0=self.psC[:, 0:nt],
                            scalar1=1.0 / c.D, scalar2=None, op0=ALU.mult)
            p.sV.inc(v.tensor_scalar(out=stat_ap(slot_r), in0=self.psB[:, 0:nt],
                                     scalar1=1.0 / c.D, scalar2=None, op0=ALU.mult))
        self.on("vector", fv2)
        self.barrier()

        def fv3(v):
            v.tensor_tensor(out=self.tscr[:, 0:nt], in0=stat_ap(slot_m),
                            in1=stat_ap(slot_m), op=ALU.mult)
            v.drain()
            v.tensor_tensor(out=self.tscr[:, 0:nt], in0=stat_ap(slot_r),
                            in1=self.tscr[:, 0:nt], op=ALU.subtract)
            v.drain()
            p.sV.inc(v.tensor_scalar(out=self.tscr[:, 0:nt], in0=self.tscr[:, 0:nt],
                                     scalar1=eps, scalar2=None, op0=ALU.add))
        self.on("vector", fv3)
        self.barrier()

        def fs2(s):
            p.sS.inc(s.activation(self.tscr[:, 64:64 + nt], self.tscr[:, 0:nt],
                                  ACT.Sqrt))
        self.on("scalar", fs2)
        self.barrier()

        def fv4(v):
            p.sV.inc(v.reciprocal(out=stat_ap(slot_r), in_=self.tscr[:, 64:64 + nt]))
        self.on("vector", fv4)
        self.barrier()

        self.col2row(stat_ap(slot_m), 0, nt, self.psC)
        self.col2row(stat_ap(slot_r), 1024, nt, self.psC)
        self.barrier()
        self.bcast(self.rows[0:1, 0:n], self.bc1, n, self.psC)
        self.bcast(self.rows[0:1, 1024:1024 + n], self.bc2, n, self.psB)
        self.barrier()

        if xn_chunks is None:
            return

        dm2, vm2 = {}, {}
        for k in range(nd):
            par = k % 2

            def fl(g, k=k, par=par):
                if k >= 2:
                    g.wait_ge(p.sV.h, vm2[k - 2])
                sem = p.sDG if par == 0 else p.sDGb
                trio = (p.sDG, p.sDGb, p.sSTG)
                if sem.n > 0:
                    g.wait_ge(sem.h, sem.n)
                if k < 2:
                    for s__ in trio:
                        if s__ is not sem and s__.n > 0:
                            g.wait_ge(s__.h, s__.n)
                    for s__ in (p.sV, p.sS):
                        if s__.n > 0:
                            g.wait_ge(s__.h, s__.n)
                sem.dma(g.dma_start(out=self.XR[:, par * 1024:par * 1024 + n],
                                      in_=src[k * 128:(k + 1) * 128, :]))
            self.on("gpsimd", fl)
            dm2[k] = ((p.sDG if par == 0 else p.sDGb), (p.sDG if par == 0 else p.sDGb).n)

            def fv5(v, k=k, par=par):
                v.wait_ge(dm2[k][0].h, dm2[k][1])
                v.drain()
                xr = self.XR[:, par * 1024:par * 1024 + n]
                v.tensor_tensor(out=self.scr2[:, 0:n], in0=xr,
                                in1=self.bc1[:, 0:n], op=ALU.subtract)
                v.drain()
                v.tensor_tensor(out=xn_chunks[k],
                                in0=self.scr2[:, 0:n], in1=self.bc2[:, 0:n],
                                op=ALU.mult)
                if slot_c is not None:
                    if k == 0:
                        v.tensor_scalar(out=self.maxA[:, 0:n],
                                        in0=self.scr2[:, 0:n], scalar1=1.0,
                                        scalar2=None, op0=ALU.mult)
                        v.tensor_scalar(out=self.maxB[:, 0:n],
                                        in0=self.scr2[:, 0:n], scalar1=-1.0,
                                        scalar2=None, op0=ALU.mult)
                    else:
                        v.scalar_tensor_tensor(out=self.maxA[:, 0:n],
                                               in0=self.scr2[:, 0:n], scalar=1.0,
                                               in1=self.maxA[:, 0:n],
                                               op0=ALU.mult, op1=ALU.max)
                        v.scalar_tensor_tensor(out=self.maxB[:, 0:n],
                                               in0=self.scr2[:, 0:n], scalar=-1.0,
                                               in1=self.maxB[:, 0:n],
                                               op0=ALU.mult, op1=ALU.max)
                p.sV.inc(v.drain())
            self.on("vector", fv5)
            vm2[k] = p.sV.n
        self.barrier()

        if slot_c is None:
            return
        fin_max = self.maxA

        def fvcmb(v):
            p.sV.inc(v.tensor_tensor(out=self.maxA[:, 0:n], in0=self.maxA[:, 0:n],
                                     in1=self.maxB[:, 0:n], op=ALU.max))
        self.on("vector", fvcmb)
        self.barrier()

        def ft3(te):
            for t in range(nt):
                p.sT.inc(te.transpose(self.psC[:, t * 128:(t + 1) * 128],
                                      fin_max[:, t * 128:(t + 1) * 128], self.idf[:]))
        self.on("tensor", ft3)
        self.barrier()

        def fv6(v):
            for t in range(nt):
                v.reduce_max(self.tscr[:, t:t + 1],
                             self.psC[:, t * 128:(t + 1) * 128], axis=AX)
            v.drain()
            v.tensor_tensor(out=self.tscr[:, 64:64 + nt], in0=self.tscr[:, 0:nt],
                            in1=stat_ap(slot_r), op=ALU.mult)
            v.drain()
            if aq_eps is not None:
                v.tensor_scalar(out=self.tscr[:, 64:64 + nt],
                                in0=self.tscr[:, 64:64 + nt],
                                scalar1=aq_eps, scalar2=None, op0=ALU.max)
                v.drain()
            v.reciprocal(out=self.tscr[:, 128:128 + nt], in_=self.tscr[:, 64:64 + nt])
            v.drain()
            v.tensor_scalar(out=self.tscr[:, 128:128 + nt],
                            in0=self.tscr[:, 128:128 + nt], scalar1=127.0,
                            scalar2=None, op0=ALU.mult)
            v.drain()
            v.tensor_scalar(out=stat_ap(slot_c + 1), in0=self.tscr[:, 64:64 + nt],
                            scalar1=1.0 / 127.0, scalar2=None, op0=ALU.mult)
            p.sV.inc(v.tensor_tensor(out=stat_ap(slot_c), in0=self.tscr[:, 128:128 + nt],
                                     in1=stat_ap(slot_r), op=ALU.mult))
        self.on("vector", fv6)
        self.barrier()

        self.barrier()

    def qx_pass(self, src, slot_m, slot_c, t0, tn, dst):
        p, c = self.p, self.c
        S = c.NT
        self.col2row(self.stat[:, slot_m * S:slot_m * S + c.NT], 0, c.NT, self.psC)
        self.col2row(self.stat[:, slot_c * S:slot_c * S + c.NT], 1024, c.NT, self.psC)
        self.bcast(self.rows[0:1, t0:t0 + tn], self.bc1, tn, self.psC)
        self.bcast(self.rows[0:1, 1024 + t0:1024 + t0 + tn], self.bc2, self.psC.shape[-1] if False else tn, self.psB)
        dm3, vm3 = {}, {}
        for k in range(c.ND):
            par = k % 2

            def fl(g, k=k, par=par):
                if k >= 2:
                    g.wait_ge(p.sV.h, vm3[k - 2])
                sem = p.sDG if par == 0 else p.sDGb
                trio = (p.sDG, p.sDGb, p.sSTG)
                if sem.n > 0:
                    g.wait_ge(sem.h, sem.n)
                if k < 2:
                    for s__ in trio:
                        if s__ is not sem and s__.n > 0:
                            g.wait_ge(s__.h, s__.n)
                    for s__ in (p.sV, p.sS):
                        if s__.n > 0:
                            g.wait_ge(s__.h, s__.n)
                sem.dma(g.dma_start(out=self.XR[:, par * 1024:par * 1024 + tn],
                                      in_=src[k * 128:(k + 1) * 128, t0:t0 + tn]))
            self.on("gpsimd", fl)
            dm3[k] = ((p.sDG if par == 0 else p.sDGb), (p.sDG if par == 0 else p.sDGb).n)

            def fv7(v, k=k, par=par):
                v.wait_ge(dm3[k][0].h, dm3[k][1])
                v.drain()
                xr = self.XR[:, par * 1024:par * 1024 + tn]
                v.tensor_tensor(out=self.scr2[:, 0:tn], in0=xr,
                                in1=self.bc1[:, 0:tn], op=ALU.subtract)
                v.drain()
                v.tensor_tensor(out=self.scr2[:, 0:tn], in0=self.scr2[:, 0:tn],
                                in1=self.bc2[:, 0:tn], op=ALU.mult)
                v.drain()
                p.sV.inc(v.tensor_scalar(out=dst[:, k * tn:(k + 1) * tn],
                                         in0=self.scr2[:, 0:tn], scalar1=MAGIC,
                                         scalar2=MAGIC, op0=ALU.add,
                                         op1=ALU.subtract))
            self.on("vector", fv7)
            vm3[k] = p.sV.n
        self.barrier()


    def rope_tables(self):
        p, c = self.p, self.c
        nk = c.NKT * 128
        half = c.HD // 2

        def fg(g):
            g.iota(self.csJ[:, 0:1], pattern=[[0, 1]], base=0, channel_multiplier=1,
                   allow_small_or_imprecise_dtypes=True)
            g.memset(self.csJ[:, 1:2], float(half))
            g.drain()
            g.affine_select(out=self.csJ[:, 1:2], in_=self.csJ[:, 1:2],
                            pattern=[[0, 1]], compare_op=ALU.is_ge, fill=0.0,
                            base=-half, channel_multiplier=1)
            g.drain()
            g.iota(self.cosT[:, 0:nk], pattern=[[1, nk]], base=0,
                   channel_multiplier=0, allow_small_or_imprecise_dtypes=True)
            p.sG.inc(g.drain())
        self.on("gpsimd", fg)
        gm = p.sG.n

        def fv0(v):
            v.wait_ge(p.sG.h, gm)
            p.sV.inc(v.tensor_tensor(out=self.csJ[:, 0:1], in0=self.csJ[:, 0:1],
                                     in1=self.csJ[:, 1:2], op=ALU.subtract))
        self.on("vector", fv0)
        vm0 = p.sV.n

        def fs(s):
            s.wait_ge(p.sV.h, vm0)
            p.sS.inc(s.activation(self.csJ[:, 1:2], self.csJ[:, 0:1], ACT.Exp,
                                  scale=-float(np.log(c.ROPE_BASE)) / half))
        self.on("scalar", fs)
        sm = p.sS.n

        def fv(v):
            v.wait_ge(p.sS.h, sm)
            v.tensor_scalar(out=self.cosT[:, 0:nk], in0=self.cosT[:, 0:nk],
                            scalar1=self.params_b[:, 0:1], scalar2=None,
                            op0=ALU.add)
            v.drain()
            v.tensor_scalar(out=self.cosT[:, 0:nk], in0=self.cosT[:, 0:nk],
                            scalar1=float(-c.WIN), scalar2=self.csJ[:, 1:2],
                            op0=ALU.add, op1=ALU.mult)
            v.drain()
            v.tensor_copy(out=self.sinT[:, 0:nk], in_=self.cosT[:, 0:nk])
            v.tensor_scalar(out=self.cosT[:, 0:nk], in0=self.cosT[:, 0:nk],
                            scalar1=float(np.pi / 2), scalar2=None, op0=ALU.add)
            v.drain()
            for buf in (self.sinT, self.cosT):
                v.tensor_scalar(out=self.XR[:, 0:nk], in0=buf[:, 0:nk],
                                scalar1=1.0 / LN2PI, scalar2=MAGIC, op0=ALU.mult,
                                op1=ALU.add)
                v.drain()
                v.tensor_scalar(out=self.XR[:, 0:nk], in0=self.XR[:, 0:nk],
                                scalar1=MAGIC, scalar2=-LN2PI, op0=ALU.subtract,
                                op1=ALU.mult)
                v.drain()
                v.tensor_tensor(out=buf[:, 0:nk], in0=buf[:, 0:nk],
                                in1=self.XR[:, 0:nk], op=ALU.add)
                v.drain()
            p.sV.inc(v.drain())
        self.on("vector", fv)
        vm = p.sV.n

        def fs2(s):
            s.wait_ge(p.sV.h, vm)
            s.activation(self.sinT[:, 0:nk], self.sinT[:, 0:nk], ACT.Sin)
            s.drain()
            p.sS.inc(s.activation(self.cosT[:, 0:nk], self.cosT[:, 0:nk],
                                  ACT.Sin))
        self.on("scalar", fs2)
        self.barrier()



    def _attn_segs(self, kind):
        c = self.c
        HOFF = ((c.WIN + 511) // 512) * 512
        segs = []
        if kind == "q":
            spans = [(0, c.T, 0)]
        else:
            spans = [(0, c.WIN, 0), (HOFF, c.T, c.WIN)]
        for off, wd, coff in spans:
            j = 0
            while j < wd:
                o = off + j
                w = min(512 - (o % 512), wd - j)
                ps = self.psA if o < 1024 else self.psB
                po = o if o < 1024 else o - 1024
                segs.append((ps, po, w, coff + j))
                j += w
        return segs

    def rope_scale(self, kind, dst):
        p, c = self.p, self.c
        segs = self._attn_segs(kind)
        cs0 = c.WIN if kind == "q" else 0

        def fv(v):
            for ps, po, w, co in segs:
                bck = self.bc1 if co < 1024 else self.bc2
                bo = co if co < 1024 else co - 1024
                qq = ps[:, po:po + w]
                cosv = self.cosT[:, cs0 + co:cs0 + co + w]
                sinv = self.sinT[:, cs0 + co:cs0 + co + w]
                XA = self.XR[:, 0:w]
                XB = self.XR[:, 1024:1024 + w]
                v.tensor_copy(out=XA[0:64, :], in_=ps[64:128, po:po + w])
                v.tensor_copy(out=XA[64:128, :], in_=ps[0:64, po:po + w])
                v.drain()
                v.tensor_tensor(out=XB, in0=qq, in1=cosv, op=ALU.mult)
                v.tensor_tensor(out=XA, in0=XA, in1=sinv, op=ALU.mult)
                v.drain()
                v.scalar_tensor_tensor(out=XB, in0=XA, scalar=self.sgn[:],
                                       in1=XB, op0=ALU.mult, op1=ALU.add)
                v.drain()
                v.tensor_tensor(out=dst[:, co:co + w], in0=XB,
                                in1=bck[:, bo:bo + w], op=ALU.mult)
                v.drain()
        self.step("vector", fv)

    def rms_scale_row(self, kind, gain_idx):
        p, c = self.p, self.c
        segs = self._attn_segs(kind)
        ncols = (c.T if kind == "q" else c.WIN + c.T)

        def fs(s):
            for ps, po, w, co in segs:
                dump = self.maxA if co < 1024 else self.maxB
                do = co if co < 1024 else co - 1024
                s.activation(dump[:, do:do + w], ps[:, po:po + w], ACT.Square)
        self.step("scalar", fs)

        def ft(te):
            for j in range(0, ncols, 512):
                w = min(512, ncols - j)
                src = (self.maxA[:, j:j + w] if j < 1024
                       else self.maxB[:, j - 1024:j - 1024 + w])
                pdst = (self.psC[0:1, j:j + w] if j < 1024
                        else self.psY[0:1, j - 1024:j - 1024 + w])
                te.matmul(pdst, self.ones_col[:], src, start=True, stop=True)
        self.step("tensor", ft)

        def fv(v):
            v.tensor_copy(out=self.rows[0:1, 0:min(ncols, 1024)],
                          in_=self.psC[0:1, 0:min(ncols, 1024)])
            if ncols > 1024:
                v.tensor_copy(out=self.rows[0:1, 1024:ncols],
                              in_=self.psY[0:1, 0:ncols - 1024])
            v.drain()
            v.tensor_scalar(out=self.rows[0:1, 0:ncols], in0=self.rows[0:1, 0:ncols],
                            scalar1=1.0 / c.HD, scalar2=c.EPS_QK,
                            op0=ALU.mult, op1=ALU.add)
        self.step("vector", fv)

        def fs2(s):
            s.activation(self.rows[0:1, 0:ncols], self.rows[0:1, 0:ncols], ACT.Sqrt)
        self.step("scalar", fs2)

        def fv2(v):
            v.reciprocal(out=self.rows[0:1, 0:ncols], in_=self.rows[0:1, 0:ncols])
            if gain_idx is not None:
                v.drain()
                v.tensor_scalar(out=self.rows[0:1, 0:ncols],
                                in0=self.rows[0:1, 0:ncols],
                                scalar1=self.qgain_sb[0:1, gain_idx:gain_idx + 1],
                                scalar2=float(1.0 / np.sqrt(c.HD)),
                                op0=ALU.mult, op1=ALU.mult)
        self.step("vector", fv2)

    def phase_attn(self):
        p, c, d = self.p, self.c, self.d
        NT, ND, NKT, HT, H = c.NT, c.ND, c.NKT, c.HT, c.H
        TW = c.T + c.WIN
        nk = NKT * 128
        nkv = c.KVH * c.HD // 128
        XN = self.XN

        def xn_own(k):
            return XN[:, k * TW: k * TW + c.T]

        def xn_halo(k):
            return XN[:, k * TW + c.T: k * TW + TW]

        self.rope_tables()

        slabs = ([("k", s_, d["wk"].rearrange("d (s c) -> s d c", c=128)[s_]) for s_ in range(nkv)]
                 + [("v", s_, d["wv"].rearrange("d (s c) -> s d c", c=128)[s_]) for s_ in range(nkv)])

        for kind, hh, sl in slabs:
            def fl(sync, sl=sl):
                p.sDW.dma(self.load_slab(sync, self.WSTG[0], sl))
            self.step("sync", fl)

            def fc(v):
                p.sV.inc(v.tensor_copy(out=self.WBF[0][:, 0:c.D],
                                       in_=self.WSTG[0][:, 0:c.D]))
            self.step("vector", fc)

            HOFF = ((c.WIN + 511) // 512) * 512

            def ft(te, kind=kind):
                for kk in range(ND):
                    lhs = self.WBF[0][:, kk * 128:(kk + 1) * 128]
                    if kind == "q":
                        srcs = [(xn_own(kk), 0)]
                    else:
                        srcs = [(xn_halo(kk), 0), (xn_own(kk), HOFF)]
                    for src, off in srcs:
                        w_all = src.shape[-1]
                        for j in range(0, w_all, 512):
                            w = min(512, w_all - j)
                            o = off + j
                            ps = (self.psA[:, o:o + w] if o < 1024 else
                                  self.psB[:, o - 1024:o - 1024 + w])
                            te.matmul(ps, lhs, src[:, j:j + w],
                                      start=(kk == 0), stop=(kk == ND - 1))
            self.step("tensor", ft)

            if kind == "v":
                def fv1(v):
                    HOFF = ((c.WIN + 511) // 512) * 512
                    v.tensor_copy(out=self.vTr[:, 0:c.WIN], in_=self.psA[:, 0:c.WIN])
                    for j in range(0, c.T, 512):
                        w = min(512, c.T - j)
                        o = HOFF + j
                        src = (self.psA[:, o:o + w] if o < 1024 else
                               self.psB[:, o - 1024:o - 1024 + w])
                        v.tensor_copy(out=self.vTr[:, c.WIN + j:c.WIN + j + w],
                                      in_=src)
                self.step("vector", fv1)

                def ftv(te):
                    for t in range(NKT):
                        ps = self.psC if t < 8 else self.psY
                        te.matmul(ps[:, (t % 8) * 128:(t % 8) * 128 + 128],
                                  self.vTr[:, t * 128:(t + 1) * 128], self.idb[:],
                                  start=True, stop=True)
                self.step("tensor", ftv)

                def fv2(v, hh=hh):
                    for t in range(NKT):
                        ps = self.psC if t < 8 else self.psY
                        v.tensor_copy(
                            out=self.V[:, t * (nkv * 128) + hh * 128:
                                       t * (nkv * 128) + hh * 128 + 128],
                            in_=ps[:, (t % 8) * 128:(t % 8) * 128 + 128])
                self.step("vector", fv2)
            else:
                ncols = c.WIN + c.T
                self.rms_scale_row(kind, None)
                self.bcast(self.rows[0:1, 0:min(ncols, 1024)],
                           self.bc1, min(ncols, 1024), self.psC)
                if ncols > 1024:
                    self.bcast(self.rows[0:1, 1024:ncols],
                               self.bc2, ncols - 1024, self.psY)
                self.rope_scale(kind, self.KRR[:, hh * nk:(hh + 1) * nk])
        self.barrier()

        for h in range(H):
            kv = h // (H // c.KVH)

            def fl(sync, h=h):
                p.sDW.dma(self.load_slab(sync, self.WSTG[0], d["wq"][h]))
            self.step("sync", fl)

            def fc(v):
                p.sV.inc(v.tensor_copy(out=self.WBF[0][:, 0:c.D],
                                       in_=self.WSTG[0][:, 0:c.D]))
            self.step("vector", fc)

            def ftq(te):
                for kk in range(ND):
                    for j in range(0, c.T, 512):
                        w = min(512, c.T - j)
                        te.matmul(self.psA[:, j:j + w],
                                  self.WBF[0][:, kk * 128:(kk + 1) * 128],
                                  xn_own(kk)[:, j:j + w],
                                  start=(kk == 0), stop=(kk == ND - 1))
            self.step("tensor", ftq)
            self.rms_scale_row("q", h)
            self.bcast(self.rows[0:1, 0:c.T], self.bc1, c.T, self.psC)
            self.rope_scale("q", self.QR[:, 0:c.T])

            def fve(v, h=h):
                p.sV.inc(v.tensor_scalar(
                    out=self.tscr[:, 0:NKT], in0=self.params_b[:, 3:3 + NKT],
                    scalar1=self.Mh_b[:, h:h + 1], scalar2=None, op0=ALU.subtract))
            self.step("vector", fve)

            for j in range(NKT):
                qlo, qhi = max(0, j - HT), min(NT - 1, j)
                wq_ = (qhi - qlo + 1) * 128
                slot = (j % c.W5) * (c.W5 * 128)

                def fts(te, j=j, kv=kv, qlo=qlo, wq_=wq_):
                    for o in range(0, wq_, 512):
                        w = min(512, wq_ - o)
                        te.matmul(self.psB[:, o:o + w],
                                  self.KRR[:, kv * nk + j * 128:kv * nk + (j + 1) * 128],
                                  self.QR[:, qlo * 128 + o:qlo * 128 + o + w],
                                  start=True, stop=True)
                self.step("tensor", fts)

                def fse(s, j=j, wq_=wq_, slot=slot):
                    s.activation(self.pr[:, slot:slot + wq_], self.psB[:, 0:wq_],
                                 ACT.Exp, bias=self.tscr[:, j:j + 1])
                self.step("scalar", fse)

                def fvm(v, j=j, qlo=qlo, qhi=qhi, slot=slot):
                    for qt in range(qlo, qhi + 1):
                        o = slot + (qt - qlo) * 128
                        if qt == j - HT:
                            v.tensor_tensor(out=self.pr[:, o:o + 128],
                                            in0=self.pr[:, o:o + 128],
                                            in1=self.mk_diag[:], op=ALU.mult)
                        elif qt == j:
                            v.tensor_tensor(out=self.pr[:, o:o + 128],
                                            in0=self.pr[:, o:o + 128],
                                            in1=self.mk_far[:], op=ALU.mult)
                self.step("vector", fvm)

                qt0 = j - HT
                if 0 <= qt0 <= NT - 1:
                    def ftp(te, qt0=qt0, kv=kv):
                        for kt in range(qt0, qt0 + HT + 1):
                            sl = (kt % c.W5) * (c.W5 * 128)
                            off = sl + (qt0 - max(0, kt - HT)) * 128
                            te.matmul(self.psY[:, qt0 * 128:(qt0 + 1) * 128],
                                      self.V[:, kt * (nkv * 128) + kv * 128:
                                             kt * (nkv * 128) + kv * 128 + 128],
                                      self.pr[:, off:off + 128],
                                      start=(kt == qt0), stop=(kt == qt0 + HT))
                        for kt in range(qt0, qt0 + HT + 1):
                            sl = (kt % c.W5) * (c.W5 * 128)
                            off = sl + (qt0 - max(0, kt - HT)) * 128
                            te.matmul(self.psC[0:1, qt0 * 128:(qt0 + 1) * 128],
                                      self.ones_col_b[:], self.pr[:, off:off + 128],
                                      start=(kt == qt0), stop=(kt == qt0 + HT))
                    self.step("tensor", ftp)

            def fvd(v):
                v.tensor_copy(out=self.rows[0:1, 0:c.T], in_=self.psC[0:1, 0:c.T])
                v.drain()
                v.reciprocal(out=self.rows[0:1, 0:c.T], in_=self.rows[0:1, 0:c.T])
            self.step("vector", fvd)
            self.bcast(self.rows[0:1, 0:c.T], self.bc1, c.T, self.psC)

            def fvy(v, h=h):
                v.tensor_tensor(out=self.Y[:, h * c.T:(h + 1) * c.T],
                                in0=self.psY[:, 0:c.T], in1=self.bc1[:, 0:c.T],
                                op=ALU.mult)
            self.step("vector", fvy)
        self.barrier()

        for dc in range(ND):
            def fl(sync, dc=dc):
                p.sDW.dma(self.load_slab(sync, self.WSTG[0], d["wo"][dc]))
            self.step("sync", fl)

            def fc(v):
                p.sV.inc(v.tensor_copy(out=self.WBF[0][:, 0:c.D],
                                       in_=self.WSTG[0][:, 0:c.D]))
            self.step("vector", fc)

            def ft(te):
                for kk in range(ND):
                    for j in range(0, c.T, 512):
                        w = min(512, c.T - j)
                        te.matmul(self.psA[:, j:j + w],
                                  self.WBF[0][:, kk * 128:(kk + 1) * 128],
                                  self.Y[:, kk * c.T + j:kk * c.T + j + w],
                                  start=(kk == 0), stop=(kk == ND - 1))
            self.step("tensor", ft)

            def fv(v):
                p.sV.inc(v.tensor_copy(out=self.sR[:, 0:c.T], in_=self.psA[:, 0:c.T]))
            self.step("vector", fv)

            def fst(sync, dc=dc):
                p.sST.dma(sync.dma_start(out=d["sT_sp"][dc * 128:(dc + 1) * 128, :],
                                         in_=self.sR[:, 0:c.T]))
            self.step("sync", fst)
        self.barrier()

    def bcast2(self, row_ap, out_sb, out_off, n, psum):
        p = self.p

        def ft(te):
            self._twait(te)
            for j in range(0, n, 512):
                w = min(512, n - j)
                p.sT.inc(te.matmul(psum[:, j:j + w], self.ones_row[:, 0:128],
                                   row_ap[:, j:j + w], start=True, stop=True))
        self.on("tensor", ft)
        mark = p.sT.n

        def fv(v):
            v.wait_ge(p.sT.h, mark)
            p.sV.inc(v.tensor_copy(out=out_sb[:, out_off:out_off + n],
                                   in_=psum[:, 0:n]))
        self.on("vector", fv)


    def phase_ffn0_stats(self):
        p, c, d = self.p, self.c, self.d
        mats = [(0, d["wg0"], c.NF, c.D), (1, d["wu0"], c.NF, c.D),
                (2, d["wd0"], c.ND, c.F)]

        def fz(v):
            v.memset(self.tscr[:, 0:8], 0.0)
        self.step("vector", fz)

        for m, dram, nsl, R in mats:
            qn = R // 2048 if R > 2048 else 1
            Rq = min(R, 2048)
            idx = 0
            loadm, actm, vmm = {}, {}, {}
            for i in range(nsl):
                for q4 in range(qn):
                    par = idx % 2

                    def fl(s_, i=i, q4=q4, par=par, dram=dram, Rq=Rq, idx=idx):
                        if idx >= 2:
                            s_.wait_ge(p.sS.h, actm[idx - 2])
                        sl = dram[i, q4 * Rq:(q4 + 1) * Rq, :]
                        sem = p.sDS if par == 0 else p.sDSb
                        trio = (p.sDS, p.sDSb)
                        if sem.n > 0:
                            s_.wait_ge(sem.h, sem.n)
                        if idx < 2:
                            for s__ in trio:
                                if s__ is not sem and s__.n > 0:
                                    s_.wait_ge(s__.h, s__.n)
                            for s__ in (p.sV, p.sS):
                                if s__.n > 0:
                                    s_.wait_ge(s__.h, s__.n)
                        sem.dma(self.load_slab(s_, self.WSTG[par], sl))
                    self.on("scalar", fl)
                    loadm[idx] = ((p.sDS if par == 0 else p.sDSb),
                                  (p.sDS if par == 0 else p.sDSb).n)

                    def fa(s_, idx=idx, par=par, m=m, Rq=Rq):
                        s_.wait_ge(loadm[idx][0].h, loadm[idx][1])
                        if idx >= 1:
                            s_.wait_ge(p.sV.h, vmm[idx - 1])
                        for hh in range(0, Rq, 1024):
                            w = min(1024, Rq - hh)
                            dump = self.maxA if par == 0 else self.maxB
                            s_.activation(
                                dump[:, 0:w], self.WSTG[par][:, hh:hh + w],
                                ACT.Abs,
                                accum_out=self.tscr[:, 16 + 2 * par + hh // 1024:
                                                    17 + 2 * par + hh // 1024])
                        p.sS.inc(s_.drain())
                    self.on("scalar", fa)
                    actm[idx] = p.sS.n

                    def fv(v, idx=idx, par=par, m=m, Rq=Rq):
                        v.wait_ge(p.sS.h, actm[idx])
                        for hh in range(0, Rq, 1024):
                            v.tensor_tensor(out=self.tscr[:, m:m + 1],
                                            in0=self.tscr[:, m:m + 1],
                                            in1=self.tscr[:, 16 + 2 * par + hh // 1024:
                                                          17 + 2 * par + hh // 1024],
                                            op=ALU.add)
                            v.drain()
                        p.sV.inc(v.drain())
                    self.on("vector", fv)
                    vmm[idx] = p.sV.n
                    idx += 1
        self.barrier()

        def ft(te):
            te.matmul(self.psC[0:1, 0:3], self.ones_col[:], self.tscr[:, 0:3],
                      start=True, stop=True)
        self.step("tensor", ft)

        def fv2(v):
            counts = [float(c.F * c.D)] * 3
            for m in range(3):
                v.tensor_scalar(out=self.rowsE[0:1, 256 + m:257 + m],
                                in0=self.psC[0:1, m:m + 1],
                                scalar1=1.0 / counts[m], scalar2=1e-5,
                                op0=ALU.mult, op1=ALU.max)
            v.drain()
            v.reciprocal(out=self.rowsE[0:1, 260:263], in_=self.rowsE[0:1, 256:259])
        self.step("vector", fv2)

        def ft2(te):
            te.matmul(self.psC[:, 0:3], self.ones_row[:, 0:128],
                      self.rowsE[0:1, 260:263], start=True, stop=True)
        self.step("tensor", ft2)

        def fv3(v):
            v.tensor_copy(out=self.srep[:, 0:3], in_=self.psC[:, 0:3])
        self.step("vector", fv3)

    def ffn_gateup(self, wg, wu, rhs_buf, H, conv_g, conv_u, epi, tn):
        p, c = self.p, self.c
        loadm, convm, usem = {}, {}, {}
        epim = [0]
        for fc in range(c.NF):
            for kind in (0, 1):
                i = 2 * fc + kind
                par = i % 2
                sl = (wg if kind == 0 else wu)[fc]
                conv = conv_g if kind == 0 else conv_u

                def fl(sync, i=i, par=par, sl=sl):
                    if i >= 2:
                        sync.wait_ge(p.sV.h, convm[i - 2])
                    sem = p.sDW if par == 0 else p.sDWb
                    trio = (p.sDW, p.sDWb, p.sST)
                    if sem.n > 0:
                        sync.wait_ge(sem.h, sem.n)
                    if i < 2:
                        for s__ in trio:
                            if s__ is not sem and s__.n > 0:
                                sync.wait_ge(s__.h, s__.n)
                        for s__ in (p.sV, p.sS):
                            if s__.n > 0:
                                sync.wait_ge(s__.h, s__.n)
                    sem.dma(self.load_slab(sync, self.WSTG[par], sl))
                self.on("sync", fl)
                loadm[i] = ((p.sDW if par == 0 else p.sDWb), (p.sDW if par == 0 else p.sDWb).n)

                def fc_(v, i=i, par=par, conv=conv):
                    v.wait_ge(loadm[i][0].h, loadm[i][1])
                    if i >= 2:
                        v.wait_ge(p.sT.h, usem[i - 2])
                    conv(v, i, self.WSTG[par], self.WBF[par], c.D)
                self.on("vector", fc_)
                convm[i] = p.sV.n

                def fu(te, i=i, par=par, kind=kind):
                    te.wait_ge(p.sV.h, convm[i])
                    te.wait_ge(p.sV.h, epim[0])
                    ps = self.psA if kind == 0 else self.psB
                    for kk in range(c.ND):
                        for j in range(0, tn, 512):
                            w = min(512, tn - j)
                            te.matmul(ps[:, j:j + w],
                                      self.WBF[par][:, kk * 128:(kk + 1) * 128],
                                      rhs_buf[:, kk * tn + j:kk * tn + j + w],
                                      start=(kk == 0), stop=(kk == c.ND - 1))
                    p.sT.inc(te.drain())
                self.on("tensor", fu)
                usem[i] = p.sT.n
            epi(fc)
            epim[0] = p.sV.n
        self.barrier()

    def phase_ffn0(self, H, t0, tn):
        p, c, d = self.p, self.c, self.d
        NT, ND, NF = c.NT, c.ND, c.NF
        S = c.NT
        tt0, nt2 = t0 // 128, tn // 128
        TB = 8 * c.NT

        def stat_ap(slot, off=0, n=None):
            n = nt2 if n is None else n
            return self.stat[:, slot * S + tt0 + off:slot * S + tt0 + off + n]

        self.qx_pass(d["x_own"], 0, 2, t0, tn, self.QX)

        self.col2row(self.stat[:, 3 * S:3 * S + NT], 0, NT, self.psC)

        def fvr(v):
            v.tensor_scalar(out=self.rows[0:1, 1024 + t0:1024 + t0 + tn],
                            in0=self.rows[0:1, t0:t0 + tn],
                            scalar1=self.rowsE[0:1, 257:258], scalar2=None,
                            op0=ALU.mult)
            v.drain()
            v.tensor_scalar(out=self.rows[0:1, t0:t0 + tn],
                            in0=self.rows[0:1, t0:t0 + tn],
                            scalar1=self.rowsE[0:1, 256:257], scalar2=None,
                            op0=ALU.mult)
        self.step("vector", fvr)
        self.bcast(self.rows[0:1, t0:t0 + tn], self.bc1, tn, self.psC)
        self.bcast(self.rows[0:1, 1024 + t0:1024 + t0 + tn], self.bc2, tn, self.psC)

        conv_g = self.make_conv_tern_idx(0)
        conv_u = self.make_conv_tern_idx(1)

        def epi(fc):
            def s1(v, fc=fc):
                v.tensor_tensor(out=self.scr2[:, 0:tn], in0=self.psA[:, 0:tn],
                                in1=self.bc1[:, 0:tn], op=ALU.mult)
                v.tensor_tensor(out=self.XR[:, 0:tn], in0=self.psB[:, 0:tn],
                                in1=self.bc2[:, 0:tn], op=ALU.mult)
            self.step("vector", s1)

            def s2(s, fc=fc):
                s.activation(self.tscr[:, 0:tn], self.scr2[:, 0:tn], ACT.Silu)
            self.step("scalar", s2)

            def s3(v, fc=fc):
                v.tensor_tensor(out=H[:, fc * tn:(fc + 1) * tn],
                                in0=self.tscr[:, 0:tn], in1=self.XR[:, 0:tn],
                                op=ALU.mult)
                v.drain()
                v.tensor_scalar(out=self.scr2[:, 0:tn],
                                in0=H[:, fc * tn:(fc + 1) * tn],
                                scalar1=self.lnw_sb[:, fc:fc + 1], scalar2=None,
                                op0=ALU.mult)
                v.drain()
                if fc == 0:
                    v.tensor_scalar(out=self.maxA[:, 0:tn], in0=self.scr2[:, 0:tn],
                                    scalar1=1.0, scalar2=None, op0=ALU.mult)
                    v.tensor_scalar(out=self.maxB[:, 0:tn], in0=self.scr2[:, 0:tn],
                                    scalar1=-1.0, scalar2=None, op0=ALU.mult)
                else:
                    v.scalar_tensor_tensor(out=self.maxA[:, 0:tn],
                                           in0=self.scr2[:, 0:tn], scalar=1.0,
                                           in1=self.maxA[:, 0:tn], op0=ALU.mult,
                                           op1=ALU.max)
                    v.scalar_tensor_tensor(out=self.maxB[:, 0:tn],
                                           in0=self.scr2[:, 0:tn], scalar=-1.0,
                                           in1=self.maxB[:, 0:tn], op0=ALU.mult,
                                           op1=ALU.max)
            self.step("vector", s3)

            def s4(s, fc=fc):
                s.activation(self.tscr[:, 0:tn], H[:, fc * tn:(fc + 1) * tn],
                             ACT.Square)
            self.step("scalar", s4)

            def s5(te, fc=fc):
                for j in range(0, tn, 512):
                    w = min(512, tn - j)
                    te.matmul(self.psC[0:1, j:j + w], self.ones_col[:],
                              self.tscr[:, j:j + w], start=(fc == 0),
                              stop=(fc == NF - 1))
            self.step("tensor", s5)

        self.ffn_gateup(d["wg0"], d["wu0"], self.QX, H, conv_g, conv_u, epi, tn)

        fin_max = self.maxA

        def fq1(v):
            v.tensor_tensor(out=self.maxA[:, 0:tn], in0=self.maxA[:, 0:tn],
                            in1=self.maxB[:, 0:tn], op=ALU.max)
            v.drain()
            v.tensor_copy(out=self.rows[0:1, 0:tn], in_=self.psC[0:1, 0:tn])
        self.step("vector", fq1)

        def fq2(te):
            for t in range(nt2):
                te.transpose(self.psB[:, t:t + 1],
                             self.rows[0:1, t * 128:(t + 1) * 128], self.idf[0:1, 0:1])
            for t in range(nt2):
                te.transpose(self.psY[:, t * 128:(t + 1) * 128],
                             fin_max[:, t * 128:(t + 1) * 128], self.idf[:])
        self.step("tensor", fq2)

        def fq3(v):
            st = self.stat
            for t in range(nt2):
                v.reduce_max(st[:, TB + t:TB + t + 1], self.psY[:, t * 128:(t + 1) * 128],
                             axis=AX)
            v.drain()
            v.tensor_scalar(out=st[:, TB + 8:TB + 8 + nt2], in0=self.psB[:, 0:nt2],
                            scalar1=1.0 / c.F, scalar2=c.EPS_FFN0_RMS,
                            op0=ALU.mult, op1=ALU.add)
        self.step("vector", fq3)

        def fq4(s):
            s.activation(self.stat[:, TB + 8:TB + 8 + nt2], self.stat[:, TB + 8:TB + 8 + nt2],
                         ACT.Sqrt)
        self.step("scalar", fq4)

        def fq5(v):
            st = self.stat
            v.reciprocal(out=st[:, TB + 8:TB + 8 + nt2], in_=st[:, TB + 8:TB + 8 + nt2])
            v.drain()
            v.tensor_tensor(out=st[:, TB + 16:TB + 16 + nt2], in0=st[:, TB + 8:TB + 8 + nt2],
                            in1=st[:, TB:TB + nt2], op=ALU.mult)
            v.drain()
            v.tensor_scalar(out=st[:, TB + 16:TB + 16 + nt2], in0=st[:, TB + 16:TB + 16 + nt2],
                            scalar1=c.EPS_FFN0_AQ, scalar2=None, op0=ALU.max)
            v.drain()
            v.reciprocal(out=st[:, TB + 24:TB + 24 + nt2], in_=st[:, TB + 16:TB + 16 + nt2])
            v.drain()
            v.tensor_scalar(out=st[:, TB + 24:TB + 24 + nt2], in0=st[:, TB + 24:TB + 24 + nt2],
                            scalar1=127.0, scalar2=None, op0=ALU.mult)
            v.drain()
            v.tensor_tensor(out=st[:, TB + 24:TB + 24 + nt2], in0=st[:, TB + 24:TB + 24 + nt2],
                            in1=st[:, TB + 8:TB + 8 + nt2], op=ALU.mult)
            v.drain()
            v.tensor_scalar(out=st[:, TB + 16:TB + 16 + nt2], in0=st[:, TB + 16:TB + 16 + nt2],
                            scalar1=1.0 / 127.0, scalar2=None, op0=ALU.mult)
        self.step("vector", fq5)

        self.col2row(self.stat[:, TB + 24:TB + 24 + nt2], 0, nt2, self.psC)
        self.col2row(self.stat[:, TB + 16:TB + 16 + nt2], 1024, nt2, self.psC)

        def fq6(v):
            v.tensor_scalar(out=self.rows[0:1, 1024:1024 + tn],
                            in0=self.rows[0:1, 1024:1024 + tn],
                            scalar1=self.rowsE[0:1, 258:259], scalar2=None,
                            op0=ALU.mult)
        self.step("vector", fq6)
        self.bcast(self.rows[0:1, 0:tn], self.bc1, tn, self.psC)
        self.bcast(self.rows[0:1, 1024:1024 + tn], self.bc2, tn, self.psC)

        for fc in range(NF):
            def fqz(v, fc=fc):
                hsl = H[:, fc * tn:(fc + 1) * tn]
                v.tensor_scalar(out=self.scr2[:, 0:tn], in0=hsl,
                                scalar1=self.lnw_sb[:, fc:fc + 1], scalar2=None,
                                op0=ALU.mult)
                v.drain()
                v.tensor_tensor(out=self.scr2[:, 0:tn], in0=self.scr2[:, 0:tn],
                                in1=self.bc1[:, 0:tn], op=ALU.mult)
                v.drain()
                v.tensor_scalar(out=hsl, in0=self.scr2[:, 0:tn], scalar1=MAGIC,
                                scalar2=MAGIC, op0=ALU.add, op1=ALU.subtract)
            self.step("vector", fqz)
        self.barrier()

    def phase_down_combine(self, H, wd, conv, unscale_bc, layer, t0, tn):
        p, c, d = self.p, self.c, self.d
        ND, NF, NT = c.ND, c.NF, c.NT
        xsrc = d["x_own"] if layer == 0 else d["xT_sp"]
        loadm, convm, usem = {}, {}, {}
        epim = [0]
        idx = 0
        NQ4 = max(1, c.F // 2048)
        RQ4 = min(2048, c.F)
        for dc in range(ND):
            for q4 in range(NQ4):
                par = idx % 2

                def fl(sync, dc=dc, q4=q4, par=par, idx=idx):
                    if idx >= 2:
                        sync.wait_ge(p.sV.h, convm[idx - 2])
                    sl = wd[dc, q4 * RQ4:(q4 + 1) * RQ4, :]
                    sem = p.sDW if par == 0 else p.sDWb
                    trio = (p.sDW, p.sDWb, p.sST)
                    if sem.n > 0:
                        sync.wait_ge(sem.h, sem.n)
                    if idx < 2:
                        for s__ in trio:
                            if s__ is not sem and s__.n > 0:
                                sync.wait_ge(s__.h, s__.n)
                        for s__ in (p.sV, p.sS):
                            if s__.n > 0:
                                sync.wait_ge(s__.h, s__.n)
                    sem.dma(self.load_slab(sync, self.WSTG[par], sl))
                self.on("sync", fl)
                loadm[idx] = ((p.sDW if par == 0 else p.sDWb), (p.sDW if par == 0 else p.sDWb).n)

                def fc_(v, idx=idx, par=par):
                    v.wait_ge(loadm[idx][0].h, loadm[idx][1])
                    if idx >= 2:
                        v.wait_ge(p.sT.h, usem[idx - 2])
                    conv(v, idx, self.WSTG[par], self.WBF[par], RQ4)
                self.on("vector", fc_)
                convm[idx] = p.sV.n

                def fu(te, dc=dc, q4=q4, par=par, idx=idx):
                    te.wait_ge(p.sV.h, convm[idx])
                    if q4 == 0:
                        te.wait_ge(p.sV.h, epim[0])
                    ps = self.psA if dc % 2 == 0 else self.psB
                    for kk in range(RQ4 // 128):
                        fq = q4 * (RQ4 // 128) + kk
                        for j in range(0, tn, 512):
                            w = min(512, tn - j)
                            te.matmul(ps[:, j:j + w],
                                      self.WBF[par][:, kk * 128:(kk + 1) * 128],
                                      H[:, fq * tn + j:fq * tn + j + w],
                                      start=(fq == 0), stop=(fq == NF - 1))
                    p.sT.inc(te.drain())
                self.on("tensor", fu)
                usem[idx] = p.sT.n
                idx += 1

            ps = self.psA if dc % 2 == 0 else self.psB

            def e0(sync, dc=dc):
                p.sDW.dma(sync.dma_start(
                    out=self.sR[:, 0:tn],
                    in_=d["sT_sp"][dc * 128:(dc + 1) * 128, t0:t0 + tn]))
            self.step("sync", e0)

            def e0b(g, dc=dc):
                p.sDG.dma(g.dma_start(out=self.XR[:, 0:tn],
                                      in_=xsrc[dc * 128:(dc + 1) * 128, t0:t0 + tn]))
            self.step("gpsimd", e0b)

            def e1(v, dc=dc, ps=ps):
                v.tensor_tensor(out=self.scr2[:, 0:tn], in0=ps[:, 0:tn],
                                in1=unscale_bc[:, 0:tn], op=ALU.mult)
            self.step("vector", e1)

            def e2(s, dc=dc):
                s.activation(self.tscr[:, 0:tn], self.scr2[:, 0:tn], ACT.Silu)
            self.step("scalar", e2)

            def e3(v, dc=dc):
                v.tensor_tensor(out=self.scr2[:, 0:tn], in0=self.sR[:, 0:tn],
                                in1=self.tscr[:, 0:tn], op=ALU.mult)
                v.drain()
                v.tensor_tensor(out=self.XR[:, 1024:1024 + tn],
                                in0=self.XR[:, 0:tn], in1=self.scr2[:, 0:tn],
                                op=ALU.add)
            self.step("vector", e3)

            if layer == 0:
                def e4(s, dc=dc):
                    s.activation(self.tscr[:, 0:tn], self.XR[:, 1024:1024 + tn],
                                 ACT.Square)
                self.step("scalar", e4)

                def e5(te, dc=dc):
                    for j in range(0, tn, 512):
                        w = min(512, tn - j)
                        te.matmul(self.psC[0:1, t0 + j:t0 + j + w], self.ones_col[:],
                                  self.XR[:, 1024 + j:1024 + j + w],
                                  start=(dc == 0), stop=(dc == ND - 1))
                        te.matmul(self.psY[0:1, t0 + j:t0 + j + w], self.ones_col[:],
                                  self.tscr[:, j:j + w],
                                  start=(dc == 0), stop=(dc == ND - 1))
                self.step("tensor", e5)

            def e6(g, dc=dc):
                dst = d["xT_sp"] if layer == 0 else d["y_out"]
                p.sSTG.dma(g.dma_start(out=dst[dc * 128:(dc + 1) * 128, t0:t0 + tn],
                                       in_=self.XR[:, 1024:1024 + tn]))
            self.step("gpsimd", e6)
            epim[0] = p.sV.n
        self.barrier()


    def l1_stats(self, t0, tn):
        p, c = self.p, self.c
        S = c.NT
        tt0, nt2 = t0 // 128, tn // 128

        def f0(v):
            v.tensor_copy(out=self.rows[0:1, 0:tn], in_=self.psC[0:1, t0:t0 + tn])
            v.tensor_copy(out=self.rows[0:1, 1024:1024 + tn],
                          in_=self.psY[0:1, t0:t0 + tn])
        self.step("vector", f0)

        def f1(te):
            for t in range(nt2):
                te.transpose(self.psC[:, t:t + 1],
                             self.rows[0:1, t * 128:(t + 1) * 128], self.idf[0:1, 0:1])
                te.transpose(self.psB[:, t:t + 1],
                             self.rows[0:1, 1024 + t * 128:1024 + (t + 1) * 128],
                             self.idf[0:1, 0:1])
        self.step("tensor", f1)

        def f2(v):
            st = self.stat
            v.tensor_scalar(out=st[:, 4 * S + tt0:4 * S + tt0 + nt2],
                            in0=self.psC[:, 0:nt2],
                            scalar1=1.0 / c.D, scalar2=None, op0=ALU.mult)
            v.tensor_scalar(out=st[:, 5 * S + tt0:5 * S + tt0 + nt2],
                            in0=self.psB[:, 0:nt2],
                            scalar1=1.0 / c.D, scalar2=None, op0=ALU.mult)
            v.drain()
            v.tensor_tensor(out=self.tscr[:, 0:nt2],
                            in0=st[:, 4 * S + tt0:4 * S + tt0 + nt2],
                            in1=st[:, 4 * S + tt0:4 * S + tt0 + nt2], op=ALU.mult)
            v.drain()
            v.tensor_tensor(out=self.tscr[:, 0:nt2],
                            in0=st[:, 5 * S + tt0:5 * S + tt0 + nt2],
                            in1=self.tscr[:, 0:nt2], op=ALU.subtract)
            v.drain()
            v.tensor_scalar(out=self.tscr[:, 0:nt2], in0=self.tscr[:, 0:nt2],
                            scalar1=c.EPS_LN, scalar2=None, op0=ALU.add)
        self.step("vector", f2)

        def f3(s):
            s.activation(self.tscr[:, 0:nt2], self.tscr[:, 0:nt2], ACT.Sqrt)
        self.step("scalar", f3)

        def f4(v):
            v.reciprocal(out=self.stat[:, 5 * S + t0 // 128:5 * S + t0 // 128 + nt2],
                         in_=self.tscr[:, 0:nt2])
        self.step("vector", f4)

    def phase_l1_entry(self, XN1):
        p, c, d = self.p, self.c, self.d
        NT, ND = c.NT, c.ND
        S = c.NT

        def stat_ap(slot):
            return self.stat[:, slot * S:slot * S + NT]

        self.col2row(stat_ap(4), 0, NT, self.psC)
        self.col2row(stat_ap(5), 1024, NT, self.psC)
        self.bcast(self.rows[0:1, 0:c.T], self.bc1, c.T, self.psC)
        self.bcast(self.rows[0:1, 1024:1024 + c.T], self.bc2, c.T, self.psC)

        dm, vm, tm = {}, {}, {}
        for k in range(ND):
            par = k % 2

            def fl(g, k=k, par=par):
                if k >= 2:
                    g.wait_ge(p.sT.h, tm[k - 2])
                sem = p.sDG if par == 0 else p.sDGb
                trio = (p.sDG, p.sDGb, p.sSTG)
                if sem.n > 0:
                    g.wait_ge(sem.h, sem.n)
                if k < 2:
                    for s__ in trio:
                        if s__ is not sem and s__.n > 0:
                            g.wait_ge(s__.h, s__.n)
                    for s__ in (p.sV, p.sS):
                        if s__.n > 0:
                            g.wait_ge(s__.h, s__.n)
                sem.dma(g.dma_start(out=self.XR[:, par * 1024:par * 1024 + c.T],
                                    in_=d["xT_sp"][k * 128:(k + 1) * 128, :]))
            self.on("gpsimd", fl)
            dm[k] = ((p.sDG if par == 0 else p.sDGb), (p.sDG if par == 0 else p.sDGb).n)

            def fv(v, k=k, par=par):
                v.wait_ge(dm[k][0].h, dm[k][1])
                if k >= 1:
                    v.wait_ge(p.sT.h, tm[k - 1])
                v.drain()
                xr = self.XR[:, par * 1024:par * 1024 + c.T]
                v.tensor_tensor(out=self.scr2[:, 0:c.T], in0=xr,
                                in1=self.bc1[:, 0:c.T], op=ALU.subtract)
                v.drain()
                v.tensor_tensor(out=self.scr2[:, 0:c.T], in0=self.scr2[:, 0:c.T],
                                in1=self.bc2[:, 0:c.T], op=ALU.mult)
                v.drain()
                v.tensor_copy(out=XN1[:, k * c.T:(k + 1) * c.T],
                              in_=self.scr2[:, 0:c.T])
                p.sV.inc(v.tensor_tensor(out=self.tscr[:, 0:c.T],
                                         in0=self.scr2[:, 0:c.T],
                                         in1=self.scr2[:, 0:c.T], op=ALU.mult))
            self.on("vector", fv)
            vm[k] = p.sV.n

            def ft(te, k=k, par=par):
                te.wait_ge(p.sV.h, vm[k])
                for j in range(0, c.T, 512):
                    w = min(512, c.T - j)
                    te.matmul(self.psC[0:1, j:j + w], self.ones_col[:],
                              self.scr2[:, j:j + w], start=(k == 0),
                              stop=(k == ND - 1))
                    p.sT.inc(te.matmul(self.psY[0:1, j:j + w], self.ones_col[:],
                                       self.tscr[:, j:j + w], start=(k == 0),
                                       stop=(k == ND - 1)))
            self.on("tensor", ft)
            tm[k] = p.sT.n
        self.barrier()

        def g0(v):
            v.tensor_copy(out=self.rows[0:1, 0:c.T], in_=self.psC[0:1, 0:c.T])
            v.tensor_copy(out=self.rows[0:1, 1024:1024 + c.T], in_=self.psY[0:1, 0:c.T])
        self.step("vector", g0)

        def g1(te):
            for t in range(NT):
                te.transpose(self.psC[:, t:t + 1],
                             self.rows[0:1, t * 128:(t + 1) * 128], self.idf[0:1, 0:1])
                te.transpose(self.psB[:, t:t + 1],
                             self.rows[0:1, 1024 + t * 128:1024 + (t + 1) * 128],
                             self.idf[0:1, 0:1])
        self.step("tensor", g1)

        def g2(v):
            st = self.stat
            v.tensor_scalar(out=st[:, 6 * S:6 * S + NT], in0=self.psC[:, 0:NT],
                            scalar1=1.0 / c.D, scalar2=None, op0=ALU.mult)
            v.tensor_scalar(out=self.tscr[:, 0:NT], in0=self.psB[:, 0:NT],
                            scalar1=1.0 / c.D, scalar2=None, op0=ALU.mult)
            v.drain()
            v.tensor_tensor(out=self.tscr[:, 64:64 + NT], in0=st[:, 6 * S:6 * S + NT],
                            in1=st[:, 6 * S:6 * S + NT], op=ALU.mult)
            v.drain()
            v.tensor_tensor(out=self.tscr[:, 0:NT], in0=self.tscr[:, 0:NT],
                            in1=self.tscr[:, 64:64 + NT], op=ALU.subtract)
            v.drain()
            v.tensor_scalar(out=self.tscr[:, 0:NT], in0=self.tscr[:, 0:NT],
                            scalar1=c.EPS_FFN1_RMS, scalar2=None, op0=ALU.add)
        self.step("vector", g2)

        def g3(s):
            s.activation(self.tscr[:, 0:NT], self.tscr[:, 0:NT], ACT.Sqrt)
        self.step("scalar", g3)

        def g4(v):
            v.reciprocal(out=self.stat[:, 7 * S:7 * S + NT], in_=self.tscr[:, 0:NT])
        self.step("vector", g4)

        self.barrier()

    def phase_l1_xq(self):
        p, c, d = self.p, self.c, self.d
        NT, ND = c.NT, c.ND
        S = c.NT
        st = self.stat

        def sl(k):
            return st[:, k * S:k * S + NT]

        def p0(v):
            v.tensor_tensor(out=self.tscr[:, 0:NT], in0=sl(4), in1=sl(5),
                            op=ALU.mult)
            v.drain()
            v.tensor_tensor(out=self.tscr[:, 0:NT], in0=self.tscr[:, 0:NT],
                            in1=sl(6), op=ALU.add)
        self.step("vector", p0)
        self.col2row(sl(5), 0, NT, self.psC)
        self.col2row(self.tscr[:, 0:NT], 1024, NT, self.psC)
        self.bcast(self.rows[0:1, 0:c.T], self.bc1, c.T, self.psC)
        self.bcast(self.rows[0:1, 1024:1024 + c.T], self.bc2, c.T, self.psC)

        dm, vm = {}, {}
        for k in range(ND):
            par = k % 2

            def fl(g, k=k, par=par):
                if k >= 2:
                    g.wait_ge(p.sV.h, vm[k - 2])
                sem = p.sDG if par == 0 else p.sDGb
                trio = (p.sDG, p.sDGb, p.sSTG)
                if sem.n > 0:
                    g.wait_ge(sem.h, sem.n)
                if k < 2:
                    for s__ in trio:
                        if s__ is not sem and s__.n > 0:
                            g.wait_ge(s__.h, s__.n)
                    for s__ in (p.sV, p.sS):
                        if s__.n > 0:
                            g.wait_ge(s__.h, s__.n)
                sem.dma(g.dma_start(out=self.XR[:, par * 1024:par * 1024 + c.T],
                                      in_=d["xT_sp"][k * 128:(k + 1) * 128, :]))
            self.on("gpsimd", fl)
            dm[k] = ((p.sDG if par == 0 else p.sDGb), (p.sDG if par == 0 else p.sDGb).n)

            def fv(v, k=k, par=par):
                v.wait_ge(dm[k][0].h, dm[k][1])
                v.drain()
                xr = self.XR[:, par * 1024:par * 1024 + c.T]
                v.tensor_tensor(out=self.scr2[:, 0:c.T], in0=xr,
                                in1=self.bc1[:, 0:c.T], op=ALU.mult)
                v.drain()
                v.tensor_tensor(out=self.scr2[:, 0:c.T], in0=self.scr2[:, 0:c.T],
                                in1=self.bc2[:, 0:c.T], op=ALU.subtract)
                v.drain()
                if k == 0:
                    v.tensor_scalar(out=self.maxA[:, 0:c.T], in0=self.scr2[:, 0:c.T],
                                    scalar1=1.0, scalar2=None, op0=ALU.mult)
                    v.tensor_scalar(out=self.maxB[:, 0:c.T], in0=self.scr2[:, 0:c.T],
                                    scalar1=-1.0, scalar2=None, op0=ALU.mult)
                else:
                    v.scalar_tensor_tensor(out=self.maxA[:, 0:c.T],
                                           in0=self.scr2[:, 0:c.T], scalar=1.0,
                                           in1=self.maxA[:, 0:c.T], op0=ALU.mult,
                                           op1=ALU.max)
                    v.scalar_tensor_tensor(out=self.maxB[:, 0:c.T],
                                           in0=self.scr2[:, 0:c.T], scalar=-1.0,
                                           in1=self.maxB[:, 0:c.T], op0=ALU.mult,
                                           op1=ALU.max)
                p.sV.inc(v.drain())
            self.on("vector", fv)
            vm[k] = p.sV.n
        self.barrier()

        def h0(v):
            v.tensor_tensor(out=self.maxA[:, 0:c.T], in0=self.maxA[:, 0:c.T],
                            in1=self.maxB[:, 0:c.T], op=ALU.max)
        self.step("vector", h0)

        def h1(te):
            for t in range(NT):
                te.transpose(self.psY[:, t * 128:(t + 1) * 128],
                             self.maxA[:, t * 128:(t + 1) * 128], self.idf[:])
        self.step("tensor", h1)

        def h2(v):
            for t in range(NT):
                v.reduce_max(self.tscr[:, t:t + 1], self.psY[:, t * 128:(t + 1) * 128],
                             axis=AX)
            v.drain()
            v.reciprocal(out=self.tscr[:, 64:64 + NT], in_=self.tscr[:, 0:NT])
            v.drain()
            v.tensor_scalar(out=sl(2), in0=self.tscr[:, 64:64 + NT],
                            scalar1=127.0, scalar2=None, op0=ALU.mult)
            v.tensor_tensor(out=self.tscr[:, 0:NT], in0=self.tscr[:, 0:NT],
                            in1=sl(7), op=ALU.mult)
            v.drain()
            v.tensor_scalar(out=sl(3), in0=self.tscr[:, 0:NT],
                            scalar1=1.0 / 127.0, scalar2=None, op0=ALU.mult)
            v.drain()
            v.tensor_tensor(out=sl(1), in0=sl(5), in1=sl(2), op=ALU.mult)
            v.drain()
            v.tensor_tensor(out=self.tscr[:, 0:NT], in0=sl(4), in1=sl(1),
                            op=ALU.mult)
            v.tensor_tensor(out=self.tscr[:, 64:64 + NT], in0=sl(6), in1=sl(2),
                            op=ALU.mult)
            v.drain()
            v.tensor_tensor(out=sl(0), in0=self.tscr[:, 0:NT],
                            in1=self.tscr[:, 64:64 + NT], op=ALU.add)
        self.step("vector", h2)
        self.barrier()

    def phase_l1_proj(self, XN1, RAWP, DT2, ZGR):
        p, c, d = self.p, self.c, self.d
        NT, ND, NCH, H1, KVH1 = c.NT, c.ND, c.NCH, c.H1, c.KVH1
        YCH = self.XR

        def gn0(sync):
            p.sDW.dma(sync.dma_start(out=self.rows[0:1, 0:c.ZGD], in_=d["gnw"][:]))
        self.step("sync", gn0)
        self.bcast(self.rows[0:1, 0:min(c.ZGD, 1024)], self.gnw_b,
                   min(c.ZGD, 1024), self.psC)
        if c.ZGD > 1024:
            self.bcast2(self.rows[0:1, 1024:c.ZGD], self.gnw_b, 1024,
                        c.ZGD - 1024, self.psC)
        self.barrier()
        RS = c.T + 32
        NSL = c.NC1 + 1
        VCH = c.KVH1 * c.P1 // 128
        QOF = 2 * VCH

        def raw(j):
            return RAWP[:, j * RS:j * RS + c.T + 1]

        def post(j):
            s_ = (j - 1) if j > 0 else (NSL - 1)
            return RAWP[:, s_ * RS: s_ * RS + c.T]

        loadm, convm, usem = {}, {}, {}
        epim = [0]
        zgm = [0]
        for pc in range(c.NPC):
            par = pc % 2

            def fl(sync, pc=pc, par=par):
                if pc >= 2:
                    sync.wait_ge(p.sV.h, convm[pc - 2])
                sem = p.sDW if par == 0 else p.sDWb
                trio = (p.sDW, p.sDWb, p.sST)
                if sem.n > 0:
                    sync.wait_ge(sem.h, sem.n)
                if pc < 2:
                    for s__ in trio:
                        if s__ is not sem and s__.n > 0:
                            sync.wait_ge(s__.h, s__.n)
                    for s__ in (p.sV, p.sS):
                        if s__.n > 0:
                            sync.wait_ge(s__.h, s__.n)
                sem.dma(self.load_slab(sync, self.WSTG[par], d["wvkq"][pc]))
            self.on("sync", fl)
            loadm[pc] = ((p.sDW if par == 0 else p.sDWb), (p.sDW if par == 0 else p.sDWb).n)

            def fc_(v, pc=pc, par=par):
                v.wait_ge(loadm[pc][0].h, loadm[pc][1])
                if pc >= 2:
                    v.wait_ge(p.sT.h, usem[pc - 2])
                self.conv_cast(v, pc, self.WSTG[par], self.WBF[par], c.D)
            self.on("vector", fc_)
            convm[pc] = p.sV.n

            def fu(te, pc=pc, par=par):
                te.wait_ge(p.sV.h, convm[pc])
                te.wait_ge(p.sV.h, epim[0])
                ps = self.psA if pc % 2 == 0 else self.psB
                for kk in range(ND):
                    for j in range(0, c.T, 512):
                        w = min(512, c.T - j)
                        te.matmul(ps[:, j:j + w],
                                  self.WBF[par][:, kk * 128:(kk + 1) * 128],
                                  XN1[:, kk * c.T + j:kk * c.T + j + w],
                                  start=(kk == 0), stop=(kk == ND - 1))
                p.sT.inc(te.drain())
            self.on("tensor", fu)
            usem[pc] = p.sT.n

            def fe(v, pc=pc):
                v.wait_ge(p.sT.h, usem[pc])
                ps = self.psA if pc % 2 == 0 else self.psB
                if pc < c.NC1:
                    v.tensor_copy(out=raw(pc)[:, 1:c.T + 1], in_=ps[:, 0:c.T])
                elif pc < c.NC1 + c.NZ1:
                    if zgm[0]:
                        v.wait_ge(p.sSTG.h, zgm[0])
                    v.tensor_copy(out=ZGR[:, 0:c.T], in_=ps[:, 0:c.T])
                else:
                    v.tensor_copy(out=self.DTR[0:c.H1, 0:c.T], in_=ps[0:c.H1, 0:c.T])
                p.sV.inc(v.drain())
            self.on("vector", fe)
            epim[0] = p.sV.n

            if c.NC1 <= pc < c.NC1 + c.NZ1:
                def fz(g, pc=pc, mark=epim[0], prev=zgm[0]):
                    g.wait_ge(p.sV.h, mark)
                    if prev:
                        g.wait_ge(p.sSTG.h, prev)
                    p.sSTG.dma(g.dma_start(
                        out=d["zg_sp"][(pc - c.NC1) * 128:(pc - c.NC1 + 1) * 128, :],
                        in_=ZGR[:, 0:c.T]))
                self.on("gpsimd", fz)
                zgm[0] = p.sSTG.n
        self.barrier()

    def phase_l1_ssd(self, RAWP, S0ST, YTT, DT2, AT, AcsTok,
                     csums, DSTT, EAT, XVB, BPB, SCOR, S0cur, Sinit,
                     ZGR):
        p, c, d = self.p, self.c, self.d
        NT, ND, NCH, H1, KVH1 = c.NT, c.ND, c.NCH, c.H1, c.KVH1
        YCH = self.XR
        LEX = self.WBF[0][:, 0:1024]
        PDT = self.WBF[1][:, 0:1024]
        RS = c.T + 32
        NSL = c.NC1 + 1
        VCH = c.KVH1 * c.P1 // 128
        QOF = 2 * VCH

        def raw(j):
            return RAWP[:, j * RS:j * RS + c.T + 1]

        def post(j):
            s_ = (j - 1) if j > 0 else (NSL - 1)
            return RAWP[:, s_ * RS: s_ * RS + c.T]

        def gz(v):
            v.memset(S0cur[:, 0:c.H1 * 32], 0.0)
        self.step("vector", gz)

        rawlast = RAWP[:, 0:c.NC1 * RS].rearrange("p (j r) -> p j r", r=RS)[:, :, c.T]
        rawfirst = RAWP[:, 0:c.NC1 * RS].rearrange("p (j r) -> p j r", r=RS)[:, :, 0]

        def c0(v):
            v.tensor_scalar(out=self.scr2[:, 0:c.NC1], in0=rawlast,
                            scalar1=self.params_b[:, 2:3], scalar2=None,
                            op0=ALU.mult)
        self.step("vector", c0)

        def c1(g):
            p.sSTG.dma(g.dma_start(out=d["cc1_in"][:], in_=self.scr2[:, 0:c.NC1]))
        self.step("gpsimd", c1)

        def c2_(g):
            p.sCC.inc(g.collective_compute(
                "AllReduce", mybir.AluOpType.add, ins=[d["cc1_in"][:]],
                outs=[d["cc1_out"][:]], replica_groups=self.rg))
        self.step("gpsimd", c2_)

        def c2b(g):
            p.sDG.dma(g.dma_start(out=self.scr2[:, 64:64 + c.NC1],
                                  in_=d["cc1_out"][:]))
        self.step("gpsimd", c2b)

        def c3(v):
            v.tensor_scalar(out=rawfirst, in0=self.scr2[:, 64:64 + c.NC1],
                            scalar1=self.params_b[:, 1:2], scalar2=None,
                            op0=ALU.mult)
        self.step("vector", c3)

        for j in range(c.NC1):
            def v0(v, j=j):
                v.tensor_scalar(out=self.scr2[:, 0:c.T], in0=raw(j)[:, 1:c.T + 1],
                                scalar1=self.cw1[:, j:j + 1], scalar2=None,
                                op0=ALU.mult)
                v.drain()
                v.scalar_tensor_tensor(out=self.scr2[:, 0:c.T],
                                       in0=raw(j)[:, 0:c.T],
                                       scalar=self.cw0[:, j:j + 1],
                                       in1=self.scr2[:, 0:c.T], op0=ALU.mult,
                                       op1=ALU.add)
            self.step("vector", v0)

            def s0(s, j=j):
                s.activation(post(j), self.scr2[:, 0:c.T], ACT.Silu,
                             bias=self.cb_[:, j:j + 1])
            self.step("scalar", s0)

        def d0a(v):
            v.tensor_scalar(out=self.DTR[0:c.H1, :], in0=self.DTR[0:c.H1, :],
                            scalar1=self.dtb_sb[:], scalar2=None, op0=ALU.add)
            v.drain()
            v.scalar_tensor_tensor(out=self.scr2[0:c.H1, 0:c.T],
                                   in0=self.DTR[0:c.H1, :], scalar=-1.0,
                                   in1=self.DTR[0:c.H1, :], op0=ALU.mult,
                                   op1=ALU.min)
            v.drain()
            v.tensor_scalar(out=DT2[0:c.H1, :], in0=self.DTR[0:c.H1, :],
                            scalar1=0.0, scalar2=None, op0=ALU.max)
        self.step("vector", d0a)

        def d0b(s):
            s.activation(self.scr2[0:c.H1, 0:c.T], self.scr2[0:c.H1, 0:c.T],
                         ACT.Exp)
            s.drain()
            s.activation(self.scr2[0:c.H1, 0:c.T], self.scr2[0:c.H1, 0:c.T],
                         ACT.Ln, bias=1.0)
        self.step("scalar", d0b)

        def d0c(v):
            v.tensor_tensor(out=DT2[0:c.H1, :], in0=DT2[0:c.H1, :],
                            in1=self.scr2[0:c.H1, 0:c.T], op=ALU.add)
        self.step("vector", d0c)

        def d1(v):
            v.tensor_scalar(out=AT[0:c.H1, :], in0=DT2[0:c.H1, :],
                            scalar1=self.negA[:], scalar2=None, op0=ALU.mult)
        self.step("vector", d1)

        for ch in range(NCH):
            def a0(te, ch=ch):
                te.transpose(self.psC[:, 0:c.H1], AT[0:c.H1, ch * 128:(ch + 1) * 128],
                             self.idf[0:c.H1, 0:c.H1])
                te.transpose(self.psC[:, 128:128 + c.H1],
                             DT2[0:c.H1, ch * 128:(ch + 1) * 128],
                             self.idf[0:c.H1, 0:c.H1])
            self.step("tensor", a0)

            def a1(v, ch=ch):
                v.tensor_copy(out=self.tscr[:, 0:c.H1], in_=self.psC[:, 0:c.H1])
                v.tensor_copy(out=DSTT[:, (NCH + ch) * c.H1:(NCH + ch + 1) * c.H1],
                              in_=self.psC[:, 128:128 + c.H1])
            self.step("vector", a1)

            def a2(te, ch=ch):
                te.matmul(self.psB[:, 0:c.H1], self.uones[:], self.tscr[:, 0:c.H1],
                          start=True, stop=True)
            self.step("tensor", a2)

            def a3(v, ch=ch):
                v.tensor_copy(out=AcsTok[:, ch * c.H1:(ch + 1) * c.H1],
                              in_=self.psB[:, 0:c.H1])
            self.step("vector", a3)

            def a4(te, ch=ch):
                te.transpose(self.psC[0:c.H1, 0:128],
                             AcsTok[:, ch * c.H1:(ch + 1) * c.H1],
                             self.idf[:])
            self.step("tensor", a4)

            def a5(v, ch=ch):
                v.tensor_copy(out=csums[:, ch:ch + 1],
                              in_=self.psC[0:c.H1, 127:128])
            self.step("vector", a5)

        def b1(te):
            for ch in range(NCH):
                te.transpose(self.psC[0:1, ch * c.H1:(ch + 1) * c.H1],
                             csums[:, ch:ch + 1], self.idf[0:c.H1, 0:c.H1])
        self.step("tensor", b1)

        def b2(v):
            v.tensor_copy(out=self.rowsE[0:1, 0:NCH * c.H1],
                          in_=self.psC[0:1, 0:NCH * c.H1])
            v.drain()
            v.memset(self.rowsE[0:1, 512:512 + c.H1], 0.0)
            for ch in range(1, NCH):
                v.drain()
                v.tensor_tensor(
                    out=self.rowsE[0:1, 512 + ch * c.H1:512 + (ch + 1) * c.H1],
                    in0=self.rowsE[0:1, 512 + (ch - 1) * c.H1:512 + ch * c.H1],
                    in1=self.rowsE[0:1, (ch - 1) * c.H1:ch * c.H1], op=ALU.add)
        self.step("vector", b2)

        def b3(s):
            s.activation(self.rowsE[0:1, 0:NCH * c.H1], self.rowsE[0:1, 0:NCH * c.H1],
                         ACT.Exp)
            s.drain()
            s.activation(self.rowsE[0:1, 512:512 + NCH * c.H1],
                         self.rowsE[0:1, 512:512 + NCH * c.H1], ACT.Exp)
        self.step("scalar", b3)


        NKC = 2 * VCH
        for ch in range(NCH):
            def t0(te, ch=ch):
                for j in range(NKC):
                    te.matmul(self.psC[:, j * 128:(j + 1) * 128],
                              post(j)[:, ch * 128:(ch + 1) * 128], self.idb[:],
                              start=True, stop=True)
            self.step("tensor", t0)

            def t1(v, ch=ch):
                for j in range(NKC):
                    dst = XVB if j < VCH else BPB
                    jj = j if j < VCH else j - VCH
                    v.tensor_copy(out=dst[:, c.H1 * 64 + jj * 128:c.H1 * 64 + jj * 128 + 128],
                                  in_=self.psC[:, j * 128:(j + 1) * 128])
            self.step("vector", t1)

            def t2(v, ch=ch):
                v.tensor_copy(out=self.scr2[:, 0:c.H1],
                              in_=AcsTok[:, ch * c.H1:(ch + 1) * c.H1])
            self.step("vector", t2)

            def t3(te, ch=ch):
                te.matmul(self.psC[:, 0:c.H1], self.ones_row[:, 0:128],
                          self.rowsE[0:1, ch * c.H1:(ch + 1) * c.H1],
                          start=True, stop=True)
            self.step("tensor", t3)

            def t4(v, ch=ch):
                v.tensor_scalar(out=self.scr2[:, 64:64 + c.H1],
                                in0=AcsTok[:, ch * c.H1:(ch + 1) * c.H1],
                                scalar1=-1.0, scalar2=None, op0=ALU.mult)
            self.step("vector", t4)

            def t5(s, ch=ch):
                s.activation(self.scr2[:, 64:64 + c.H1], self.scr2[:, 64:64 + c.H1],
                             ACT.Exp)
            self.step("scalar", t5)

            def t6(v, ch=ch):
                v.tensor_tensor(out=DSTT[:, ch * c.H1:(ch + 1) * c.H1],
                                in0=self.scr2[:, 64:64 + c.H1],
                                in1=self.psC[:, 0:c.H1], op=ALU.mult)
            self.step("vector", t6)

            def t7(v, ch=ch):
                for h in range(c.H1):
                    kv = h // c.G1
                    vsl = XVB[:, c.H1 * 64 + (kv // 2) * 128 + (kv % 2) * 64:
                              c.H1 * 64 + (kv // 2) * 128 + (kv % 2) * 64 + 64]
                    ksl = BPB[:, c.H1 * 64 + (kv // 2) * 128 + (kv % 2) * 64:
                              c.H1 * 64 + (kv // 2) * 128 + (kv % 2) * 64 + 64]
                    v.tensor_scalar(out=XVB[:, h * 64:(h + 1) * 64], in0=vsl,
                                    scalar1=DSTT[:, (NCH + ch) * c.H1 + h:
                                                 (NCH + ch) * c.H1 + h + 1],
                                    scalar2=None, op0=ALU.mult)
                    v.tensor_scalar(out=BPB[:, h * 64:(h + 1) * 64], in0=ksl,
                                    scalar1=DSTT[:, ch * c.H1 + h:ch * c.H1 + h + 1],
                                    scalar2=None, op0=ALU.mult)
            self.step("vector", t7)

            def t8(te, ch=ch):
                for h in range(c.H1):
                    te.matmul(self.psY[64 * (h % 2):64 * (h % 2) + 64,
                                       (h // 2) * 64:(h // 2) * 64 + 64],
                              BPB[:, h * 64:(h + 1) * 64],
                              XVB[:, h * 64:(h + 1) * 64], start=True, stop=True)
            self.step("tensor", t8)

            def t9(te, ch=ch):
                ech = self.rowsE[0:1, ch * c.H1:(ch + 1) * c.H1]
                r0 = ech.rearrange("o (i two) -> o i two", two=2)[:, :, 0:1]
                r1 = ech.rearrange("o (i two) -> o i two", two=2)[:, :, 1:2]
                nh2 = c.H1 // 2
                for a_ in range(0, nh2, 8):
                    b_ = min(8, nh2 - a_)
                    te.matmul(self.psC[0:64, a_ * 64:(a_ + b_) * 64],
                              self.ones_row[:, 0:64],
                              r0[:, a_:a_ + b_, :].to_broadcast([1, b_, 64]),
                              start=True, stop=True)
                    te.matmul(self.psC[64:128, a_ * 64:(a_ + b_) * 64],
                              self.ones_row[:, 0:64],
                              r1[:, a_:a_ + b_, :].to_broadcast([1, b_, 64]),
                              start=True, stop=True)
            self.step("tensor", t9)

            def t10(v, ch=ch):
                v.tensor_copy(out=S0ST[:, ch * c.H1 * 32:(ch + 1) * c.H1 * 32],
                              in_=S0cur[:, 0:c.H1 * 32])
                v.drain()
                v.tensor_tensor(out=S0cur[:, 0:c.H1 * 32], in0=S0cur[:, 0:c.H1 * 32],
                                in1=self.psC[:, 0:c.H1 * 32], op=ALU.mult)
                v.drain()
                v.tensor_tensor(out=S0cur[:, 0:c.H1 * 32], in0=S0cur[:, 0:c.H1 * 32],
                                in1=self.psY[:, 0:c.H1 * 32], op=ALU.add)
            self.step("vector", t10)

        def u0(v):
            v.tensor_scalar(out=self.scr2[:, 0:c.H1 * 32], in0=S0cur[:, 0:c.H1 * 32],
                            scalar1=self.params_b[:, 2:3], scalar2=None,
                            op0=ALU.mult)
        self.step("vector", u0)

        def u1(g):
            p.sSTG.dma(g.dma_start(out=d["cc2_in"][:, 0:c.H1 * 32],
                                  in_=self.scr2[:, 0:c.H1 * 32]))
        self.step("gpsimd", u1)

        def u2(g):
            p.sCC.inc(g.collective_compute(
                "AllReduce", mybir.AluOpType.add, ins=[d["cc2_in"][:]],
                outs=[d["cc2_out"][:]], replica_groups=self.rg))
        self.step("gpsimd", u2)

        def u2b(g):
            p.sDG.dma(g.dma_start(out=self.scr2[:, 0:c.H1 * 32],
                                  in_=d["cc2_out"][:, 0:c.H1 * 32]))
        self.step("gpsimd", u2b)
        self.dbg_dump("dbg_scr2cc", self.scr2[:, 0:c.H1 * 32])

        def u3(v):
            v.tensor_scalar(out=Sinit[:, 0:c.H1 * 32], in0=self.scr2[:, 0:c.H1 * 32],
                            scalar1=self.params_b[:, 1:2], scalar2=None,
                            op0=ALU.mult)
        self.step("vector", u3)
        self.dbg_dump("dbg_s0cur", S0cur[:, 0:c.H1 * 32])
        self.dbg_dump("dbg_sinit", Sinit[:, 0:c.H1 * 32])
        if "dbg_cc2i" in self.dbg:
            def fdcc(sync):
                for s_ in (p.sT, p.sV, p.sS, p.sG, p.sST, p.sSTG):
                    if s_.n:
                        sync.wait_ge(s_.h, s_.n)
                p.sDW.dma(sync.dma_start(out=d["dbg_cc2i"][:], in_=d["cc2_in"][:]))
                p.sDW.dma(sync.dma_start(out=d["dbg_cc2o"][:], in_=d["cc2_out"][:]))
            self.on("sync", fdcc)

        for ch in range(NCH):
            def w0(te, ch=ch):
                ech = self.rowsE[0:1, 512 + ch * c.H1:512 + (ch + 1) * c.H1]
                r0 = ech.rearrange("o (i two) -> o i two", two=2)[:, :, 0:1]
                r1 = ech.rearrange("o (i two) -> o i two", two=2)[:, :, 1:2]
                nh2 = c.H1 // 2
                for a_ in range(0, nh2, 8):
                    b_ = min(8, nh2 - a_)
                    te.matmul(self.psC[0:64, a_ * 64:(a_ + b_) * 64],
                              self.ones_row[:, 0:64],
                              r0[:, a_:a_ + b_, :].to_broadcast([1, b_, 64]),
                              start=True, stop=True)
                    te.matmul(self.psC[64:128, a_ * 64:(a_ + b_) * 64],
                              self.ones_row[:, 0:64],
                              r1[:, a_:a_ + b_, :].to_broadcast([1, b_, 64]),
                              start=True, stop=True)
            self.step("tensor", w0)

            def w1(v, ch=ch):
                v.tensor_tensor(out=self.scr2[:, 0:c.H1 * 32],
                                in0=Sinit[:, 0:c.H1 * 32], in1=self.psC[:, 0:c.H1 * 32],
                                op=ALU.mult)
                v.drain()
                v.tensor_tensor(out=SCOR[:, 0:c.H1 * 32], in0=self.scr2[:, 0:c.H1 * 32],
                                in1=S0ST[:, ch * c.H1 * 32:(ch + 1) * c.H1 * 32],
                                op=ALU.add)
            self.step("vector", w1)

            def w2(s, ch=ch):
                s.activation(EAT[:, 0:c.H1], AcsTok[:, ch * c.H1:(ch + 1) * c.H1],
                             ACT.Exp)
            self.step("scalar", w2)

            def w3(te, ch=ch):
                for j in range(VCH):
                    te.matmul(self.psC[:, j * 128:(j + 1) * 128],
                              post(j)[:, ch * 128:(ch + 1) * 128], self.idb[:],
                              start=True, stop=True)
            self.step("tensor", w3)

            def w4(v, ch=ch):
                for j in range(VCH):
                    v.tensor_copy(out=XVB[:, c.H1 * 64 + j * 128:c.H1 * 64 + (j + 1) * 128],
                                  in_=self.psC[:, j * 128:(j + 1) * 128])
            self.step("vector", w4)

            def w5(v, ch=ch):
                for h in range(c.H1):
                    kv = h // c.G1
                    vsl = XVB[:, c.H1 * 64 + (kv // 2) * 128 + (kv % 2) * 64:
                              c.H1 * 64 + (kv // 2) * 128 + (kv % 2) * 64 + 64]
                    v.tensor_scalar(out=XVB[:, h * 64:(h + 1) * 64], in0=vsl,
                                    scalar1=DSTT[:, (NCH + ch) * c.H1 + h:
                                                 (NCH + ch) * c.H1 + h + 1],
                                    scalar2=None, op0=ALU.mult)
            self.step("vector", w5)

            KSW = self.WBF[0][:, 1024:1024 + VCH * 128]

            def w6(v, ch=ch, KSW=KSW):
                for j in range(VCH):
                    kch = post(VCH + j)[:, ch * 128:(ch + 1) * 128]
                    v.tensor_copy(out=KSW[0:64, j * 128:(j + 1) * 128],
                                  in_=kch[64:128, :])
                    v.tensor_copy(out=KSW[64:128, j * 128:(j + 1) * 128],
                                  in_=kch[0:64, :])
            self.step("vector", w6)

            for hg in range(c.H1 // 8):
                h0_ = hg * 8

                def x0(te, ch=ch, h0_=h0_):
                    for i in range(8):
                        h = h0_ + i
                        te.transpose(self.psB[0:1, i * 128:(i + 1) * 128],
                                     AcsTok[:, ch * c.H1 + h:ch * c.H1 + h + 1],
                                     self.idf[:])
                self.step("tensor", x0)

                def x1(v, ch=ch):
                    v.tensor_copy(out=self.rows[0:1, 0:1024],
                                  in_=self.psB[0:1, 0:1024])
                self.step("vector", x1)

                def x2(te, ch=ch, h0_=h0_):
                    for i in range(8):
                        h = h0_ + i
                        kv = h // c.G1
                        te.matmul(self.psC[:, i * 128:(i + 1) * 128],
                                  self.ones_row[:, 0:128],
                                  self.rows[0:1, i * 128:(i + 1) * 128],
                                  start=True, stop=True)
                        qsl = post(QOF + h // 2)[64 * (h % 2):64 * (h % 2) + 64,
                                                 ch * 128:(ch + 1) * 128]
                        if (kv % 2) == (h % 2):
                            ksl = post(VCH + kv // 2)[
                                64 * (kv % 2):64 * (kv % 2) + 64,
                                ch * 128:(ch + 1) * 128]
                        else:
                            ksl = KSW[64 * (h % 2):64 * (h % 2) + 64,
                                      (kv // 2) * 128:(kv // 2) * 128 + 128]
                        te.matmul(self.psB[:, i * 128:(i + 1) * 128], ksl, qsl,
                                  start=True, stop=True)
                self.step("tensor", x2)

                def x3(v, ch=ch, h0_=h0_):
                    for i in range(8):
                        h = h0_ + i
                        v.tensor_scalar(out=self.scr2[:, i * 128:(i + 1) * 128],
                                        in0=self.psC[:, i * 128:(i + 1) * 128],
                                        scalar1=AcsTok[:, ch * c.H1 + h:
                                                       ch * c.H1 + h + 1],
                                        scalar2=None, op0=ALU.subtract)
                    v.drain()
                    v.tensor_tensor(
                        out=self.scr2[:, 0:1024].rearrange("p (i q) -> p i q", q=128),
                        in0=self.scr2[:, 0:1024].rearrange("p (i q) -> p i q", q=128),
                        in1=self.mk_lowT[:, None, :].to_broadcast([128, 8, 128]),
                        op=ALU.add)
                self.step("vector", x3)

                def x4(s, ch=ch):
                    s.activation(LEX, self.scr2[:, 0:1024], ACT.Exp)
                self.step("scalar", x4)

                def x5(v, ch=ch):
                    v.tensor_tensor(out=PDT, in0=self.psB[:, 0:1024],
                                    in1=LEX, op=ALU.mult)
                self.step("vector", x5)

                def x6(te, ch=ch, h0_=h0_):
                    for i in range(8):
                        h = h0_ + i
                        te.matmul(self.psA[:, (h % 16) * 64:(h % 16) * 64 + 64],
                                  PDT[:, i * 128:(i + 1) * 128],
                                  XVB[:, h * 64:(h + 1) * 64],
                                  start=True, stop=True)
                        qsl = post(QOF + h // 2)[64 * (h % 2):64 * (h % 2) + 64,
                                                 ch * 128:(ch + 1) * 128]
                        ssl = SCOR[64 * (h % 2):64 * (h % 2) + 64,
                                   (h // 2) * 64:(h // 2) * 64 + 64]
                        te.matmul(self.psY[:, (h % 16) * 64:(h % 16) * 64 + 64],
                                  qsl, ssl, start=True, stop=True)
                self.step("tensor", x6)

                def x7(v, ch=ch, h0_=h0_):
                    for i in range(8):
                        h = h0_ + i
                        v.tensor_scalar(
                            out=YCH[:, h * 64:(h + 1) * 64],
                            in0=self.psY[:, (h % 16) * 64:(h % 16) * 64 + 64],
                            scalar1=EAT[:, h:h + 1], scalar2=None, op0=ALU.mult)
                    v.drain()
                    for i in range(8):
                        h = h0_ + i
                        v.tensor_tensor(
                            out=YCH[:, h * 64:(h + 1) * 64],
                            in0=YCH[:, h * 64:(h + 1) * 64],
                            in1=self.psA[:, (h % 16) * 64:(h % 16) * 64 + 64],
                            op=ALU.add)
                self.step("vector", x7)

            def y0(sy, ch=ch):
                p.sDW.dma(sy.dma_start(out=ZGR[:, 0:c.ZGD],
                                       in_=d["zg_sp"][:, ch * 128:(ch + 1) * 128],
                                       transpose=True))
            self.step("sync", y0)

            def y1(s, ch=ch):
                hz = min(1024, c.ZGD)
                s.activation(self.scr2[:, 0:hz], YCH[:, 0:hz], ACT.Square)
                if c.ZGD > 1024:
                    s.activation(self.tscr[:, 0:c.ZGD - 1024],
                                 YCH[:, 1024:c.ZGD], ACT.Square)
            self.step("scalar", y1)

            def y2(v, ch=ch):
                hz = min(1024, c.ZGD)
                v.reduce_sum(EAT[:, 0:hz // 64],
                             self.scr2[:, 0:hz].rearrange("p (h q) -> p h q", q=64),
                             axis=AX)
                if c.ZGD > 1024:
                    v.reduce_sum(EAT[:, 16:16 + (c.ZGD - 1024) // 64],
                                 self.tscr[:, 0:c.ZGD - 1024].rearrange(
                                     "p (h q) -> p h q", q=64), axis=AX)
                v.drain()
                v.tensor_scalar(out=EAT[:, 0:c.H1], in0=EAT[:, 0:c.H1],
                                scalar1=1.0 / c.P1, scalar2=c.EPS_SSD,
                                op0=ALU.mult, op1=ALU.add)
            self.step("vector", y2)

            def y3(s, ch=ch):
                s.activation(EAT[:, 0:c.H1], EAT[:, 0:c.H1], ACT.Sqrt)
            self.step("scalar", y3)

            def y4(v, ch=ch):
                v.reciprocal(out=EAT[:, 0:c.H1], in_=EAT[:, 0:c.H1])
                v.drain()
                v.tensor_tensor(
                    out=YCH[:, 0:c.ZGD].rearrange("p (h q) -> p h q", q=64),
                    in0=YCH[:, 0:c.ZGD].rearrange("p (h q) -> p h q", q=64),
                    in1=EAT[:, 0:c.H1, None].to_broadcast([128, c.H1, 64]),
                    op=ALU.mult)
                v.drain()
                v.tensor_tensor(out=YCH[:, 0:c.ZGD], in0=YCH[:, 0:c.ZGD],
                                in1=self.gnw_b[:, 0:c.ZGD], op=ALU.mult)
            self.step("vector", y4)

            def y5(s, ch=ch):
                hz = min(1024, c.ZGD)
                s.activation(self.scr2[:, 0:hz], ZGR[:, 0:hz], ACT.Silu)
                if c.ZGD > 1024:
                    s.activation(self.tscr[:, 0:c.ZGD - 1024],
                                 ZGR[:, 1024:c.ZGD], ACT.Silu)
            self.step("scalar", y5)

            def y6(v, ch=ch):
                hz = min(1024, c.ZGD)
                v.tensor_tensor(out=YCH[:, 0:hz], in0=YCH[:, 0:hz],
                                in1=self.scr2[:, 0:hz], op=ALU.mult)
                if c.ZGD > 1024:
                    v.tensor_tensor(out=YCH[:, 1024:c.ZGD], in0=YCH[:, 1024:c.ZGD],
                                    in1=self.tscr[:, 0:c.ZGD - 1024], op=ALU.mult)
            self.step("vector", y6)

            def y7(te, ch=ch):
                for k in range(c.NZ1):
                    ps = self.psA if k < 8 else self.psB
                    te.transpose(ps[:, (k % 8) * 128:(k % 8) * 128 + 128],
                                 YCH[:, k * 128:(k + 1) * 128], self.idf[:])
            self.step("tensor", y7)

            def y8(v, ch=ch):
                for k in range(c.NZ1):
                    ps = self.psA if k < 8 else self.psB
                    v.tensor_copy(out=YTT[:, k * c.T + ch * 128:k * c.T + ch * 128 + 128],
                                  in_=ps[:, (k % 8) * 128:(k % 8) * 128 + 128])
            self.step("vector", y8)
        self.barrier()

        for dc in range(ND):
            def fl(sync, dc=dc):
                p.sDW.dma(self.load_slab(sync, self.WSTG[0], d["wo1"][dc]))
            self.step("sync", fl)

            def fcv(v):
                p.sV.inc(v.tensor_copy(out=self.WBF[0][:, 0:c.ZGD],
                                       in_=self.WSTG[0][:, 0:c.ZGD]))
            self.step("vector", fcv)

            def ft(te):
                for kk in range(c.NZ1):
                    for j in range(0, c.T, 512):
                        w = min(512, c.T - j)
                        te.matmul(self.psA[:, j:j + w],
                                  self.WBF[0][:, kk * 128:(kk + 1) * 128],
                                  YTT[:, kk * c.T + j:kk * c.T + j + w],
                                  start=(kk == 0), stop=(kk == c.NZ1 - 1))
            self.step("tensor", ft)

            def fv(v):
                p.sV.inc(v.tensor_copy(out=self.sR[:, 0:c.T], in_=self.psA[:, 0:c.T]))
            self.step("vector", fv)

            def fst(sync, dc=dc):
                p.sST.dma(sync.dma_start(out=d["sT_sp"][dc * 128:(dc + 1) * 128, :],
                                         in_=self.sR[:, 0:c.T]))
            self.step("sync", fst)
        self.barrier()

    def phase_ffn1(self, P, t0, tn):
        p, c, d = self.p, self.c, self.d
        S = c.NT
        self.qx_pass_l1(t0, tn)
        self.col2row(self.stat[:, 3 * S:3 * S + c.NT], 0, c.NT, self.psC)

        def f0(v):
            v.tensor_scalar(out=self.rows[0:1, t0:t0 + tn],
                            in0=self.rows[0:1, t0:t0 + tn],
                            scalar1=2.0, scalar2=None, op0=ALU.mult)
        self.step("vector", f0)
        self.bcast(self.rows[0:1, t0:t0 + tn], self.bc1, tn, self.psC)

        def epi(fc):
            def s1(v, fc=fc):
                v.tensor_tensor(out=self.scr2[:, 0:tn], in0=self.psA[:, 0:tn],
                                in1=self.bc1[:, 0:tn], op=ALU.mult)
                v.tensor_tensor(out=self.XR[:, 0:tn], in0=self.psB[:, 0:tn],
                                in1=self.bc1[:, 0:tn], op=ALU.mult)
            self.step("vector", s1)

            def s2(s, fc=fc):
                s.activation(self.tscr[:, 0:tn], self.scr2[:, 0:tn], ACT.Silu)
            self.step("scalar", s2)

            def s3(v, fc=fc):
                v.tensor_tensor(out=P[:, fc * tn:(fc + 1) * tn],
                                in0=self.tscr[:, 0:tn], in1=self.XR[:, 0:tn],
                                op=ALU.mult)
            self.step("vector", s3)

        self.ffn_gateup(d["wg1"], d["wu1"], self.QX, P,
                        self.conv_sign, self.conv_sign, epi, tn)

        def f1(v):
            v.memset(self.bc2[:, 0:tn], 2.0)
        self.step("vector", f1)
        self.barrier()

    def qx_pass_l1(self, t0, tn):
        p, c, d = self.p, self.c, self.d
        S = c.NT
        self.col2row(self.stat[:, 1 * S:1 * S + c.NT], 0, c.NT, self.psC)
        self.col2row(self.stat[:, 0 * S:0 * S + c.NT], 1024, c.NT, self.psC)
        self.bcast(self.rows[0:1, t0:t0 + tn], self.bc1, tn, self.psC)
        self.bcast(self.rows[0:1, 1024 + t0:1024 + t0 + tn], self.bc2, tn, self.psB)
        dm3, vm3 = {}, {}
        for k in range(c.ND):
            par = k % 2

            def fl(g, k=k, par=par):
                if k >= 2:
                    g.wait_ge(p.sV.h, vm3[k - 2])
                sem = p.sDG if par == 0 else p.sDGb
                if k >= 2:
                    g.wait_ge(sem.h, dm3[k - 2][1])
                else:
                    for s__ in (p.sDG, p.sDGb, p.sSTG):
                        if s__.n > 0:
                            g.wait_ge(s__.h, s__.n)
                sem.dma(g.dma_start(
                    out=self.XR[:, par * 1024:par * 1024 + tn],
                    in_=d["xT_sp"][k * 128:(k + 1) * 128, t0:t0 + tn]))
            self.on("gpsimd", fl)
            dm3[k] = ((p.sDG if par == 0 else p.sDGb), (p.sDG if par == 0 else p.sDGb).n)

            def fv(v, k=k, par=par):
                v.wait_ge(dm3[k][0].h, dm3[k][1])
                v.drain()
                xr = self.XR[:, par * 1024:par * 1024 + tn]
                v.tensor_tensor(out=self.scr2[:, 0:tn], in0=xr,
                                in1=self.bc1[:, 0:tn], op=ALU.mult)
                v.drain()
                v.tensor_tensor(out=self.scr2[:, 0:tn], in0=self.scr2[:, 0:tn],
                                in1=self.bc2[:, 0:tn], op=ALU.subtract)
                v.drain()
                p.sV.inc(v.tensor_scalar(out=self.QX[:, k * tn:(k + 1) * tn],
                                         in0=self.scr2[:, 0:tn], scalar1=MAGIC,
                                         scalar2=MAGIC, op0=ALU.add,
                                         op1=ALU.subtract))
            self.on("vector", fv)
            vm3[k] = p.sV.n
        self.barrier()


def build_body(p, block, d, rg, dbg_out, dbg):
    b = B(p, block, d, rg, dbg)
    c = p.cfg
    nc = p.nc
    TW = c.T + c.WIN
    b.phase_init()

    with contextlib.ExitStack() as scope0:
        XN = scope0.enter_context(nc.sbuf_tensor("XN0", [128, c.ND * TW], BF16))
        b.XN = XN
        own = [XN[:, k * TW:k * TW + c.T] for k in range(c.ND)]
        halo = [XN[:, k * TW + c.T:k * TW + TW] for k in range(c.ND)]
        b.ln_pass(d["x_own"], c.T, 0, 1, c.EPS_LN, xn_chunks=own, slot_c=2,
                  aq_eps=c.EPS_FFN0_AQ)
        b.ln_pass(d["x_halo"], c.WIN, 6, 7, c.EPS_LN, xn_chunks=halo)
        b.dbg_dump("dbg_xn", XN[:])
        b.phase_ffn0_stats()
        nk = c.NKT * 128
        with contextlib.ExitStack() as sa:
            b.cosT = sa.enter_context(nc.sbuf_tensor("cosT", [128, nk], F32))
            b.sinT = sa.enter_context(nc.sbuf_tensor("sinT", [128, nk], F32))
            b.vTr = sa.enter_context(nc.sbuf_tensor("vTr", [128, nk], BF16))
            b.V = sa.enter_context(nc.sbuf_tensor("Vtok", [128, c.NKT * c.KVH * 128], BF16))
            b.KRR = sa.enter_context(nc.sbuf_tensor("KRR", [128, c.KVH * nk], BF16))
            b.QR = sa.enter_context(nc.sbuf_tensor("QR", [128, c.T], BF16))
            b.Y = sa.enter_context(nc.sbuf_tensor("Yat", [128, c.H * c.T], BF16))
            b.pr = sa.enter_context(nc.sbuf_tensor("prb", [128, c.W5 * c.W5 * 128], BF16))
            b.phase_attn()
            b.dbg_dump("dbg_y", b.Y[:])
        if "dbg_s0" in b.dbg:
            def fds(sync):
                for s_ in (p.sT, p.sV, p.sS, p.sG):
                    sync.wait_ge(s_.h, s_.n)
                if p.sST.n:
                    sync.wait_ge(p.sST.h, p.sST.n)
                p.sDW.dma(sync.dma_start(out=d["dbg_s0"][:], in_=d["sT_sp"][:]))
            block.sync(fds)
    if True:
        T2 = c.T // 2
        with contextlib.ExitStack() as sf:
            QX0 = sf.enter_context(nc.sbuf_tensor("QX0", [128, c.ND * T2], BF16))
            H = sf.enter_context(nc.sbuf_tensor("H0", [128, c.NF * T2], BF16))
            b.QX = QX0
            for half in range(2):
                b.phase_ffn0(H, half * T2, T2)
                if half == 0:
                    b.dbg_dump("dbg_qh", H[:])
                b.phase_down_combine(H, d["wd0"], b.make_conv_tern_idx(2),
                                     b.bc2, 0, half * T2, T2)
                b.l1_stats(half * T2, T2)
    if "dbg_x1" in b.dbg:
        def fdx1(sync):
            for s_ in (p.sT, p.sV, p.sS, p.sG, p.sST, p.sSTG):
                if s_.n:
                    sync.wait_ge(s_.h, s_.n)
            p.sDW.dma(sync.dma_start(out=d["dbg_x1"][:], in_=d["xT_sp"][:]))
        block.sync(fdx1)

    with contextlib.ExitStack() as s1:
        RS = c.T + 32
        NSL = c.NC1 + 1
        RAWP = s1.enter_context(nc.sbuf_tensor("RAWP", [128, NSL * RS], BF16,
                                               side="right"))
        S0ST = s1.enter_context(nc.sbuf_tensor("S0ST", [128, c.NCH * c.H1 * 32],
                                               BF16, side="right"))
        b.gnw_b = s1.enter_context(nc.sbuf_tensor("gnw_b", [128, c.ZGD], F32,
                                                  side="right"))
        DT2 = s1.enter_context(nc.sbuf_tensor("DT2", [c.H1, c.T], F32, side="right"))
        b.DTR = s1.enter_context(nc.sbuf_tensor("DTR", [c.H1, c.T], F32, side="right"))
        ZGR = s1.enter_context(nc.sbuf_tensor("ZGR", [128, max(c.ZGD, c.T)], BF16,
                                              side="right"))
        with contextlib.ExitStack() as s2:
            XN1 = s2.enter_context(nc.sbuf_tensor("XN1", [128, c.ND * c.T], BF16))
            b.phase_l1_entry(XN1)
            b.dbg_dump("dbg_xn1", XN1[:])
            b.phase_l1_proj(XN1, RAWP, DT2, ZGR)
            b.dbg_dump("dbg_raw", RAWP[:])
        with contextlib.ExitStack() as s3:
            YTT = s3.enter_context(nc.sbuf_tensor("YTT", [128, c.NZ1 * c.T], BF16))
            AcsTok = s3.enter_context(nc.sbuf_tensor("AcsTok", [128, c.NCH * c.H1], F32))
            csums = s3.enter_context(nc.sbuf_tensor("csums", [c.H1, c.NCH], F32))
            DSTT = s3.enter_context(nc.sbuf_tensor("DSTT", [128, 2 * c.NCH * c.H1], F32))
            EAT = s3.enter_context(nc.sbuf_tensor("EAT", [128, max(c.H1, 32)], F32))
            XVB = s3.enter_context(nc.sbuf_tensor("XVB", [128, c.H1 * 64 + 512], BF16))
            BPB = s3.enter_context(nc.sbuf_tensor("BPB", [128, c.H1 * 64 + 512], BF16))
            SCOR = s3.enter_context(nc.sbuf_tensor("SCOR", [128, c.H1 * 32], BF16))
            S0cur = b.maxA[:, 0:c.H1 * 32]
            Sinit = b.maxB[:, 0:c.H1 * 32]
            b.paranoid = bool(int(os.environ.get("MK_PARANOID", "0")))
            b.phase_l1_ssd(RAWP, S0ST, YTT, DT2, b.DTR, AcsTok,
                           csums, DSTT, EAT, XVB, BPB, SCOR,
                           S0cur, Sinit, ZGR)
            b.paranoid = False
            b.dbg_dump("dbg_ytt", YTT[:])
            b.dbg_dump("dbg_dt2", DT2[:])
        if "dbg_s1" in b.dbg:
            def fds1(sync):
                for s_ in (p.sT, p.sV, p.sS, p.sG, p.sST, p.sSTG):
                    if s_.n:
                        sync.wait_ge(s_.h, s_.n)
                p.sDW.dma(sync.dma_start(out=d["dbg_s1"][:], in_=d["sT_sp"][:]))
            block.sync(fds1)
    with contextlib.ExitStack() as s4:
        QX1 = s4.enter_context(nc.sbuf_tensor("QX1", [128, c.ND * (c.T // 2)], BF16))
        b.QX = QX1
        b.phase_l1_xq()
        T2 = c.T // 2
        with contextlib.ExitStack() as s5:
            P1 = s5.enter_context(nc.sbuf_tensor("P1", [128, c.NF * T2], BF16))
            for half in range(2):
                b.phase_ffn1(P1, half * T2, T2)
                b.phase_down_combine(P1, d["wd1"], b.conv_sign, b.bc2, 1,
                                     half * T2, T2)

    fin = {s: s.n for s in (p.sDG, p.sDGb, p.sDW, p.sDWb, p.sDS, p.sDSb,
                            p.sST, p.sSTG, p.sCC)}

    def fend(sync):
        for s, v in fin.items():
            sync.wait_ge(s.h, v)
    block.sync(fend)


# version 05
# speedup vs baseline: 1.0124x; 1.0124x over previous
import contextlib

import os
import numpy as np

import concourse.bass as bass
import concourse.mybir as mybir
from concourse.bass_utils import run_bass_kernel_spmd

F32 = mybir.dt.float32
BF16 = mybir.dt.bfloat16
AX = mybir.AxisListType.X
ALU = mybir.AluOpType
ACT = mybir.ActivationFunctionType

MAGIC = 12582912.0
NEGINF = -1e30
LN2PI = 6.283185307179586


class Cfg:
    def __init__(self, small=False):
        if small:
            self.T = 256
            self.D = 512
            self.H = 4
            self.KVH = 2
            self.WIN = 256
            self.F = 1024
            self.H1 = 8
            self.KVH1 = 4
        else:
            self.T = 1024
            self.D = 2048
            self.H = 16
            self.KVH = 4
            self.WIN = 512
            self.F = 8192
            self.H1 = 32
            self.KVH1 = 8
        self.HD = 128
        self.P1 = 64
        self.CH = 128
        self.NT = self.T // 128
        self.ND = self.D // 128
        self.NF = self.F // 128
        self.HT = self.WIN // 128
        self.NKT = self.NT + self.HT
        self.W5 = self.HT + 1
        self.CONVD = (self.H1 + 2 * self.KVH1) * self.P1
        self.NC1 = self.CONVD // 128
        self.ZGD = self.H1 * self.P1
        self.NZ1 = self.ZGD // 128
        self.PROJ = self.CONVD + self.ZGD + self.H1
        self.NPC = (self.PROJ + 127) // 128
        self.PROJ_PAD = self.NPC * 128
        self.NCH = self.T // self.CH
        self.G1 = self.H1 // self.KVH1
        self.EPS_LN = 1e-5
        self.EPS_QK = 1.1920928955078125e-07
        self.EPS_FFN0_AQ = 1e-5
        self.EPS_FFN0_RMS = 1e-6
        self.EPS_SSD = 1e-5
        self.EPS_FFN1_RMS = 1e-8
        self.ROPE_BASE = 10000.0


class Sem:

    def __init__(self, stack, nc, name):
        self.h = stack.enter_context(nc.semaphore(name))
        self.n = 0

    def inc(self, instr, k=1):
        instr.then_inc(self.h, k)
        self.n += k
        return instr

    def dma(self, instr):
        return self.inc(instr, 16)


class P:

    def __init__(self, nc, cfg, stack):
        self.nc = nc
        self.cfg = cfg
        self.stack = stack
        self.sT = Sem(stack, nc, "sT")
        self.sV = Sem(stack, nc, "sV")
        self.sS = Sem(stack, nc, "sS")
        self.sG = Sem(stack, nc, "sG")
        self.sCC = Sem(stack, nc, "sCC")
        self.sDW = Sem(stack, nc, "sDW")
        self.sDWb = Sem(stack, nc, "sDWb")
        self.sDG = Sem(stack, nc, "sDG")
        self.sDGb = Sem(stack, nc, "sDGb")
        self.sDS = Sem(stack, nc, "sDS")
        self.sDSb = Sem(stack, nc, "sDSb")
        self.sST = Sem(stack, nc, "sST")
        self.sSTG = Sem(stack, nc, "sSTG")
        self.sTf = Sem(stack, nc, "sTf")
        self.sVf = Sem(stack, nc, "sVf")
        self.all_sems = [self.sT, self.sV, self.sS, self.sG,
                         self.sDW, self.sDWb, self.sDG, self.sDGb,
                         self.sDS, self.sDSb, self.sST, self.sSTG,
                         self.sTf, self.sVf]

    def sb(self, name, shape, dtype, side="left"):
        return self.stack.enter_context(
            self.nc.sbuf_tensor(name, shape, dtype, side=side))

    def ps(self, name, shape, dtype=F32):
        return self.stack.enter_context(self.nc.psum_tensor(name, shape, dtype))


def _emit_barrier(p, block, engines=("tensor", "vector", "scalar", "gpsimd")):
    marks = {}
    for e in engines:
        def emit_mark(eng, e=e):
            sem = {"tensor": p.sT, "vector": p.sV, "scalar": p.sS,
                   "gpsimd": p.sG}[e]
            sem.inc(eng.drain())
            marks[e] = sem.n
        getattr(block, e)(emit_mark)
    for e in engines:
        def emit_wait(eng, e=e):
            for o in engines:
                if o == e:
                    continue
                sem = {"tensor": p.sT, "vector": p.sV, "scalar": p.sS,
                       "gpsimd": p.sG}[o]
                eng.wait_ge(sem.h, marks[o])
        getattr(block, e)(emit_wait)



def _tile_wT(w):
    return np.ascontiguousarray(w.T)


def _slab_stream(wt, chunk):
    din, dout = wt.shape
    assert dout % chunk == 0
    return np.ascontiguousarray(
        wt.reshape(din, dout // chunk, chunk).transpose(1, 0, 2))


def _pcol(v):
    n = v.shape[0]
    assert n % 128 == 0
    return np.ascontiguousarray(v.reshape(n // 128, 128).T)


def stage_inputs(cfg, inputs):
    c = cfg
    x = np.asarray(inputs["L0_attn_x"], np.float32)
    B, S, D = x.shape
    assert D == c.D and S == 2 * c.T

    wq = _slab_stream(_tile_wT(np.asarray(inputs["L0_attn_Wq"], np.float32)), 128)
    wk = np.ascontiguousarray(_tile_wT(np.asarray(inputs["L0_attn_Wk"], np.float32)))
    wv = np.ascontiguousarray(_tile_wT(np.asarray(inputs["L0_attn_Wv"], np.float32)))
    wo = _slab_stream(_tile_wT(np.asarray(inputs["L0_attn_Wo"], np.float32)), 128)
    qgain = np.asarray(inputs["L0_attn_q_gain"], np.float32).reshape(1, c.H)

    wg0 = _slab_stream(_tile_wT(np.asarray(inputs["L0_ffn_w_gate"], np.float32)), 128)
    wu0 = _slab_stream(_tile_wT(np.asarray(inputs["L0_ffn_w_up"], np.float32)), 128)
    wd0 = _slab_stream(_tile_wT(np.asarray(inputs["L0_ffn_w_down"], np.float32)), 128)
    lnw = _pcol(np.asarray(inputs["L0_ffn_ln_weight"], np.float32))

    wvkq = _tile_wT(np.asarray(inputs["L1_attn_W_vkqgdt"], np.float32))
    pad = np.zeros((c.D, c.PROJ_PAD - c.PROJ), np.float32)
    wvkq = _slab_stream(np.ascontiguousarray(np.concatenate([wvkq, pad], 1)), 128)
    convw = np.asarray(inputs["L1_attn_conv_w"], np.float32)
    convw0 = _pcol(convw[:, 0])
    convw1 = _pcol(convw[:, 1])
    convb = _pcol(np.asarray(inputs["L1_attn_conv_b"], np.float32))
    dtb = np.asarray(inputs["L1_attn_dt_bias"], np.float32).reshape(c.H1, 1)
    alog = np.asarray(inputs["L1_attn_A_log_bias"], np.float32).reshape(c.H1, 1)
    gnw = np.tile(np.asarray(inputs["L1_attn_gnorm_w"], np.float32), c.H1).reshape(1, c.ZGD)
    wo1 = _slab_stream(_tile_wT(np.asarray(inputs["L1_attn_W_o"], np.float32)), 128)

    wg1 = _slab_stream(_tile_wT(np.asarray(inputs["L1_ffn_W_g"], np.float32)), 128)
    wu1 = _slab_stream(_tile_wT(np.asarray(inputs["L1_ffn_W_u"], np.float32)), 128)
    wd1 = _slab_stream(_tile_wT(np.asarray(inputs["L1_ffn_W_d"], np.float32)), 128)

    shared = dict(
        wq=wq, wk=wk, wv=wv, wo=wo, qgain=qgain,
        wg0=wg0, wu0=wu0, wd0=wd0, lnw=lnw,
        wvkq=wvkq, convw0=convw0, convw1=convw1, convb=convb,
        dtb=dtb, alog=alog, gnw=gnw, wo1=wo1,
        wg1=wg1, wu1=wu1, wd1=wd1,
    )

    in_maps = []
    for core in range(2 * B):
        b, half = core // 2, core % 2
        t0 = half * c.T
        x_own = np.ascontiguousarray(x[b, t0:t0 + c.T].T)
        if half == 0:
            x_halo = np.zeros((c.D, c.WIN), np.float32)
        else:
            x_halo = np.ascontiguousarray(x[b, t0 - c.WIN:t0].T)
        params = np.zeros((1, 32), np.float32)
        params[0, 0] = float(t0)
        params[0, 1] = float(half)
        params[0, 2] = float(1 - half)
        for j in range(c.NKT):
            params[0, 3 + j] = NEGINF if (half == 0 and j < c.HT) else 0.0
        m = dict(shared)
        m.update(x_own=x_own, x_halo=x_halo, params=params)
        in_maps.append(m)
    return in_maps



def build_program(cfg, dbg=(), n_cores=8):
    c = cfg
    nc = bass.Bass("TRN2", target_bir_lowering=False)
    stack = contextlib.ExitStack()

    d = {}
    d["x_own"] = nc.dram_tensor("x_own", [c.D, c.T], F32, kind="ExternalInput").ap()
    d["x_halo"] = nc.dram_tensor("x_halo", [c.D, c.WIN], F32, kind="ExternalInput").ap()
    d["params"] = nc.dram_tensor("params", [1, 32], F32, kind="ExternalInput").ap()
    d["wq"] = nc.dram_tensor("wq", [c.H, c.D, 128], F32, kind="ExternalInput").ap()
    d["wk"] = nc.dram_tensor("wk", [c.D, c.KVH * c.HD], F32, kind="ExternalInput").ap()
    d["wv"] = nc.dram_tensor("wv", [c.D, c.KVH * c.HD], F32, kind="ExternalInput").ap()
    d["wo"] = nc.dram_tensor("wo", [c.ND, c.D, 128], F32, kind="ExternalInput").ap()
    d["qgain"] = nc.dram_tensor("qgain", [1, c.H], F32, kind="ExternalInput").ap()
    d["wg0"] = nc.dram_tensor("wg0", [c.NF, c.D, 128], F32, kind="ExternalInput").ap()
    d["wu0"] = nc.dram_tensor("wu0", [c.NF, c.D, 128], F32, kind="ExternalInput").ap()
    d["wd0"] = nc.dram_tensor("wd0", [c.ND, c.F, 128], F32, kind="ExternalInput").ap()
    d["lnw"] = nc.dram_tensor("lnw", [128, c.NF], F32, kind="ExternalInput").ap()
    d["wvkq"] = nc.dram_tensor("wvkq", [c.NPC, c.D, 128], F32, kind="ExternalInput").ap()
    d["convw0"] = nc.dram_tensor("convw0", [128, c.NC1], F32, kind="ExternalInput").ap()
    d["convw1"] = nc.dram_tensor("convw1", [128, c.NC1], F32, kind="ExternalInput").ap()
    d["convb"] = nc.dram_tensor("convb", [128, c.NC1], F32, kind="ExternalInput").ap()
    d["dtb"] = nc.dram_tensor("dtb", [c.H1, 1], F32, kind="ExternalInput").ap()
    d["alog"] = nc.dram_tensor("alog", [c.H1, 1], F32, kind="ExternalInput").ap()
    d["gnw"] = nc.dram_tensor("gnw", [1, c.ZGD], F32, kind="ExternalInput").ap()
    d["wo1"] = nc.dram_tensor("wo1", [c.ND, c.D, 128], F32, kind="ExternalInput").ap()
    d["wg1"] = nc.dram_tensor("wg1", [c.NF, c.D, 128], F32, kind="ExternalInput").ap()
    d["wu1"] = nc.dram_tensor("wu1", [c.NF, c.D, 128], F32, kind="ExternalInput").ap()
    d["wd1"] = nc.dram_tensor("wd1", [c.ND, c.F, 128], F32, kind="ExternalInput").ap()
    d["y_out"] = nc.dram_tensor("y_out", [c.D, c.T], F32, kind="ExternalOutput").ap()
    d["xT_sp"] = nc.dram_tensor("xT_sp", [c.D, c.T], F32, kind="Internal").ap()
    d["sT_sp"] = nc.dram_tensor("sT_sp", [c.D, c.T], BF16, kind="Internal").ap()
    d["zg_sp"] = nc.dram_tensor("zg_sp", [c.ZGD, c.T], BF16, kind="Internal").ap()
    d["cc1_in"] = nc.dram_tensor("cc1_in", [128, c.NC1], F32, kind="Internal").ap()
    d["cc1_out"] = nc.dram_tensor("cc1_out", [128, c.NC1], F32, kind="Internal").ap()
    d["cc2_in"] = nc.dram_tensor("cc2_in", [128, c.H1 * c.P1 // 2], F32, kind="Internal").ap()
    d["cc2_out"] = nc.dram_tensor("cc2_out", [128, c.H1 * c.P1 // 2], F32, kind="Internal").ap()
    for name, shape, dt_ in dbg:
        d[name] = nc.dram_tensor(name, shape, dt_, kind="ExternalOutput").ap()

    rg = [[2 * i, 2 * i + 1] for i in range(n_cores // 2)]

    p = P(nc, c, stack)
    dbg_out = {}

    with nc.Block() as block:
        build_body(p, block, d, rg, dbg_out, dbg)

    stack.close()
    nc.finalize()
    return nc


def kernel(**inputs):
    cfg = Cfg(small=False)
    in_maps = stage_inputs(cfg, inputs)
    nc = build_program(cfg)
    res = run_bass_kernel_spmd(nc, in_maps, core_ids=list(range(8)))
    out = np.empty((4, 2048, 2048), np.float32)
    for core in range(8):
        b, half = core // 2, core % 2
        out[b, half * cfg.T:(half + 1) * cfg.T] = res.results[core]["y_out"].T
    return out




class B:

    def __init__(self, p, block, d, rg, dbg):
        self.p, self.block, self.d, self.rg = p, block, d, rg
        self.nc = p.nc
        self.c = p.cfg
        self.dbg = set(n for (n, _, _) in dbg)
        self.qwaited = {}
        self.paranoid = False
        self.drained = set()

    def on(self, eng, fn):
        self.drained.discard(eng)
        getattr(self.block, eng)(fn)

    def barrier(self):
        _emit_barrier(self.p, self.block)
        self.drained.update(("tensor", "vector", "scalar", "gpsimd"))

    def dbg_dump(self, name, src_ap):
        if name not in self.dbg:
            return
        p, d = self.p, self.d
        marks = {s: s.n for s in (p.sT, p.sV, p.sS, p.sG, p.sDW, p.sDWb, p.sDG,
                                  p.sDGb, p.sDS, p.sDSb, p.sST, p.sSTG, p.sCC)
                 if s.n > 0}

        def f(sync):
            for s, v in marks.items():
                sync.wait_ge(s.h, v)
            p.sDW.dma(sync.dma_start(out=d[name][:], in_=src_ap))
        self.on("sync", f)


    def _twait(self, te):
        p = self.p
        for s in (p.sV, p.sS, p.sG):
            if s.n:
                te.wait_ge(s.h, s.n)
        for s in (p.sDW, p.sDWb, p.sDG, p.sDGb, p.sDS, p.sDSb, p.sST, p.sSTG,
                  p.sCC):
            if s.n > self.qwaited.get(s, 0):
                te.wait_ge(s.h, s.n)
                self.qwaited[s] = s.n

    def step(self, eng, fn):
        p = self.p
        sems = {"tensor": p.sT, "vector": p.sV, "scalar": p.sS, "gpsimd": p.sG}
        own = sems.get(eng)
        marks = [(s, s.n) for nm, s in sems.items() if nm != eng and s.n > 0]
        own_q = {"sync": (p.sDW, p.sDWb, p.sST),
                 "gpsimd": (p.sDG, p.sDGb, p.sSTG, p.sCC),
                 "scalar": (p.sDS, p.sDSb)}.get(eng, ())
        for s in own_q:
            if s.n > 0:
                marks.append((s, s.n))
                self.qwaited[s] = max(self.qwaited.get(s, 0), s.n)
        for s in (p.sDW, p.sDWb, p.sDG, p.sDGb, p.sDS, p.sDSb, p.sST, p.sSTG,
                  p.sCC):
            if s not in own_q and s.n > self.qwaited.get(s, 0):
                marks.append((s, s.n))
                self.qwaited[s] = s.n

        lead_drain = eng != "sync" and eng not in self.drained

        def wrapped(e):
            for s, v in marks:
                e.wait_ge(s.h, v)
            if lead_drain:
                e.drain()
            fn(e)
            if eng == "sync":
                pass
            else:
                own.inc(e.drain())
        self.on(eng, wrapped)
        if eng != "sync":
            self.drained.add(eng)
        if self.paranoid:
            self.barrier()

    def load_slab(self, eng, stg, sl):
        R = sl.shape[0]
        return eng.dma_start(
            out=stg[:, 0:R].rearrange("p (k c) -> p k c", c=128),
            in_=sl.rearrange("(k p) c -> p k c", p=128))

    def col2row(self, stat_ap, row_off, nt, psum):
        p = self.p

        def ft(te):
            self._twait(te)
            for t in range(nt):
                p.sT.inc(te.transpose(psum[0:1, t * 128:(t + 1) * 128],
                                      stat_ap[:, t:t + 1], self.idf[:]))
        self.on("tensor", ft)
        mark = p.sT.n

        def fv(v):
            v.wait_ge(p.sT.h, mark)
            p.sV.inc(v.tensor_copy(out=self.rows[0:1, row_off:row_off + 128 * nt],
                                   in_=psum[0:1, 0:128 * nt]))
        self.on("vector", fv)

    def bcast(self, row_ap, out_sb, n, psum):
        p = self.p

        def ft(te):
            self._twait(te)
            for j in range(0, n, 512):
                w = min(512, n - j)
                p.sT.inc(te.matmul(psum[:, j:j + w], self.ones_row[:, 0:128],
                                   row_ap[:, j:j + w], start=True, stop=True))
        self.on("tensor", ft)
        mark = p.sT.n

        def fv(v):
            v.wait_ge(p.sT.h, mark)
            p.sV.inc(v.tensor_copy(out=out_sb[:, 0:n], in_=psum[:, 0:n]))
        self.on("vector", fv)

    def stream_slabs(self, slabs, conv, use, extra_v=None):
        p = self.p
        loadm, convm, usem = {}, {}, {}
        for i, sl in enumerate(slabs):
            par = i % 2
            stg, wbf = self.WSTG[par], self.WBF[par]
            R = sl.shape[0]
            assert R % 128 == 0 and R <= stg.shape[1]

            def fl(sync, i=i, stg=stg, sl=sl, R=R):
                if i >= 2:
                    sync.wait_ge(p.sV.h, convm[i - 2])
                sem = p.sDW if par == 0 else p.sDWb
                trio = (p.sDW, p.sDWb, p.sST)
                if sem.n > 0:
                    sync.wait_ge(sem.h, sem.n)
                if i < 2:
                    for s__ in trio:
                        if s__ is not sem and s__.n > 0:
                            sync.wait_ge(s__.h, s__.n)
                    for s__ in (p.sV, p.sS):
                        if s__.n > 0:
                            sync.wait_ge(s__.h, s__.n)
                sem.dma(self.load_slab(sync, stg, sl))
            self.on("sync", fl)
            loadm[i] = ((p.sDW if par == 0 else p.sDWb), (p.sDW if par == 0 else p.sDWb).n)

            def fc(v, i=i, stg=stg, wbf=wbf, R=R):
                v.wait_ge(loadm[i][0].h, loadm[i][1])
                if i >= 2:
                    v.wait_ge(p.sT.h, usem[i - 2])
                conv(v, i, stg, wbf, R)
                if extra_v is not None:
                    extra_v(v, i, stg, R)
            self.on("vector", fc)
            convm[i] = p.sV.n

            def fu(te, i=i, wbf=wbf, R=R):
                te.wait_ge(p.sV.h, convm[i])
                use(te, i, wbf, R)
            self.on("tensor", fu)
            usem[i] = p.sT.n
        return usem

    def conv_cast(self, v, i, stg, wbf, R):
        self.p.sV.inc(v.tensor_copy(out=wbf[:, 0:R], in_=stg[:, 0:R]))

    def make_conv_tern_idx(self, m):
        p = self.p

        def conv(v, i, stg, wbf, R):
            v.tensor_scalar(out=stg[:, 0:R], in0=stg[:, 0:R],
                            scalar1=self.srep[:, m:m + 1], scalar2=1.0,
                            op0=ALU.mult, op1=ALU.min)
            v.drain()
            v.tensor_scalar(out=stg[:, 0:R], in0=stg[:, 0:R], scalar1=-1.0,
                            scalar2=MAGIC, op0=ALU.max, op1=ALU.add)
            v.drain()
            p.sV.inc(v.tensor_scalar(out=wbf[:, 0:R], in0=stg[:, 0:R],
                                     scalar1=MAGIC, scalar2=None,
                                     op0=ALU.subtract))
        return conv

    def conv_sign(self, v, i, stg, wbf, R):
        self.p.sV.inc(v.tensor_scalar(out=wbf[:, 0:R], in0=stg[:, 0:R],
                                      scalar1=0.0, scalar2=0.5,
                                      op0=ALU.is_gt, op1=ALU.subtract))


    def phase_init(self):
        p, c, d = self.p, self.c, self.d
        self.idf = p.sb("idf", [128, 128], F32)
        self.idb = p.sb("idb", [128, 128], BF16)
        self.ones_row = p.sb("ones_row", [1, 128], F32)
        self.ones_col = p.sb("ones_col", [128, 1], F32)
        self.ones_col_b = p.sb("ones_col_b", [128, 1], BF16)
        self.uones = p.sb("uones", [128, 128], F32)
        self.mk_diag = p.sb("mk_diag", [128, 128], BF16)
        self.mk_far = p.sb("mk_far", [128, 128], BF16)
        self.mk_lowT = p.sb("mk_lowT", [128, 128], F32)
        self.rows = p.sb("rows", [1, 2048], F32)
        self.params_b = p.sb("params_b", [128, 32], F32)
        self.qgain_sb = p.sb("qgain_sb", [1, c.H], F32)
        self.Mh_b = p.sb("Mh_b", [128, c.H], F32)
        self.lnw_sb = p.sb("lnw_sb", [128, c.NF], F32)
        self.cw0 = p.sb("cw0", [128, c.NC1], F32)
        self.cw1 = p.sb("cw1", [128, c.NC1], F32)
        self.cb_ = p.sb("cb_", [128, c.NC1], F32)
        self.dtb_sb = p.sb("dtb_sb", [c.H1, 1], F32)
        self.negA = p.sb("negA", [c.H1, 1], F32)
        self.stat = p.sb("stat", [128, 8 * c.NT + 32], F32)
        self.tscr = p.sb("tscr", [128, 1024], F32)
        self.scr2 = p.sb("scr2", [128, 1024], F32)
        self.bc1 = p.sb("bc1", [128, 1024], F32)
        self.bc2 = p.sb("bc2", [128, 1024], F32)
        self.maxA = p.sb("maxA", [128, 1024], F32)
        self.maxB = p.sb("maxB", [128, 1024], F32)
        self.WSTG = [p.sb(f"wstg{i}", [128, 2048], F32) for i in range(2)]
        self.WBF = [p.sb(f"wbf{i}", [128, 2048], BF16) for i in range(2)]
        self.srep = p.sb("srep", [128, 3], F32)
        self.XR = p.sb("XR", [128, 2048], F32)
        self.sR = p.sb("sR", [128, 1024], BF16)
        self.rowsE = p.sb("rowsE", [1, 768], F32)
        self.csJ = p.sb("csJ", [128, 2], F32)
        self.sgn = p.sb("sgn", [128, 1], F32)
        self.XN = None
        self.QX = None

        psA = self.psA = p.ps("psA", [128, 1024])
        self.psB = p.ps("psB", [128, 1024])
        self.psC = p.ps("psC", [128, 1024])
        self.psY = p.ps("psY", [128, 1024])

        def fg(g):
            g.memset(self.idf[:], 0.0)
            g.memset(self.uones[:], 0.0)
            g.memset(self.mk_diag[:], 1.0)
            g.memset(self.mk_far[:], 1.0)
            g.memset(self.mk_lowT[:], 0.0)
            g.memset(self.ones_row[:], 1.0)
            g.memset(self.ones_col[:], 1.0)
            g.memset(self.ones_col_b[:], 1.0)
            g.memset(self.sgn[:], 1.0)
            g.drain()
            g.affine_select(out=self.sgn[:], in_=self.sgn[:], pattern=[[0, 1]],
                            compare_op=ALU.is_ge, fill=-1.0, base=63,
                            channel_multiplier=-1)
            g.drain()
            g.affine_select(out=self.idf[:], in_=self.idf[:], pattern=[[-1, 128]],
                            compare_op=ALU.not_equal, fill=1.0, base=0,
                            channel_multiplier=1)
            g.affine_select(out=self.uones[:], in_=self.uones[:], pattern=[[1, 128]],
                            compare_op=ALU.is_ge, fill=-1.0, base=0,
                            channel_multiplier=-1)
            g.affine_select(out=self.mk_diag[:], in_=self.mk_diag[:],
                            pattern=[[1, 128]], compare_op=ALU.is_ge, fill=0.0,
                            base=0, channel_multiplier=-1)
            g.affine_select(out=self.mk_far[:], in_=self.mk_far[:],
                            pattern=[[-1, 128]], compare_op=ALU.is_ge, fill=0.0,
                            base=-1, channel_multiplier=1)
            g.affine_select(out=self.mk_lowT[:], in_=self.mk_lowT[:],
                            pattern=[[1, 128]], compare_op=ALU.is_ge, fill=NEGINF,
                            base=0, channel_multiplier=-1)
            p.sG.inc(g.drain())
        self.on("gpsimd", fg)
        gm = p.sG.n

        def fv(v):
            v.wait_ge(p.sG.h, gm)
            p.sV.inc(v.tensor_scalar(out=self.uones[:], in0=self.uones[:],
                                     scalar1=1.0, scalar2=None, op0=ALU.add))
            p.sV.inc(v.tensor_copy(out=self.idb[:], in_=self.idf[:]))
        self.on("vector", fv)

        def fl(sync):
            p.sDW.dma(sync.dma_start(out=self.rows[0:1, 0:32], in_=d["params"][:]))
            p.sDW.dma(sync.dma_start(out=self.qgain_sb[:], in_=d["qgain"][:]))
            p.sDW.dma(sync.dma_start(out=self.lnw_sb[:], in_=d["lnw"][:]))
            p.sDW.dma(sync.dma_start(out=self.cw0[:], in_=d["convw0"][:]))
            p.sDW.dma(sync.dma_start(out=self.cw1[:], in_=d["convw1"][:]))
            p.sDW.dma(sync.dma_start(out=self.cb_[:], in_=d["convb"][:]))
            p.sDW.dma(sync.dma_start(out=self.dtb_sb[:], in_=d["dtb"][:]))
            p.sDW.dma(sync.dma_start(out=self.negA[:], in_=d["alog"][:]))
        self.on("sync", fl)
        ldm = p.sDW.n

        def fwait(v):
            v.wait_ge(p.sDW.h, ldm)
            p.sV.inc(v.drain())
        self.on("vector", fwait)
        self.bcast(self.rows[0:1, 0:32], self.params_b, 32, self.psC)

        def fs(s):
            s.wait_ge(p.sDW.h, ldm)
            s.activation(self.negA[:], self.negA[:], ACT.Exp)
            s.drain()
            s.activation(self.rows[0:1, 64:64 + c.H], self.qgain_sb[:],
                         ACT.Abs, scale=float(np.sqrt(c.HD)))
            p.sS.inc(s.drain())
        self.on("scalar", fs)
        sm = p.sS.n

        def fv3(v):
            v.wait_ge(p.sS.h, sm)
            p.sV.inc(v.tensor_scalar(out=self.negA[:], in0=self.negA[:],
                                     scalar1=-1.0, scalar2=None, op0=ALU.mult))
        self.on("vector", fv3)
        self.bcast(self.rows[0:1, 64:64 + c.H], self.Mh_b, c.H, self.psC)
        self.barrier()


    def ln_pass(self, src, n, slot_m, slot_r, eps, xn_chunks=None, slot_c=None,
                aq_eps=None):
        p, c = self.p, self.c
        nd, nt = c.ND, n // 128
        S = c.NT

        def stat_ap(slot):
            return self.stat[:, slot * S:slot * S + nt]

        dm, sm = {}, {}
        for k in range(nd):
            par = k % 2

            def fl(g, k=k, par=par):
                if k >= 2:
                    g.wait_ge(p.sT.h, sm[k - 2])
                sem = p.sDG if par == 0 else p.sDGb
                trio = (p.sDG, p.sDGb, p.sSTG)
                if sem.n > 0:
                    g.wait_ge(sem.h, sem.n)
                if k < 2:
                    for s__ in trio:
                        if s__ is not sem and s__.n > 0:
                            g.wait_ge(s__.h, s__.n)
                    for s__ in (p.sV, p.sS):
                        if s__.n > 0:
                            g.wait_ge(s__.h, s__.n)
                sem.dma(g.dma_start(out=self.XR[:, par * 1024:par * 1024 + n],
                                      in_=src[k * 128:(k + 1) * 128, :]))
            self.on("gpsimd", fl)
            dm[k] = ((p.sDG if par == 0 else p.sDGb), (p.sDG if par == 0 else p.sDGb).n)

            def fs(s, k=k, par=par):
                s.wait_ge(dm[k][0].h, dm[k][1])
                sq = self.maxA if par == 0 else self.maxB
                p.sS.inc(s.activation(sq[:, 0:n],
                                      self.XR[:, par * 1024:par * 1024 + n],
                                      ACT.Square))
            self.on("scalar", fs)
            sqm = p.sS.n

            def ft(te, k=k, par=par, sqm=sqm):
                te.wait_ge(dm[k][0].h, dm[k][1])
                te.wait_ge(p.sS.h, sqm)
                sq = self.maxA if par == 0 else self.maxB
                for j in range(0, n, 512):
                    w = min(512, n - j)
                    te.matmul(self.psC[0:1, j:j + w], self.ones_col[:],
                              self.XR[:, par * 1024 + j:par * 1024 + j + w],
                              start=(k == 0), stop=(k == nd - 1))
                    p.sT.inc(te.matmul(
                        self.psA[0:1, j:j + w], self.ones_col[:],
                        sq[:, j:j + w],
                        start=(k == 0), stop=(k == nd - 1)))
            self.on("tensor", ft)
            sm[k] = p.sT.n
        self.barrier()

        def fv(v):
            v.tensor_copy(out=self.rows[0:1, 0:n], in_=self.psC[0:1, 0:n])
            p.sV.inc(v.tensor_copy(out=self.rows[0:1, 1024:1024 + n],
                                   in_=self.psA[0:1, 0:n]))
        self.on("vector", fv)
        self.barrier()

        def ft2(te):
            for t in range(nt):
                te.transpose(self.psC[:, t:t + 1],
                             self.rows[0:1, t * 128:(t + 1) * 128], self.idf[0:1, 0:1])
                p.sT.inc(te.transpose(
                    self.psB[:, t:t + 1],
                    self.rows[0:1, 1024 + t * 128:1024 + (t + 1) * 128],
                    self.idf[0:1, 0:1]))
        self.on("tensor", ft2)
        self.barrier()

        def fv2(v):
            v.tensor_scalar(out=stat_ap(slot_m), in0=self.psC[:, 0:nt],
                            scalar1=1.0 / c.D, scalar2=None, op0=ALU.mult)
            p.sV.inc(v.tensor_scalar(out=stat_ap(slot_r), in0=self.psB[:, 0:nt],
                                     scalar1=1.0 / c.D, scalar2=None, op0=ALU.mult))
        self.on("vector", fv2)
        self.barrier()

        def fv3(v):
            v.tensor_tensor(out=self.tscr[:, 0:nt], in0=stat_ap(slot_m),
                            in1=stat_ap(slot_m), op=ALU.mult)
            v.drain()
            v.tensor_tensor(out=self.tscr[:, 0:nt], in0=stat_ap(slot_r),
                            in1=self.tscr[:, 0:nt], op=ALU.subtract)
            v.drain()
            p.sV.inc(v.tensor_scalar(out=self.tscr[:, 0:nt], in0=self.tscr[:, 0:nt],
                                     scalar1=eps, scalar2=None, op0=ALU.add))
        self.on("vector", fv3)
        self.barrier()

        def fs2(s):
            p.sS.inc(s.activation(self.tscr[:, 64:64 + nt], self.tscr[:, 0:nt],
                                  ACT.Sqrt))
        self.on("scalar", fs2)
        self.barrier()

        def fv4(v):
            p.sV.inc(v.reciprocal(out=stat_ap(slot_r), in_=self.tscr[:, 64:64 + nt]))
        self.on("vector", fv4)
        self.barrier()

        self.col2row(stat_ap(slot_m), 0, nt, self.psC)
        self.col2row(stat_ap(slot_r), 1024, nt, self.psC)
        self.barrier()
        self.bcast(self.rows[0:1, 0:n], self.bc1, n, self.psC)
        self.bcast(self.rows[0:1, 1024:1024 + n], self.bc2, n, self.psB)
        self.barrier()

        if xn_chunks is None:
            return

        dm2, vm2 = {}, {}
        for k in range(nd):
            par = k % 2

            def fl(g, k=k, par=par):
                if k >= 2:
                    g.wait_ge(p.sV.h, vm2[k - 2])
                sem = p.sDG if par == 0 else p.sDGb
                trio = (p.sDG, p.sDGb, p.sSTG)
                if sem.n > 0:
                    g.wait_ge(sem.h, sem.n)
                if k < 2:
                    for s__ in trio:
                        if s__ is not sem and s__.n > 0:
                            g.wait_ge(s__.h, s__.n)
                    for s__ in (p.sV, p.sS):
                        if s__.n > 0:
                            g.wait_ge(s__.h, s__.n)
                sem.dma(g.dma_start(out=self.XR[:, par * 1024:par * 1024 + n],
                                      in_=src[k * 128:(k + 1) * 128, :]))
            self.on("gpsimd", fl)
            dm2[k] = ((p.sDG if par == 0 else p.sDGb), (p.sDG if par == 0 else p.sDGb).n)

            def fv5(v, k=k, par=par):
                v.wait_ge(dm2[k][0].h, dm2[k][1])
                v.drain()
                xr = self.XR[:, par * 1024:par * 1024 + n]
                v.tensor_tensor(out=self.scr2[:, 0:n], in0=xr,
                                in1=self.bc1[:, 0:n], op=ALU.subtract)
                v.drain()
                v.tensor_tensor(out=xn_chunks[k],
                                in0=self.scr2[:, 0:n], in1=self.bc2[:, 0:n],
                                op=ALU.mult)
                if slot_c is not None:
                    if k == 0:
                        v.tensor_scalar(out=self.maxA[:, 0:n],
                                        in0=self.scr2[:, 0:n], scalar1=1.0,
                                        scalar2=None, op0=ALU.mult)
                        v.tensor_scalar(out=self.maxB[:, 0:n],
                                        in0=self.scr2[:, 0:n], scalar1=-1.0,
                                        scalar2=None, op0=ALU.mult)
                    else:
                        v.scalar_tensor_tensor(out=self.maxA[:, 0:n],
                                               in0=self.scr2[:, 0:n], scalar=1.0,
                                               in1=self.maxA[:, 0:n],
                                               op0=ALU.mult, op1=ALU.max)
                        v.scalar_tensor_tensor(out=self.maxB[:, 0:n],
                                               in0=self.scr2[:, 0:n], scalar=-1.0,
                                               in1=self.maxB[:, 0:n],
                                               op0=ALU.mult, op1=ALU.max)
                p.sV.inc(v.drain())
            self.on("vector", fv5)
            vm2[k] = p.sV.n
        self.barrier()

        if slot_c is None:
            return
        fin_max = self.maxA

        def fvcmb(v):
            p.sV.inc(v.tensor_tensor(out=self.maxA[:, 0:n], in0=self.maxA[:, 0:n],
                                     in1=self.maxB[:, 0:n], op=ALU.max))
        self.on("vector", fvcmb)
        self.barrier()

        def ft3(te):
            for t in range(nt):
                p.sT.inc(te.transpose(self.psC[:, t * 128:(t + 1) * 128],
                                      fin_max[:, t * 128:(t + 1) * 128], self.idf[:]))
        self.on("tensor", ft3)
        self.barrier()

        def fv6(v):
            for t in range(nt):
                v.reduce_max(self.tscr[:, t:t + 1],
                             self.psC[:, t * 128:(t + 1) * 128], axis=AX)
            v.drain()
            v.tensor_tensor(out=self.tscr[:, 64:64 + nt], in0=self.tscr[:, 0:nt],
                            in1=stat_ap(slot_r), op=ALU.mult)
            v.drain()
            if aq_eps is not None:
                v.tensor_scalar(out=self.tscr[:, 64:64 + nt],
                                in0=self.tscr[:, 64:64 + nt],
                                scalar1=aq_eps, scalar2=None, op0=ALU.max)
                v.drain()
            v.reciprocal(out=self.tscr[:, 128:128 + nt], in_=self.tscr[:, 64:64 + nt])
            v.drain()
            v.tensor_scalar(out=self.tscr[:, 128:128 + nt],
                            in0=self.tscr[:, 128:128 + nt], scalar1=127.0,
                            scalar2=None, op0=ALU.mult)
            v.drain()
            v.tensor_scalar(out=stat_ap(slot_c + 1), in0=self.tscr[:, 64:64 + nt],
                            scalar1=1.0 / 127.0, scalar2=None, op0=ALU.mult)
            p.sV.inc(v.tensor_tensor(out=stat_ap(slot_c), in0=self.tscr[:, 128:128 + nt],
                                     in1=stat_ap(slot_r), op=ALU.mult))
        self.on("vector", fv6)
        self.barrier()

        self.barrier()

    def qx_pass(self, src, slot_m, slot_c, t0, tn, dst):
        p, c = self.p, self.c
        S = c.NT
        self.col2row(self.stat[:, slot_m * S:slot_m * S + c.NT], 0, c.NT, self.psC)
        self.col2row(self.stat[:, slot_c * S:slot_c * S + c.NT], 1024, c.NT, self.psC)
        self.bcast(self.rows[0:1, t0:t0 + tn], self.bc1, tn, self.psC)
        self.bcast(self.rows[0:1, 1024 + t0:1024 + t0 + tn], self.bc2, self.psC.shape[-1] if False else tn, self.psB)
        dm3, vm3 = {}, {}
        for k in range(c.ND):
            par = k % 2

            def fl(g, k=k, par=par):
                if k >= 2:
                    g.wait_ge(p.sV.h, vm3[k - 2])
                sem = p.sDG if par == 0 else p.sDGb
                trio = (p.sDG, p.sDGb, p.sSTG)
                if sem.n > 0:
                    g.wait_ge(sem.h, sem.n)
                if k < 2:
                    for s__ in trio:
                        if s__ is not sem and s__.n > 0:
                            g.wait_ge(s__.h, s__.n)
                    for s__ in (p.sV, p.sS):
                        if s__.n > 0:
                            g.wait_ge(s__.h, s__.n)
                sem.dma(g.dma_start(out=self.XR[:, par * 1024:par * 1024 + tn],
                                      in_=src[k * 128:(k + 1) * 128, t0:t0 + tn]))
            self.on("gpsimd", fl)
            dm3[k] = ((p.sDG if par == 0 else p.sDGb), (p.sDG if par == 0 else p.sDGb).n)

            def fv7(v, k=k, par=par):
                v.wait_ge(dm3[k][0].h, dm3[k][1])
                v.drain()
                xr = self.XR[:, par * 1024:par * 1024 + tn]
                v.tensor_tensor(out=self.scr2[:, 0:tn], in0=xr,
                                in1=self.bc1[:, 0:tn], op=ALU.subtract)
                v.drain()
                v.tensor_tensor(out=self.scr2[:, 0:tn], in0=self.scr2[:, 0:tn],
                                in1=self.bc2[:, 0:tn], op=ALU.mult)
                v.drain()
                p.sV.inc(v.tensor_scalar(out=dst[:, k * tn:(k + 1) * tn],
                                         in0=self.scr2[:, 0:tn], scalar1=MAGIC,
                                         scalar2=MAGIC, op0=ALU.add,
                                         op1=ALU.subtract))
            self.on("vector", fv7)
            vm3[k] = p.sV.n
        self.barrier()


    def rope_tables(self):
        p, c = self.p, self.c
        nk = c.NKT * 128
        half = c.HD // 2

        def fg(g):
            g.iota(self.csJ[:, 0:1], pattern=[[0, 1]], base=0, channel_multiplier=1,
                   allow_small_or_imprecise_dtypes=True)
            g.memset(self.csJ[:, 1:2], float(half))
            g.drain()
            g.affine_select(out=self.csJ[:, 1:2], in_=self.csJ[:, 1:2],
                            pattern=[[0, 1]], compare_op=ALU.is_ge, fill=0.0,
                            base=-half, channel_multiplier=1)
            g.drain()
            g.iota(self.cosT[:, 0:nk], pattern=[[1, nk]], base=0,
                   channel_multiplier=0, allow_small_or_imprecise_dtypes=True)
            p.sG.inc(g.drain())
        self.on("gpsimd", fg)
        gm = p.sG.n

        def fv0(v):
            v.wait_ge(p.sG.h, gm)
            p.sV.inc(v.tensor_tensor(out=self.csJ[:, 0:1], in0=self.csJ[:, 0:1],
                                     in1=self.csJ[:, 1:2], op=ALU.subtract))
        self.on("vector", fv0)
        vm0 = p.sV.n

        def fs(s):
            s.wait_ge(p.sV.h, vm0)
            p.sS.inc(s.activation(self.csJ[:, 1:2], self.csJ[:, 0:1], ACT.Exp,
                                  scale=-float(np.log(c.ROPE_BASE)) / half))
        self.on("scalar", fs)
        sm = p.sS.n

        def fv(v):
            v.wait_ge(p.sS.h, sm)
            v.tensor_scalar(out=self.cosT[:, 0:nk], in0=self.cosT[:, 0:nk],
                            scalar1=self.params_b[:, 0:1], scalar2=None,
                            op0=ALU.add)
            v.drain()
            v.tensor_scalar(out=self.cosT[:, 0:nk], in0=self.cosT[:, 0:nk],
                            scalar1=float(-c.WIN), scalar2=self.csJ[:, 1:2],
                            op0=ALU.add, op1=ALU.mult)
            v.drain()
            v.tensor_copy(out=self.sinT[:, 0:nk], in_=self.cosT[:, 0:nk])
            v.tensor_scalar(out=self.cosT[:, 0:nk], in0=self.cosT[:, 0:nk],
                            scalar1=float(np.pi / 2), scalar2=None, op0=ALU.add)
            v.drain()
            for buf in (self.sinT, self.cosT):
                v.tensor_scalar(out=self.XR[:, 0:nk], in0=buf[:, 0:nk],
                                scalar1=1.0 / LN2PI, scalar2=MAGIC, op0=ALU.mult,
                                op1=ALU.add)
                v.drain()
                v.tensor_scalar(out=self.XR[:, 0:nk], in0=self.XR[:, 0:nk],
                                scalar1=MAGIC, scalar2=-LN2PI, op0=ALU.subtract,
                                op1=ALU.mult)
                v.drain()
                v.tensor_tensor(out=buf[:, 0:nk], in0=buf[:, 0:nk],
                                in1=self.XR[:, 0:nk], op=ALU.add)
                v.drain()
            p.sV.inc(v.drain())
        self.on("vector", fv)
        vm = p.sV.n

        def fs2(s):
            s.wait_ge(p.sV.h, vm)
            s.activation(self.sinT[:, 0:nk], self.sinT[:, 0:nk], ACT.Sin)
            s.drain()
            p.sS.inc(s.activation(self.cosT[:, 0:nk], self.cosT[:, 0:nk],
                                  ACT.Sin))
        self.on("scalar", fs2)
        self.barrier()



    def _attn_segs(self, kind):
        c = self.c
        HOFF = ((c.WIN + 511) // 512) * 512
        segs = []
        if kind == "q":
            spans = [(0, c.T, 0)]
        else:
            spans = [(0, c.WIN, 0), (HOFF, c.T, c.WIN)]
        for off, wd, coff in spans:
            j = 0
            while j < wd:
                o = off + j
                w = min(512 - (o % 512), wd - j)
                ps = self.psA if o < 1024 else self.psB
                po = o if o < 1024 else o - 1024
                segs.append((ps, po, w, coff + j))
                j += w
        return segs

    def rope_scale(self, kind, dst):
        p, c = self.p, self.c
        segs = self._attn_segs(kind)
        cs0 = c.WIN if kind == "q" else 0

        def fv(v):
            for ps, po, w, co in segs:
                bck = self.bc1 if co < 1024 else self.bc2
                bo = co if co < 1024 else co - 1024
                qq = ps[:, po:po + w]
                cosv = self.cosT[:, cs0 + co:cs0 + co + w]
                sinv = self.sinT[:, cs0 + co:cs0 + co + w]
                XA = self.XR[:, 0:w]
                XB = self.XR[:, 1024:1024 + w]
                v.tensor_copy(out=XA[0:64, :], in_=ps[64:128, po:po + w])
                v.tensor_copy(out=XA[64:128, :], in_=ps[0:64, po:po + w])
                v.drain()
                v.tensor_tensor(out=XB, in0=qq, in1=cosv, op=ALU.mult)
                v.tensor_tensor(out=XA, in0=XA, in1=sinv, op=ALU.mult)
                v.drain()
                v.scalar_tensor_tensor(out=XB, in0=XA, scalar=self.sgn[:],
                                       in1=XB, op0=ALU.mult, op1=ALU.add)
                v.drain()
                v.tensor_tensor(out=dst[:, co:co + w], in0=XB,
                                in1=bck[:, bo:bo + w], op=ALU.mult)
                v.drain()
        self.step("vector", fv)

    def rms_scale_row(self, kind, gain_idx):
        p, c = self.p, self.c
        segs = self._attn_segs(kind)
        ncols = (c.T if kind == "q" else c.WIN + c.T)

        def fs(s):
            for ps, po, w, co in segs:
                dump = self.maxA if co < 1024 else self.maxB
                do = co if co < 1024 else co - 1024
                s.activation(dump[:, do:do + w], ps[:, po:po + w], ACT.Square)
        self.step("scalar", fs)

        def ft(te):
            for j in range(0, ncols, 512):
                w = min(512, ncols - j)
                src = (self.maxA[:, j:j + w] if j < 1024
                       else self.maxB[:, j - 1024:j - 1024 + w])
                pdst = (self.psC[0:1, j:j + w] if j < 1024
                        else self.psY[0:1, j - 1024:j - 1024 + w])
                te.matmul(pdst, self.ones_col[:], src, start=True, stop=True)
        self.step("tensor", ft)

        def fv(v):
            v.tensor_copy(out=self.rows[0:1, 0:min(ncols, 1024)],
                          in_=self.psC[0:1, 0:min(ncols, 1024)])
            if ncols > 1024:
                v.tensor_copy(out=self.rows[0:1, 1024:ncols],
                              in_=self.psY[0:1, 0:ncols - 1024])
            v.drain()
            v.tensor_scalar(out=self.rows[0:1, 0:ncols], in0=self.rows[0:1, 0:ncols],
                            scalar1=1.0 / c.HD, scalar2=c.EPS_QK,
                            op0=ALU.mult, op1=ALU.add)
        self.step("vector", fv)

        def fs2(s):
            s.activation(self.rows[0:1, 0:ncols], self.rows[0:1, 0:ncols], ACT.Sqrt)
        self.step("scalar", fs2)

        def fv2(v):
            v.reciprocal(out=self.rows[0:1, 0:ncols], in_=self.rows[0:1, 0:ncols])
            if gain_idx is not None:
                v.drain()
                v.tensor_scalar(out=self.rows[0:1, 0:ncols],
                                in0=self.rows[0:1, 0:ncols],
                                scalar1=self.qgain_sb[0:1, gain_idx:gain_idx + 1],
                                scalar2=float(1.0 / np.sqrt(c.HD)),
                                op0=ALU.mult, op1=ALU.mult)
        self.step("vector", fv2)

    def phase_attn(self):
        p, c, d = self.p, self.c, self.d
        NT, ND, NKT, HT, H = c.NT, c.ND, c.NKT, c.HT, c.H
        TW = c.T + c.WIN
        nk = NKT * 128
        nkv = c.KVH * c.HD // 128
        XN = self.XN

        def xn_own(k):
            return XN[:, k * TW: k * TW + c.T]

        def xn_halo(k):
            return XN[:, k * TW + c.T: k * TW + TW]

        self.rope_tables()

        slabs = ([("k", s_, d["wk"].rearrange("d (s c) -> s d c", c=128)[s_]) for s_ in range(nkv)]
                 + [("v", s_, d["wv"].rearrange("d (s c) -> s d c", c=128)[s_]) for s_ in range(nkv)])

        for kind, hh, sl in slabs:
            def fl(sync, sl=sl):
                p.sDW.dma(self.load_slab(sync, self.WSTG[0], sl))
            self.step("sync", fl)

            def fc(v):
                p.sV.inc(v.tensor_copy(out=self.WBF[0][:, 0:c.D],
                                       in_=self.WSTG[0][:, 0:c.D]))
            self.step("vector", fc)

            HOFF = ((c.WIN + 511) // 512) * 512

            def ft(te, kind=kind):
                for kk in range(ND):
                    lhs = self.WBF[0][:, kk * 128:(kk + 1) * 128]
                    if kind == "q":
                        srcs = [(xn_own(kk), 0)]
                    else:
                        srcs = [(xn_halo(kk), 0), (xn_own(kk), HOFF)]
                    for src, off in srcs:
                        w_all = src.shape[-1]
                        for j in range(0, w_all, 512):
                            w = min(512, w_all - j)
                            o = off + j
                            ps = (self.psA[:, o:o + w] if o < 1024 else
                                  self.psB[:, o - 1024:o - 1024 + w])
                            te.matmul(ps, lhs, src[:, j:j + w],
                                      start=(kk == 0), stop=(kk == ND - 1))
            self.step("tensor", ft)

            if kind == "v":
                def fv1(v):
                    HOFF = ((c.WIN + 511) // 512) * 512
                    v.tensor_copy(out=self.vTr[:, 0:c.WIN], in_=self.psA[:, 0:c.WIN])
                    for j in range(0, c.T, 512):
                        w = min(512, c.T - j)
                        o = HOFF + j
                        src = (self.psA[:, o:o + w] if o < 1024 else
                               self.psB[:, o - 1024:o - 1024 + w])
                        v.tensor_copy(out=self.vTr[:, c.WIN + j:c.WIN + j + w],
                                      in_=src)
                self.step("vector", fv1)

                def ftv(te):
                    for t in range(NKT):
                        ps = self.psC if t < 8 else self.psY
                        te.matmul(ps[:, (t % 8) * 128:(t % 8) * 128 + 128],
                                  self.vTr[:, t * 128:(t + 1) * 128], self.idb[:],
                                  start=True, stop=True)
                self.step("tensor", ftv)

                def fv2(v, hh=hh):
                    for t in range(NKT):
                        ps = self.psC if t < 8 else self.psY
                        v.tensor_copy(
                            out=self.V[:, t * (nkv * 128) + hh * 128:
                                       t * (nkv * 128) + hh * 128 + 128],
                            in_=ps[:, (t % 8) * 128:(t % 8) * 128 + 128])
                self.step("vector", fv2)
            else:
                ncols = c.WIN + c.T
                self.rms_scale_row(kind, None)
                self.bcast(self.rows[0:1, 0:min(ncols, 1024)],
                           self.bc1, min(ncols, 1024), self.psC)
                if ncols > 1024:
                    self.bcast(self.rows[0:1, 1024:ncols],
                               self.bc2, ncols - 1024, self.psY)
                self.rope_scale(kind, self.KRR[:, hh * nk:(hh + 1) * nk])
        self.barrier()

        for h in range(H):
            kv = h // (H // c.KVH)

            def fl(sync, h=h):
                p.sDW.dma(self.load_slab(sync, self.WSTG[0], d["wq"][h]))
            self.step("sync", fl)

            def fc(v):
                p.sV.inc(v.tensor_copy(out=self.WBF[0][:, 0:c.D],
                                       in_=self.WSTG[0][:, 0:c.D]))
            self.step("vector", fc)

            def ftq(te):
                for kk in range(ND):
                    for j in range(0, c.T, 512):
                        w = min(512, c.T - j)
                        te.matmul(self.psA[:, j:j + w],
                                  self.WBF[0][:, kk * 128:(kk + 1) * 128],
                                  xn_own(kk)[:, j:j + w],
                                  start=(kk == 0), stop=(kk == ND - 1))
            self.step("tensor", ftq)
            self.rms_scale_row("q", h)
            self.bcast(self.rows[0:1, 0:c.T], self.bc1, c.T, self.psC)
            self.rope_scale("q", self.QR[:, 0:c.T])

            def fve(v, h=h):
                p.sV.inc(v.tensor_scalar(
                    out=self.tscr[:, 0:NKT], in0=self.params_b[:, 3:3 + NKT],
                    scalar1=self.Mh_b[:, h:h + 1], scalar2=None, op0=ALU.subtract))
            self.step("vector", fve)

            for j in range(NKT):
                qlo, qhi = max(0, j - HT), min(NT - 1, j)
                wq_ = (qhi - qlo + 1) * 128
                slot = (j % c.W5) * (c.W5 * 128)

                def fts(te, j=j, kv=kv, qlo=qlo, wq_=wq_):
                    for o in range(0, wq_, 512):
                        w = min(512, wq_ - o)
                        te.matmul(self.psB[:, o:o + w],
                                  self.KRR[:, kv * nk + j * 128:kv * nk + (j + 1) * 128],
                                  self.QR[:, qlo * 128 + o:qlo * 128 + o + w],
                                  start=True, stop=True)
                self.step("tensor", fts)

                def fse(s, j=j, wq_=wq_, slot=slot):
                    s.activation(self.pr[:, slot:slot + wq_], self.psB[:, 0:wq_],
                                 ACT.Exp, bias=self.tscr[:, j:j + 1])
                self.step("scalar", fse)

                def fvm(v, j=j, qlo=qlo, qhi=qhi, slot=slot):
                    for qt in range(qlo, qhi + 1):
                        o = slot + (qt - qlo) * 128
                        if qt == j - HT:
                            v.tensor_tensor(out=self.pr[:, o:o + 128],
                                            in0=self.pr[:, o:o + 128],
                                            in1=self.mk_diag[:], op=ALU.mult)
                        elif qt == j:
                            v.tensor_tensor(out=self.pr[:, o:o + 128],
                                            in0=self.pr[:, o:o + 128],
                                            in1=self.mk_far[:], op=ALU.mult)
                self.step("vector", fvm)

                qt0 = j - HT
                if 0 <= qt0 <= NT - 1:
                    def ftp(te, qt0=qt0, kv=kv):
                        for kt in range(qt0, qt0 + HT + 1):
                            sl = (kt % c.W5) * (c.W5 * 128)
                            off = sl + (qt0 - max(0, kt - HT)) * 128
                            te.matmul(self.psY[:, qt0 * 128:(qt0 + 1) * 128],
                                      self.V[:, kt * (nkv * 128) + kv * 128:
                                             kt * (nkv * 128) + kv * 128 + 128],
                                      self.pr[:, off:off + 128],
                                      start=(kt == qt0), stop=(kt == qt0 + HT))
                        for kt in range(qt0, qt0 + HT + 1):
                            sl = (kt % c.W5) * (c.W5 * 128)
                            off = sl + (qt0 - max(0, kt - HT)) * 128
                            te.matmul(self.psC[0:1, qt0 * 128:(qt0 + 1) * 128],
                                      self.ones_col_b[:], self.pr[:, off:off + 128],
                                      start=(kt == qt0), stop=(kt == qt0 + HT))
                    self.step("tensor", ftp)

            def fvd(v):
                v.tensor_copy(out=self.rows[0:1, 0:c.T], in_=self.psC[0:1, 0:c.T])
                v.drain()
                v.reciprocal(out=self.rows[0:1, 0:c.T], in_=self.rows[0:1, 0:c.T])
            self.step("vector", fvd)
            self.bcast(self.rows[0:1, 0:c.T], self.bc1, c.T, self.psC)

            def fvy(v, h=h):
                v.tensor_tensor(out=self.Y[:, h * c.T:(h + 1) * c.T],
                                in0=self.psY[:, 0:c.T], in1=self.bc1[:, 0:c.T],
                                op=ALU.mult)
            self.step("vector", fvy)
        self.barrier()

        for dc in range(ND):
            def fl(sync, dc=dc):
                p.sDW.dma(self.load_slab(sync, self.WSTG[0], d["wo"][dc]))
            self.step("sync", fl)

            def fc(v):
                p.sV.inc(v.tensor_copy(out=self.WBF[0][:, 0:c.D],
                                       in_=self.WSTG[0][:, 0:c.D]))
            self.step("vector", fc)

            def ft(te):
                for kk in range(ND):
                    for j in range(0, c.T, 512):
                        w = min(512, c.T - j)
                        te.matmul(self.psA[:, j:j + w],
                                  self.WBF[0][:, kk * 128:(kk + 1) * 128],
                                  self.Y[:, kk * c.T + j:kk * c.T + j + w],
                                  start=(kk == 0), stop=(kk == ND - 1))
            self.step("tensor", ft)

            def fv(v):
                p.sV.inc(v.tensor_copy(out=self.sR[:, 0:c.T], in_=self.psA[:, 0:c.T]))
            self.step("vector", fv)

            def fst(sync, dc=dc):
                p.sST.dma(sync.dma_start(out=d["sT_sp"][dc * 128:(dc + 1) * 128, :],
                                         in_=self.sR[:, 0:c.T]))
            self.step("sync", fst)
        self.barrier()

    def bcast2(self, row_ap, out_sb, out_off, n, psum):
        p = self.p

        def ft(te):
            self._twait(te)
            for j in range(0, n, 512):
                w = min(512, n - j)
                p.sT.inc(te.matmul(psum[:, j:j + w], self.ones_row[:, 0:128],
                                   row_ap[:, j:j + w], start=True, stop=True))
        self.on("tensor", ft)
        mark = p.sT.n

        def fv(v):
            v.wait_ge(p.sT.h, mark)
            p.sV.inc(v.tensor_copy(out=out_sb[:, out_off:out_off + n],
                                   in_=psum[:, 0:n]))
        self.on("vector", fv)


    def phase_ffn0_stats(self):
        p, c, d = self.p, self.c, self.d
        mats = [(0, d["wg0"], c.NF, c.D), (1, d["wu0"], c.NF, c.D),
                (2, d["wd0"], c.ND, c.F)]

        def fz(v):
            v.memset(self.tscr[:, 0:8], 0.0)
        self.step("vector", fz)

        for m, dram, nsl, R in mats:
            qn = R // 2048 if R > 2048 else 1
            Rq = min(R, 2048)
            idx = 0
            loadm, actm, vmm = {}, {}, {}
            for i in range(nsl):
                for q4 in range(qn):
                    par = idx % 2

                    def fl(s_, i=i, q4=q4, par=par, dram=dram, Rq=Rq, idx=idx):
                        if idx >= 2:
                            s_.wait_ge(p.sS.h, actm[idx - 2])
                        sl = dram[i, q4 * Rq:(q4 + 1) * Rq, :]
                        sem = p.sDS if par == 0 else p.sDSb
                        trio = (p.sDS, p.sDSb)
                        if sem.n > 0:
                            s_.wait_ge(sem.h, sem.n)
                        if idx < 2:
                            for s__ in trio:
                                if s__ is not sem and s__.n > 0:
                                    s_.wait_ge(s__.h, s__.n)
                            for s__ in (p.sV, p.sS):
                                if s__.n > 0:
                                    s_.wait_ge(s__.h, s__.n)
                        sem.dma(self.load_slab(s_, self.WSTG[par], sl))
                    self.on("scalar", fl)
                    loadm[idx] = ((p.sDS if par == 0 else p.sDSb),
                                  (p.sDS if par == 0 else p.sDSb).n)

                    def fa(s_, idx=idx, par=par, m=m, Rq=Rq):
                        s_.wait_ge(loadm[idx][0].h, loadm[idx][1])
                        if idx >= 1:
                            s_.wait_ge(p.sV.h, vmm[idx - 1])
                        for hh in range(0, Rq, 1024):
                            w = min(1024, Rq - hh)
                            dump = self.maxA if par == 0 else self.maxB
                            s_.activation(
                                dump[:, 0:w], self.WSTG[par][:, hh:hh + w],
                                ACT.Abs,
                                accum_out=self.tscr[:, 16 + 2 * par + hh // 1024:
                                                    17 + 2 * par + hh // 1024])
                        p.sS.inc(s_.drain())
                    self.on("scalar", fa)
                    actm[idx] = p.sS.n

                    def fv(v, idx=idx, par=par, m=m, Rq=Rq):
                        v.wait_ge(p.sS.h, actm[idx])
                        for hh in range(0, Rq, 1024):
                            v.tensor_tensor(out=self.tscr[:, m:m + 1],
                                            in0=self.tscr[:, m:m + 1],
                                            in1=self.tscr[:, 16 + 2 * par + hh // 1024:
                                                          17 + 2 * par + hh // 1024],
                                            op=ALU.add)
                            v.drain()
                        p.sV.inc(v.drain())
                    self.on("vector", fv)
                    vmm[idx] = p.sV.n
                    idx += 1
        self.barrier()

        def ft(te):
            te.matmul(self.psC[0:1, 0:3], self.ones_col[:], self.tscr[:, 0:3],
                      start=True, stop=True)
        self.step("tensor", ft)

        def fv2(v):
            counts = [float(c.F * c.D)] * 3
            for m in range(3):
                v.tensor_scalar(out=self.rowsE[0:1, 256 + m:257 + m],
                                in0=self.psC[0:1, m:m + 1],
                                scalar1=1.0 / counts[m], scalar2=1e-5,
                                op0=ALU.mult, op1=ALU.max)
            v.drain()
            v.reciprocal(out=self.rowsE[0:1, 260:263], in_=self.rowsE[0:1, 256:259])
        self.step("vector", fv2)

        def ft2(te):
            te.matmul(self.psC[:, 0:3], self.ones_row[:, 0:128],
                      self.rowsE[0:1, 260:263], start=True, stop=True)
        self.step("tensor", ft2)

        def fv3(v):
            v.tensor_copy(out=self.srep[:, 0:3], in_=self.psC[:, 0:3])
        self.step("vector", fv3)

    def ffn_gateup(self, wg, wu, rhs_buf, H, conv_g, conv_u, epi, tn):
        p, c = self.p, self.c
        loadm, convm, usem = {}, {}, {}
        epim = [0]
        for fc in range(c.NF):
            for kind in (0, 1):
                i = 2 * fc + kind
                par = i % 2
                sl = (wg if kind == 0 else wu)[fc]
                conv = conv_g if kind == 0 else conv_u

                def fl(sync, i=i, par=par, sl=sl):
                    if i >= 2:
                        sync.wait_ge(p.sV.h, convm[i - 2])
                    sem = p.sDW if par == 0 else p.sDWb
                    trio = (p.sDW, p.sDWb, p.sST)
                    if sem.n > 0:
                        sync.wait_ge(sem.h, sem.n)
                    if i < 2:
                        for s__ in trio:
                            if s__ is not sem and s__.n > 0:
                                sync.wait_ge(s__.h, s__.n)
                        for s__ in (p.sV, p.sS):
                            if s__.n > 0:
                                sync.wait_ge(s__.h, s__.n)
                    sem.dma(self.load_slab(sync, self.WSTG[par], sl))
                self.on("sync", fl)
                loadm[i] = ((p.sDW if par == 0 else p.sDWb), (p.sDW if par == 0 else p.sDWb).n)

                def fc_(v, i=i, par=par, conv=conv):
                    v.wait_ge(loadm[i][0].h, loadm[i][1])
                    if i >= 2:
                        v.wait_ge(p.sT.h, usem[i - 2])
                    conv(v, i, self.WSTG[par], self.WBF[par], c.D)
                self.on("vector", fc_)
                convm[i] = p.sV.n

                def fu(te, i=i, par=par, kind=kind):
                    te.wait_ge(p.sV.h, convm[i])
                    te.wait_ge(p.sV.h, epim[0])
                    ps = self.psA if kind == 0 else self.psB
                    for kk in range(c.ND):
                        for j in range(0, tn, 512):
                            w = min(512, tn - j)
                            te.matmul(ps[:, j:j + w],
                                      self.WBF[par][:, kk * 128:(kk + 1) * 128],
                                      rhs_buf[:, kk * tn + j:kk * tn + j + w],
                                      start=(kk == 0), stop=(kk == c.ND - 1))
                    p.sT.inc(te.drain())
                self.on("tensor", fu)
                usem[i] = p.sT.n
            epi(fc)
            epim[0] = p.sV.n
        self.barrier()

    def phase_ffn0(self, H, t0, tn):
        p, c, d = self.p, self.c, self.d
        NT, ND, NF = c.NT, c.ND, c.NF
        S = c.NT
        tt0, nt2 = t0 // 128, tn // 128
        TB = 8 * c.NT

        def stat_ap(slot, off=0, n=None):
            n = nt2 if n is None else n
            return self.stat[:, slot * S + tt0 + off:slot * S + tt0 + off + n]

        self.qx_pass(d["x_own"], 0, 2, t0, tn, self.QX)

        self.col2row(self.stat[:, 3 * S:3 * S + NT], 0, NT, self.psC)

        def fvr(v):
            v.tensor_scalar(out=self.rows[0:1, 1024 + t0:1024 + t0 + tn],
                            in0=self.rows[0:1, t0:t0 + tn],
                            scalar1=self.rowsE[0:1, 257:258], scalar2=None,
                            op0=ALU.mult)
            v.drain()
            v.tensor_scalar(out=self.rows[0:1, t0:t0 + tn],
                            in0=self.rows[0:1, t0:t0 + tn],
                            scalar1=self.rowsE[0:1, 256:257], scalar2=None,
                            op0=ALU.mult)
        self.step("vector", fvr)
        self.bcast(self.rows[0:1, t0:t0 + tn], self.bc1, tn, self.psC)
        self.bcast(self.rows[0:1, 1024 + t0:1024 + t0 + tn], self.bc2, tn, self.psC)

        conv_g = self.make_conv_tern_idx(0)
        conv_u = self.make_conv_tern_idx(1)

        def epi(fc):
            def s1(v, fc=fc):
                v.tensor_tensor(out=self.scr2[:, 0:tn], in0=self.psA[:, 0:tn],
                                in1=self.bc1[:, 0:tn], op=ALU.mult)
                v.tensor_tensor(out=self.XR[:, 0:tn], in0=self.psB[:, 0:tn],
                                in1=self.bc2[:, 0:tn], op=ALU.mult)
            self.step("vector", s1)

            def s2(s, fc=fc):
                s.activation(self.tscr[:, 0:tn], self.scr2[:, 0:tn], ACT.Silu)
            self.step("scalar", s2)

            def s3(v, fc=fc):
                v.tensor_tensor(out=H[:, fc * tn:(fc + 1) * tn],
                                in0=self.tscr[:, 0:tn], in1=self.XR[:, 0:tn],
                                op=ALU.mult)
                v.drain()
                v.tensor_scalar(out=self.scr2[:, 0:tn],
                                in0=H[:, fc * tn:(fc + 1) * tn],
                                scalar1=self.lnw_sb[:, fc:fc + 1], scalar2=None,
                                op0=ALU.mult)
                v.drain()
                if fc == 0:
                    v.tensor_scalar(out=self.maxA[:, 0:tn], in0=self.scr2[:, 0:tn],
                                    scalar1=1.0, scalar2=None, op0=ALU.mult)
                    v.tensor_scalar(out=self.maxB[:, 0:tn], in0=self.scr2[:, 0:tn],
                                    scalar1=-1.0, scalar2=None, op0=ALU.mult)
                else:
                    v.scalar_tensor_tensor(out=self.maxA[:, 0:tn],
                                           in0=self.scr2[:, 0:tn], scalar=1.0,
                                           in1=self.maxA[:, 0:tn], op0=ALU.mult,
                                           op1=ALU.max)
                    v.scalar_tensor_tensor(out=self.maxB[:, 0:tn],
                                           in0=self.scr2[:, 0:tn], scalar=-1.0,
                                           in1=self.maxB[:, 0:tn], op0=ALU.mult,
                                           op1=ALU.max)
            self.step("vector", s3)

            def s4(s, fc=fc):
                s.activation(self.tscr[:, 0:tn], H[:, fc * tn:(fc + 1) * tn],
                             ACT.Square)
            self.step("scalar", s4)

            def s5(te, fc=fc):
                for j in range(0, tn, 512):
                    w = min(512, tn - j)
                    te.matmul(self.psC[0:1, j:j + w], self.ones_col[:],
                              self.tscr[:, j:j + w], start=(fc == 0),
                              stop=(fc == NF - 1))
            self.step("tensor", s5)

        self.ffn_gateup(d["wg0"], d["wu0"], self.QX, H, conv_g, conv_u, epi, tn)

        fin_max = self.maxA

        def fq1(v):
            v.tensor_tensor(out=self.maxA[:, 0:tn], in0=self.maxA[:, 0:tn],
                            in1=self.maxB[:, 0:tn], op=ALU.max)
            v.drain()
            v.tensor_copy(out=self.rows[0:1, 0:tn], in_=self.psC[0:1, 0:tn])
        self.step("vector", fq1)

        def fq2(te):
            for t in range(nt2):
                te.transpose(self.psB[:, t:t + 1],
                             self.rows[0:1, t * 128:(t + 1) * 128], self.idf[0:1, 0:1])
            for t in range(nt2):
                te.transpose(self.psY[:, t * 128:(t + 1) * 128],
                             fin_max[:, t * 128:(t + 1) * 128], self.idf[:])
        self.step("tensor", fq2)

        def fq3(v):
            st = self.stat
            for t in range(nt2):
                v.reduce_max(st[:, TB + t:TB + t + 1], self.psY[:, t * 128:(t + 1) * 128],
                             axis=AX)
            v.drain()
            v.tensor_scalar(out=st[:, TB + 8:TB + 8 + nt2], in0=self.psB[:, 0:nt2],
                            scalar1=1.0 / c.F, scalar2=c.EPS_FFN0_RMS,
                            op0=ALU.mult, op1=ALU.add)
        self.step("vector", fq3)

        def fq4(s):
            s.activation(self.stat[:, TB + 8:TB + 8 + nt2], self.stat[:, TB + 8:TB + 8 + nt2],
                         ACT.Sqrt)
        self.step("scalar", fq4)

        def fq5(v):
            st = self.stat
            v.reciprocal(out=st[:, TB + 8:TB + 8 + nt2], in_=st[:, TB + 8:TB + 8 + nt2])
            v.drain()
            v.tensor_tensor(out=st[:, TB + 16:TB + 16 + nt2], in0=st[:, TB + 8:TB + 8 + nt2],
                            in1=st[:, TB:TB + nt2], op=ALU.mult)
            v.drain()
            v.tensor_scalar(out=st[:, TB + 16:TB + 16 + nt2], in0=st[:, TB + 16:TB + 16 + nt2],
                            scalar1=c.EPS_FFN0_AQ, scalar2=None, op0=ALU.max)
            v.drain()
            v.reciprocal(out=st[:, TB + 24:TB + 24 + nt2], in_=st[:, TB + 16:TB + 16 + nt2])
            v.drain()
            v.tensor_scalar(out=st[:, TB + 24:TB + 24 + nt2], in0=st[:, TB + 24:TB + 24 + nt2],
                            scalar1=127.0, scalar2=None, op0=ALU.mult)
            v.drain()
            v.tensor_tensor(out=st[:, TB + 24:TB + 24 + nt2], in0=st[:, TB + 24:TB + 24 + nt2],
                            in1=st[:, TB + 8:TB + 8 + nt2], op=ALU.mult)
            v.drain()
            v.tensor_scalar(out=st[:, TB + 16:TB + 16 + nt2], in0=st[:, TB + 16:TB + 16 + nt2],
                            scalar1=1.0 / 127.0, scalar2=None, op0=ALU.mult)
        self.step("vector", fq5)

        self.col2row(self.stat[:, TB + 24:TB + 24 + nt2], 0, nt2, self.psC)
        self.col2row(self.stat[:, TB + 16:TB + 16 + nt2], 1024, nt2, self.psC)

        def fq6(v):
            v.tensor_scalar(out=self.rows[0:1, 1024:1024 + tn],
                            in0=self.rows[0:1, 1024:1024 + tn],
                            scalar1=self.rowsE[0:1, 258:259], scalar2=None,
                            op0=ALU.mult)
        self.step("vector", fq6)
        self.bcast(self.rows[0:1, 0:tn], self.bc1, tn, self.psC)
        self.bcast(self.rows[0:1, 1024:1024 + tn], self.bc2, tn, self.psC)

        for fc in range(NF):
            def fqz(v, fc=fc):
                hsl = H[:, fc * tn:(fc + 1) * tn]
                v.tensor_scalar(out=self.scr2[:, 0:tn], in0=hsl,
                                scalar1=self.lnw_sb[:, fc:fc + 1], scalar2=None,
                                op0=ALU.mult)
                v.drain()
                v.tensor_tensor(out=self.scr2[:, 0:tn], in0=self.scr2[:, 0:tn],
                                in1=self.bc1[:, 0:tn], op=ALU.mult)
                v.drain()
                v.tensor_scalar(out=hsl, in0=self.scr2[:, 0:tn], scalar1=MAGIC,
                                scalar2=MAGIC, op0=ALU.add, op1=ALU.subtract)
            self.step("vector", fqz)
        self.barrier()

    def phase_down_combine(self, H, wd, conv, unscale_bc, layer, t0, tn):
        p, c, d = self.p, self.c, self.d
        ND, NF, NT = c.ND, c.NF, c.NT
        xsrc = d["x_own"] if layer == 0 else d["xT_sp"]
        loadm, convm, usem = {}, {}, {}
        epim = [0]
        idx = 0
        NQ4 = max(1, c.F // 2048)
        RQ4 = min(2048, c.F)
        for dc in range(ND):
            for q4 in range(NQ4):
                par = idx % 2

                def fl(sync, dc=dc, q4=q4, par=par, idx=idx):
                    if idx >= 2:
                        sync.wait_ge(p.sV.h, convm[idx - 2])
                    sl = wd[dc, q4 * RQ4:(q4 + 1) * RQ4, :]
                    sem = p.sDW if par == 0 else p.sDWb
                    trio = (p.sDW, p.sDWb, p.sST)
                    if sem.n > 0:
                        sync.wait_ge(sem.h, sem.n)
                    if idx < 2:
                        for s__ in trio:
                            if s__ is not sem and s__.n > 0:
                                sync.wait_ge(s__.h, s__.n)
                        for s__ in (p.sV, p.sS):
                            if s__.n > 0:
                                sync.wait_ge(s__.h, s__.n)
                    sem.dma(self.load_slab(sync, self.WSTG[par], sl))
                self.on("sync", fl)
                loadm[idx] = ((p.sDW if par == 0 else p.sDWb), (p.sDW if par == 0 else p.sDWb).n)

                def fc_(v, idx=idx, par=par):
                    v.wait_ge(loadm[idx][0].h, loadm[idx][1])
                    if idx >= 2:
                        v.wait_ge(p.sT.h, usem[idx - 2])
                    conv(v, idx, self.WSTG[par], self.WBF[par], RQ4)
                self.on("vector", fc_)
                convm[idx] = p.sV.n

                def fu(te, dc=dc, q4=q4, par=par, idx=idx):
                    te.wait_ge(p.sV.h, convm[idx])
                    if q4 == 0:
                        te.wait_ge(p.sV.h, epim[0])
                    ps = self.psA if dc % 2 == 0 else self.psB
                    for kk in range(RQ4 // 128):
                        fq = q4 * (RQ4 // 128) + kk
                        for j in range(0, tn, 512):
                            w = min(512, tn - j)
                            te.matmul(ps[:, j:j + w],
                                      self.WBF[par][:, kk * 128:(kk + 1) * 128],
                                      H[:, fq * tn + j:fq * tn + j + w],
                                      start=(fq == 0), stop=(fq == NF - 1))
                    p.sT.inc(te.drain())
                self.on("tensor", fu)
                usem[idx] = p.sT.n
                idx += 1

            ps = self.psA if dc % 2 == 0 else self.psB

            def e0(sync, dc=dc):
                p.sDW.dma(sync.dma_start(
                    out=self.sR[:, 0:tn],
                    in_=d["sT_sp"][dc * 128:(dc + 1) * 128, t0:t0 + tn]))
            self.step("sync", e0)

            def e0b(g, dc=dc):
                p.sDG.dma(g.dma_start(out=self.XR[:, 0:tn],
                                      in_=xsrc[dc * 128:(dc + 1) * 128, t0:t0 + tn]))
            self.step("gpsimd", e0b)

            def e1(v, dc=dc, ps=ps):
                v.tensor_tensor(out=self.scr2[:, 0:tn], in0=ps[:, 0:tn],
                                in1=unscale_bc[:, 0:tn], op=ALU.mult)
            self.step("vector", e1)

            def e2(s, dc=dc):
                s.activation(self.tscr[:, 0:tn], self.scr2[:, 0:tn], ACT.Silu)
            self.step("scalar", e2)

            def e3(v, dc=dc):
                v.tensor_tensor(out=self.scr2[:, 0:tn], in0=self.sR[:, 0:tn],
                                in1=self.tscr[:, 0:tn], op=ALU.mult)
                v.drain()
                v.tensor_tensor(out=self.XR[:, 1024:1024 + tn],
                                in0=self.XR[:, 0:tn], in1=self.scr2[:, 0:tn],
                                op=ALU.add)
            self.step("vector", e3)

            if layer == 0:
                def e4(s, dc=dc):
                    s.activation(self.tscr[:, 0:tn], self.XR[:, 1024:1024 + tn],
                                 ACT.Square)
                self.step("scalar", e4)

                def e5(te, dc=dc):
                    for j in range(0, tn, 512):
                        w = min(512, tn - j)
                        te.matmul(self.psC[0:1, t0 + j:t0 + j + w], self.ones_col[:],
                                  self.XR[:, 1024 + j:1024 + j + w],
                                  start=(dc == 0), stop=(dc == ND - 1))
                        te.matmul(self.psY[0:1, t0 + j:t0 + j + w], self.ones_col[:],
                                  self.tscr[:, j:j + w],
                                  start=(dc == 0), stop=(dc == ND - 1))
                self.step("tensor", e5)

            def e6(g, dc=dc):
                dst = d["xT_sp"] if layer == 0 else d["y_out"]
                p.sSTG.dma(g.dma_start(out=dst[dc * 128:(dc + 1) * 128, t0:t0 + tn],
                                       in_=self.XR[:, 1024:1024 + tn]))
            self.step("gpsimd", e6)
            epim[0] = p.sV.n
        self.barrier()


    def l1_stats(self, t0, tn):
        p, c = self.p, self.c
        S = c.NT
        tt0, nt2 = t0 // 128, tn // 128

        def f0(v):
            v.tensor_copy(out=self.rows[0:1, 0:tn], in_=self.psC[0:1, t0:t0 + tn])
            v.tensor_copy(out=self.rows[0:1, 1024:1024 + tn],
                          in_=self.psY[0:1, t0:t0 + tn])
        self.step("vector", f0)

        def f1(te):
            for t in range(nt2):
                te.transpose(self.psC[:, t:t + 1],
                             self.rows[0:1, t * 128:(t + 1) * 128], self.idf[0:1, 0:1])
                te.transpose(self.psB[:, t:t + 1],
                             self.rows[0:1, 1024 + t * 128:1024 + (t + 1) * 128],
                             self.idf[0:1, 0:1])
        self.step("tensor", f1)

        def f2(v):
            st = self.stat
            v.tensor_scalar(out=st[:, 4 * S + tt0:4 * S + tt0 + nt2],
                            in0=self.psC[:, 0:nt2],
                            scalar1=1.0 / c.D, scalar2=None, op0=ALU.mult)
            v.tensor_scalar(out=st[:, 5 * S + tt0:5 * S + tt0 + nt2],
                            in0=self.psB[:, 0:nt2],
                            scalar1=1.0 / c.D, scalar2=None, op0=ALU.mult)
            v.drain()
            v.tensor_tensor(out=self.tscr[:, 0:nt2],
                            in0=st[:, 4 * S + tt0:4 * S + tt0 + nt2],
                            in1=st[:, 4 * S + tt0:4 * S + tt0 + nt2], op=ALU.mult)
            v.drain()
            v.tensor_tensor(out=self.tscr[:, 0:nt2],
                            in0=st[:, 5 * S + tt0:5 * S + tt0 + nt2],
                            in1=self.tscr[:, 0:nt2], op=ALU.subtract)
            v.drain()
            v.tensor_scalar(out=self.tscr[:, 0:nt2], in0=self.tscr[:, 0:nt2],
                            scalar1=c.EPS_LN, scalar2=None, op0=ALU.add)
        self.step("vector", f2)

        def f3(s):
            s.activation(self.tscr[:, 0:nt2], self.tscr[:, 0:nt2], ACT.Sqrt)
        self.step("scalar", f3)

        def f4(v):
            v.reciprocal(out=self.stat[:, 5 * S + t0 // 128:5 * S + t0 // 128 + nt2],
                         in_=self.tscr[:, 0:nt2])
        self.step("vector", f4)

    def phase_l1_entry(self, XN1):
        p, c, d = self.p, self.c, self.d
        NT, ND = c.NT, c.ND
        S = c.NT

        def stat_ap(slot):
            return self.stat[:, slot * S:slot * S + NT]

        self.col2row(stat_ap(4), 0, NT, self.psC)
        self.col2row(stat_ap(5), 1024, NT, self.psC)
        self.bcast(self.rows[0:1, 0:c.T], self.bc1, c.T, self.psC)
        self.bcast(self.rows[0:1, 1024:1024 + c.T], self.bc2, c.T, self.psC)

        dm, vm, tm = {}, {}, {}
        for k in range(ND):
            par = k % 2

            def fl(g, k=k, par=par):
                if k >= 2:
                    g.wait_ge(p.sT.h, tm[k - 2])
                sem = p.sDG if par == 0 else p.sDGb
                trio = (p.sDG, p.sDGb, p.sSTG)
                if sem.n > 0:
                    g.wait_ge(sem.h, sem.n)
                if k < 2:
                    for s__ in trio:
                        if s__ is not sem and s__.n > 0:
                            g.wait_ge(s__.h, s__.n)
                    for s__ in (p.sV, p.sS):
                        if s__.n > 0:
                            g.wait_ge(s__.h, s__.n)
                sem.dma(g.dma_start(out=self.XR[:, par * 1024:par * 1024 + c.T],
                                    in_=d["xT_sp"][k * 128:(k + 1) * 128, :]))
            self.on("gpsimd", fl)
            dm[k] = ((p.sDG if par == 0 else p.sDGb), (p.sDG if par == 0 else p.sDGb).n)

            def fv(v, k=k, par=par):
                v.wait_ge(dm[k][0].h, dm[k][1])
                if k >= 1:
                    v.wait_ge(p.sT.h, tm[k - 1])
                v.drain()
                xr = self.XR[:, par * 1024:par * 1024 + c.T]
                v.tensor_tensor(out=self.scr2[:, 0:c.T], in0=xr,
                                in1=self.bc1[:, 0:c.T], op=ALU.subtract)
                v.drain()
                v.tensor_tensor(out=self.scr2[:, 0:c.T], in0=self.scr2[:, 0:c.T],
                                in1=self.bc2[:, 0:c.T], op=ALU.mult)
                v.drain()
                v.tensor_copy(out=XN1[:, k * c.T:(k + 1) * c.T],
                              in_=self.scr2[:, 0:c.T])
                p.sV.inc(v.tensor_tensor(out=self.tscr[:, 0:c.T],
                                         in0=self.scr2[:, 0:c.T],
                                         in1=self.scr2[:, 0:c.T], op=ALU.mult))
            self.on("vector", fv)
            vm[k] = p.sV.n

            def ft(te, k=k, par=par):
                te.wait_ge(p.sV.h, vm[k])
                for j in range(0, c.T, 512):
                    w = min(512, c.T - j)
                    te.matmul(self.psC[0:1, j:j + w], self.ones_col[:],
                              self.scr2[:, j:j + w], start=(k == 0),
                              stop=(k == ND - 1))
                    p.sT.inc(te.matmul(self.psY[0:1, j:j + w], self.ones_col[:],
                                       self.tscr[:, j:j + w], start=(k == 0),
                                       stop=(k == ND - 1)))
            self.on("tensor", ft)
            tm[k] = p.sT.n
        self.barrier()

        def g0(v):
            v.tensor_copy(out=self.rows[0:1, 0:c.T], in_=self.psC[0:1, 0:c.T])
            v.tensor_copy(out=self.rows[0:1, 1024:1024 + c.T], in_=self.psY[0:1, 0:c.T])
        self.step("vector", g0)

        def g1(te):
            for t in range(NT):
                te.transpose(self.psC[:, t:t + 1],
                             self.rows[0:1, t * 128:(t + 1) * 128], self.idf[0:1, 0:1])
                te.transpose(self.psB[:, t:t + 1],
                             self.rows[0:1, 1024 + t * 128:1024 + (t + 1) * 128],
                             self.idf[0:1, 0:1])
        self.step("tensor", g1)

        def g2(v):
            st = self.stat
            v.tensor_scalar(out=st[:, 6 * S:6 * S + NT], in0=self.psC[:, 0:NT],
                            scalar1=1.0 / c.D, scalar2=None, op0=ALU.mult)
            v.tensor_scalar(out=self.tscr[:, 0:NT], in0=self.psB[:, 0:NT],
                            scalar1=1.0 / c.D, scalar2=None, op0=ALU.mult)
            v.drain()
            v.tensor_tensor(out=self.tscr[:, 64:64 + NT], in0=st[:, 6 * S:6 * S + NT],
                            in1=st[:, 6 * S:6 * S + NT], op=ALU.mult)
            v.drain()
            v.tensor_tensor(out=self.tscr[:, 0:NT], in0=self.tscr[:, 0:NT],
                            in1=self.tscr[:, 64:64 + NT], op=ALU.subtract)
            v.drain()
            v.tensor_scalar(out=self.tscr[:, 0:NT], in0=self.tscr[:, 0:NT],
                            scalar1=c.EPS_FFN1_RMS, scalar2=None, op0=ALU.add)
        self.step("vector", g2)

        def g3(s):
            s.activation(self.tscr[:, 0:NT], self.tscr[:, 0:NT], ACT.Sqrt)
        self.step("scalar", g3)

        def g4(v):
            v.reciprocal(out=self.stat[:, 7 * S:7 * S + NT], in_=self.tscr[:, 0:NT])
        self.step("vector", g4)

        self.barrier()

    def phase_l1_xq(self):
        p, c, d = self.p, self.c, self.d
        NT, ND = c.NT, c.ND
        S = c.NT
        st = self.stat

        def sl(k):
            return st[:, k * S:k * S + NT]

        def p0(v):
            v.tensor_tensor(out=self.tscr[:, 0:NT], in0=sl(4), in1=sl(5),
                            op=ALU.mult)
            v.drain()
            v.tensor_tensor(out=self.tscr[:, 0:NT], in0=self.tscr[:, 0:NT],
                            in1=sl(6), op=ALU.add)
        self.step("vector", p0)
        self.col2row(sl(5), 0, NT, self.psC)
        self.col2row(self.tscr[:, 0:NT], 1024, NT, self.psC)
        self.bcast(self.rows[0:1, 0:c.T], self.bc1, c.T, self.psC)
        self.bcast(self.rows[0:1, 1024:1024 + c.T], self.bc2, c.T, self.psC)

        dm, vm = {}, {}
        for k in range(ND):
            par = k % 2

            def fl(g, k=k, par=par):
                if k >= 2:
                    g.wait_ge(p.sV.h, vm[k - 2])
                sem = p.sDG if par == 0 else p.sDGb
                trio = (p.sDG, p.sDGb, p.sSTG)
                if sem.n > 0:
                    g.wait_ge(sem.h, sem.n)
                if k < 2:
                    for s__ in trio:
                        if s__ is not sem and s__.n > 0:
                            g.wait_ge(s__.h, s__.n)
                    for s__ in (p.sV, p.sS):
                        if s__.n > 0:
                            g.wait_ge(s__.h, s__.n)
                sem.dma(g.dma_start(out=self.XR[:, par * 1024:par * 1024 + c.T],
                                      in_=d["xT_sp"][k * 128:(k + 1) * 128, :]))
            self.on("gpsimd", fl)
            dm[k] = ((p.sDG if par == 0 else p.sDGb), (p.sDG if par == 0 else p.sDGb).n)

            def fv(v, k=k, par=par):
                v.wait_ge(dm[k][0].h, dm[k][1])
                v.drain()
                xr = self.XR[:, par * 1024:par * 1024 + c.T]
                v.tensor_tensor(out=self.scr2[:, 0:c.T], in0=xr,
                                in1=self.bc1[:, 0:c.T], op=ALU.mult)
                v.drain()
                v.tensor_tensor(out=self.scr2[:, 0:c.T], in0=self.scr2[:, 0:c.T],
                                in1=self.bc2[:, 0:c.T], op=ALU.subtract)
                v.drain()
                if k == 0:
                    v.tensor_scalar(out=self.maxA[:, 0:c.T], in0=self.scr2[:, 0:c.T],
                                    scalar1=1.0, scalar2=None, op0=ALU.mult)
                    v.tensor_scalar(out=self.maxB[:, 0:c.T], in0=self.scr2[:, 0:c.T],
                                    scalar1=-1.0, scalar2=None, op0=ALU.mult)
                else:
                    v.scalar_tensor_tensor(out=self.maxA[:, 0:c.T],
                                           in0=self.scr2[:, 0:c.T], scalar=1.0,
                                           in1=self.maxA[:, 0:c.T], op0=ALU.mult,
                                           op1=ALU.max)
                    v.scalar_tensor_tensor(out=self.maxB[:, 0:c.T],
                                           in0=self.scr2[:, 0:c.T], scalar=-1.0,
                                           in1=self.maxB[:, 0:c.T], op0=ALU.mult,
                                           op1=ALU.max)
                p.sV.inc(v.drain())
            self.on("vector", fv)
            vm[k] = p.sV.n
        self.barrier()

        def h0(v):
            v.tensor_tensor(out=self.maxA[:, 0:c.T], in0=self.maxA[:, 0:c.T],
                            in1=self.maxB[:, 0:c.T], op=ALU.max)
        self.step("vector", h0)

        def h1(te):
            for t in range(NT):
                te.transpose(self.psY[:, t * 128:(t + 1) * 128],
                             self.maxA[:, t * 128:(t + 1) * 128], self.idf[:])
        self.step("tensor", h1)

        def h2(v):
            for t in range(NT):
                v.reduce_max(self.tscr[:, t:t + 1], self.psY[:, t * 128:(t + 1) * 128],
                             axis=AX)
            v.drain()
            v.reciprocal(out=self.tscr[:, 64:64 + NT], in_=self.tscr[:, 0:NT])
            v.drain()
            v.tensor_scalar(out=sl(2), in0=self.tscr[:, 64:64 + NT],
                            scalar1=127.0, scalar2=None, op0=ALU.mult)
            v.tensor_tensor(out=self.tscr[:, 0:NT], in0=self.tscr[:, 0:NT],
                            in1=sl(7), op=ALU.mult)
            v.drain()
            v.tensor_scalar(out=sl(3), in0=self.tscr[:, 0:NT],
                            scalar1=1.0 / 127.0, scalar2=None, op0=ALU.mult)
            v.drain()
            v.tensor_tensor(out=sl(1), in0=sl(5), in1=sl(2), op=ALU.mult)
            v.drain()
            v.tensor_tensor(out=self.tscr[:, 0:NT], in0=sl(4), in1=sl(1),
                            op=ALU.mult)
            v.tensor_tensor(out=self.tscr[:, 64:64 + NT], in0=sl(6), in1=sl(2),
                            op=ALU.mult)
            v.drain()
            v.tensor_tensor(out=sl(0), in0=self.tscr[:, 0:NT],
                            in1=self.tscr[:, 64:64 + NT], op=ALU.add)
        self.step("vector", h2)
        self.barrier()

    def phase_l1_proj(self, XN1, RAWP, DT2, ZGR):
        p, c, d = self.p, self.c, self.d
        NT, ND, NCH, H1, KVH1 = c.NT, c.ND, c.NCH, c.H1, c.KVH1
        YCH = self.XR

        def gn0(sync):
            p.sDW.dma(sync.dma_start(out=self.rows[0:1, 0:c.ZGD], in_=d["gnw"][:]))
        self.step("sync", gn0)
        self.bcast(self.rows[0:1, 0:min(c.ZGD, 1024)], self.gnw_b,
                   min(c.ZGD, 1024), self.psC)
        if c.ZGD > 1024:
            self.bcast2(self.rows[0:1, 1024:c.ZGD], self.gnw_b, 1024,
                        c.ZGD - 1024, self.psC)
        self.barrier()
        RS = c.T + 32
        NSL = c.NC1 + 1
        VCH = c.KVH1 * c.P1 // 128
        QOF = 2 * VCH

        def raw(j):
            return RAWP[:, j * RS:j * RS + c.T + 1]

        def post(j):
            s_ = (j - 1) if j > 0 else (NSL - 1)
            return RAWP[:, s_ * RS: s_ * RS + c.T]

        loadm, convm, usem = {}, {}, {}
        epim = [0]
        zgm = [0]
        for pc in range(c.NPC):
            par = pc % 2

            def fl(sync, pc=pc, par=par):
                if pc >= 2:
                    sync.wait_ge(p.sV.h, convm[pc - 2])
                sem = p.sDW if par == 0 else p.sDWb
                trio = (p.sDW, p.sDWb, p.sST)
                if sem.n > 0:
                    sync.wait_ge(sem.h, sem.n)
                if pc < 2:
                    for s__ in trio:
                        if s__ is not sem and s__.n > 0:
                            sync.wait_ge(s__.h, s__.n)
                    for s__ in (p.sV, p.sS):
                        if s__.n > 0:
                            sync.wait_ge(s__.h, s__.n)
                sem.dma(self.load_slab(sync, self.WSTG[par], d["wvkq"][pc]))
            self.on("sync", fl)
            loadm[pc] = ((p.sDW if par == 0 else p.sDWb), (p.sDW if par == 0 else p.sDWb).n)

            def fc_(v, pc=pc, par=par):
                v.wait_ge(loadm[pc][0].h, loadm[pc][1])
                if pc >= 2:
                    v.wait_ge(p.sT.h, usem[pc - 2])
                self.conv_cast(v, pc, self.WSTG[par], self.WBF[par], c.D)
            self.on("vector", fc_)
            convm[pc] = p.sV.n

            def fu(te, pc=pc, par=par):
                te.wait_ge(p.sV.h, convm[pc])
                te.wait_ge(p.sV.h, epim[0])
                ps = self.psA if pc % 2 == 0 else self.psB
                for kk in range(ND):
                    for j in range(0, c.T, 512):
                        w = min(512, c.T - j)
                        te.matmul(ps[:, j:j + w],
                                  self.WBF[par][:, kk * 128:(kk + 1) * 128],
                                  XN1[:, kk * c.T + j:kk * c.T + j + w],
                                  start=(kk == 0), stop=(kk == ND - 1))
                p.sT.inc(te.drain())
            self.on("tensor", fu)
            usem[pc] = p.sT.n

            def fe(v, pc=pc):
                v.wait_ge(p.sT.h, usem[pc])
                ps = self.psA if pc % 2 == 0 else self.psB
                if pc < c.NC1:
                    v.tensor_copy(out=raw(pc)[:, 1:c.T + 1], in_=ps[:, 0:c.T])
                elif pc < c.NC1 + c.NZ1:
                    if zgm[0]:
                        v.wait_ge(p.sSTG.h, zgm[0])
                    v.tensor_copy(out=ZGR[:, 0:c.T], in_=ps[:, 0:c.T])
                else:
                    v.tensor_copy(out=self.DTR[0:c.H1, 0:c.T], in_=ps[0:c.H1, 0:c.T])
                p.sV.inc(v.drain())
            self.on("vector", fe)
            epim[0] = p.sV.n

            if c.NC1 <= pc < c.NC1 + c.NZ1:
                def fz(g, pc=pc, mark=epim[0], prev=zgm[0]):
                    g.wait_ge(p.sV.h, mark)
                    if prev:
                        g.wait_ge(p.sSTG.h, prev)
                    p.sSTG.dma(g.dma_start(
                        out=d["zg_sp"][(pc - c.NC1) * 128:(pc - c.NC1 + 1) * 128, :],
                        in_=ZGR[:, 0:c.T]))
                self.on("gpsimd", fz)
                zgm[0] = p.sSTG.n
        self.barrier()

    def phase_l1_ssd(self, RAWP, S0ST, YTT, DT2, AT, AcsTok,
                     csums, DSTT, EAT, XVB, BPB, SCOR, S0cur, Sinit,
                     ZGR):
        p, c, d = self.p, self.c, self.d
        NT, ND, NCH, H1, KVH1 = c.NT, c.ND, c.NCH, c.H1, c.KVH1
        YCH = self.XR
        LEX = self.WBF[0][:, 0:1024]
        PDT = self.WBF[1][:, 0:1024]
        RS = c.T + 32
        NSL = c.NC1 + 1
        VCH = c.KVH1 * c.P1 // 128
        QOF = 2 * VCH

        def raw(j):
            return RAWP[:, j * RS:j * RS + c.T + 1]

        def post(j):
            s_ = (j - 1) if j > 0 else (NSL - 1)
            return RAWP[:, s_ * RS: s_ * RS + c.T]

        def gz(v):
            v.memset(S0cur[:, 0:c.H1 * 32], 0.0)
        self.step("vector", gz)

        rawlast = RAWP[:, 0:c.NC1 * RS].rearrange("p (j r) -> p j r", r=RS)[:, :, c.T]
        rawfirst = RAWP[:, 0:c.NC1 * RS].rearrange("p (j r) -> p j r", r=RS)[:, :, 0]

        def c0(v):
            v.tensor_scalar(out=self.scr2[:, 0:c.NC1], in0=rawlast,
                            scalar1=self.params_b[:, 2:3], scalar2=None,
                            op0=ALU.mult)
        self.step("vector", c0)

        def c1(g):
            p.sSTG.dma(g.dma_start(out=d["cc1_in"][:], in_=self.scr2[:, 0:c.NC1]))
        self.step("gpsimd", c1)

        def c2_(g):
            p.sCC.inc(g.collective_compute(
                "AllReduce", mybir.AluOpType.add, ins=[d["cc1_in"][:]],
                outs=[d["cc1_out"][:]], replica_groups=self.rg))
        self.step("gpsimd", c2_)

        def c2b(g):
            p.sDG.dma(g.dma_start(out=self.scr2[:, 64:64 + c.NC1],
                                  in_=d["cc1_out"][:]))
        self.step("gpsimd", c2b)

        def c3(v):
            v.tensor_scalar(out=rawfirst, in0=self.scr2[:, 64:64 + c.NC1],
                            scalar1=self.params_b[:, 1:2], scalar2=None,
                            op0=ALU.mult)
        self.step("vector", c3)

        for j in range(c.NC1):
            def v0(v, j=j):
                v.tensor_scalar(out=self.scr2[:, 0:c.T], in0=raw(j)[:, 1:c.T + 1],
                                scalar1=self.cw1[:, j:j + 1], scalar2=None,
                                op0=ALU.mult)
                v.drain()
                v.scalar_tensor_tensor(out=self.scr2[:, 0:c.T],
                                       in0=raw(j)[:, 0:c.T],
                                       scalar=self.cw0[:, j:j + 1],
                                       in1=self.scr2[:, 0:c.T], op0=ALU.mult,
                                       op1=ALU.add)
            self.step("vector", v0)

            def s0(s, j=j):
                s.activation(post(j), self.scr2[:, 0:c.T], ACT.Silu,
                             bias=self.cb_[:, j:j + 1])
            self.step("scalar", s0)

        def d0a(v):
            v.tensor_scalar(out=self.DTR[0:c.H1, :], in0=self.DTR[0:c.H1, :],
                            scalar1=self.dtb_sb[:], scalar2=None, op0=ALU.add)
            v.drain()
            v.scalar_tensor_tensor(out=self.scr2[0:c.H1, 0:c.T],
                                   in0=self.DTR[0:c.H1, :], scalar=-1.0,
                                   in1=self.DTR[0:c.H1, :], op0=ALU.mult,
                                   op1=ALU.min)
            v.drain()
            v.tensor_scalar(out=DT2[0:c.H1, :], in0=self.DTR[0:c.H1, :],
                            scalar1=0.0, scalar2=None, op0=ALU.max)
        self.step("vector", d0a)

        def d0b(s):
            s.activation(self.scr2[0:c.H1, 0:c.T], self.scr2[0:c.H1, 0:c.T],
                         ACT.Exp)
            s.drain()
            s.activation(self.scr2[0:c.H1, 0:c.T], self.scr2[0:c.H1, 0:c.T],
                         ACT.Ln, bias=1.0)
        self.step("scalar", d0b)

        def d0c(v):
            v.tensor_tensor(out=DT2[0:c.H1, :], in0=DT2[0:c.H1, :],
                            in1=self.scr2[0:c.H1, 0:c.T], op=ALU.add)
        self.step("vector", d0c)

        def d1(v):
            v.tensor_scalar(out=AT[0:c.H1, :], in0=DT2[0:c.H1, :],
                            scalar1=self.negA[:], scalar2=None, op0=ALU.mult)
        self.step("vector", d1)

        for ch in range(NCH):
            def a0(te, ch=ch):
                te.transpose(self.psC[:, 0:c.H1], AT[0:c.H1, ch * 128:(ch + 1) * 128],
                             self.idf[0:c.H1, 0:c.H1])
                te.transpose(self.psC[:, 128:128 + c.H1],
                             DT2[0:c.H1, ch * 128:(ch + 1) * 128],
                             self.idf[0:c.H1, 0:c.H1])
            self.step("tensor", a0)

            def a1(v, ch=ch):
                v.tensor_copy(out=self.tscr[:, 0:c.H1], in_=self.psC[:, 0:c.H1])
                v.tensor_copy(out=DSTT[:, (NCH + ch) * c.H1:(NCH + ch + 1) * c.H1],
                              in_=self.psC[:, 128:128 + c.H1])
            self.step("vector", a1)

            def a2(te, ch=ch):
                te.matmul(self.psB[:, 0:c.H1], self.uones[:], self.tscr[:, 0:c.H1],
                          start=True, stop=True)
            self.step("tensor", a2)

            def a3(v, ch=ch):
                v.tensor_copy(out=AcsTok[:, ch * c.H1:(ch + 1) * c.H1],
                              in_=self.psB[:, 0:c.H1])
            self.step("vector", a3)

            def a4(te, ch=ch):
                te.transpose(self.psC[0:c.H1, 0:128],
                             AcsTok[:, ch * c.H1:(ch + 1) * c.H1],
                             self.idf[:])
            self.step("tensor", a4)

            def a5(v, ch=ch):
                v.tensor_copy(out=csums[:, ch:ch + 1],
                              in_=self.psC[0:c.H1, 127:128])
            self.step("vector", a5)

        def b1(te):
            for ch in range(NCH):
                te.transpose(self.psC[0:1, ch * c.H1:(ch + 1) * c.H1],
                             csums[:, ch:ch + 1], self.idf[0:c.H1, 0:c.H1])
        self.step("tensor", b1)

        def b2(v):
            v.tensor_copy(out=self.rowsE[0:1, 0:NCH * c.H1],
                          in_=self.psC[0:1, 0:NCH * c.H1])
            v.drain()
            v.memset(self.rowsE[0:1, 512:512 + c.H1], 0.0)
            for ch in range(1, NCH):
                v.drain()
                v.tensor_tensor(
                    out=self.rowsE[0:1, 512 + ch * c.H1:512 + (ch + 1) * c.H1],
                    in0=self.rowsE[0:1, 512 + (ch - 1) * c.H1:512 + ch * c.H1],
                    in1=self.rowsE[0:1, (ch - 1) * c.H1:ch * c.H1], op=ALU.add)
        self.step("vector", b2)

        def b3(s):
            s.activation(self.rowsE[0:1, 0:NCH * c.H1], self.rowsE[0:1, 0:NCH * c.H1],
                         ACT.Exp)
            s.drain()
            s.activation(self.rowsE[0:1, 512:512 + NCH * c.H1],
                         self.rowsE[0:1, 512:512 + NCH * c.H1], ACT.Exp)
        self.step("scalar", b3)


        NKC = 2 * VCH
        for ch in range(NCH):
            def t0(te, ch=ch):
                for j in range(NKC):
                    te.matmul(self.psC[:, j * 128:(j + 1) * 128],
                              post(j)[:, ch * 128:(ch + 1) * 128], self.idb[:],
                              start=True, stop=True)
            self.step("tensor", t0)

            def t1(v, ch=ch):
                for j in range(NKC):
                    dst = XVB if j < VCH else BPB
                    jj = j if j < VCH else j - VCH
                    v.tensor_copy(out=dst[:, c.H1 * 64 + jj * 128:c.H1 * 64 + jj * 128 + 128],
                                  in_=self.psC[:, j * 128:(j + 1) * 128])
            self.step("vector", t1)

            def t2(v, ch=ch):
                v.tensor_copy(out=self.scr2[:, 0:c.H1],
                              in_=AcsTok[:, ch * c.H1:(ch + 1) * c.H1])
            self.step("vector", t2)

            def t3(te, ch=ch):
                te.matmul(self.psC[:, 0:c.H1], self.ones_row[:, 0:128],
                          self.rowsE[0:1, ch * c.H1:(ch + 1) * c.H1],
                          start=True, stop=True)
            self.step("tensor", t3)

            def t4(v, ch=ch):
                v.tensor_scalar(out=self.scr2[:, 64:64 + c.H1],
                                in0=AcsTok[:, ch * c.H1:(ch + 1) * c.H1],
                                scalar1=-1.0, scalar2=None, op0=ALU.mult)
            self.step("vector", t4)

            def t5(s, ch=ch):
                s.activation(self.scr2[:, 64:64 + c.H1], self.scr2[:, 64:64 + c.H1],
                             ACT.Exp)
            self.step("scalar", t5)

            def t6(v, ch=ch):
                v.tensor_tensor(out=DSTT[:, ch * c.H1:(ch + 1) * c.H1],
                                in0=self.scr2[:, 64:64 + c.H1],
                                in1=self.psC[:, 0:c.H1], op=ALU.mult)
            self.step("vector", t6)

            def t7(v, ch=ch):
                for h in range(c.H1):
                    kv = h // c.G1
                    vsl = XVB[:, c.H1 * 64 + (kv // 2) * 128 + (kv % 2) * 64:
                              c.H1 * 64 + (kv // 2) * 128 + (kv % 2) * 64 + 64]
                    ksl = BPB[:, c.H1 * 64 + (kv // 2) * 128 + (kv % 2) * 64:
                              c.H1 * 64 + (kv // 2) * 128 + (kv % 2) * 64 + 64]
                    v.tensor_scalar(out=XVB[:, h * 64:(h + 1) * 64], in0=vsl,
                                    scalar1=DSTT[:, (NCH + ch) * c.H1 + h:
                                                 (NCH + ch) * c.H1 + h + 1],
                                    scalar2=None, op0=ALU.mult)
                    v.tensor_scalar(out=BPB[:, h * 64:(h + 1) * 64], in0=ksl,
                                    scalar1=DSTT[:, ch * c.H1 + h:ch * c.H1 + h + 1],
                                    scalar2=None, op0=ALU.mult)
            self.step("vector", t7)

            def t8(te, ch=ch):
                for h in range(c.H1):
                    te.matmul(self.psY[64 * (h % 2):64 * (h % 2) + 64,
                                       (h // 2) * 64:(h // 2) * 64 + 64],
                              BPB[:, h * 64:(h + 1) * 64],
                              XVB[:, h * 64:(h + 1) * 64], start=True, stop=True)
            self.step("tensor", t8)

            def t9(te, ch=ch):
                ech = self.rowsE[0:1, ch * c.H1:(ch + 1) * c.H1]
                r0 = ech.rearrange("o (i two) -> o i two", two=2)[:, :, 0:1]
                r1 = ech.rearrange("o (i two) -> o i two", two=2)[:, :, 1:2]
                nh2 = c.H1 // 2
                for a_ in range(0, nh2, 8):
                    b_ = min(8, nh2 - a_)
                    te.matmul(self.psC[0:64, a_ * 64:(a_ + b_) * 64],
                              self.ones_row[:, 0:64],
                              r0[:, a_:a_ + b_, :].to_broadcast([1, b_, 64]),
                              start=True, stop=True)
                    te.matmul(self.psC[64:128, a_ * 64:(a_ + b_) * 64],
                              self.ones_row[:, 0:64],
                              r1[:, a_:a_ + b_, :].to_broadcast([1, b_, 64]),
                              start=True, stop=True)
            self.step("tensor", t9)

            def t10(v, ch=ch):
                v.tensor_copy(out=S0ST[:, ch * c.H1 * 32:(ch + 1) * c.H1 * 32],
                              in_=S0cur[:, 0:c.H1 * 32])
                v.drain()
                v.tensor_tensor(out=S0cur[:, 0:c.H1 * 32], in0=S0cur[:, 0:c.H1 * 32],
                                in1=self.psC[:, 0:c.H1 * 32], op=ALU.mult)
                v.drain()
                v.tensor_tensor(out=S0cur[:, 0:c.H1 * 32], in0=S0cur[:, 0:c.H1 * 32],
                                in1=self.psY[:, 0:c.H1 * 32], op=ALU.add)
            self.step("vector", t10)

        def u0(v):
            v.tensor_scalar(out=self.scr2[:, 0:c.H1 * 32], in0=S0cur[:, 0:c.H1 * 32],
                            scalar1=self.params_b[:, 2:3], scalar2=None,
                            op0=ALU.mult)
        self.step("vector", u0)

        def u1(g):
            p.sSTG.dma(g.dma_start(out=d["cc2_in"][:, 0:c.H1 * 32],
                                  in_=self.scr2[:, 0:c.H1 * 32]))
        self.step("gpsimd", u1)

        def u2(g):
            p.sCC.inc(g.collective_compute(
                "AllReduce", mybir.AluOpType.add, ins=[d["cc2_in"][:]],
                outs=[d["cc2_out"][:]], replica_groups=self.rg))
        self.step("gpsimd", u2)

        def u2b(g):
            p.sDG.dma(g.dma_start(out=self.scr2[:, 0:c.H1 * 32],
                                  in_=d["cc2_out"][:, 0:c.H1 * 32]))
        self.step("gpsimd", u2b)
        self.dbg_dump("dbg_scr2cc", self.scr2[:, 0:c.H1 * 32])

        def u3(v):
            v.tensor_scalar(out=Sinit[:, 0:c.H1 * 32], in0=self.scr2[:, 0:c.H1 * 32],
                            scalar1=self.params_b[:, 1:2], scalar2=None,
                            op0=ALU.mult)
        self.step("vector", u3)
        self.dbg_dump("dbg_s0cur", S0cur[:, 0:c.H1 * 32])
        self.dbg_dump("dbg_sinit", Sinit[:, 0:c.H1 * 32])
        if "dbg_cc2i" in self.dbg:
            def fdcc(sync):
                for s_ in (p.sT, p.sV, p.sS, p.sG, p.sST, p.sSTG):
                    if s_.n:
                        sync.wait_ge(s_.h, s_.n)
                p.sDW.dma(sync.dma_start(out=d["dbg_cc2i"][:], in_=d["cc2_in"][:]))
                p.sDW.dma(sync.dma_start(out=d["dbg_cc2o"][:], in_=d["cc2_out"][:]))
            self.on("sync", fdcc)

        for ch in range(NCH):
            def w0(te, ch=ch):
                ech = self.rowsE[0:1, 512 + ch * c.H1:512 + (ch + 1) * c.H1]
                r0 = ech.rearrange("o (i two) -> o i two", two=2)[:, :, 0:1]
                r1 = ech.rearrange("o (i two) -> o i two", two=2)[:, :, 1:2]
                nh2 = c.H1 // 2
                for a_ in range(0, nh2, 8):
                    b_ = min(8, nh2 - a_)
                    te.matmul(self.psC[0:64, a_ * 64:(a_ + b_) * 64],
                              self.ones_row[:, 0:64],
                              r0[:, a_:a_ + b_, :].to_broadcast([1, b_, 64]),
                              start=True, stop=True)
                    te.matmul(self.psC[64:128, a_ * 64:(a_ + b_) * 64],
                              self.ones_row[:, 0:64],
                              r1[:, a_:a_ + b_, :].to_broadcast([1, b_, 64]),
                              start=True, stop=True)
            self.step("tensor", w0)

            def w1(v, ch=ch):
                v.tensor_tensor(out=self.scr2[:, 0:c.H1 * 32],
                                in0=Sinit[:, 0:c.H1 * 32], in1=self.psC[:, 0:c.H1 * 32],
                                op=ALU.mult)
                v.drain()
                v.tensor_tensor(out=SCOR[:, 0:c.H1 * 32], in0=self.scr2[:, 0:c.H1 * 32],
                                in1=S0ST[:, ch * c.H1 * 32:(ch + 1) * c.H1 * 32],
                                op=ALU.add)
            self.step("vector", w1)

            def w2(s, ch=ch):
                s.activation(EAT[:, 0:c.H1], AcsTok[:, ch * c.H1:(ch + 1) * c.H1],
                             ACT.Exp)
            self.step("scalar", w2)

            def w3(te, ch=ch):
                for j in range(VCH):
                    te.matmul(self.psC[:, j * 128:(j + 1) * 128],
                              post(j)[:, ch * 128:(ch + 1) * 128], self.idb[:],
                              start=True, stop=True)
            self.step("tensor", w3)

            def w4(v, ch=ch):
                for j in range(VCH):
                    v.tensor_copy(out=XVB[:, c.H1 * 64 + j * 128:c.H1 * 64 + (j + 1) * 128],
                                  in_=self.psC[:, j * 128:(j + 1) * 128])
            self.step("vector", w4)

            def w5(v, ch=ch):
                for h in range(c.H1):
                    kv = h // c.G1
                    vsl = XVB[:, c.H1 * 64 + (kv // 2) * 128 + (kv % 2) * 64:
                              c.H1 * 64 + (kv // 2) * 128 + (kv % 2) * 64 + 64]
                    v.tensor_scalar(out=XVB[:, h * 64:(h + 1) * 64], in0=vsl,
                                    scalar1=DSTT[:, (NCH + ch) * c.H1 + h:
                                                 (NCH + ch) * c.H1 + h + 1],
                                    scalar2=None, op0=ALU.mult)
            self.step("vector", w5)

            KSW = self.WBF[0][:, 1024:1024 + VCH * 128]

            def w6(v, ch=ch, KSW=KSW):
                for j in range(VCH):
                    kch = post(VCH + j)[:, ch * 128:(ch + 1) * 128]
                    v.tensor_copy(out=KSW[0:64, j * 128:(j + 1) * 128],
                                  in_=kch[64:128, :])
                    v.tensor_copy(out=KSW[64:128, j * 128:(j + 1) * 128],
                                  in_=kch[0:64, :])
            self.step("vector", w6)

            for hg in range(c.H1 // 8):
                h0_ = hg * 8

                def x0(te, ch=ch, h0_=h0_):
                    for i in range(8):
                        h = h0_ + i
                        te.transpose(self.psB[0:1, i * 128:(i + 1) * 128],
                                     AcsTok[:, ch * c.H1 + h:ch * c.H1 + h + 1],
                                     self.idf[:])
                self.step("tensor", x0)

                def x1(v, ch=ch):
                    v.tensor_copy(out=self.rows[0:1, 0:1024],
                                  in_=self.psB[0:1, 0:1024])
                self.step("vector", x1)

                def x2(te, ch=ch, h0_=h0_):
                    for i in range(8):
                        h = h0_ + i
                        kv = h // c.G1
                        te.matmul(self.psC[:, i * 128:(i + 1) * 128],
                                  self.ones_row[:, 0:128],
                                  self.rows[0:1, i * 128:(i + 1) * 128],
                                  start=True, stop=True)
                        qsl = post(QOF + h // 2)[64 * (h % 2):64 * (h % 2) + 64,
                                                 ch * 128:(ch + 1) * 128]
                        if (kv % 2) == (h % 2):
                            ksl = post(VCH + kv // 2)[
                                64 * (kv % 2):64 * (kv % 2) + 64,
                                ch * 128:(ch + 1) * 128]
                        else:
                            ksl = KSW[64 * (h % 2):64 * (h % 2) + 64,
                                      (kv // 2) * 128:(kv // 2) * 128 + 128]
                        te.matmul(self.psB[:, i * 128:(i + 1) * 128], ksl, qsl,
                                  start=True, stop=True)
                self.step("tensor", x2)

                def x3(v, ch=ch, h0_=h0_):
                    for i in range(8):
                        h = h0_ + i
                        v.tensor_scalar(out=self.scr2[:, i * 128:(i + 1) * 128],
                                        in0=self.psC[:, i * 128:(i + 1) * 128],
                                        scalar1=AcsTok[:, ch * c.H1 + h:
                                                       ch * c.H1 + h + 1],
                                        scalar2=None, op0=ALU.subtract)
                    v.drain()
                    v.tensor_tensor(
                        out=self.scr2[:, 0:1024].rearrange("p (i q) -> p i q", q=128),
                        in0=self.scr2[:, 0:1024].rearrange("p (i q) -> p i q", q=128),
                        in1=self.mk_lowT[:, None, :].to_broadcast([128, 8, 128]),
                        op=ALU.add)
                self.step("vector", x3)

                def x4(s, ch=ch):
                    s.activation(LEX, self.scr2[:, 0:1024], ACT.Exp)
                self.step("scalar", x4)

                def x5(v, ch=ch):
                    v.tensor_tensor(out=PDT, in0=self.psB[:, 0:1024],
                                    in1=LEX, op=ALU.mult)
                self.step("vector", x5)

                def x6(te, ch=ch, h0_=h0_):
                    for i in range(8):
                        h = h0_ + i
                        te.matmul(self.psA[:, (h % 16) * 64:(h % 16) * 64 + 64],
                                  PDT[:, i * 128:(i + 1) * 128],
                                  XVB[:, h * 64:(h + 1) * 64],
                                  start=True, stop=True)
                        qsl = post(QOF + h // 2)[64 * (h % 2):64 * (h % 2) + 64,
                                                 ch * 128:(ch + 1) * 128]
                        ssl = SCOR[64 * (h % 2):64 * (h % 2) + 64,
                                   (h // 2) * 64:(h // 2) * 64 + 64]
                        te.matmul(self.psY[:, (h % 16) * 64:(h % 16) * 64 + 64],
                                  qsl, ssl, start=True, stop=True)
                self.step("tensor", x6)

                def x7(v, ch=ch, h0_=h0_):
                    for i in range(8):
                        h = h0_ + i
                        v.tensor_scalar(
                            out=YCH[:, h * 64:(h + 1) * 64],
                            in0=self.psY[:, (h % 16) * 64:(h % 16) * 64 + 64],
                            scalar1=EAT[:, h:h + 1], scalar2=None, op0=ALU.mult)
                    v.drain()
                    for i in range(8):
                        h = h0_ + i
                        v.tensor_tensor(
                            out=YCH[:, h * 64:(h + 1) * 64],
                            in0=YCH[:, h * 64:(h + 1) * 64],
                            in1=self.psA[:, (h % 16) * 64:(h % 16) * 64 + 64],
                            op=ALU.add)
                self.step("vector", x7)

            def y0(sy, ch=ch):
                p.sDW.dma(sy.dma_start(out=ZGR[:, 0:c.ZGD],
                                       in_=d["zg_sp"][:, ch * 128:(ch + 1) * 128],
                                       transpose=True))
            self.step("sync", y0)

            def y1(s, ch=ch):
                hz = min(1024, c.ZGD)
                s.activation(self.scr2[:, 0:hz], YCH[:, 0:hz], ACT.Square)
                if c.ZGD > 1024:
                    s.activation(self.tscr[:, 0:c.ZGD - 1024],
                                 YCH[:, 1024:c.ZGD], ACT.Square)
            self.step("scalar", y1)

            def y2(v, ch=ch):
                hz = min(1024, c.ZGD)
                v.reduce_sum(EAT[:, 0:hz // 64],
                             self.scr2[:, 0:hz].rearrange("p (h q) -> p h q", q=64),
                             axis=AX)
                if c.ZGD > 1024:
                    v.reduce_sum(EAT[:, 16:16 + (c.ZGD - 1024) // 64],
                                 self.tscr[:, 0:c.ZGD - 1024].rearrange(
                                     "p (h q) -> p h q", q=64), axis=AX)
                v.drain()
                v.tensor_scalar(out=EAT[:, 0:c.H1], in0=EAT[:, 0:c.H1],
                                scalar1=1.0 / c.P1, scalar2=c.EPS_SSD,
                                op0=ALU.mult, op1=ALU.add)
            self.step("vector", y2)

            def y3(s, ch=ch):
                s.activation(EAT[:, 0:c.H1], EAT[:, 0:c.H1], ACT.Sqrt)
            self.step("scalar", y3)

            def y4(v, ch=ch):
                v.reciprocal(out=EAT[:, 0:c.H1], in_=EAT[:, 0:c.H1])
                v.drain()
                v.tensor_tensor(
                    out=YCH[:, 0:c.ZGD].rearrange("p (h q) -> p h q", q=64),
                    in0=YCH[:, 0:c.ZGD].rearrange("p (h q) -> p h q", q=64),
                    in1=EAT[:, 0:c.H1, None].to_broadcast([128, c.H1, 64]),
                    op=ALU.mult)
                v.drain()
                v.tensor_tensor(out=YCH[:, 0:c.ZGD], in0=YCH[:, 0:c.ZGD],
                                in1=self.gnw_b[:, 0:c.ZGD], op=ALU.mult)
            self.step("vector", y4)

            def y5(s, ch=ch):
                hz = min(1024, c.ZGD)
                s.activation(self.scr2[:, 0:hz], ZGR[:, 0:hz], ACT.Silu)
                if c.ZGD > 1024:
                    s.activation(self.tscr[:, 0:c.ZGD - 1024],
                                 ZGR[:, 1024:c.ZGD], ACT.Silu)
            self.step("scalar", y5)

            def y6(v, ch=ch):
                hz = min(1024, c.ZGD)
                v.tensor_tensor(out=YCH[:, 0:hz], in0=YCH[:, 0:hz],
                                in1=self.scr2[:, 0:hz], op=ALU.mult)
                if c.ZGD > 1024:
                    v.tensor_tensor(out=YCH[:, 1024:c.ZGD], in0=YCH[:, 1024:c.ZGD],
                                    in1=self.tscr[:, 0:c.ZGD - 1024], op=ALU.mult)
            self.step("vector", y6)

            def y7(te, ch=ch):
                for k in range(c.NZ1):
                    ps = self.psA if k < 8 else self.psB
                    te.transpose(ps[:, (k % 8) * 128:(k % 8) * 128 + 128],
                                 YCH[:, k * 128:(k + 1) * 128], self.idf[:])
            self.step("tensor", y7)

            def y8(v, ch=ch):
                for k in range(c.NZ1):
                    ps = self.psA if k < 8 else self.psB
                    v.tensor_copy(out=YTT[:, k * c.T + ch * 128:k * c.T + ch * 128 + 128],
                                  in_=ps[:, (k % 8) * 128:(k % 8) * 128 + 128])
            self.step("vector", y8)
        self.barrier()

        for dc in range(ND):
            def fl(sync, dc=dc):
                p.sDW.dma(self.load_slab(sync, self.WSTG[0], d["wo1"][dc]))
            self.step("sync", fl)

            def fcv(v):
                p.sV.inc(v.tensor_copy(out=self.WBF[0][:, 0:c.ZGD],
                                       in_=self.WSTG[0][:, 0:c.ZGD]))
            self.step("vector", fcv)

            def ft(te):
                for kk in range(c.NZ1):
                    for j in range(0, c.T, 512):
                        w = min(512, c.T - j)
                        te.matmul(self.psA[:, j:j + w],
                                  self.WBF[0][:, kk * 128:(kk + 1) * 128],
                                  YTT[:, kk * c.T + j:kk * c.T + j + w],
                                  start=(kk == 0), stop=(kk == c.NZ1 - 1))
            self.step("tensor", ft)

            def fv(v):
                p.sV.inc(v.tensor_copy(out=self.sR[:, 0:c.T], in_=self.psA[:, 0:c.T]))
            self.step("vector", fv)

            def fst(sync, dc=dc):
                p.sST.dma(sync.dma_start(out=d["sT_sp"][dc * 128:(dc + 1) * 128, :],
                                         in_=self.sR[:, 0:c.T]))
            self.step("sync", fst)
        self.barrier()

    def phase_ffn1(self, P, t0, tn):
        p, c, d = self.p, self.c, self.d
        S = c.NT
        self.qx_pass_l1(t0, tn)
        self.col2row(self.stat[:, 3 * S:3 * S + c.NT], 0, c.NT, self.psC)

        def f0(v):
            v.tensor_scalar(out=self.rows[0:1, t0:t0 + tn],
                            in0=self.rows[0:1, t0:t0 + tn],
                            scalar1=2.0, scalar2=None, op0=ALU.mult)
        self.step("vector", f0)
        self.bcast(self.rows[0:1, t0:t0 + tn], self.bc1, tn, self.psC)

        def epi(fc):
            def s1(v, fc=fc):
                v.tensor_tensor(out=self.scr2[:, 0:tn], in0=self.psA[:, 0:tn],
                                in1=self.bc1[:, 0:tn], op=ALU.mult)
                v.tensor_tensor(out=self.XR[:, 0:tn], in0=self.psB[:, 0:tn],
                                in1=self.bc1[:, 0:tn], op=ALU.mult)
            self.step("vector", s1)

            def s2(s, fc=fc):
                s.activation(self.tscr[:, 0:tn], self.scr2[:, 0:tn], ACT.Silu)
            self.step("scalar", s2)

            def s3(v, fc=fc):
                v.tensor_tensor(out=P[:, fc * tn:(fc + 1) * tn],
                                in0=self.tscr[:, 0:tn], in1=self.XR[:, 0:tn],
                                op=ALU.mult)
            self.step("vector", s3)

        self.ffn_gateup(d["wg1"], d["wu1"], self.QX, P,
                        self.conv_sign, self.conv_sign, epi, tn)

        def f1(v):
            v.memset(self.bc2[:, 0:tn], 2.0)
        self.step("vector", f1)
        self.barrier()

    def qx_pass_l1(self, t0, tn):
        p, c, d = self.p, self.c, self.d
        S = c.NT
        self.col2row(self.stat[:, 1 * S:1 * S + c.NT], 0, c.NT, self.psC)
        self.col2row(self.stat[:, 0 * S:0 * S + c.NT], 1024, c.NT, self.psC)
        self.bcast(self.rows[0:1, t0:t0 + tn], self.bc1, tn, self.psC)
        self.bcast(self.rows[0:1, 1024 + t0:1024 + t0 + tn], self.bc2, tn, self.psB)
        dm3, vm3 = {}, {}
        for k in range(c.ND):
            par = k % 2

            def fl(g, k=k, par=par):
                if k >= 2:
                    g.wait_ge(p.sV.h, vm3[k - 2])
                sem = p.sDG if par == 0 else p.sDGb
                if k >= 2:
                    g.wait_ge(sem.h, dm3[k - 2][1])
                else:
                    for s__ in (p.sDG, p.sDGb, p.sSTG):
                        if s__.n > 0:
                            g.wait_ge(s__.h, s__.n)
                sem.dma(g.dma_start(
                    out=self.XR[:, par * 1024:par * 1024 + tn],
                    in_=d["xT_sp"][k * 128:(k + 1) * 128, t0:t0 + tn]))
            self.on("gpsimd", fl)
            dm3[k] = ((p.sDG if par == 0 else p.sDGb), (p.sDG if par == 0 else p.sDGb).n)

            def fv(v, k=k, par=par):
                v.wait_ge(dm3[k][0].h, dm3[k][1])
                v.drain()
                xr = self.XR[:, par * 1024:par * 1024 + tn]
                v.tensor_tensor(out=self.scr2[:, 0:tn], in0=xr,
                                in1=self.bc1[:, 0:tn], op=ALU.mult)
                v.drain()
                v.tensor_tensor(out=self.scr2[:, 0:tn], in0=self.scr2[:, 0:tn],
                                in1=self.bc2[:, 0:tn], op=ALU.subtract)
                v.drain()
                p.sV.inc(v.tensor_scalar(out=self.QX[:, k * tn:(k + 1) * tn],
                                         in0=self.scr2[:, 0:tn], scalar1=MAGIC,
                                         scalar2=MAGIC, op0=ALU.add,
                                         op1=ALU.subtract))
            self.on("vector", fv)
            vm3[k] = p.sV.n
        self.barrier()


def build_body(p, block, d, rg, dbg_out, dbg):
    b = B(p, block, d, rg, dbg)
    c = p.cfg
    nc = p.nc
    TW = c.T + c.WIN
    b.phase_init()

    with contextlib.ExitStack() as scope0:
        XN = scope0.enter_context(nc.sbuf_tensor("XN0", [128, c.ND * TW], BF16))
        b.XN = XN
        own = [XN[:, k * TW:k * TW + c.T] for k in range(c.ND)]
        halo = [XN[:, k * TW + c.T:k * TW + TW] for k in range(c.ND)]
        b.ln_pass(d["x_own"], c.T, 0, 1, c.EPS_LN, xn_chunks=own, slot_c=2,
                  aq_eps=c.EPS_FFN0_AQ)
        b.ln_pass(d["x_halo"], c.WIN, 6, 7, c.EPS_LN, xn_chunks=halo)
        b.dbg_dump("dbg_xn", XN[:])
        b.phase_ffn0_stats()
        nk = c.NKT * 128
        with contextlib.ExitStack() as sa:
            b.cosT = sa.enter_context(nc.sbuf_tensor("cosT", [128, nk], F32))
            b.sinT = sa.enter_context(nc.sbuf_tensor("sinT", [128, nk], F32))
            b.vTr = sa.enter_context(nc.sbuf_tensor("vTr", [128, nk], BF16))
            b.V = sa.enter_context(nc.sbuf_tensor("Vtok", [128, c.NKT * c.KVH * 128], BF16))
            b.KRR = sa.enter_context(nc.sbuf_tensor("KRR", [128, c.KVH * nk], BF16))
            b.QR = sa.enter_context(nc.sbuf_tensor("QR", [128, c.T], BF16))
            b.Y = sa.enter_context(nc.sbuf_tensor("Yat", [128, c.H * c.T], BF16))
            b.pr = sa.enter_context(nc.sbuf_tensor("prb", [128, c.W5 * c.W5 * 128], BF16))
            b.phase_attn()
            b.dbg_dump("dbg_y", b.Y[:])
        if "dbg_s0" in b.dbg:
            def fds(sync):
                for s_ in (p.sT, p.sV, p.sS, p.sG):
                    sync.wait_ge(s_.h, s_.n)
                if p.sST.n:
                    sync.wait_ge(p.sST.h, p.sST.n)
                p.sDW.dma(sync.dma_start(out=d["dbg_s0"][:], in_=d["sT_sp"][:]))
            block.sync(fds)
    if True:
        T2 = c.T // 2
        with contextlib.ExitStack() as sf:
            QX0 = sf.enter_context(nc.sbuf_tensor("QX0", [128, c.ND * T2], BF16))
            H = sf.enter_context(nc.sbuf_tensor("H0", [128, c.NF * T2], BF16))
            b.QX = QX0
            for half in range(2):
                b.phase_ffn0(H, half * T2, T2)
                if half == 0:
                    b.dbg_dump("dbg_qh", H[:])
                b.phase_down_combine(H, d["wd0"], b.make_conv_tern_idx(2),
                                     b.bc2, 0, half * T2, T2)
                b.l1_stats(half * T2, T2)
    if "dbg_x1" in b.dbg:
        def fdx1(sync):
            for s_ in (p.sT, p.sV, p.sS, p.sG, p.sST, p.sSTG):
                if s_.n:
                    sync.wait_ge(s_.h, s_.n)
            p.sDW.dma(sync.dma_start(out=d["dbg_x1"][:], in_=d["xT_sp"][:]))
        block.sync(fdx1)

    with contextlib.ExitStack() as s1:
        RS = c.T + 32
        NSL = c.NC1 + 1
        RAWP = s1.enter_context(nc.sbuf_tensor("RAWP", [128, NSL * RS], BF16,
                                               side="right"))
        S0ST = s1.enter_context(nc.sbuf_tensor("S0ST", [128, c.NCH * c.H1 * 32],
                                               BF16, side="right"))
        b.gnw_b = s1.enter_context(nc.sbuf_tensor("gnw_b", [128, c.ZGD], F32,
                                                  side="right"))
        DT2 = s1.enter_context(nc.sbuf_tensor("DT2", [c.H1, c.T], F32, side="right"))
        b.DTR = s1.enter_context(nc.sbuf_tensor("DTR", [c.H1, c.T], F32, side="right"))
        ZGR = s1.enter_context(nc.sbuf_tensor("ZGR", [128, max(c.ZGD, c.T)], BF16,
                                              side="right"))
        with contextlib.ExitStack() as s2:
            XN1 = s2.enter_context(nc.sbuf_tensor("XN1", [128, c.ND * c.T], BF16))
            b.phase_l1_entry(XN1)
            b.dbg_dump("dbg_xn1", XN1[:])
            b.phase_l1_proj(XN1, RAWP, DT2, ZGR)
            b.dbg_dump("dbg_raw", RAWP[:])
        with contextlib.ExitStack() as s3:
            YTT = s3.enter_context(nc.sbuf_tensor("YTT", [128, c.NZ1 * c.T], BF16))
            AcsTok = s3.enter_context(nc.sbuf_tensor("AcsTok", [128, c.NCH * c.H1], F32))
            csums = s3.enter_context(nc.sbuf_tensor("csums", [c.H1, c.NCH], F32))
            DSTT = s3.enter_context(nc.sbuf_tensor("DSTT", [128, 2 * c.NCH * c.H1], F32))
            EAT = s3.enter_context(nc.sbuf_tensor("EAT", [128, max(c.H1, 32)], F32))
            XVB = s3.enter_context(nc.sbuf_tensor("XVB", [128, c.H1 * 64 + 512], BF16))
            BPB = s3.enter_context(nc.sbuf_tensor("BPB", [128, c.H1 * 64 + 512], BF16))
            SCOR = s3.enter_context(nc.sbuf_tensor("SCOR", [128, c.H1 * 32], BF16))
            S0cur = b.maxA[:, 0:c.H1 * 32]
            Sinit = b.maxB[:, 0:c.H1 * 32]
            b.paranoid = bool(int(os.environ.get("MK_PARANOID", "0")))
            b.phase_l1_ssd(RAWP, S0ST, YTT, DT2, b.DTR, AcsTok,
                           csums, DSTT, EAT, XVB, BPB, SCOR,
                           S0cur, Sinit, ZGR)
            b.paranoid = False
            b.dbg_dump("dbg_ytt", YTT[:])
            b.dbg_dump("dbg_dt2", DT2[:])
        if "dbg_s1" in b.dbg:
            def fds1(sync):
                for s_ in (p.sT, p.sV, p.sS, p.sG, p.sST, p.sSTG):
                    if s_.n:
                        sync.wait_ge(s_.h, s_.n)
                p.sDW.dma(sync.dma_start(out=d["dbg_s1"][:], in_=d["sT_sp"][:]))
            block.sync(fds1)
    with contextlib.ExitStack() as s4:
        QX1 = s4.enter_context(nc.sbuf_tensor("QX1", [128, c.ND * (c.T // 2)], BF16))
        b.QX = QX1
        b.phase_l1_xq()
        T2 = c.T // 2
        with contextlib.ExitStack() as s5:
            P1 = s5.enter_context(nc.sbuf_tensor("P1", [128, c.NF * T2], BF16))
            for half in range(2):
                b.phase_ffn1(P1, half * T2, T2)
                b.phase_down_combine(P1, d["wd1"], b.conv_sign, b.bc2, 1,
                                     half * T2, T2)

    fin = {s: s.n for s in (p.sDG, p.sDGb, p.sDW, p.sDWb, p.sDS, p.sDSb,
                            p.sST, p.sSTG, p.sCC)}

    def fend(sync):
        for s, v in fin.items():
            sync.wait_ge(s.h, v)
    block.sync(fend)


# version 06
# speedup vs baseline: 1.0150x; 1.0025x over previous
import contextlib

import os
import numpy as np

import concourse.bass as bass
import concourse.mybir as mybir
from concourse.bass_utils import run_bass_kernel_spmd

F32 = mybir.dt.float32
BF16 = mybir.dt.bfloat16
AX = mybir.AxisListType.X
ALU = mybir.AluOpType
ACT = mybir.ActivationFunctionType

MAGIC = 12582912.0
NEGINF = -1e30
LN2PI = 6.283185307179586


class Cfg:
    def __init__(self, small=False):
        if small:
            self.T = 256
            self.D = 512
            self.H = 4
            self.KVH = 2
            self.WIN = 256
            self.F = 1024
            self.H1 = 8
            self.KVH1 = 4
        else:
            self.T = 1024
            self.D = 2048
            self.H = 16
            self.KVH = 4
            self.WIN = 512
            self.F = 8192
            self.H1 = 32
            self.KVH1 = 8
        self.HD = 128
        self.P1 = 64
        self.CH = 128
        self.NT = self.T // 128
        self.ND = self.D // 128
        self.NF = self.F // 128
        self.HT = self.WIN // 128
        self.NKT = self.NT + self.HT
        self.W5 = self.HT + 1
        self.CONVD = (self.H1 + 2 * self.KVH1) * self.P1
        self.NC1 = self.CONVD // 128
        self.ZGD = self.H1 * self.P1
        self.NZ1 = self.ZGD // 128
        self.PROJ = self.CONVD + self.ZGD + self.H1
        self.NPC = (self.PROJ + 127) // 128
        self.PROJ_PAD = self.NPC * 128
        self.NCH = self.T // self.CH
        self.G1 = self.H1 // self.KVH1
        self.EPS_LN = 1e-5
        self.EPS_QK = 1.1920928955078125e-07
        self.EPS_FFN0_AQ = 1e-5
        self.EPS_FFN0_RMS = 1e-6
        self.EPS_SSD = 1e-5
        self.EPS_FFN1_RMS = 1e-8
        self.ROPE_BASE = 10000.0


class Sem:

    def __init__(self, stack, nc, name):
        self.h = stack.enter_context(nc.semaphore(name))
        self.n = 0

    def inc(self, instr, k=1):
        instr.then_inc(self.h, k)
        self.n += k
        return instr

    def dma(self, instr):
        return self.inc(instr, 16)


class P:

    def __init__(self, nc, cfg, stack):
        self.nc = nc
        self.cfg = cfg
        self.stack = stack
        self.sT = Sem(stack, nc, "sT")
        self.sV = Sem(stack, nc, "sV")
        self.sS = Sem(stack, nc, "sS")
        self.sG = Sem(stack, nc, "sG")
        self.sCC = Sem(stack, nc, "sCC")
        self.sDW = Sem(stack, nc, "sDW")
        self.sDWb = Sem(stack, nc, "sDWb")
        self.sDG = Sem(stack, nc, "sDG")
        self.sDGb = Sem(stack, nc, "sDGb")
        self.sDS = Sem(stack, nc, "sDS")
        self.sDSb = Sem(stack, nc, "sDSb")
        self.sST = Sem(stack, nc, "sST")
        self.sSTG = Sem(stack, nc, "sSTG")
        self.sTf = Sem(stack, nc, "sTf")
        self.sVf = Sem(stack, nc, "sVf")
        self.all_sems = [self.sT, self.sV, self.sS, self.sG,
                         self.sDW, self.sDWb, self.sDG, self.sDGb,
                         self.sDS, self.sDSb, self.sST, self.sSTG,
                         self.sTf, self.sVf]

    def sb(self, name, shape, dtype, side="left"):
        return self.stack.enter_context(
            self.nc.sbuf_tensor(name, shape, dtype, side=side))

    def ps(self, name, shape, dtype=F32):
        return self.stack.enter_context(self.nc.psum_tensor(name, shape, dtype))


def _emit_barrier(p, block, engines=("tensor", "vector", "scalar", "gpsimd")):
    marks = {}
    for e in engines:
        def emit_mark(eng, e=e):
            sem = {"tensor": p.sT, "vector": p.sV, "scalar": p.sS,
                   "gpsimd": p.sG}[e]
            sem.inc(eng.drain())
            marks[e] = sem.n
        getattr(block, e)(emit_mark)
    for e in engines:
        def emit_wait(eng, e=e):
            for o in engines:
                if o == e:
                    continue
                sem = {"tensor": p.sT, "vector": p.sV, "scalar": p.sS,
                       "gpsimd": p.sG}[o]
                eng.wait_ge(sem.h, marks[o])
        getattr(block, e)(emit_wait)



def _tile_wT(w):
    return np.ascontiguousarray(w.T)


def _slab_stream(wt, chunk):
    din, dout = wt.shape
    assert dout % chunk == 0
    return np.ascontiguousarray(
        wt.reshape(din, dout // chunk, chunk).transpose(1, 0, 2))


def _pcol(v):
    n = v.shape[0]
    assert n % 128 == 0
    return np.ascontiguousarray(v.reshape(n // 128, 128).T)


def stage_inputs(cfg, inputs):
    c = cfg
    x = np.asarray(inputs["L0_attn_x"], np.float32)
    B, S, D = x.shape
    assert D == c.D and S == 2 * c.T

    wq = _slab_stream(_tile_wT(np.asarray(inputs["L0_attn_Wq"], np.float32)), 128)
    wk = np.ascontiguousarray(_tile_wT(np.asarray(inputs["L0_attn_Wk"], np.float32)))
    wv = np.ascontiguousarray(_tile_wT(np.asarray(inputs["L0_attn_Wv"], np.float32)))
    wo = _slab_stream(_tile_wT(np.asarray(inputs["L0_attn_Wo"], np.float32)), 128)
    qgain = np.asarray(inputs["L0_attn_q_gain"], np.float32).reshape(1, c.H)

    wg0 = _slab_stream(_tile_wT(np.asarray(inputs["L0_ffn_w_gate"], np.float32)), 128)
    wu0 = _slab_stream(_tile_wT(np.asarray(inputs["L0_ffn_w_up"], np.float32)), 128)
    wd0 = _slab_stream(_tile_wT(np.asarray(inputs["L0_ffn_w_down"], np.float32)), 128)
    lnw = _pcol(np.asarray(inputs["L0_ffn_ln_weight"], np.float32))

    wvkq = _tile_wT(np.asarray(inputs["L1_attn_W_vkqgdt"], np.float32))
    pad = np.zeros((c.D, c.PROJ_PAD - c.PROJ), np.float32)
    wvkq = _slab_stream(np.ascontiguousarray(np.concatenate([wvkq, pad], 1)), 128)
    convw = np.asarray(inputs["L1_attn_conv_w"], np.float32)
    convw0 = _pcol(convw[:, 0])
    convw1 = _pcol(convw[:, 1])
    convb = _pcol(np.asarray(inputs["L1_attn_conv_b"], np.float32))
    dtb = np.asarray(inputs["L1_attn_dt_bias"], np.float32).reshape(c.H1, 1)
    alog = np.asarray(inputs["L1_attn_A_log_bias"], np.float32).reshape(c.H1, 1)
    gnw = np.tile(np.asarray(inputs["L1_attn_gnorm_w"], np.float32), c.H1).reshape(1, c.ZGD)
    wo1 = _slab_stream(_tile_wT(np.asarray(inputs["L1_attn_W_o"], np.float32)), 128)

    wg1 = _slab_stream(_tile_wT(np.asarray(inputs["L1_ffn_W_g"], np.float32)), 128)
    wu1 = _slab_stream(_tile_wT(np.asarray(inputs["L1_ffn_W_u"], np.float32)), 128)
    wd1 = _slab_stream(_tile_wT(np.asarray(inputs["L1_ffn_W_d"], np.float32)), 128)

    shared = dict(
        wq=wq, wk=wk, wv=wv, wo=wo, qgain=qgain,
        wg0=wg0, wu0=wu0, wd0=wd0, lnw=lnw,
        wvkq=wvkq, convw0=convw0, convw1=convw1, convb=convb,
        dtb=dtb, alog=alog, gnw=gnw, wo1=wo1,
        wg1=wg1, wu1=wu1, wd1=wd1,
    )

    in_maps = []
    for core in range(2 * B):
        b, half = core // 2, core % 2
        t0 = half * c.T
        x_own = np.ascontiguousarray(x[b, t0:t0 + c.T].T)
        if half == 0:
            x_halo = np.zeros((c.D, c.WIN), np.float32)
        else:
            x_halo = np.ascontiguousarray(x[b, t0 - c.WIN:t0].T)
        params = np.zeros((1, 32), np.float32)
        params[0, 0] = float(t0)
        params[0, 1] = float(half)
        params[0, 2] = float(1 - half)
        for j in range(c.NKT):
            params[0, 3 + j] = NEGINF if (half == 0 and j < c.HT) else 0.0
        m = dict(shared)
        m.update(x_own=x_own, x_halo=x_halo, params=params)
        in_maps.append(m)
    return in_maps



def build_program(cfg, dbg=(), n_cores=8):
    c = cfg
    nc = bass.Bass("TRN2", target_bir_lowering=False)
    stack = contextlib.ExitStack()

    d = {}
    d["x_own"] = nc.dram_tensor("x_own", [c.D, c.T], F32, kind="ExternalInput").ap()
    d["x_halo"] = nc.dram_tensor("x_halo", [c.D, c.WIN], F32, kind="ExternalInput").ap()
    d["params"] = nc.dram_tensor("params", [1, 32], F32, kind="ExternalInput").ap()
    d["wq"] = nc.dram_tensor("wq", [c.H, c.D, 128], F32, kind="ExternalInput").ap()
    d["wk"] = nc.dram_tensor("wk", [c.D, c.KVH * c.HD], F32, kind="ExternalInput").ap()
    d["wv"] = nc.dram_tensor("wv", [c.D, c.KVH * c.HD], F32, kind="ExternalInput").ap()
    d["wo"] = nc.dram_tensor("wo", [c.ND, c.D, 128], F32, kind="ExternalInput").ap()
    d["qgain"] = nc.dram_tensor("qgain", [1, c.H], F32, kind="ExternalInput").ap()
    d["wg0"] = nc.dram_tensor("wg0", [c.NF, c.D, 128], F32, kind="ExternalInput").ap()
    d["wu0"] = nc.dram_tensor("wu0", [c.NF, c.D, 128], F32, kind="ExternalInput").ap()
    d["wd0"] = nc.dram_tensor("wd0", [c.ND, c.F, 128], F32, kind="ExternalInput").ap()
    d["lnw"] = nc.dram_tensor("lnw", [128, c.NF], F32, kind="ExternalInput").ap()
    d["wvkq"] = nc.dram_tensor("wvkq", [c.NPC, c.D, 128], F32, kind="ExternalInput").ap()
    d["convw0"] = nc.dram_tensor("convw0", [128, c.NC1], F32, kind="ExternalInput").ap()
    d["convw1"] = nc.dram_tensor("convw1", [128, c.NC1], F32, kind="ExternalInput").ap()
    d["convb"] = nc.dram_tensor("convb", [128, c.NC1], F32, kind="ExternalInput").ap()
    d["dtb"] = nc.dram_tensor("dtb", [c.H1, 1], F32, kind="ExternalInput").ap()
    d["alog"] = nc.dram_tensor("alog", [c.H1, 1], F32, kind="ExternalInput").ap()
    d["gnw"] = nc.dram_tensor("gnw", [1, c.ZGD], F32, kind="ExternalInput").ap()
    d["wo1"] = nc.dram_tensor("wo1", [c.ND, c.D, 128], F32, kind="ExternalInput").ap()
    d["wg1"] = nc.dram_tensor("wg1", [c.NF, c.D, 128], F32, kind="ExternalInput").ap()
    d["wu1"] = nc.dram_tensor("wu1", [c.NF, c.D, 128], F32, kind="ExternalInput").ap()
    d["wd1"] = nc.dram_tensor("wd1", [c.ND, c.F, 128], F32, kind="ExternalInput").ap()
    d["y_out"] = nc.dram_tensor("y_out", [c.D, c.T], F32, kind="ExternalOutput").ap()
    d["xT_sp"] = nc.dram_tensor("xT_sp", [c.D, c.T], F32, kind="Internal").ap()
    d["sT_sp"] = nc.dram_tensor("sT_sp", [c.D, c.T], BF16, kind="Internal").ap()
    d["zg_sp"] = nc.dram_tensor("zg_sp", [c.ZGD, c.T], BF16, kind="Internal").ap()
    d["cc1_in"] = nc.dram_tensor("cc1_in", [128, c.NC1], F32, kind="Internal").ap()
    d["cc1_out"] = nc.dram_tensor("cc1_out", [128, c.NC1], F32, kind="Internal").ap()
    d["cc2_in"] = nc.dram_tensor("cc2_in", [128, c.H1 * c.P1 // 2], F32, kind="Internal").ap()
    d["cc2_out"] = nc.dram_tensor("cc2_out", [128, c.H1 * c.P1 // 2], F32, kind="Internal").ap()
    for name, shape, dt_ in dbg:
        d[name] = nc.dram_tensor(name, shape, dt_, kind="ExternalOutput").ap()

    rg = [[2 * i, 2 * i + 1] for i in range(n_cores // 2)]

    p = P(nc, c, stack)
    dbg_out = {}

    with nc.Block() as block:
        build_body(p, block, d, rg, dbg_out, dbg)

    stack.close()
    nc.finalize()
    return nc


def kernel(**inputs):
    cfg = Cfg(small=False)
    in_maps = stage_inputs(cfg, inputs)
    nc = build_program(cfg)
    res = run_bass_kernel_spmd(nc, in_maps, core_ids=list(range(8)))
    out = np.empty((4, 2048, 2048), np.float32)
    for core in range(8):
        b, half = core // 2, core % 2
        out[b, half * cfg.T:(half + 1) * cfg.T] = res.results[core]["y_out"].T
    return out




class B:

    def __init__(self, p, block, d, rg, dbg):
        self.p, self.block, self.d, self.rg = p, block, d, rg
        self.nc = p.nc
        self.c = p.cfg
        self.dbg = set(n for (n, _, _) in dbg)
        self.qwaited = {}
        self.paranoid = False
        self.drained = set()

    def on(self, eng, fn):
        self.drained.discard(eng)
        getattr(self.block, eng)(fn)

    def barrier(self):
        _emit_barrier(self.p, self.block)
        self.drained.update(("tensor", "vector", "scalar", "gpsimd"))

    def dbg_dump(self, name, src_ap):
        if name not in self.dbg:
            return
        p, d = self.p, self.d
        marks = {s: s.n for s in (p.sT, p.sV, p.sS, p.sG, p.sDW, p.sDWb, p.sDG,
                                  p.sDGb, p.sDS, p.sDSb, p.sST, p.sSTG, p.sCC)
                 if s.n > 0}

        def f(sync):
            for s, v in marks.items():
                sync.wait_ge(s.h, v)
            p.sDW.dma(sync.dma_start(out=d[name][:], in_=src_ap))
        self.on("sync", f)


    def _twait(self, te):
        p = self.p
        for s in (p.sV, p.sS, p.sG):
            if s.n:
                te.wait_ge(s.h, s.n)
        for s in (p.sDW, p.sDWb, p.sDG, p.sDGb, p.sDS, p.sDSb, p.sST, p.sSTG,
                  p.sCC):
            if s.n > self.qwaited.get(s, 0):
                te.wait_ge(s.h, s.n)
                self.qwaited[s] = s.n

    def step(self, eng, fn):
        p = self.p
        sems = {"tensor": p.sT, "vector": p.sV, "scalar": p.sS, "gpsimd": p.sG}
        own = sems.get(eng)
        marks = [(s, s.n) for nm, s in sems.items() if nm != eng and s.n > 0]
        own_q = {"sync": (p.sDW, p.sDWb, p.sST),
                 "gpsimd": (p.sDG, p.sDGb, p.sSTG, p.sCC),
                 "scalar": (p.sDS, p.sDSb)}.get(eng, ())
        for s in own_q:
            if s.n > 0:
                marks.append((s, s.n))
                self.qwaited[s] = max(self.qwaited.get(s, 0), s.n)
        for s in (p.sDW, p.sDWb, p.sDG, p.sDGb, p.sDS, p.sDSb, p.sST, p.sSTG,
                  p.sCC):
            if s not in own_q and s.n > self.qwaited.get(s, 0):
                marks.append((s, s.n))
                self.qwaited[s] = s.n

        lead_drain = eng != "sync" and eng not in self.drained

        def wrapped(e):
            for s, v in marks:
                e.wait_ge(s.h, v)
            if lead_drain:
                e.drain()
            fn(e)
            if eng == "sync":
                pass
            else:
                own.inc(e.drain())
        self.on(eng, wrapped)
        if eng != "sync":
            self.drained.add(eng)
        if self.paranoid:
            self.barrier()

    def load_slab(self, eng, stg, sl):
        R = sl.shape[0]
        return eng.dma_start(
            out=stg[:, 0:R].rearrange("p (k c) -> p k c", c=128),
            in_=sl.rearrange("(k p) c -> p k c", p=128))

    def col2row(self, stat_ap, row_off, nt, psum):
        p = self.p

        def ft(te):
            self._twait(te)
            for t in range(nt):
                p.sT.inc(te.transpose(psum[0:1, t * 128:(t + 1) * 128],
                                      stat_ap[:, t:t + 1], self.idf[:]))
        self.on("tensor", ft)
        mark = p.sT.n

        def fv(v):
            v.wait_ge(p.sT.h, mark)
            p.sV.inc(v.tensor_copy(out=self.rows[0:1, row_off:row_off + 128 * nt],
                                   in_=psum[0:1, 0:128 * nt]))
        self.on("vector", fv)

    def bcast(self, row_ap, out_sb, n, psum):
        p = self.p

        def ft(te):
            self._twait(te)
            for j in range(0, n, 512):
                w = min(512, n - j)
                p.sT.inc(te.matmul(psum[:, j:j + w], self.ones_row[:, 0:128],
                                   row_ap[:, j:j + w], start=True, stop=True))
        self.on("tensor", ft)
        mark = p.sT.n

        def fv(v):
            v.wait_ge(p.sT.h, mark)
            p.sV.inc(v.tensor_copy(out=out_sb[:, 0:n], in_=psum[:, 0:n]))
        self.on("vector", fv)

    def stream_slabs(self, slabs, conv, use, extra_v=None):
        p = self.p
        loadm, convm, usem = {}, {}, {}
        for i, sl in enumerate(slabs):
            par = i % 2
            stg, wbf = self.WSTG[par], self.WBF[par]
            R = sl.shape[0]
            assert R % 128 == 0 and R <= stg.shape[1]

            def fl(sync, i=i, stg=stg, sl=sl, R=R):
                if i >= 2:
                    sync.wait_ge(p.sV.h, convm[i - 2])
                sem = p.sDW if par == 0 else p.sDWb
                trio = (p.sDW, p.sDWb, p.sST)
                if sem.n > 0:
                    sync.wait_ge(sem.h, sem.n)
                if i < 2:
                    for s__ in trio:
                        if s__ is not sem and s__.n > 0:
                            sync.wait_ge(s__.h, s__.n)
                    for s__ in (p.sV, p.sS):
                        if s__.n > 0:
                            sync.wait_ge(s__.h, s__.n)
                sem.dma(self.load_slab(sync, stg, sl))
            self.on("sync", fl)
            loadm[i] = ((p.sDW if par == 0 else p.sDWb), (p.sDW if par == 0 else p.sDWb).n)

            def fc(v, i=i, stg=stg, wbf=wbf, R=R):
                v.wait_ge(loadm[i][0].h, loadm[i][1])
                if i >= 2:
                    v.wait_ge(p.sT.h, usem[i - 2])
                conv(v, i, stg, wbf, R)
                if extra_v is not None:
                    extra_v(v, i, stg, R)
            self.on("vector", fc)
            convm[i] = p.sV.n

            def fu(te, i=i, wbf=wbf, R=R):
                te.wait_ge(p.sV.h, convm[i])
                use(te, i, wbf, R)
            self.on("tensor", fu)
            usem[i] = p.sT.n
        return usem

    def conv_cast(self, v, i, stg, wbf, R):
        self.p.sV.inc(v.tensor_copy(out=wbf[:, 0:R], in_=stg[:, 0:R]))

    def make_conv_tern_idx(self, m):
        p = self.p

        def conv(v, i, stg, wbf, R):
            v.tensor_scalar(out=stg[:, 0:R], in0=stg[:, 0:R],
                            scalar1=self.srep[:, m:m + 1], scalar2=1.0,
                            op0=ALU.mult, op1=ALU.min)
            v.drain()
            v.tensor_scalar(out=stg[:, 0:R], in0=stg[:, 0:R], scalar1=-1.0,
                            scalar2=MAGIC, op0=ALU.max, op1=ALU.add)
            v.drain()
            p.sV.inc(v.tensor_scalar(out=wbf[:, 0:R], in0=stg[:, 0:R],
                                     scalar1=MAGIC, scalar2=None,
                                     op0=ALU.subtract))
        return conv

    def conv_sign(self, v, i, stg, wbf, R):
        self.p.sV.inc(v.tensor_scalar(out=wbf[:, 0:R], in0=stg[:, 0:R],
                                      scalar1=0.0, scalar2=0.5,
                                      op0=ALU.is_gt, op1=ALU.subtract))


    def phase_init(self):
        p, c, d = self.p, self.c, self.d
        self.idf = p.sb("idf", [128, 128], F32)
        self.idb = p.sb("idb", [128, 128], BF16)
        self.ones_row = p.sb("ones_row", [1, 128], F32)
        self.ones_col = p.sb("ones_col", [128, 1], F32)
        self.ones_col_b = p.sb("ones_col_b", [128, 1], BF16)
        self.uones = p.sb("uones", [128, 128], F32)
        self.mk_diag = p.sb("mk_diag", [128, 128], BF16)
        self.mk_far = p.sb("mk_far", [128, 128], BF16)
        self.mk_lowT = p.sb("mk_lowT", [128, 128], F32)
        self.rows = p.sb("rows", [1, 2048], F32)
        self.params_b = p.sb("params_b", [128, 32], F32)
        self.qgain_sb = p.sb("qgain_sb", [1, c.H], F32)
        self.Mh_b = p.sb("Mh_b", [128, c.H], F32)
        self.lnw_sb = p.sb("lnw_sb", [128, c.NF], F32)
        self.cw0 = p.sb("cw0", [128, c.NC1], F32)
        self.cw1 = p.sb("cw1", [128, c.NC1], F32)
        self.cb_ = p.sb("cb_", [128, c.NC1], F32)
        self.dtb_sb = p.sb("dtb_sb", [c.H1, 1], F32)
        self.negA = p.sb("negA", [c.H1, 1], F32)
        self.stat = p.sb("stat", [128, 8 * c.NT + 32], F32)
        self.tscr = p.sb("tscr", [128, 1024], F32)
        self.scr2 = p.sb("scr2", [128, 1024], F32)
        self.bc1 = p.sb("bc1", [128, 1024], F32)
        self.bc2 = p.sb("bc2", [128, 1024], F32)
        self.maxA = p.sb("maxA", [128, 1024], F32)
        self.maxB = p.sb("maxB", [128, 1024], F32)
        self.WSTG = [p.sb(f"wstg{i}", [128, 2048], F32) for i in range(2)]
        self.WBF = [p.sb(f"wbf{i}", [128, 2048], BF16) for i in range(2)]
        self.srep = p.sb("srep", [128, 3], F32)
        self.XR = p.sb("XR", [128, 2048], F32)
        self.sR = p.sb("sR", [128, 1024], BF16)
        self.rowsE = p.sb("rowsE", [1, 768], F32)
        self.csJ = p.sb("csJ", [128, 2], F32)
        self.sgn = p.sb("sgn", [128, 1], F32)
        self.XN = None
        self.QX = None

        psA = self.psA = p.ps("psA", [128, 1024])
        self.psB = p.ps("psB", [128, 1024])
        self.psC = p.ps("psC", [128, 1024])
        self.psY = p.ps("psY", [128, 1024])

        def fg(g):
            g.memset(self.idf[:], 0.0)
            g.memset(self.uones[:], 0.0)
            g.memset(self.mk_diag[:], 1.0)
            g.memset(self.mk_far[:], 1.0)
            g.memset(self.mk_lowT[:], 0.0)
            g.memset(self.ones_row[:], 1.0)
            g.memset(self.ones_col[:], 1.0)
            g.memset(self.ones_col_b[:], 1.0)
            g.memset(self.sgn[:], 1.0)
            g.drain()
            g.affine_select(out=self.sgn[:], in_=self.sgn[:], pattern=[[0, 1]],
                            compare_op=ALU.is_ge, fill=-1.0, base=63,
                            channel_multiplier=-1)
            g.drain()
            g.affine_select(out=self.idf[:], in_=self.idf[:], pattern=[[-1, 128]],
                            compare_op=ALU.not_equal, fill=1.0, base=0,
                            channel_multiplier=1)
            g.affine_select(out=self.uones[:], in_=self.uones[:], pattern=[[1, 128]],
                            compare_op=ALU.is_ge, fill=-1.0, base=0,
                            channel_multiplier=-1)
            g.affine_select(out=self.mk_diag[:], in_=self.mk_diag[:],
                            pattern=[[1, 128]], compare_op=ALU.is_ge, fill=0.0,
                            base=0, channel_multiplier=-1)
            g.affine_select(out=self.mk_far[:], in_=self.mk_far[:],
                            pattern=[[-1, 128]], compare_op=ALU.is_ge, fill=0.0,
                            base=-1, channel_multiplier=1)
            g.affine_select(out=self.mk_lowT[:], in_=self.mk_lowT[:],
                            pattern=[[1, 128]], compare_op=ALU.is_ge, fill=NEGINF,
                            base=0, channel_multiplier=-1)
            p.sG.inc(g.drain())
        self.on("gpsimd", fg)
        gm = p.sG.n

        def fv(v):
            v.wait_ge(p.sG.h, gm)
            p.sV.inc(v.tensor_scalar(out=self.uones[:], in0=self.uones[:],
                                     scalar1=1.0, scalar2=None, op0=ALU.add))
            p.sV.inc(v.tensor_copy(out=self.idb[:], in_=self.idf[:]))
        self.on("vector", fv)

        def fl(sync):
            p.sDW.dma(sync.dma_start(out=self.rows[0:1, 0:32], in_=d["params"][:]))
            p.sDW.dma(sync.dma_start(out=self.qgain_sb[:], in_=d["qgain"][:]))
            p.sDW.dma(sync.dma_start(out=self.lnw_sb[:], in_=d["lnw"][:]))
            p.sDW.dma(sync.dma_start(out=self.cw0[:], in_=d["convw0"][:]))
            p.sDW.dma(sync.dma_start(out=self.cw1[:], in_=d["convw1"][:]))
            p.sDW.dma(sync.dma_start(out=self.cb_[:], in_=d["convb"][:]))
            p.sDW.dma(sync.dma_start(out=self.dtb_sb[:], in_=d["dtb"][:]))
            p.sDW.dma(sync.dma_start(out=self.negA[:], in_=d["alog"][:]))
        self.on("sync", fl)
        ldm = p.sDW.n

        def fwait(v):
            v.wait_ge(p.sDW.h, ldm)
            p.sV.inc(v.drain())
        self.on("vector", fwait)
        self.bcast(self.rows[0:1, 0:32], self.params_b, 32, self.psC)

        def fs(s):
            s.wait_ge(p.sDW.h, ldm)
            s.activation(self.negA[:], self.negA[:], ACT.Exp)
            s.drain()
            s.activation(self.rows[0:1, 64:64 + c.H], self.qgain_sb[:],
                         ACT.Abs, scale=float(np.sqrt(c.HD)))
            p.sS.inc(s.drain())
        self.on("scalar", fs)
        sm = p.sS.n

        def fv3(v):
            v.wait_ge(p.sS.h, sm)
            p.sV.inc(v.tensor_scalar(out=self.negA[:], in0=self.negA[:],
                                     scalar1=-1.0, scalar2=None, op0=ALU.mult))
        self.on("vector", fv3)
        self.bcast(self.rows[0:1, 64:64 + c.H], self.Mh_b, c.H, self.psC)
        self.barrier()


    def ln_pass(self, src, n, slot_m, slot_r, eps, xn_chunks=None, slot_c=None,
                aq_eps=None):
        p, c = self.p, self.c
        nd, nt = c.ND, n // 128
        S = c.NT

        def stat_ap(slot):
            return self.stat[:, slot * S:slot * S + nt]

        dm, sm = {}, {}
        for k in range(nd):
            par = k % 2

            def fl(g, k=k, par=par):
                if k >= 2:
                    g.wait_ge(p.sT.h, sm[k - 2])
                sem = p.sDG if par == 0 else p.sDGb
                trio = (p.sDG, p.sDGb, p.sSTG)
                if sem.n > 0:
                    g.wait_ge(sem.h, sem.n)
                if k < 2:
                    for s__ in trio:
                        if s__ is not sem and s__.n > 0:
                            g.wait_ge(s__.h, s__.n)
                    for s__ in (p.sV, p.sS):
                        if s__.n > 0:
                            g.wait_ge(s__.h, s__.n)
                sem.dma(g.dma_start(out=self.XR[:, par * 1024:par * 1024 + n],
                                      in_=src[k * 128:(k + 1) * 128, :]))
            self.on("gpsimd", fl)
            dm[k] = ((p.sDG if par == 0 else p.sDGb), (p.sDG if par == 0 else p.sDGb).n)

            def fs(s, k=k, par=par):
                s.wait_ge(dm[k][0].h, dm[k][1])
                sq = self.maxA if par == 0 else self.maxB
                p.sS.inc(s.activation(sq[:, 0:n],
                                      self.XR[:, par * 1024:par * 1024 + n],
                                      ACT.Square))
            self.on("scalar", fs)
            sqm = p.sS.n

            def ft(te, k=k, par=par, sqm=sqm):
                te.wait_ge(dm[k][0].h, dm[k][1])
                te.wait_ge(p.sS.h, sqm)
                sq = self.maxA if par == 0 else self.maxB
                for j in range(0, n, 512):
                    w = min(512, n - j)
                    te.matmul(self.psC[0:1, j:j + w], self.ones_col[:],
                              self.XR[:, par * 1024 + j:par * 1024 + j + w],
                              start=(k == 0), stop=(k == nd - 1))
                    p.sT.inc(te.matmul(
                        self.psA[0:1, j:j + w], self.ones_col[:],
                        sq[:, j:j + w],
                        start=(k == 0), stop=(k == nd - 1)))
            self.on("tensor", ft)
            sm[k] = p.sT.n
        self.barrier()

        def fv(v):
            v.tensor_copy(out=self.rows[0:1, 0:n], in_=self.psC[0:1, 0:n])
            p.sV.inc(v.tensor_copy(out=self.rows[0:1, 1024:1024 + n],
                                   in_=self.psA[0:1, 0:n]))
        self.on("vector", fv)
        self.barrier()

        def ft2(te):
            for t in range(nt):
                te.transpose(self.psC[:, t:t + 1],
                             self.rows[0:1, t * 128:(t + 1) * 128], self.idf[0:1, 0:1])
                p.sT.inc(te.transpose(
                    self.psB[:, t:t + 1],
                    self.rows[0:1, 1024 + t * 128:1024 + (t + 1) * 128],
                    self.idf[0:1, 0:1]))
        self.on("tensor", ft2)
        self.barrier()

        def fv2(v):
            v.tensor_scalar(out=stat_ap(slot_m), in0=self.psC[:, 0:nt],
                            scalar1=1.0 / c.D, scalar2=None, op0=ALU.mult)
            p.sV.inc(v.tensor_scalar(out=stat_ap(slot_r), in0=self.psB[:, 0:nt],
                                     scalar1=1.0 / c.D, scalar2=None, op0=ALU.mult))
        self.on("vector", fv2)
        self.barrier()

        def fv3(v):
            v.tensor_tensor(out=self.tscr[:, 0:nt], in0=stat_ap(slot_m),
                            in1=stat_ap(slot_m), op=ALU.mult)
            v.drain()
            v.tensor_tensor(out=self.tscr[:, 0:nt], in0=stat_ap(slot_r),
                            in1=self.tscr[:, 0:nt], op=ALU.subtract)
            v.drain()
            p.sV.inc(v.tensor_scalar(out=self.tscr[:, 0:nt], in0=self.tscr[:, 0:nt],
                                     scalar1=eps, scalar2=None, op0=ALU.add))
        self.on("vector", fv3)
        self.barrier()

        def fs2(s):
            p.sS.inc(s.activation(self.tscr[:, 64:64 + nt], self.tscr[:, 0:nt],
                                  ACT.Sqrt))
        self.on("scalar", fs2)
        self.barrier()

        def fv4(v):
            p.sV.inc(v.reciprocal(out=stat_ap(slot_r), in_=self.tscr[:, 64:64 + nt]))
        self.on("vector", fv4)
        self.barrier()

        self.col2row(stat_ap(slot_m), 0, nt, self.psC)
        self.col2row(stat_ap(slot_r), 1024, nt, self.psC)
        self.barrier()
        self.bcast(self.rows[0:1, 0:n], self.bc1, n, self.psC)
        self.bcast(self.rows[0:1, 1024:1024 + n], self.bc2, n, self.psB)
        self.barrier()

        if xn_chunks is None:
            return

        dm2, vm2 = {}, {}
        for k in range(nd):
            par = k % 2

            def fl(g, k=k, par=par):
                if k >= 2:
                    g.wait_ge(p.sV.h, vm2[k - 2])
                sem = p.sDG if par == 0 else p.sDGb
                trio = (p.sDG, p.sDGb, p.sSTG)
                if sem.n > 0:
                    g.wait_ge(sem.h, sem.n)
                if k < 2:
                    for s__ in trio:
                        if s__ is not sem and s__.n > 0:
                            g.wait_ge(s__.h, s__.n)
                    for s__ in (p.sV, p.sS):
                        if s__.n > 0:
                            g.wait_ge(s__.h, s__.n)
                sem.dma(g.dma_start(out=self.XR[:, par * 1024:par * 1024 + n],
                                      in_=src[k * 128:(k + 1) * 128, :]))
            self.on("gpsimd", fl)
            dm2[k] = ((p.sDG if par == 0 else p.sDGb), (p.sDG if par == 0 else p.sDGb).n)

            def fv5(v, k=k, par=par):
                v.wait_ge(dm2[k][0].h, dm2[k][1])
                v.drain()
                xr = self.XR[:, par * 1024:par * 1024 + n]
                v.tensor_tensor(out=self.scr2[:, 0:n], in0=xr,
                                in1=self.bc1[:, 0:n], op=ALU.subtract)
                v.drain()
                v.tensor_tensor(out=xn_chunks[k],
                                in0=self.scr2[:, 0:n], in1=self.bc2[:, 0:n],
                                op=ALU.mult)
                if slot_c is not None:
                    if k == 0:
                        v.tensor_scalar(out=self.maxA[:, 0:n],
                                        in0=self.scr2[:, 0:n], scalar1=1.0,
                                        scalar2=None, op0=ALU.mult)
                        v.tensor_scalar(out=self.maxB[:, 0:n],
                                        in0=self.scr2[:, 0:n], scalar1=-1.0,
                                        scalar2=None, op0=ALU.mult)
                    else:
                        v.scalar_tensor_tensor(out=self.maxA[:, 0:n],
                                               in0=self.scr2[:, 0:n], scalar=1.0,
                                               in1=self.maxA[:, 0:n],
                                               op0=ALU.mult, op1=ALU.max)
                        v.scalar_tensor_tensor(out=self.maxB[:, 0:n],
                                               in0=self.scr2[:, 0:n], scalar=-1.0,
                                               in1=self.maxB[:, 0:n],
                                               op0=ALU.mult, op1=ALU.max)
                p.sV.inc(v.drain())
            self.on("vector", fv5)
            vm2[k] = p.sV.n
        self.barrier()

        if slot_c is None:
            return
        fin_max = self.maxA

        def fvcmb(v):
            p.sV.inc(v.tensor_tensor(out=self.maxA[:, 0:n], in0=self.maxA[:, 0:n],
                                     in1=self.maxB[:, 0:n], op=ALU.max))
        self.on("vector", fvcmb)
        self.barrier()

        def ft3(te):
            for t in range(nt):
                p.sT.inc(te.transpose(self.psC[:, t * 128:(t + 1) * 128],
                                      fin_max[:, t * 128:(t + 1) * 128], self.idf[:]))
        self.on("tensor", ft3)
        self.barrier()

        def fv6(v):
            for t in range(nt):
                v.reduce_max(self.tscr[:, t:t + 1],
                             self.psC[:, t * 128:(t + 1) * 128], axis=AX)
            v.drain()
            v.tensor_tensor(out=self.tscr[:, 64:64 + nt], in0=self.tscr[:, 0:nt],
                            in1=stat_ap(slot_r), op=ALU.mult)
            v.drain()
            if aq_eps is not None:
                v.tensor_scalar(out=self.tscr[:, 64:64 + nt],
                                in0=self.tscr[:, 64:64 + nt],
                                scalar1=aq_eps, scalar2=None, op0=ALU.max)
                v.drain()
            v.reciprocal(out=self.tscr[:, 128:128 + nt], in_=self.tscr[:, 64:64 + nt])
            v.drain()
            v.tensor_scalar(out=self.tscr[:, 128:128 + nt],
                            in0=self.tscr[:, 128:128 + nt], scalar1=127.0,
                            scalar2=None, op0=ALU.mult)
            v.drain()
            v.tensor_scalar(out=stat_ap(slot_c + 1), in0=self.tscr[:, 64:64 + nt],
                            scalar1=1.0 / 127.0, scalar2=None, op0=ALU.mult)
            p.sV.inc(v.tensor_tensor(out=stat_ap(slot_c), in0=self.tscr[:, 128:128 + nt],
                                     in1=stat_ap(slot_r), op=ALU.mult))
        self.on("vector", fv6)
        self.barrier()

        self.barrier()

    def qx_pass(self, src, slot_m, slot_c, t0, tn, dst):
        p, c = self.p, self.c
        S = c.NT
        self.col2row(self.stat[:, slot_m * S:slot_m * S + c.NT], 0, c.NT, self.psC)
        self.col2row(self.stat[:, slot_c * S:slot_c * S + c.NT], 1024, c.NT, self.psC)
        self.bcast(self.rows[0:1, t0:t0 + tn], self.bc1, tn, self.psC)
        self.bcast(self.rows[0:1, 1024 + t0:1024 + t0 + tn], self.bc2, self.psC.shape[-1] if False else tn, self.psB)
        dm3, vm3 = {}, {}
        for k in range(c.ND):
            par = k % 2

            def fl(g, k=k, par=par):
                if k >= 2:
                    g.wait_ge(p.sV.h, vm3[k - 2])
                sem = p.sDG if par == 0 else p.sDGb
                trio = (p.sDG, p.sDGb, p.sSTG)
                if sem.n > 0:
                    g.wait_ge(sem.h, sem.n)
                if k < 2:
                    for s__ in trio:
                        if s__ is not sem and s__.n > 0:
                            g.wait_ge(s__.h, s__.n)
                    for s__ in (p.sV, p.sS):
                        if s__.n > 0:
                            g.wait_ge(s__.h, s__.n)
                sem.dma(g.dma_start(out=self.XR[:, par * 1024:par * 1024 + tn],
                                      in_=src[k * 128:(k + 1) * 128, t0:t0 + tn]))
            self.on("gpsimd", fl)
            dm3[k] = ((p.sDG if par == 0 else p.sDGb), (p.sDG if par == 0 else p.sDGb).n)

            def fv7(v, k=k, par=par):
                v.wait_ge(dm3[k][0].h, dm3[k][1])
                v.drain()
                xr = self.XR[:, par * 1024:par * 1024 + tn]
                v.tensor_tensor(out=self.scr2[:, 0:tn], in0=xr,
                                in1=self.bc1[:, 0:tn], op=ALU.subtract)
                v.drain()
                v.tensor_tensor(out=self.scr2[:, 0:tn], in0=self.scr2[:, 0:tn],
                                in1=self.bc2[:, 0:tn], op=ALU.mult)
                v.drain()
                p.sV.inc(v.tensor_scalar(out=dst[:, k * tn:(k + 1) * tn],
                                         in0=self.scr2[:, 0:tn], scalar1=MAGIC,
                                         scalar2=MAGIC, op0=ALU.add,
                                         op1=ALU.subtract))
            self.on("vector", fv7)
            vm3[k] = p.sV.n
        self.barrier()


    def rope_tables(self):
        p, c = self.p, self.c
        nk = c.NKT * 128
        half = c.HD // 2

        def fg(g):
            g.iota(self.csJ[:, 0:1], pattern=[[0, 1]], base=0, channel_multiplier=1,
                   allow_small_or_imprecise_dtypes=True)
            g.memset(self.csJ[:, 1:2], float(half))
            g.drain()
            g.affine_select(out=self.csJ[:, 1:2], in_=self.csJ[:, 1:2],
                            pattern=[[0, 1]], compare_op=ALU.is_ge, fill=0.0,
                            base=-half, channel_multiplier=1)
            g.drain()
            g.iota(self.cosT[:, 0:nk], pattern=[[1, nk]], base=0,
                   channel_multiplier=0, allow_small_or_imprecise_dtypes=True)
            p.sG.inc(g.drain())
        self.on("gpsimd", fg)
        gm = p.sG.n

        def fv0(v):
            v.wait_ge(p.sG.h, gm)
            p.sV.inc(v.tensor_tensor(out=self.csJ[:, 0:1], in0=self.csJ[:, 0:1],
                                     in1=self.csJ[:, 1:2], op=ALU.subtract))
        self.on("vector", fv0)
        vm0 = p.sV.n

        def fs(s):
            s.wait_ge(p.sV.h, vm0)
            p.sS.inc(s.activation(self.csJ[:, 1:2], self.csJ[:, 0:1], ACT.Exp,
                                  scale=-float(np.log(c.ROPE_BASE)) / half))
        self.on("scalar", fs)
        sm = p.sS.n

        def fv(v):
            v.wait_ge(p.sS.h, sm)
            v.tensor_scalar(out=self.cosT[:, 0:nk], in0=self.cosT[:, 0:nk],
                            scalar1=self.params_b[:, 0:1], scalar2=None,
                            op0=ALU.add)
            v.drain()
            v.tensor_scalar(out=self.cosT[:, 0:nk], in0=self.cosT[:, 0:nk],
                            scalar1=float(-c.WIN), scalar2=self.csJ[:, 1:2],
                            op0=ALU.add, op1=ALU.mult)
            v.drain()
            v.tensor_copy(out=self.sinT[:, 0:nk], in_=self.cosT[:, 0:nk])
            v.tensor_scalar(out=self.cosT[:, 0:nk], in0=self.cosT[:, 0:nk],
                            scalar1=float(np.pi / 2), scalar2=None, op0=ALU.add)
            v.drain()
            for buf in (self.sinT, self.cosT):
                v.tensor_scalar(out=self.XR[:, 0:nk], in0=buf[:, 0:nk],
                                scalar1=1.0 / LN2PI, scalar2=MAGIC, op0=ALU.mult,
                                op1=ALU.add)
                v.drain()
                v.tensor_scalar(out=self.XR[:, 0:nk], in0=self.XR[:, 0:nk],
                                scalar1=MAGIC, scalar2=-LN2PI, op0=ALU.subtract,
                                op1=ALU.mult)
                v.drain()
                v.tensor_tensor(out=buf[:, 0:nk], in0=buf[:, 0:nk],
                                in1=self.XR[:, 0:nk], op=ALU.add)
                v.drain()
            p.sV.inc(v.drain())
        self.on("vector", fv)
        vm = p.sV.n

        def fs2(s):
            s.wait_ge(p.sV.h, vm)
            s.activation(self.sinT[:, 0:nk], self.sinT[:, 0:nk], ACT.Sin)
            s.drain()
            p.sS.inc(s.activation(self.cosT[:, 0:nk], self.cosT[:, 0:nk],
                                  ACT.Sin))
        self.on("scalar", fs2)
        self.barrier()



    def _attn_segs(self, kind):
        c = self.c
        HOFF = ((c.WIN + 511) // 512) * 512
        segs = []
        if kind == "q":
            spans = [(0, c.T, 0)]
        else:
            spans = [(0, c.WIN, 0), (HOFF, c.T, c.WIN)]
        for off, wd, coff in spans:
            j = 0
            while j < wd:
                o = off + j
                w = min(512 - (o % 512), wd - j)
                ps = self.psA if o < 1024 else self.psB
                po = o if o < 1024 else o - 1024
                segs.append((ps, po, w, coff + j))
                j += w
        return segs

    def rope_scale(self, kind, dst):
        p, c = self.p, self.c
        segs = self._attn_segs(kind)
        cs0 = c.WIN if kind == "q" else 0

        def fv(v):
            for ps, po, w, co in segs:
                bck = self.bc1 if co < 1024 else self.bc2
                bo = co if co < 1024 else co - 1024
                qq = ps[:, po:po + w]
                cosv = self.cosT[:, cs0 + co:cs0 + co + w]
                sinv = self.sinT[:, cs0 + co:cs0 + co + w]
                XA = self.XR[:, 0:w]
                XB = self.XR[:, 1024:1024 + w]
                v.tensor_copy(out=XA[0:64, :], in_=ps[64:128, po:po + w])
                v.tensor_copy(out=XA[64:128, :], in_=ps[0:64, po:po + w])
                v.drain()
                v.tensor_tensor(out=XB, in0=qq, in1=cosv, op=ALU.mult)
                v.tensor_tensor(out=XA, in0=XA, in1=sinv, op=ALU.mult)
                v.drain()
                v.scalar_tensor_tensor(out=XB, in0=XA, scalar=self.sgn[:],
                                       in1=XB, op0=ALU.mult, op1=ALU.add)
                v.drain()
                v.tensor_tensor(out=dst[:, co:co + w], in0=XB,
                                in1=bck[:, bo:bo + w], op=ALU.mult)
                v.drain()
        self.step("vector", fv)

    def rms_scale_row(self, kind, gain_idx):
        p, c = self.p, self.c
        segs = self._attn_segs(kind)
        ncols = (c.T if kind == "q" else c.WIN + c.T)

        def fs(s):
            for ps, po, w, co in segs:
                dump = self.maxA if co < 1024 else self.maxB
                do = co if co < 1024 else co - 1024
                s.activation(dump[:, do:do + w], ps[:, po:po + w], ACT.Square)
        self.step("scalar", fs)

        def ft(te):
            for j in range(0, ncols, 512):
                w = min(512, ncols - j)
                src = (self.maxA[:, j:j + w] if j < 1024
                       else self.maxB[:, j - 1024:j - 1024 + w])
                pdst = (self.psC[0:1, j:j + w] if j < 1024
                        else self.psY[0:1, j - 1024:j - 1024 + w])
                te.matmul(pdst, self.ones_col[:], src, start=True, stop=True)
        self.step("tensor", ft)

        def fv(v):
            v.tensor_copy(out=self.rows[0:1, 0:min(ncols, 1024)],
                          in_=self.psC[0:1, 0:min(ncols, 1024)])
            if ncols > 1024:
                v.tensor_copy(out=self.rows[0:1, 1024:ncols],
                              in_=self.psY[0:1, 0:ncols - 1024])
            v.drain()
            v.tensor_scalar(out=self.rows[0:1, 0:ncols], in0=self.rows[0:1, 0:ncols],
                            scalar1=1.0 / c.HD, scalar2=c.EPS_QK,
                            op0=ALU.mult, op1=ALU.add)
        self.step("vector", fv)

        def fs2(s):
            s.activation(self.rows[0:1, 0:ncols], self.rows[0:1, 0:ncols], ACT.Sqrt)
        self.step("scalar", fs2)

        def fv2(v):
            v.reciprocal(out=self.rows[0:1, 0:ncols], in_=self.rows[0:1, 0:ncols])
            if gain_idx is not None:
                v.drain()
                v.tensor_scalar(out=self.rows[0:1, 0:ncols],
                                in0=self.rows[0:1, 0:ncols],
                                scalar1=self.qgain_sb[0:1, gain_idx:gain_idx + 1],
                                scalar2=float(1.0 / np.sqrt(c.HD)),
                                op0=ALU.mult, op1=ALU.mult)
        self.step("vector", fv2)

    def phase_attn(self):
        p, c, d = self.p, self.c, self.d
        NT, ND, NKT, HT, H = c.NT, c.ND, c.NKT, c.HT, c.H
        TW = c.T + c.WIN
        nk = NKT * 128
        nkv = c.KVH * c.HD // 128
        XN = self.XN

        def xn_own(k):
            return XN[:, k * TW: k * TW + c.T]

        def xn_halo(k):
            return XN[:, k * TW + c.T: k * TW + TW]

        self.rope_tables()

        slabs = ([("k", s_, d["wk"].rearrange("d (s c) -> s d c", c=128)[s_]) for s_ in range(nkv)]
                 + [("v", s_, d["wv"].rearrange("d (s c) -> s d c", c=128)[s_]) for s_ in range(nkv)])

        for kind, hh, sl in slabs:
            def fl(sync, sl=sl):
                p.sDW.dma(self.load_slab(sync, self.WSTG[0], sl))
            self.step("sync", fl)

            def fc(v):
                p.sV.inc(v.tensor_copy(out=self.WBF[0][:, 0:c.D],
                                       in_=self.WSTG[0][:, 0:c.D]))
            self.step("vector", fc)

            HOFF = ((c.WIN + 511) // 512) * 512

            def ft(te, kind=kind):
                for kk in range(ND):
                    lhs = self.WBF[0][:, kk * 128:(kk + 1) * 128]
                    if kind == "q":
                        srcs = [(xn_own(kk), 0)]
                    else:
                        srcs = [(xn_halo(kk), 0), (xn_own(kk), HOFF)]
                    for src, off in srcs:
                        w_all = src.shape[-1]
                        for j in range(0, w_all, 512):
                            w = min(512, w_all - j)
                            o = off + j
                            ps = (self.psA[:, o:o + w] if o < 1024 else
                                  self.psB[:, o - 1024:o - 1024 + w])
                            te.matmul(ps, lhs, src[:, j:j + w],
                                      start=(kk == 0), stop=(kk == ND - 1))
            self.step("tensor", ft)

            if kind == "v":
                def fv1(v):
                    HOFF = ((c.WIN + 511) // 512) * 512
                    v.tensor_copy(out=self.vTr[:, 0:c.WIN], in_=self.psA[:, 0:c.WIN])
                    for j in range(0, c.T, 512):
                        w = min(512, c.T - j)
                        o = HOFF + j
                        src = (self.psA[:, o:o + w] if o < 1024 else
                               self.psB[:, o - 1024:o - 1024 + w])
                        v.tensor_copy(out=self.vTr[:, c.WIN + j:c.WIN + j + w],
                                      in_=src)
                self.step("vector", fv1)

                def ftv(te):
                    for t in range(NKT):
                        ps = self.psC if t < 8 else self.psY
                        te.matmul(ps[:, (t % 8) * 128:(t % 8) * 128 + 128],
                                  self.vTr[:, t * 128:(t + 1) * 128], self.idb[:],
                                  start=True, stop=True)
                self.step("tensor", ftv)

                def fv2(v, hh=hh):
                    for t in range(NKT):
                        ps = self.psC if t < 8 else self.psY
                        v.tensor_copy(
                            out=self.V[:, t * (nkv * 128) + hh * 128:
                                       t * (nkv * 128) + hh * 128 + 128],
                            in_=ps[:, (t % 8) * 128:(t % 8) * 128 + 128])
                self.step("vector", fv2)
            else:
                ncols = c.WIN + c.T
                self.rms_scale_row(kind, None)
                self.bcast(self.rows[0:1, 0:min(ncols, 1024)],
                           self.bc1, min(ncols, 1024), self.psC)
                if ncols > 1024:
                    self.bcast(self.rows[0:1, 1024:ncols],
                               self.bc2, ncols - 1024, self.psY)
                self.rope_scale(kind, self.KRR[:, hh * nk:(hh + 1) * nk])
        self.barrier()

        for h in range(H):
            kv = h // (H // c.KVH)

            def fl(sync, h=h):
                p.sDW.dma(self.load_slab(sync, self.WSTG[0], d["wq"][h]))
            self.step("sync", fl)

            def fc(v):
                p.sV.inc(v.tensor_copy(out=self.WBF[0][:, 0:c.D],
                                       in_=self.WSTG[0][:, 0:c.D]))
            self.step("vector", fc)

            def ftq(te):
                for kk in range(ND):
                    for j in range(0, c.T, 512):
                        w = min(512, c.T - j)
                        te.matmul(self.psA[:, j:j + w],
                                  self.WBF[0][:, kk * 128:(kk + 1) * 128],
                                  xn_own(kk)[:, j:j + w],
                                  start=(kk == 0), stop=(kk == ND - 1))
            self.step("tensor", ftq)
            self.rms_scale_row("q", h)
            self.bcast(self.rows[0:1, 0:c.T], self.bc1, c.T, self.psC)
            self.rope_scale("q", self.QR[:, 0:c.T])

            def fve(v, h=h):
                p.sV.inc(v.tensor_scalar(
                    out=self.tscr[:, 0:NKT], in0=self.params_b[:, 3:3 + NKT],
                    scalar1=self.Mh_b[:, h:h + 1], scalar2=None, op0=ALU.subtract))
            self.step("vector", fve)

            for j in range(NKT):
                qlo, qhi = max(0, j - HT), min(NT - 1, j)
                wq_ = (qhi - qlo + 1) * 128
                slot = (j % c.W5) * (c.W5 * 128)

                def fts(te, j=j, kv=kv, qlo=qlo, wq_=wq_):
                    for o in range(0, wq_, 512):
                        w = min(512, wq_ - o)
                        te.matmul(self.psB[:, o:o + w],
                                  self.KRR[:, kv * nk + j * 128:kv * nk + (j + 1) * 128],
                                  self.QR[:, qlo * 128 + o:qlo * 128 + o + w],
                                  start=True, stop=True)
                self.step("tensor", fts)

                def fse(s, j=j, wq_=wq_, slot=slot):
                    s.activation(self.pr[:, slot:slot + wq_], self.psB[:, 0:wq_],
                                 ACT.Exp, bias=self.tscr[:, j:j + 1])
                self.step("scalar", fse)

                def fvm(v, j=j, qlo=qlo, qhi=qhi, slot=slot):
                    for qt in range(qlo, qhi + 1):
                        o = slot + (qt - qlo) * 128
                        if qt == j - HT:
                            v.tensor_tensor(out=self.pr[:, o:o + 128],
                                            in0=self.pr[:, o:o + 128],
                                            in1=self.mk_diag[:], op=ALU.mult)
                        elif qt == j:
                            v.tensor_tensor(out=self.pr[:, o:o + 128],
                                            in0=self.pr[:, o:o + 128],
                                            in1=self.mk_far[:], op=ALU.mult)
                self.step("vector", fvm)

                qt0 = j - HT
                if 0 <= qt0 <= NT - 1:
                    def ftp(te, qt0=qt0, kv=kv):
                        for kt in range(qt0, qt0 + HT + 1):
                            sl = (kt % c.W5) * (c.W5 * 128)
                            off = sl + (qt0 - max(0, kt - HT)) * 128
                            te.matmul(self.psY[:, qt0 * 128:(qt0 + 1) * 128],
                                      self.V[:, kt * (nkv * 128) + kv * 128:
                                             kt * (nkv * 128) + kv * 128 + 128],
                                      self.pr[:, off:off + 128],
                                      start=(kt == qt0), stop=(kt == qt0 + HT))
                        for kt in range(qt0, qt0 + HT + 1):
                            sl = (kt % c.W5) * (c.W5 * 128)
                            off = sl + (qt0 - max(0, kt - HT)) * 128
                            te.matmul(self.psC[0:1, qt0 * 128:(qt0 + 1) * 128],
                                      self.ones_col_b[:], self.pr[:, off:off + 128],
                                      start=(kt == qt0), stop=(kt == qt0 + HT))
                    self.step("tensor", ftp)

            def fvd(v):
                v.tensor_copy(out=self.rows[0:1, 0:c.T], in_=self.psC[0:1, 0:c.T])
                v.drain()
                v.reciprocal(out=self.rows[0:1, 0:c.T], in_=self.rows[0:1, 0:c.T])
            self.step("vector", fvd)
            self.bcast(self.rows[0:1, 0:c.T], self.bc1, c.T, self.psC)

            def fvy(v, h=h):
                v.tensor_tensor(out=self.Y[:, h * c.T:(h + 1) * c.T],
                                in0=self.psY[:, 0:c.T], in1=self.bc1[:, 0:c.T],
                                op=ALU.mult)
            self.step("vector", fvy)
        self.barrier()

        for dc in range(ND):
            def fl(sync, dc=dc):
                p.sDW.dma(self.load_slab(sync, self.WSTG[0], d["wo"][dc]))
            self.step("sync", fl)

            def fc(v):
                p.sV.inc(v.tensor_copy(out=self.WBF[0][:, 0:c.D],
                                       in_=self.WSTG[0][:, 0:c.D]))
            self.step("vector", fc)

            def ft(te):
                for kk in range(ND):
                    for j in range(0, c.T, 512):
                        w = min(512, c.T - j)
                        te.matmul(self.psA[:, j:j + w],
                                  self.WBF[0][:, kk * 128:(kk + 1) * 128],
                                  self.Y[:, kk * c.T + j:kk * c.T + j + w],
                                  start=(kk == 0), stop=(kk == ND - 1))
            self.step("tensor", ft)

            def fv(v):
                p.sV.inc(v.tensor_copy(out=self.sR[:, 0:c.T], in_=self.psA[:, 0:c.T]))
            self.step("vector", fv)

            def fst(sync, dc=dc):
                p.sST.dma(sync.dma_start(out=d["sT_sp"][dc * 128:(dc + 1) * 128, :],
                                         in_=self.sR[:, 0:c.T]))
            self.step("sync", fst)
        self.barrier()

    def bcast2(self, row_ap, out_sb, out_off, n, psum):
        p = self.p

        def ft(te):
            self._twait(te)
            for j in range(0, n, 512):
                w = min(512, n - j)
                p.sT.inc(te.matmul(psum[:, j:j + w], self.ones_row[:, 0:128],
                                   row_ap[:, j:j + w], start=True, stop=True))
        self.on("tensor", ft)
        mark = p.sT.n

        def fv(v):
            v.wait_ge(p.sT.h, mark)
            p.sV.inc(v.tensor_copy(out=out_sb[:, out_off:out_off + n],
                                   in_=psum[:, 0:n]))
        self.on("vector", fv)


    def phase_ffn0_stats(self):
        p, c, d = self.p, self.c, self.d
        mats = [(0, d["wg0"], c.NF, c.D), (1, d["wu0"], c.NF, c.D),
                (2, d["wd0"], c.ND, c.F)]

        def fz(v):
            v.memset(self.tscr[:, 0:8], 0.0)
        self.step("vector", fz)

        for m, dram, nsl, R in mats:
            qn = R // 2048 if R > 2048 else 1
            Rq = min(R, 2048)
            idx = 0
            loadm, actm, vmm = {}, {}, {}
            for i in range(nsl):
                for q4 in range(qn):
                    par = idx % 2

                    def fl(s_, i=i, q4=q4, par=par, dram=dram, Rq=Rq, idx=idx):
                        if idx >= 2:
                            s_.wait_ge(p.sS.h, actm[idx - 2])
                        sl = dram[i, q4 * Rq:(q4 + 1) * Rq, :]
                        sem = p.sDS if par == 0 else p.sDSb
                        trio = (p.sDS, p.sDSb)
                        if sem.n > 0:
                            s_.wait_ge(sem.h, sem.n)
                        if idx < 2:
                            for s__ in trio:
                                if s__ is not sem and s__.n > 0:
                                    s_.wait_ge(s__.h, s__.n)
                            for s__ in (p.sV, p.sS):
                                if s__.n > 0:
                                    s_.wait_ge(s__.h, s__.n)
                        sem.dma(self.load_slab(s_, self.WSTG[par], sl))
                    self.on("scalar", fl)
                    loadm[idx] = ((p.sDS if par == 0 else p.sDSb),
                                  (p.sDS if par == 0 else p.sDSb).n)

                    def fa(s_, idx=idx, par=par, m=m, Rq=Rq):
                        s_.wait_ge(loadm[idx][0].h, loadm[idx][1])
                        if idx >= 1:
                            s_.wait_ge(p.sV.h, vmm[idx - 1])
                        for hh in range(0, Rq, 1024):
                            w = min(1024, Rq - hh)
                            dump = self.maxA if par == 0 else self.maxB
                            s_.activation(
                                dump[:, 0:w], self.WSTG[par][:, hh:hh + w],
                                ACT.Abs,
                                accum_out=self.tscr[:, 16 + 2 * par + hh // 1024:
                                                    17 + 2 * par + hh // 1024])
                        p.sS.inc(s_.drain())
                    self.on("scalar", fa)
                    actm[idx] = p.sS.n

                    def fv(v, idx=idx, par=par, m=m, Rq=Rq):
                        v.wait_ge(p.sS.h, actm[idx])
                        for hh in range(0, Rq, 1024):
                            v.tensor_tensor(out=self.tscr[:, m:m + 1],
                                            in0=self.tscr[:, m:m + 1],
                                            in1=self.tscr[:, 16 + 2 * par + hh // 1024:
                                                          17 + 2 * par + hh // 1024],
                                            op=ALU.add)
                            v.drain()
                        p.sV.inc(v.drain())
                    self.on("vector", fv)
                    vmm[idx] = p.sV.n
                    idx += 1
        self.barrier()

        def ft(te):
            te.matmul(self.psC[0:1, 0:3], self.ones_col[:], self.tscr[:, 0:3],
                      start=True, stop=True)
        self.step("tensor", ft)

        def fv2(v):
            counts = [float(c.F * c.D)] * 3
            for m in range(3):
                v.tensor_scalar(out=self.rowsE[0:1, 256 + m:257 + m],
                                in0=self.psC[0:1, m:m + 1],
                                scalar1=1.0 / counts[m], scalar2=1e-5,
                                op0=ALU.mult, op1=ALU.max)
            v.drain()
            v.reciprocal(out=self.rowsE[0:1, 260:263], in_=self.rowsE[0:1, 256:259])
        self.step("vector", fv2)

        def ft2(te):
            te.matmul(self.psC[:, 0:3], self.ones_row[:, 0:128],
                      self.rowsE[0:1, 260:263], start=True, stop=True)
        self.step("tensor", ft2)

        def fv3(v):
            v.tensor_copy(out=self.srep[:, 0:3], in_=self.psC[:, 0:3])
        self.step("vector", fv3)

    def ffn_gateup(self, wg, wu, rhs_buf, H, conv_g, conv_u, epi, tn):
        p, c = self.p, self.c
        loadm, convm, usem = {}, {}, {}
        epim = [0]
        for fc in range(c.NF):
            for kind in (0, 1):
                i = 2 * fc + kind
                par = i % 2
                sl = (wg if kind == 0 else wu)[fc]
                conv = conv_g if kind == 0 else conv_u

                def fl(sync, i=i, par=par, sl=sl):
                    if i >= 2:
                        sync.wait_ge(p.sV.h, convm[i - 2])
                    sem = p.sDW if par == 0 else p.sDWb
                    trio = (p.sDW, p.sDWb, p.sST)
                    if sem.n > 0:
                        sync.wait_ge(sem.h, sem.n)
                    if i < 2:
                        for s__ in trio:
                            if s__ is not sem and s__.n > 0:
                                sync.wait_ge(s__.h, s__.n)
                        for s__ in (p.sV, p.sS):
                            if s__.n > 0:
                                sync.wait_ge(s__.h, s__.n)
                    sem.dma(self.load_slab(sync, self.WSTG[par], sl))
                self.on("sync", fl)
                loadm[i] = ((p.sDW if par == 0 else p.sDWb), (p.sDW if par == 0 else p.sDWb).n)

                def fc_(v, i=i, par=par, conv=conv):
                    v.wait_ge(loadm[i][0].h, loadm[i][1])
                    if i >= 2:
                        v.wait_ge(p.sT.h, usem[i - 2])
                    conv(v, i, self.WSTG[par], self.WBF[par], c.D)
                self.on("vector", fc_)
                convm[i] = p.sV.n

                def fu(te, i=i, par=par, kind=kind):
                    te.wait_ge(p.sV.h, convm[i])
                    te.wait_ge(p.sV.h, epim[0])
                    ps = self.psA if kind == 0 else self.psB
                    for kk in range(c.ND):
                        for j in range(0, tn, 512):
                            w = min(512, tn - j)
                            te.matmul(ps[:, j:j + w],
                                      self.WBF[par][:, kk * 128:(kk + 1) * 128],
                                      rhs_buf[:, kk * tn + j:kk * tn + j + w],
                                      start=(kk == 0), stop=(kk == c.ND - 1))
                    p.sT.inc(te.drain())
                self.on("tensor", fu)
                usem[i] = p.sT.n
            m_ = epi(fc)
            epim[0] = m_ if m_ is not None else p.sV.n
        self.barrier()

    def phase_ffn0(self, H, t0, tn):
        p, c, d = self.p, self.c, self.d
        NT, ND, NF = c.NT, c.ND, c.NF
        S = c.NT
        tt0, nt2 = t0 // 128, tn // 128
        TB = 8 * c.NT

        def stat_ap(slot, off=0, n=None):
            n = nt2 if n is None else n
            return self.stat[:, slot * S + tt0 + off:slot * S + tt0 + off + n]

        self.qx_pass(d["x_own"], 0, 2, t0, tn, self.QX)

        self.col2row(self.stat[:, 3 * S:3 * S + NT], 0, NT, self.psC)

        def fvr(v):
            v.tensor_scalar(out=self.rows[0:1, 1024 + t0:1024 + t0 + tn],
                            in0=self.rows[0:1, t0:t0 + tn],
                            scalar1=self.rowsE[0:1, 257:258], scalar2=None,
                            op0=ALU.mult)
            v.drain()
            v.tensor_scalar(out=self.rows[0:1, t0:t0 + tn],
                            in0=self.rows[0:1, t0:t0 + tn],
                            scalar1=self.rowsE[0:1, 256:257], scalar2=None,
                            op0=ALU.mult)
        self.step("vector", fvr)
        self.bcast(self.rows[0:1, t0:t0 + tn], self.bc1, tn, self.psC)
        self.bcast(self.rows[0:1, 1024 + t0:1024 + t0 + tn], self.bc2, tn, self.psC)

        conv_g = self.make_conv_tern_idx(0)
        conv_u = self.make_conv_tern_idx(1)

        def epi(fc):
            def s1(v, fc=fc):
                v.tensor_tensor(out=self.scr2[:, 0:tn], in0=self.psA[:, 0:tn],
                                in1=self.bc1[:, 0:tn], op=ALU.mult)
                v.tensor_tensor(out=self.XR[:, 0:tn], in0=self.psB[:, 0:tn],
                                in1=self.bc2[:, 0:tn], op=ALU.mult)
            self.step("vector", s1)
            ps_free = p.sV.n

            def s2(s, fc=fc):
                s.activation(self.tscr[:, 0:tn], self.scr2[:, 0:tn], ACT.Silu)
            self.step("scalar", s2)

            def s3(v, fc=fc):
                v.tensor_tensor(out=H[:, fc * tn:(fc + 1) * tn],
                                in0=self.tscr[:, 0:tn], in1=self.XR[:, 0:tn],
                                op=ALU.mult)
                v.drain()
                v.tensor_scalar(out=self.scr2[:, 0:tn],
                                in0=H[:, fc * tn:(fc + 1) * tn],
                                scalar1=self.lnw_sb[:, fc:fc + 1], scalar2=None,
                                op0=ALU.mult)
                v.drain()
                if fc == 0:
                    v.tensor_scalar(out=self.maxA[:, 0:tn], in0=self.scr2[:, 0:tn],
                                    scalar1=1.0, scalar2=None, op0=ALU.mult)
                    v.tensor_scalar(out=self.maxB[:, 0:tn], in0=self.scr2[:, 0:tn],
                                    scalar1=-1.0, scalar2=None, op0=ALU.mult)
                else:
                    v.scalar_tensor_tensor(out=self.maxA[:, 0:tn],
                                           in0=self.scr2[:, 0:tn], scalar=1.0,
                                           in1=self.maxA[:, 0:tn], op0=ALU.mult,
                                           op1=ALU.max)
                    v.scalar_tensor_tensor(out=self.maxB[:, 0:tn],
                                           in0=self.scr2[:, 0:tn], scalar=-1.0,
                                           in1=self.maxB[:, 0:tn], op0=ALU.mult,
                                           op1=ALU.max)
            self.step("vector", s3)

            def s4(s, fc=fc):
                s.activation(self.tscr[:, 0:tn], H[:, fc * tn:(fc + 1) * tn],
                             ACT.Square)
            self.step("scalar", s4)

            def s5(te, fc=fc):
                for j in range(0, tn, 512):
                    w = min(512, tn - j)
                    te.matmul(self.psC[0:1, j:j + w], self.ones_col[:],
                              self.tscr[:, j:j + w], start=(fc == 0),
                              stop=(fc == NF - 1))
            self.step("tensor", s5)
            return ps_free

        self.ffn_gateup(d["wg0"], d["wu0"], self.QX, H, conv_g, conv_u, epi, tn)

        fin_max = self.maxA

        def fq1(v):
            v.tensor_tensor(out=self.maxA[:, 0:tn], in0=self.maxA[:, 0:tn],
                            in1=self.maxB[:, 0:tn], op=ALU.max)
            v.drain()
            v.tensor_copy(out=self.rows[0:1, 0:tn], in_=self.psC[0:1, 0:tn])
        self.step("vector", fq1)

        def fq2(te):
            for t in range(nt2):
                te.transpose(self.psB[:, t:t + 1],
                             self.rows[0:1, t * 128:(t + 1) * 128], self.idf[0:1, 0:1])
            for t in range(nt2):
                te.transpose(self.psY[:, t * 128:(t + 1) * 128],
                             fin_max[:, t * 128:(t + 1) * 128], self.idf[:])
        self.step("tensor", fq2)

        def fq3(v):
            st = self.stat
            for t in range(nt2):
                v.reduce_max(st[:, TB + t:TB + t + 1], self.psY[:, t * 128:(t + 1) * 128],
                             axis=AX)
            v.drain()
            v.tensor_scalar(out=st[:, TB + 8:TB + 8 + nt2], in0=self.psB[:, 0:nt2],
                            scalar1=1.0 / c.F, scalar2=c.EPS_FFN0_RMS,
                            op0=ALU.mult, op1=ALU.add)
        self.step("vector", fq3)

        def fq4(s):
            s.activation(self.stat[:, TB + 8:TB + 8 + nt2], self.stat[:, TB + 8:TB + 8 + nt2],
                         ACT.Sqrt)
        self.step("scalar", fq4)

        def fq5(v):
            st = self.stat
            v.reciprocal(out=st[:, TB + 8:TB + 8 + nt2], in_=st[:, TB + 8:TB + 8 + nt2])
            v.drain()
            v.tensor_tensor(out=st[:, TB + 16:TB + 16 + nt2], in0=st[:, TB + 8:TB + 8 + nt2],
                            in1=st[:, TB:TB + nt2], op=ALU.mult)
            v.drain()
            v.tensor_scalar(out=st[:, TB + 16:TB + 16 + nt2], in0=st[:, TB + 16:TB + 16 + nt2],
                            scalar1=c.EPS_FFN0_AQ, scalar2=None, op0=ALU.max)
            v.drain()
            v.reciprocal(out=st[:, TB + 24:TB + 24 + nt2], in_=st[:, TB + 16:TB + 16 + nt2])
            v.drain()
            v.tensor_scalar(out=st[:, TB + 24:TB + 24 + nt2], in0=st[:, TB + 24:TB + 24 + nt2],
                            scalar1=127.0, scalar2=None, op0=ALU.mult)
            v.drain()
            v.tensor_tensor(out=st[:, TB + 24:TB + 24 + nt2], in0=st[:, TB + 24:TB + 24 + nt2],
                            in1=st[:, TB + 8:TB + 8 + nt2], op=ALU.mult)
            v.drain()
            v.tensor_scalar(out=st[:, TB + 16:TB + 16 + nt2], in0=st[:, TB + 16:TB + 16 + nt2],
                            scalar1=1.0 / 127.0, scalar2=None, op0=ALU.mult)
        self.step("vector", fq5)

        self.col2row(self.stat[:, TB + 24:TB + 24 + nt2], 0, nt2, self.psC)
        self.col2row(self.stat[:, TB + 16:TB + 16 + nt2], 1024, nt2, self.psC)

        def fq6(v):
            v.tensor_scalar(out=self.rows[0:1, 1024:1024 + tn],
                            in0=self.rows[0:1, 1024:1024 + tn],
                            scalar1=self.rowsE[0:1, 258:259], scalar2=None,
                            op0=ALU.mult)
        self.step("vector", fq6)
        self.bcast(self.rows[0:1, 0:tn], self.bc1, tn, self.psC)
        self.bcast(self.rows[0:1, 1024:1024 + tn], self.bc2, tn, self.psC)

        for fc in range(NF):
            def fqz(v, fc=fc):
                hsl = H[:, fc * tn:(fc + 1) * tn]
                v.tensor_scalar(out=self.scr2[:, 0:tn], in0=hsl,
                                scalar1=self.lnw_sb[:, fc:fc + 1], scalar2=None,
                                op0=ALU.mult)
                v.drain()
                v.tensor_tensor(out=self.scr2[:, 0:tn], in0=self.scr2[:, 0:tn],
                                in1=self.bc1[:, 0:tn], op=ALU.mult)
                v.drain()
                v.tensor_scalar(out=hsl, in0=self.scr2[:, 0:tn], scalar1=MAGIC,
                                scalar2=MAGIC, op0=ALU.add, op1=ALU.subtract)
            self.step("vector", fqz)
        self.barrier()

    def phase_down_combine(self, H, wd, conv, unscale_bc, layer, t0, tn):
        p, c, d = self.p, self.c, self.d
        ND, NF, NT = c.ND, c.NF, c.NT
        xsrc = d["x_own"] if layer == 0 else d["xT_sp"]
        loadm, convm, usem = {}, {}, {}
        epim = [0, 0]
        idx = 0
        NQ4 = max(1, c.F // 2048)
        RQ4 = min(2048, c.F)
        for dc in range(ND):
            for q4 in range(NQ4):
                par = idx % 2

                def fl(sync, dc=dc, q4=q4, par=par, idx=idx):
                    if idx >= 2:
                        sync.wait_ge(p.sV.h, convm[idx - 2])
                    sl = wd[dc, q4 * RQ4:(q4 + 1) * RQ4, :]
                    sem = p.sDW if par == 0 else p.sDWb
                    trio = (p.sDW, p.sDWb, p.sST)
                    if sem.n > 0:
                        sync.wait_ge(sem.h, sem.n)
                    if idx < 2:
                        for s__ in trio:
                            if s__ is not sem and s__.n > 0:
                                sync.wait_ge(s__.h, s__.n)
                        for s__ in (p.sV, p.sS):
                            if s__.n > 0:
                                sync.wait_ge(s__.h, s__.n)
                    sem.dma(self.load_slab(sync, self.WSTG[par], sl))
                self.on("sync", fl)
                loadm[idx] = ((p.sDW if par == 0 else p.sDWb), (p.sDW if par == 0 else p.sDWb).n)

                def fc_(v, idx=idx, par=par):
                    v.wait_ge(loadm[idx][0].h, loadm[idx][1])
                    if idx >= 2:
                        v.wait_ge(p.sT.h, usem[idx - 2])
                    conv(v, idx, self.WSTG[par], self.WBF[par], RQ4)
                self.on("vector", fc_)
                convm[idx] = p.sV.n

                def fu(te, dc=dc, q4=q4, par=par, idx=idx):
                    te.wait_ge(p.sV.h, convm[idx])
                    if q4 == 0:
                        te.wait_ge(p.sV.h, epim[dc % 2])
                    ps = self.psA if dc % 2 == 0 else self.psB
                    for kk in range(RQ4 // 128):
                        fq = q4 * (RQ4 // 128) + kk
                        for j in range(0, tn, 512):
                            w = min(512, tn - j)
                            te.matmul(ps[:, j:j + w],
                                      self.WBF[par][:, kk * 128:(kk + 1) * 128],
                                      H[:, fq * tn + j:fq * tn + j + w],
                                      start=(fq == 0), stop=(fq == NF - 1))
                    p.sT.inc(te.drain())
                self.on("tensor", fu)
                usem[idx] = p.sT.n
                idx += 1

            ps = self.psA if dc % 2 == 0 else self.psB

            def e0(sync, dc=dc):
                p.sDW.dma(sync.dma_start(
                    out=self.sR[:, 0:tn],
                    in_=d["sT_sp"][dc * 128:(dc + 1) * 128, t0:t0 + tn]))
            self.step("sync", e0)

            def e0b(g, dc=dc):
                p.sDG.dma(g.dma_start(out=self.XR[:, 0:tn],
                                      in_=xsrc[dc * 128:(dc + 1) * 128, t0:t0 + tn]))
            self.step("gpsimd", e0b)

            def e1(v, dc=dc, ps=ps):
                v.tensor_tensor(out=self.scr2[:, 0:tn], in0=ps[:, 0:tn],
                                in1=unscale_bc[:, 0:tn], op=ALU.mult)
            self.step("vector", e1)
            epim[dc % 2] = p.sV.n

            def e2(s, dc=dc):
                s.activation(self.tscr[:, 0:tn], self.scr2[:, 0:tn], ACT.Silu)
            self.step("scalar", e2)

            def e3(v, dc=dc):
                v.tensor_tensor(out=self.scr2[:, 0:tn], in0=self.sR[:, 0:tn],
                                in1=self.tscr[:, 0:tn], op=ALU.mult)
                v.drain()
                v.tensor_tensor(out=self.XR[:, 1024:1024 + tn],
                                in0=self.XR[:, 0:tn], in1=self.scr2[:, 0:tn],
                                op=ALU.add)
            self.step("vector", e3)

            if layer == 0:
                def e4(s, dc=dc):
                    s.activation(self.tscr[:, 0:tn], self.XR[:, 1024:1024 + tn],
                                 ACT.Square)
                self.step("scalar", e4)

                def e5(te, dc=dc):
                    for j in range(0, tn, 512):
                        w = min(512, tn - j)
                        te.matmul(self.psC[0:1, t0 + j:t0 + j + w], self.ones_col[:],
                                  self.XR[:, 1024 + j:1024 + j + w],
                                  start=(dc == 0), stop=(dc == ND - 1))
                        te.matmul(self.psY[0:1, t0 + j:t0 + j + w], self.ones_col[:],
                                  self.tscr[:, j:j + w],
                                  start=(dc == 0), stop=(dc == ND - 1))
                self.step("tensor", e5)

            def e6(g, dc=dc):
                dst = d["xT_sp"] if layer == 0 else d["y_out"]
                p.sSTG.dma(g.dma_start(out=dst[dc * 128:(dc + 1) * 128, t0:t0 + tn],
                                       in_=self.XR[:, 1024:1024 + tn]))
            self.step("gpsimd", e6)
        self.barrier()


    def l1_stats(self, t0, tn):
        p, c = self.p, self.c
        S = c.NT
        tt0, nt2 = t0 // 128, tn // 128

        def f0(v):
            v.tensor_copy(out=self.rows[0:1, 0:tn], in_=self.psC[0:1, t0:t0 + tn])
            v.tensor_copy(out=self.rows[0:1, 1024:1024 + tn],
                          in_=self.psY[0:1, t0:t0 + tn])
        self.step("vector", f0)

        def f1(te):
            for t in range(nt2):
                te.transpose(self.psC[:, t:t + 1],
                             self.rows[0:1, t * 128:(t + 1) * 128], self.idf[0:1, 0:1])
                te.transpose(self.psB[:, t:t + 1],
                             self.rows[0:1, 1024 + t * 128:1024 + (t + 1) * 128],
                             self.idf[0:1, 0:1])
        self.step("tensor", f1)

        def f2(v):
            st = self.stat
            v.tensor_scalar(out=st[:, 4 * S + tt0:4 * S + tt0 + nt2],
                            in0=self.psC[:, 0:nt2],
                            scalar1=1.0 / c.D, scalar2=None, op0=ALU.mult)
            v.tensor_scalar(out=st[:, 5 * S + tt0:5 * S + tt0 + nt2],
                            in0=self.psB[:, 0:nt2],
                            scalar1=1.0 / c.D, scalar2=None, op0=ALU.mult)
            v.drain()
            v.tensor_tensor(out=self.tscr[:, 0:nt2],
                            in0=st[:, 4 * S + tt0:4 * S + tt0 + nt2],
                            in1=st[:, 4 * S + tt0:4 * S + tt0 + nt2], op=ALU.mult)
            v.drain()
            v.tensor_tensor(out=self.tscr[:, 0:nt2],
                            in0=st[:, 5 * S + tt0:5 * S + tt0 + nt2],
                            in1=self.tscr[:, 0:nt2], op=ALU.subtract)
            v.drain()
            v.tensor_scalar(out=self.tscr[:, 0:nt2], in0=self.tscr[:, 0:nt2],
                            scalar1=c.EPS_LN, scalar2=None, op0=ALU.add)
        self.step("vector", f2)

        def f3(s):
            s.activation(self.tscr[:, 0:nt2], self.tscr[:, 0:nt2], ACT.Sqrt)
        self.step("scalar", f3)

        def f4(v):
            v.reciprocal(out=self.stat[:, 5 * S + t0 // 128:5 * S + t0 // 128 + nt2],
                         in_=self.tscr[:, 0:nt2])
        self.step("vector", f4)

    def phase_l1_entry(self, XN1):
        p, c, d = self.p, self.c, self.d
        NT, ND = c.NT, c.ND
        S = c.NT

        def stat_ap(slot):
            return self.stat[:, slot * S:slot * S + NT]

        self.col2row(stat_ap(4), 0, NT, self.psC)
        self.col2row(stat_ap(5), 1024, NT, self.psC)
        self.bcast(self.rows[0:1, 0:c.T], self.bc1, c.T, self.psC)
        self.bcast(self.rows[0:1, 1024:1024 + c.T], self.bc2, c.T, self.psC)

        dm, vm, tm = {}, {}, {}
        for k in range(ND):
            par = k % 2

            def fl(g, k=k, par=par):
                if k >= 2:
                    g.wait_ge(p.sT.h, tm[k - 2])
                sem = p.sDG if par == 0 else p.sDGb
                trio = (p.sDG, p.sDGb, p.sSTG)
                if sem.n > 0:
                    g.wait_ge(sem.h, sem.n)
                if k < 2:
                    for s__ in trio:
                        if s__ is not sem and s__.n > 0:
                            g.wait_ge(s__.h, s__.n)
                    for s__ in (p.sV, p.sS):
                        if s__.n > 0:
                            g.wait_ge(s__.h, s__.n)
                sem.dma(g.dma_start(out=self.XR[:, par * 1024:par * 1024 + c.T],
                                    in_=d["xT_sp"][k * 128:(k + 1) * 128, :]))
            self.on("gpsimd", fl)
            dm[k] = ((p.sDG if par == 0 else p.sDGb), (p.sDG if par == 0 else p.sDGb).n)

            def fv(v, k=k, par=par):
                v.wait_ge(dm[k][0].h, dm[k][1])
                if k >= 1:
                    v.wait_ge(p.sT.h, tm[k - 1])
                v.drain()
                xr = self.XR[:, par * 1024:par * 1024 + c.T]
                v.tensor_tensor(out=self.scr2[:, 0:c.T], in0=xr,
                                in1=self.bc1[:, 0:c.T], op=ALU.subtract)
                v.drain()
                v.tensor_tensor(out=self.scr2[:, 0:c.T], in0=self.scr2[:, 0:c.T],
                                in1=self.bc2[:, 0:c.T], op=ALU.mult)
                v.drain()
                v.tensor_copy(out=XN1[:, k * c.T:(k + 1) * c.T],
                              in_=self.scr2[:, 0:c.T])
                p.sV.inc(v.tensor_tensor(out=self.tscr[:, 0:c.T],
                                         in0=self.scr2[:, 0:c.T],
                                         in1=self.scr2[:, 0:c.T], op=ALU.mult))
            self.on("vector", fv)
            vm[k] = p.sV.n

            def ft(te, k=k, par=par):
                te.wait_ge(p.sV.h, vm[k])
                for j in range(0, c.T, 512):
                    w = min(512, c.T - j)
                    te.matmul(self.psC[0:1, j:j + w], self.ones_col[:],
                              self.scr2[:, j:j + w], start=(k == 0),
                              stop=(k == ND - 1))
                    p.sT.inc(te.matmul(self.psY[0:1, j:j + w], self.ones_col[:],
                                       self.tscr[:, j:j + w], start=(k == 0),
                                       stop=(k == ND - 1)))
            self.on("tensor", ft)
            tm[k] = p.sT.n
        self.barrier()

        def g0(v):
            v.tensor_copy(out=self.rows[0:1, 0:c.T], in_=self.psC[0:1, 0:c.T])
            v.tensor_copy(out=self.rows[0:1, 1024:1024 + c.T], in_=self.psY[0:1, 0:c.T])
        self.step("vector", g0)

        def g1(te):
            for t in range(NT):
                te.transpose(self.psC[:, t:t + 1],
                             self.rows[0:1, t * 128:(t + 1) * 128], self.idf[0:1, 0:1])
                te.transpose(self.psB[:, t:t + 1],
                             self.rows[0:1, 1024 + t * 128:1024 + (t + 1) * 128],
                             self.idf[0:1, 0:1])
        self.step("tensor", g1)

        def g2(v):
            st = self.stat
            v.tensor_scalar(out=st[:, 6 * S:6 * S + NT], in0=self.psC[:, 0:NT],
                            scalar1=1.0 / c.D, scalar2=None, op0=ALU.mult)
            v.tensor_scalar(out=self.tscr[:, 0:NT], in0=self.psB[:, 0:NT],
                            scalar1=1.0 / c.D, scalar2=None, op0=ALU.mult)
            v.drain()
            v.tensor_tensor(out=self.tscr[:, 64:64 + NT], in0=st[:, 6 * S:6 * S + NT],
                            in1=st[:, 6 * S:6 * S + NT], op=ALU.mult)
            v.drain()
            v.tensor_tensor(out=self.tscr[:, 0:NT], in0=self.tscr[:, 0:NT],
                            in1=self.tscr[:, 64:64 + NT], op=ALU.subtract)
            v.drain()
            v.tensor_scalar(out=self.tscr[:, 0:NT], in0=self.tscr[:, 0:NT],
                            scalar1=c.EPS_FFN1_RMS, scalar2=None, op0=ALU.add)
        self.step("vector", g2)

        def g3(s):
            s.activation(self.tscr[:, 0:NT], self.tscr[:, 0:NT], ACT.Sqrt)
        self.step("scalar", g3)

        def g4(v):
            v.reciprocal(out=self.stat[:, 7 * S:7 * S + NT], in_=self.tscr[:, 0:NT])
        self.step("vector", g4)

        self.barrier()

    def phase_l1_xq(self):
        p, c, d = self.p, self.c, self.d
        NT, ND = c.NT, c.ND
        S = c.NT
        st = self.stat

        def sl(k):
            return st[:, k * S:k * S + NT]

        def p0(v):
            v.tensor_tensor(out=self.tscr[:, 0:NT], in0=sl(4), in1=sl(5),
                            op=ALU.mult)
            v.drain()
            v.tensor_tensor(out=self.tscr[:, 0:NT], in0=self.tscr[:, 0:NT],
                            in1=sl(6), op=ALU.add)
        self.step("vector", p0)
        self.col2row(sl(5), 0, NT, self.psC)
        self.col2row(self.tscr[:, 0:NT], 1024, NT, self.psC)
        self.bcast(self.rows[0:1, 0:c.T], self.bc1, c.T, self.psC)
        self.bcast(self.rows[0:1, 1024:1024 + c.T], self.bc2, c.T, self.psC)

        dm, vm = {}, {}
        for k in range(ND):
            par = k % 2

            def fl(g, k=k, par=par):
                if k >= 2:
                    g.wait_ge(p.sV.h, vm[k - 2])
                sem = p.sDG if par == 0 else p.sDGb
                trio = (p.sDG, p.sDGb, p.sSTG)
                if sem.n > 0:
                    g.wait_ge(sem.h, sem.n)
                if k < 2:
                    for s__ in trio:
                        if s__ is not sem and s__.n > 0:
                            g.wait_ge(s__.h, s__.n)
                    for s__ in (p.sV, p.sS):
                        if s__.n > 0:
                            g.wait_ge(s__.h, s__.n)
                sem.dma(g.dma_start(out=self.XR[:, par * 1024:par * 1024 + c.T],
                                      in_=d["xT_sp"][k * 128:(k + 1) * 128, :]))
            self.on("gpsimd", fl)
            dm[k] = ((p.sDG if par == 0 else p.sDGb), (p.sDG if par == 0 else p.sDGb).n)

            def fv(v, k=k, par=par):
                v.wait_ge(dm[k][0].h, dm[k][1])
                v.drain()
                xr = self.XR[:, par * 1024:par * 1024 + c.T]
                v.tensor_tensor(out=self.scr2[:, 0:c.T], in0=xr,
                                in1=self.bc1[:, 0:c.T], op=ALU.mult)
                v.drain()
                v.tensor_tensor(out=self.scr2[:, 0:c.T], in0=self.scr2[:, 0:c.T],
                                in1=self.bc2[:, 0:c.T], op=ALU.subtract)
                v.drain()
                if k == 0:
                    v.tensor_scalar(out=self.maxA[:, 0:c.T], in0=self.scr2[:, 0:c.T],
                                    scalar1=1.0, scalar2=None, op0=ALU.mult)
                    v.tensor_scalar(out=self.maxB[:, 0:c.T], in0=self.scr2[:, 0:c.T],
                                    scalar1=-1.0, scalar2=None, op0=ALU.mult)
                else:
                    v.scalar_tensor_tensor(out=self.maxA[:, 0:c.T],
                                           in0=self.scr2[:, 0:c.T], scalar=1.0,
                                           in1=self.maxA[:, 0:c.T], op0=ALU.mult,
                                           op1=ALU.max)
                    v.scalar_tensor_tensor(out=self.maxB[:, 0:c.T],
                                           in0=self.scr2[:, 0:c.T], scalar=-1.0,
                                           in1=self.maxB[:, 0:c.T], op0=ALU.mult,
                                           op1=ALU.max)
                p.sV.inc(v.drain())
            self.on("vector", fv)
            vm[k] = p.sV.n
        self.barrier()

        def h0(v):
            v.tensor_tensor(out=self.maxA[:, 0:c.T], in0=self.maxA[:, 0:c.T],
                            in1=self.maxB[:, 0:c.T], op=ALU.max)
        self.step("vector", h0)

        def h1(te):
            for t in range(NT):
                te.transpose(self.psY[:, t * 128:(t + 1) * 128],
                             self.maxA[:, t * 128:(t + 1) * 128], self.idf[:])
        self.step("tensor", h1)

        def h2(v):
            for t in range(NT):
                v.reduce_max(self.tscr[:, t:t + 1], self.psY[:, t * 128:(t + 1) * 128],
                             axis=AX)
            v.drain()
            v.reciprocal(out=self.tscr[:, 64:64 + NT], in_=self.tscr[:, 0:NT])
            v.drain()
            v.tensor_scalar(out=sl(2), in0=self.tscr[:, 64:64 + NT],
                            scalar1=127.0, scalar2=None, op0=ALU.mult)
            v.tensor_tensor(out=self.tscr[:, 0:NT], in0=self.tscr[:, 0:NT],
                            in1=sl(7), op=ALU.mult)
            v.drain()
            v.tensor_scalar(out=sl(3), in0=self.tscr[:, 0:NT],
                            scalar1=1.0 / 127.0, scalar2=None, op0=ALU.mult)
            v.drain()
            v.tensor_tensor(out=sl(1), in0=sl(5), in1=sl(2), op=ALU.mult)
            v.drain()
            v.tensor_tensor(out=self.tscr[:, 0:NT], in0=sl(4), in1=sl(1),
                            op=ALU.mult)
            v.tensor_tensor(out=self.tscr[:, 64:64 + NT], in0=sl(6), in1=sl(2),
                            op=ALU.mult)
            v.drain()
            v.tensor_tensor(out=sl(0), in0=self.tscr[:, 0:NT],
                            in1=self.tscr[:, 64:64 + NT], op=ALU.add)
        self.step("vector", h2)
        self.barrier()

    def phase_l1_proj(self, XN1, RAWP, DT2, ZGR):
        p, c, d = self.p, self.c, self.d
        NT, ND, NCH, H1, KVH1 = c.NT, c.ND, c.NCH, c.H1, c.KVH1
        YCH = self.XR

        def gn0(sync):
            p.sDW.dma(sync.dma_start(out=self.rows[0:1, 0:c.ZGD], in_=d["gnw"][:]))
        self.step("sync", gn0)
        self.bcast(self.rows[0:1, 0:min(c.ZGD, 1024)], self.gnw_b,
                   min(c.ZGD, 1024), self.psC)
        if c.ZGD > 1024:
            self.bcast2(self.rows[0:1, 1024:c.ZGD], self.gnw_b, 1024,
                        c.ZGD - 1024, self.psC)
        self.barrier()
        RS = c.T + 32
        NSL = c.NC1 + 1
        VCH = c.KVH1 * c.P1 // 128
        QOF = 2 * VCH

        def raw(j):
            return RAWP[:, j * RS:j * RS + c.T + 1]

        def post(j):
            s_ = (j - 1) if j > 0 else (NSL - 1)
            return RAWP[:, s_ * RS: s_ * RS + c.T]

        loadm, convm, usem = {}, {}, {}
        epim = [0]
        zgm = [0]
        for pc in range(c.NPC):
            par = pc % 2

            def fl(sync, pc=pc, par=par):
                if pc >= 2:
                    sync.wait_ge(p.sV.h, convm[pc - 2])
                sem = p.sDW if par == 0 else p.sDWb
                trio = (p.sDW, p.sDWb, p.sST)
                if sem.n > 0:
                    sync.wait_ge(sem.h, sem.n)
                if pc < 2:
                    for s__ in trio:
                        if s__ is not sem and s__.n > 0:
                            sync.wait_ge(s__.h, s__.n)
                    for s__ in (p.sV, p.sS):
                        if s__.n > 0:
                            sync.wait_ge(s__.h, s__.n)
                sem.dma(self.load_slab(sync, self.WSTG[par], d["wvkq"][pc]))
            self.on("sync", fl)
            loadm[pc] = ((p.sDW if par == 0 else p.sDWb), (p.sDW if par == 0 else p.sDWb).n)

            def fc_(v, pc=pc, par=par):
                v.wait_ge(loadm[pc][0].h, loadm[pc][1])
                if pc >= 2:
                    v.wait_ge(p.sT.h, usem[pc - 2])
                self.conv_cast(v, pc, self.WSTG[par], self.WBF[par], c.D)
            self.on("vector", fc_)
            convm[pc] = p.sV.n

            def fu(te, pc=pc, par=par):
                te.wait_ge(p.sV.h, convm[pc])
                te.wait_ge(p.sV.h, epim[0])
                ps = self.psA if pc % 2 == 0 else self.psB
                for kk in range(ND):
                    for j in range(0, c.T, 512):
                        w = min(512, c.T - j)
                        te.matmul(ps[:, j:j + w],
                                  self.WBF[par][:, kk * 128:(kk + 1) * 128],
                                  XN1[:, kk * c.T + j:kk * c.T + j + w],
                                  start=(kk == 0), stop=(kk == ND - 1))
                p.sT.inc(te.drain())
            self.on("tensor", fu)
            usem[pc] = p.sT.n

            def fe(v, pc=pc):
                v.wait_ge(p.sT.h, usem[pc])
                ps = self.psA if pc % 2 == 0 else self.psB
                if pc < c.NC1:
                    v.tensor_copy(out=raw(pc)[:, 1:c.T + 1], in_=ps[:, 0:c.T])
                elif pc < c.NC1 + c.NZ1:
                    if zgm[0]:
                        v.wait_ge(p.sSTG.h, zgm[0])
                    v.tensor_copy(out=ZGR[:, 0:c.T], in_=ps[:, 0:c.T])
                else:
                    v.tensor_copy(out=self.DTR[0:c.H1, 0:c.T], in_=ps[0:c.H1, 0:c.T])
                p.sV.inc(v.drain())
            self.on("vector", fe)
            epim[0] = p.sV.n

            if c.NC1 <= pc < c.NC1 + c.NZ1:
                def fz(g, pc=pc, mark=epim[0], prev=zgm[0]):
                    g.wait_ge(p.sV.h, mark)
                    if prev:
                        g.wait_ge(p.sSTG.h, prev)
                    p.sSTG.dma(g.dma_start(
                        out=d["zg_sp"][(pc - c.NC1) * 128:(pc - c.NC1 + 1) * 128, :],
                        in_=ZGR[:, 0:c.T]))
                self.on("gpsimd", fz)
                zgm[0] = p.sSTG.n
        self.barrier()

    def phase_l1_ssd(self, RAWP, S0ST, YTT, DT2, AT, AcsTok,
                     csums, DSTT, EAT, XVB, BPB, SCOR, S0cur, Sinit,
                     ZGR):
        p, c, d = self.p, self.c, self.d
        NT, ND, NCH, H1, KVH1 = c.NT, c.ND, c.NCH, c.H1, c.KVH1
        YCH = self.XR
        LEX = self.WBF[0][:, 0:1024]
        PDT = self.WBF[1][:, 0:1024]
        RS = c.T + 32
        NSL = c.NC1 + 1
        VCH = c.KVH1 * c.P1 // 128
        QOF = 2 * VCH

        def raw(j):
            return RAWP[:, j * RS:j * RS + c.T + 1]

        def post(j):
            s_ = (j - 1) if j > 0 else (NSL - 1)
            return RAWP[:, s_ * RS: s_ * RS + c.T]

        def gz(v):
            v.memset(S0cur[:, 0:c.H1 * 32], 0.0)
        self.step("vector", gz)

        rawlast = RAWP[:, 0:c.NC1 * RS].rearrange("p (j r) -> p j r", r=RS)[:, :, c.T]
        rawfirst = RAWP[:, 0:c.NC1 * RS].rearrange("p (j r) -> p j r", r=RS)[:, :, 0]

        def c0(v):
            v.tensor_scalar(out=self.scr2[:, 0:c.NC1], in0=rawlast,
                            scalar1=self.params_b[:, 2:3], scalar2=None,
                            op0=ALU.mult)
        self.step("vector", c0)

        def c1(g):
            p.sSTG.dma(g.dma_start(out=d["cc1_in"][:], in_=self.scr2[:, 0:c.NC1]))
        self.step("gpsimd", c1)

        def c2_(g):
            p.sCC.inc(g.collective_compute(
                "AllReduce", mybir.AluOpType.add, ins=[d["cc1_in"][:]],
                outs=[d["cc1_out"][:]], replica_groups=self.rg))
        self.step("gpsimd", c2_)

        def c2b(g):
            p.sDG.dma(g.dma_start(out=self.scr2[:, 64:64 + c.NC1],
                                  in_=d["cc1_out"][:]))
        self.step("gpsimd", c2b)

        def c3(v):
            v.tensor_scalar(out=rawfirst, in0=self.scr2[:, 64:64 + c.NC1],
                            scalar1=self.params_b[:, 1:2], scalar2=None,
                            op0=ALU.mult)
        self.step("vector", c3)

        for j in range(c.NC1):
            def v0(v, j=j):
                v.tensor_scalar(out=self.scr2[:, 0:c.T], in0=raw(j)[:, 1:c.T + 1],
                                scalar1=self.cw1[:, j:j + 1], scalar2=None,
                                op0=ALU.mult)
                v.drain()
                v.scalar_tensor_tensor(out=self.scr2[:, 0:c.T],
                                       in0=raw(j)[:, 0:c.T],
                                       scalar=self.cw0[:, j:j + 1],
                                       in1=self.scr2[:, 0:c.T], op0=ALU.mult,
                                       op1=ALU.add)
            self.step("vector", v0)

            def s0(s, j=j):
                s.activation(post(j), self.scr2[:, 0:c.T], ACT.Silu,
                             bias=self.cb_[:, j:j + 1])
            self.step("scalar", s0)

        def d0a(v):
            v.tensor_scalar(out=self.DTR[0:c.H1, :], in0=self.DTR[0:c.H1, :],
                            scalar1=self.dtb_sb[:], scalar2=None, op0=ALU.add)
            v.drain()
            v.scalar_tensor_tensor(out=self.scr2[0:c.H1, 0:c.T],
                                   in0=self.DTR[0:c.H1, :], scalar=-1.0,
                                   in1=self.DTR[0:c.H1, :], op0=ALU.mult,
                                   op1=ALU.min)
            v.drain()
            v.tensor_scalar(out=DT2[0:c.H1, :], in0=self.DTR[0:c.H1, :],
                            scalar1=0.0, scalar2=None, op0=ALU.max)
        self.step("vector", d0a)

        def d0b(s):
            s.activation(self.scr2[0:c.H1, 0:c.T], self.scr2[0:c.H1, 0:c.T],
                         ACT.Exp)
            s.drain()
            s.activation(self.scr2[0:c.H1, 0:c.T], self.scr2[0:c.H1, 0:c.T],
                         ACT.Ln, bias=1.0)
        self.step("scalar", d0b)

        def d0c(v):
            v.tensor_tensor(out=DT2[0:c.H1, :], in0=DT2[0:c.H1, :],
                            in1=self.scr2[0:c.H1, 0:c.T], op=ALU.add)
        self.step("vector", d0c)

        def d1(v):
            v.tensor_scalar(out=AT[0:c.H1, :], in0=DT2[0:c.H1, :],
                            scalar1=self.negA[:], scalar2=None, op0=ALU.mult)
        self.step("vector", d1)

        for ch in range(NCH):
            def a0(te, ch=ch):
                te.transpose(self.psC[:, 0:c.H1], AT[0:c.H1, ch * 128:(ch + 1) * 128],
                             self.idf[0:c.H1, 0:c.H1])
                te.transpose(self.psC[:, 128:128 + c.H1],
                             DT2[0:c.H1, ch * 128:(ch + 1) * 128],
                             self.idf[0:c.H1, 0:c.H1])
            self.step("tensor", a0)

            def a1(v, ch=ch):
                v.tensor_copy(out=self.tscr[:, 0:c.H1], in_=self.psC[:, 0:c.H1])
                v.tensor_copy(out=DSTT[:, (NCH + ch) * c.H1:(NCH + ch + 1) * c.H1],
                              in_=self.psC[:, 128:128 + c.H1])
            self.step("vector", a1)

            def a2(te, ch=ch):
                te.matmul(self.psB[:, 0:c.H1], self.uones[:], self.tscr[:, 0:c.H1],
                          start=True, stop=True)
            self.step("tensor", a2)

            def a3(v, ch=ch):
                v.tensor_copy(out=AcsTok[:, ch * c.H1:(ch + 1) * c.H1],
                              in_=self.psB[:, 0:c.H1])
            self.step("vector", a3)

            def a4(te, ch=ch):
                te.transpose(self.psC[0:c.H1, 0:128],
                             AcsTok[:, ch * c.H1:(ch + 1) * c.H1],
                             self.idf[:])
            self.step("tensor", a4)

            def a5(v, ch=ch):
                v.tensor_copy(out=csums[:, ch:ch + 1],
                              in_=self.psC[0:c.H1, 127:128])
            self.step("vector", a5)

        def b1(te):
            for ch in range(NCH):
                te.transpose(self.psC[0:1, ch * c.H1:(ch + 1) * c.H1],
                             csums[:, ch:ch + 1], self.idf[0:c.H1, 0:c.H1])
        self.step("tensor", b1)

        def b2(v):
            v.tensor_copy(out=self.rowsE[0:1, 0:NCH * c.H1],
                          in_=self.psC[0:1, 0:NCH * c.H1])
            v.drain()
            v.memset(self.rowsE[0:1, 512:512 + c.H1], 0.0)
            for ch in range(1, NCH):
                v.drain()
                v.tensor_tensor(
                    out=self.rowsE[0:1, 512 + ch * c.H1:512 + (ch + 1) * c.H1],
                    in0=self.rowsE[0:1, 512 + (ch - 1) * c.H1:512 + ch * c.H1],
                    in1=self.rowsE[0:1, (ch - 1) * c.H1:ch * c.H1], op=ALU.add)
        self.step("vector", b2)

        def b3(s):
            s.activation(self.rowsE[0:1, 0:NCH * c.H1], self.rowsE[0:1, 0:NCH * c.H1],
                         ACT.Exp)
            s.drain()
            s.activation(self.rowsE[0:1, 512:512 + NCH * c.H1],
                         self.rowsE[0:1, 512:512 + NCH * c.H1], ACT.Exp)
        self.step("scalar", b3)


        NKC = 2 * VCH
        for ch in range(NCH):
            def t0(te, ch=ch):
                for j in range(NKC):
                    te.matmul(self.psC[:, j * 128:(j + 1) * 128],
                              post(j)[:, ch * 128:(ch + 1) * 128], self.idb[:],
                              start=True, stop=True)
            self.step("tensor", t0)

            def t1(v, ch=ch):
                for j in range(NKC):
                    dst = XVB if j < VCH else BPB
                    jj = j if j < VCH else j - VCH
                    v.tensor_copy(out=dst[:, c.H1 * 64 + jj * 128:c.H1 * 64 + jj * 128 + 128],
                                  in_=self.psC[:, j * 128:(j + 1) * 128])
            self.step("vector", t1)

            def t2(v, ch=ch):
                v.tensor_copy(out=self.scr2[:, 0:c.H1],
                              in_=AcsTok[:, ch * c.H1:(ch + 1) * c.H1])
            self.step("vector", t2)

            def t3(te, ch=ch):
                te.matmul(self.psC[:, 0:c.H1], self.ones_row[:, 0:128],
                          self.rowsE[0:1, ch * c.H1:(ch + 1) * c.H1],
                          start=True, stop=True)
            self.step("tensor", t3)

            def t4(v, ch=ch):
                v.tensor_scalar(out=self.scr2[:, 64:64 + c.H1],
                                in0=AcsTok[:, ch * c.H1:(ch + 1) * c.H1],
                                scalar1=-1.0, scalar2=None, op0=ALU.mult)
            self.step("vector", t4)

            def t5(s, ch=ch):
                s.activation(self.scr2[:, 64:64 + c.H1], self.scr2[:, 64:64 + c.H1],
                             ACT.Exp)
            self.step("scalar", t5)

            def t6(v, ch=ch):
                v.tensor_tensor(out=DSTT[:, ch * c.H1:(ch + 1) * c.H1],
                                in0=self.scr2[:, 64:64 + c.H1],
                                in1=self.psC[:, 0:c.H1], op=ALU.mult)
            self.step("vector", t6)

            def t7(v, ch=ch):
                for h in range(c.H1):
                    kv = h // c.G1
                    vsl = XVB[:, c.H1 * 64 + (kv // 2) * 128 + (kv % 2) * 64:
                              c.H1 * 64 + (kv // 2) * 128 + (kv % 2) * 64 + 64]
                    ksl = BPB[:, c.H1 * 64 + (kv // 2) * 128 + (kv % 2) * 64:
                              c.H1 * 64 + (kv // 2) * 128 + (kv % 2) * 64 + 64]
                    v.tensor_scalar(out=XVB[:, h * 64:(h + 1) * 64], in0=vsl,
                                    scalar1=DSTT[:, (NCH + ch) * c.H1 + h:
                                                 (NCH + ch) * c.H1 + h + 1],
                                    scalar2=None, op0=ALU.mult)
                    v.tensor_scalar(out=BPB[:, h * 64:(h + 1) * 64], in0=ksl,
                                    scalar1=DSTT[:, ch * c.H1 + h:ch * c.H1 + h + 1],
                                    scalar2=None, op0=ALU.mult)
            self.step("vector", t7)

            def t8(te, ch=ch):
                for h in range(c.H1):
                    te.matmul(self.psY[64 * (h % 2):64 * (h % 2) + 64,
                                       (h // 2) * 64:(h // 2) * 64 + 64],
                              BPB[:, h * 64:(h + 1) * 64],
                              XVB[:, h * 64:(h + 1) * 64], start=True, stop=True)
            self.step("tensor", t8)

            def t9(te, ch=ch):
                ech = self.rowsE[0:1, ch * c.H1:(ch + 1) * c.H1]
                r0 = ech.rearrange("o (i two) -> o i two", two=2)[:, :, 0:1]
                r1 = ech.rearrange("o (i two) -> o i two", two=2)[:, :, 1:2]
                nh2 = c.H1 // 2
                for a_ in range(0, nh2, 8):
                    b_ = min(8, nh2 - a_)
                    te.matmul(self.psC[0:64, a_ * 64:(a_ + b_) * 64],
                              self.ones_row[:, 0:64],
                              r0[:, a_:a_ + b_, :].to_broadcast([1, b_, 64]),
                              start=True, stop=True)
                    te.matmul(self.psC[64:128, a_ * 64:(a_ + b_) * 64],
                              self.ones_row[:, 0:64],
                              r1[:, a_:a_ + b_, :].to_broadcast([1, b_, 64]),
                              start=True, stop=True)
            self.step("tensor", t9)

            def t10(v, ch=ch):
                v.tensor_copy(out=S0ST[:, ch * c.H1 * 32:(ch + 1) * c.H1 * 32],
                              in_=S0cur[:, 0:c.H1 * 32])
                v.drain()
                v.tensor_tensor(out=S0cur[:, 0:c.H1 * 32], in0=S0cur[:, 0:c.H1 * 32],
                                in1=self.psC[:, 0:c.H1 * 32], op=ALU.mult)
                v.drain()
                v.tensor_tensor(out=S0cur[:, 0:c.H1 * 32], in0=S0cur[:, 0:c.H1 * 32],
                                in1=self.psY[:, 0:c.H1 * 32], op=ALU.add)
            self.step("vector", t10)

        def u0(v):
            v.tensor_scalar(out=self.scr2[:, 0:c.H1 * 32], in0=S0cur[:, 0:c.H1 * 32],
                            scalar1=self.params_b[:, 2:3], scalar2=None,
                            op0=ALU.mult)
        self.step("vector", u0)

        def u1(g):
            p.sSTG.dma(g.dma_start(out=d["cc2_in"][:, 0:c.H1 * 32],
                                  in_=self.scr2[:, 0:c.H1 * 32]))
        self.step("gpsimd", u1)

        def u2(g):
            p.sCC.inc(g.collective_compute(
                "AllReduce", mybir.AluOpType.add, ins=[d["cc2_in"][:]],
                outs=[d["cc2_out"][:]], replica_groups=self.rg))
        self.step("gpsimd", u2)

        def u2b(g):
            p.sDG.dma(g.dma_start(out=self.scr2[:, 0:c.H1 * 32],
                                  in_=d["cc2_out"][:, 0:c.H1 * 32]))
        self.step("gpsimd", u2b)
        self.dbg_dump("dbg_scr2cc", self.scr2[:, 0:c.H1 * 32])

        def u3(v):
            v.tensor_scalar(out=Sinit[:, 0:c.H1 * 32], in0=self.scr2[:, 0:c.H1 * 32],
                            scalar1=self.params_b[:, 1:2], scalar2=None,
                            op0=ALU.mult)
        self.step("vector", u3)
        self.dbg_dump("dbg_s0cur", S0cur[:, 0:c.H1 * 32])
        self.dbg_dump("dbg_sinit", Sinit[:, 0:c.H1 * 32])
        if "dbg_cc2i" in self.dbg:
            def fdcc(sync):
                for s_ in (p.sT, p.sV, p.sS, p.sG, p.sST, p.sSTG):
                    if s_.n:
                        sync.wait_ge(s_.h, s_.n)
                p.sDW.dma(sync.dma_start(out=d["dbg_cc2i"][:], in_=d["cc2_in"][:]))
                p.sDW.dma(sync.dma_start(out=d["dbg_cc2o"][:], in_=d["cc2_out"][:]))
            self.on("sync", fdcc)

        for ch in range(NCH):
            def w0(te, ch=ch):
                ech = self.rowsE[0:1, 512 + ch * c.H1:512 + (ch + 1) * c.H1]
                r0 = ech.rearrange("o (i two) -> o i two", two=2)[:, :, 0:1]
                r1 = ech.rearrange("o (i two) -> o i two", two=2)[:, :, 1:2]
                nh2 = c.H1 // 2
                for a_ in range(0, nh2, 8):
                    b_ = min(8, nh2 - a_)
                    te.matmul(self.psC[0:64, a_ * 64:(a_ + b_) * 64],
                              self.ones_row[:, 0:64],
                              r0[:, a_:a_ + b_, :].to_broadcast([1, b_, 64]),
                              start=True, stop=True)
                    te.matmul(self.psC[64:128, a_ * 64:(a_ + b_) * 64],
                              self.ones_row[:, 0:64],
                              r1[:, a_:a_ + b_, :].to_broadcast([1, b_, 64]),
                              start=True, stop=True)
            self.step("tensor", w0)

            def w1(v, ch=ch):
                v.tensor_tensor(out=self.scr2[:, 0:c.H1 * 32],
                                in0=Sinit[:, 0:c.H1 * 32], in1=self.psC[:, 0:c.H1 * 32],
                                op=ALU.mult)
                v.drain()
                v.tensor_tensor(out=SCOR[:, 0:c.H1 * 32], in0=self.scr2[:, 0:c.H1 * 32],
                                in1=S0ST[:, ch * c.H1 * 32:(ch + 1) * c.H1 * 32],
                                op=ALU.add)
            self.step("vector", w1)

            def w2(s, ch=ch):
                s.activation(EAT[:, 0:c.H1], AcsTok[:, ch * c.H1:(ch + 1) * c.H1],
                             ACT.Exp)
            self.step("scalar", w2)

            def w3(te, ch=ch):
                for j in range(VCH):
                    te.matmul(self.psC[:, j * 128:(j + 1) * 128],
                              post(j)[:, ch * 128:(ch + 1) * 128], self.idb[:],
                              start=True, stop=True)
            self.step("tensor", w3)

            def w4(v, ch=ch):
                for j in range(VCH):
                    v.tensor_copy(out=XVB[:, c.H1 * 64 + j * 128:c.H1 * 64 + (j + 1) * 128],
                                  in_=self.psC[:, j * 128:(j + 1) * 128])
            self.step("vector", w4)

            def w5(v, ch=ch):
                for h in range(c.H1):
                    kv = h // c.G1
                    vsl = XVB[:, c.H1 * 64 + (kv // 2) * 128 + (kv % 2) * 64:
                              c.H1 * 64 + (kv // 2) * 128 + (kv % 2) * 64 + 64]
                    v.tensor_scalar(out=XVB[:, h * 64:(h + 1) * 64], in0=vsl,
                                    scalar1=DSTT[:, (NCH + ch) * c.H1 + h:
                                                 (NCH + ch) * c.H1 + h + 1],
                                    scalar2=None, op0=ALU.mult)
            self.step("vector", w5)

            KSW = self.WBF[0][:, 1024:1024 + VCH * 128]

            def w6(v, ch=ch, KSW=KSW):
                for j in range(VCH):
                    kch = post(VCH + j)[:, ch * 128:(ch + 1) * 128]
                    v.tensor_copy(out=KSW[0:64, j * 128:(j + 1) * 128],
                                  in_=kch[64:128, :])
                    v.tensor_copy(out=KSW[64:128, j * 128:(j + 1) * 128],
                                  in_=kch[0:64, :])
            self.step("vector", w6)

            for hg in range(c.H1 // 8):
                h0_ = hg * 8

                def x0(te, ch=ch, h0_=h0_):
                    for i in range(8):
                        h = h0_ + i
                        te.transpose(self.psB[0:1, i * 128:(i + 1) * 128],
                                     AcsTok[:, ch * c.H1 + h:ch * c.H1 + h + 1],
                                     self.idf[:])
                self.step("tensor", x0)

                def x1(v, ch=ch):
                    v.tensor_copy(out=self.rows[0:1, 0:1024],
                                  in_=self.psB[0:1, 0:1024])
                self.step("vector", x1)

                def x2(te, ch=ch, h0_=h0_):
                    for i in range(8):
                        h = h0_ + i
                        kv = h // c.G1
                        te.matmul(self.psC[:, i * 128:(i + 1) * 128],
                                  self.ones_row[:, 0:128],
                                  self.rows[0:1, i * 128:(i + 1) * 128],
                                  start=True, stop=True)
                        qsl = post(QOF + h // 2)[64 * (h % 2):64 * (h % 2) + 64,
                                                 ch * 128:(ch + 1) * 128]
                        if (kv % 2) == (h % 2):
                            ksl = post(VCH + kv // 2)[
                                64 * (kv % 2):64 * (kv % 2) + 64,
                                ch * 128:(ch + 1) * 128]
                        else:
                            ksl = KSW[64 * (h % 2):64 * (h % 2) + 64,
                                      (kv // 2) * 128:(kv // 2) * 128 + 128]
                        te.matmul(self.psB[:, i * 128:(i + 1) * 128], ksl, qsl,
                                  start=True, stop=True)
                self.step("tensor", x2)

                def x3(v, ch=ch, h0_=h0_):
                    for i in range(8):
                        h = h0_ + i
                        v.tensor_scalar(out=self.scr2[:, i * 128:(i + 1) * 128],
                                        in0=self.psC[:, i * 128:(i + 1) * 128],
                                        scalar1=AcsTok[:, ch * c.H1 + h:
                                                       ch * c.H1 + h + 1],
                                        scalar2=None, op0=ALU.subtract)
                    v.drain()
                    v.tensor_tensor(
                        out=self.scr2[:, 0:1024].rearrange("p (i q) -> p i q", q=128),
                        in0=self.scr2[:, 0:1024].rearrange("p (i q) -> p i q", q=128),
                        in1=self.mk_lowT[:, None, :].to_broadcast([128, 8, 128]),
                        op=ALU.add)
                self.step("vector", x3)

                def x4(s, ch=ch):
                    s.activation(LEX, self.scr2[:, 0:1024], ACT.Exp)
                self.step("scalar", x4)

                def x5(v, ch=ch):
                    v.tensor_tensor(out=PDT, in0=self.psB[:, 0:1024],
                                    in1=LEX, op=ALU.mult)
                self.step("vector", x5)

                def x6(te, ch=ch, h0_=h0_):
                    for i in range(8):
                        h = h0_ + i
                        te.matmul(self.psA[:, (h % 16) * 64:(h % 16) * 64 + 64],
                                  PDT[:, i * 128:(i + 1) * 128],
                                  XVB[:, h * 64:(h + 1) * 64],
                                  start=True, stop=True)
                        qsl = post(QOF + h // 2)[64 * (h % 2):64 * (h % 2) + 64,
                                                 ch * 128:(ch + 1) * 128]
                        ssl = SCOR[64 * (h % 2):64 * (h % 2) + 64,
                                   (h // 2) * 64:(h // 2) * 64 + 64]
                        te.matmul(self.psY[:, (h % 16) * 64:(h % 16) * 64 + 64],
                                  qsl, ssl, start=True, stop=True)
                self.step("tensor", x6)

                def x7(v, ch=ch, h0_=h0_):
                    for i in range(8):
                        h = h0_ + i
                        v.tensor_scalar(
                            out=YCH[:, h * 64:(h + 1) * 64],
                            in0=self.psY[:, (h % 16) * 64:(h % 16) * 64 + 64],
                            scalar1=EAT[:, h:h + 1], scalar2=None, op0=ALU.mult)
                    v.drain()
                    for i in range(8):
                        h = h0_ + i
                        v.tensor_tensor(
                            out=YCH[:, h * 64:(h + 1) * 64],
                            in0=YCH[:, h * 64:(h + 1) * 64],
                            in1=self.psA[:, (h % 16) * 64:(h % 16) * 64 + 64],
                            op=ALU.add)
                self.step("vector", x7)

            def y0(sy, ch=ch):
                p.sDW.dma(sy.dma_start(out=ZGR[:, 0:c.ZGD],
                                       in_=d["zg_sp"][:, ch * 128:(ch + 1) * 128],
                                       transpose=True))
            self.step("sync", y0)

            def y1(s, ch=ch):
                hz = min(1024, c.ZGD)
                s.activation(self.scr2[:, 0:hz], YCH[:, 0:hz], ACT.Square)
                if c.ZGD > 1024:
                    s.activation(self.tscr[:, 0:c.ZGD - 1024],
                                 YCH[:, 1024:c.ZGD], ACT.Square)
            self.step("scalar", y1)

            def y2(v, ch=ch):
                hz = min(1024, c.ZGD)
                v.reduce_sum(EAT[:, 0:hz // 64],
                             self.scr2[:, 0:hz].rearrange("p (h q) -> p h q", q=64),
                             axis=AX)
                if c.ZGD > 1024:
                    v.reduce_sum(EAT[:, 16:16 + (c.ZGD - 1024) // 64],
                                 self.tscr[:, 0:c.ZGD - 1024].rearrange(
                                     "p (h q) -> p h q", q=64), axis=AX)
                v.drain()
                v.tensor_scalar(out=EAT[:, 0:c.H1], in0=EAT[:, 0:c.H1],
                                scalar1=1.0 / c.P1, scalar2=c.EPS_SSD,
                                op0=ALU.mult, op1=ALU.add)
            self.step("vector", y2)

            def y3(s, ch=ch):
                s.activation(EAT[:, 0:c.H1], EAT[:, 0:c.H1], ACT.Sqrt)
            self.step("scalar", y3)

            def y4(v, ch=ch):
                v.reciprocal(out=EAT[:, 0:c.H1], in_=EAT[:, 0:c.H1])
                v.drain()
                v.tensor_tensor(
                    out=YCH[:, 0:c.ZGD].rearrange("p (h q) -> p h q", q=64),
                    in0=YCH[:, 0:c.ZGD].rearrange("p (h q) -> p h q", q=64),
                    in1=EAT[:, 0:c.H1, None].to_broadcast([128, c.H1, 64]),
                    op=ALU.mult)
                v.drain()
                v.tensor_tensor(out=YCH[:, 0:c.ZGD], in0=YCH[:, 0:c.ZGD],
                                in1=self.gnw_b[:, 0:c.ZGD], op=ALU.mult)
            self.step("vector", y4)

            def y5(s, ch=ch):
                hz = min(1024, c.ZGD)
                s.activation(self.scr2[:, 0:hz], ZGR[:, 0:hz], ACT.Silu)
                if c.ZGD > 1024:
                    s.activation(self.tscr[:, 0:c.ZGD - 1024],
                                 ZGR[:, 1024:c.ZGD], ACT.Silu)
            self.step("scalar", y5)

            def y6(v, ch=ch):
                hz = min(1024, c.ZGD)
                v.tensor_tensor(out=YCH[:, 0:hz], in0=YCH[:, 0:hz],
                                in1=self.scr2[:, 0:hz], op=ALU.mult)
                if c.ZGD > 1024:
                    v.tensor_tensor(out=YCH[:, 1024:c.ZGD], in0=YCH[:, 1024:c.ZGD],
                                    in1=self.tscr[:, 0:c.ZGD - 1024], op=ALU.mult)
            self.step("vector", y6)

            def y7(te, ch=ch):
                for k in range(c.NZ1):
                    ps = self.psA if k < 8 else self.psB
                    te.transpose(ps[:, (k % 8) * 128:(k % 8) * 128 + 128],
                                 YCH[:, k * 128:(k + 1) * 128], self.idf[:])
            self.step("tensor", y7)

            def y8(v, ch=ch):
                for k in range(c.NZ1):
                    ps = self.psA if k < 8 else self.psB
                    v.tensor_copy(out=YTT[:, k * c.T + ch * 128:k * c.T + ch * 128 + 128],
                                  in_=ps[:, (k % 8) * 128:(k % 8) * 128 + 128])
            self.step("vector", y8)
        self.barrier()

        for dc in range(ND):
            def fl(sync, dc=dc):
                p.sDW.dma(self.load_slab(sync, self.WSTG[0], d["wo1"][dc]))
            self.step("sync", fl)

            def fcv(v):
                p.sV.inc(v.tensor_copy(out=self.WBF[0][:, 0:c.ZGD],
                                       in_=self.WSTG[0][:, 0:c.ZGD]))
            self.step("vector", fcv)

            def ft(te):
                for kk in range(c.NZ1):
                    for j in range(0, c.T, 512):
                        w = min(512, c.T - j)
                        te.matmul(self.psA[:, j:j + w],
                                  self.WBF[0][:, kk * 128:(kk + 1) * 128],
                                  YTT[:, kk * c.T + j:kk * c.T + j + w],
                                  start=(kk == 0), stop=(kk == c.NZ1 - 1))
            self.step("tensor", ft)

            def fv(v):
                p.sV.inc(v.tensor_copy(out=self.sR[:, 0:c.T], in_=self.psA[:, 0:c.T]))
            self.step("vector", fv)

            def fst(sync, dc=dc):
                p.sST.dma(sync.dma_start(out=d["sT_sp"][dc * 128:(dc + 1) * 128, :],
                                         in_=self.sR[:, 0:c.T]))
            self.step("sync", fst)
        self.barrier()

    def phase_ffn1(self, P, t0, tn):
        p, c, d = self.p, self.c, self.d
        S = c.NT
        self.qx_pass_l1(t0, tn)
        self.col2row(self.stat[:, 3 * S:3 * S + c.NT], 0, c.NT, self.psC)

        def f0(v):
            v.tensor_scalar(out=self.rows[0:1, t0:t0 + tn],
                            in0=self.rows[0:1, t0:t0 + tn],
                            scalar1=2.0, scalar2=None, op0=ALU.mult)
        self.step("vector", f0)
        self.bcast(self.rows[0:1, t0:t0 + tn], self.bc1, tn, self.psC)

        def epi(fc):
            def s1(v, fc=fc):
                v.tensor_tensor(out=self.scr2[:, 0:tn], in0=self.psA[:, 0:tn],
                                in1=self.bc1[:, 0:tn], op=ALU.mult)
                v.tensor_tensor(out=self.XR[:, 0:tn], in0=self.psB[:, 0:tn],
                                in1=self.bc1[:, 0:tn], op=ALU.mult)
            self.step("vector", s1)
            ps_free = p.sV.n

            def s2(s, fc=fc):
                s.activation(self.tscr[:, 0:tn], self.scr2[:, 0:tn], ACT.Silu)
            self.step("scalar", s2)

            def s3(v, fc=fc):
                v.tensor_tensor(out=P[:, fc * tn:(fc + 1) * tn],
                                in0=self.tscr[:, 0:tn], in1=self.XR[:, 0:tn],
                                op=ALU.mult)
            self.step("vector", s3)
            return ps_free

        self.ffn_gateup(d["wg1"], d["wu1"], self.QX, P,
                        self.conv_sign, self.conv_sign, epi, tn)

        def f1(v):
            v.memset(self.bc2[:, 0:tn], 2.0)
        self.step("vector", f1)
        self.barrier()

    def qx_pass_l1(self, t0, tn):
        p, c, d = self.p, self.c, self.d
        S = c.NT
        self.col2row(self.stat[:, 1 * S:1 * S + c.NT], 0, c.NT, self.psC)
        self.col2row(self.stat[:, 0 * S:0 * S + c.NT], 1024, c.NT, self.psC)
        self.bcast(self.rows[0:1, t0:t0 + tn], self.bc1, tn, self.psC)
        self.bcast(self.rows[0:1, 1024 + t0:1024 + t0 + tn], self.bc2, tn, self.psB)
        dm3, vm3 = {}, {}
        for k in range(c.ND):
            par = k % 2

            def fl(g, k=k, par=par):
                if k >= 2:
                    g.wait_ge(p.sV.h, vm3[k - 2])
                sem = p.sDG if par == 0 else p.sDGb
                if k >= 2:
                    g.wait_ge(sem.h, dm3[k - 2][1])
                else:
                    for s__ in (p.sDG, p.sDGb, p.sSTG):
                        if s__.n > 0:
                            g.wait_ge(s__.h, s__.n)
                sem.dma(g.dma_start(
                    out=self.XR[:, par * 1024:par * 1024 + tn],
                    in_=d["xT_sp"][k * 128:(k + 1) * 128, t0:t0 + tn]))
            self.on("gpsimd", fl)
            dm3[k] = ((p.sDG if par == 0 else p.sDGb), (p.sDG if par == 0 else p.sDGb).n)

            def fv(v, k=k, par=par):
                v.wait_ge(dm3[k][0].h, dm3[k][1])
                v.drain()
                xr = self.XR[:, par * 1024:par * 1024 + tn]
                v.tensor_tensor(out=self.scr2[:, 0:tn], in0=xr,
                                in1=self.bc1[:, 0:tn], op=ALU.mult)
                v.drain()
                v.tensor_tensor(out=self.scr2[:, 0:tn], in0=self.scr2[:, 0:tn],
                                in1=self.bc2[:, 0:tn], op=ALU.subtract)
                v.drain()
                p.sV.inc(v.tensor_scalar(out=self.QX[:, k * tn:(k + 1) * tn],
                                         in0=self.scr2[:, 0:tn], scalar1=MAGIC,
                                         scalar2=MAGIC, op0=ALU.add,
                                         op1=ALU.subtract))
            self.on("vector", fv)
            vm3[k] = p.sV.n
        self.barrier()


def build_body(p, block, d, rg, dbg_out, dbg):
    b = B(p, block, d, rg, dbg)
    c = p.cfg
    nc = p.nc
    TW = c.T + c.WIN
    b.phase_init()

    with contextlib.ExitStack() as scope0:
        XN = scope0.enter_context(nc.sbuf_tensor("XN0", [128, c.ND * TW], BF16))
        b.XN = XN
        own = [XN[:, k * TW:k * TW + c.T] for k in range(c.ND)]
        halo = [XN[:, k * TW + c.T:k * TW + TW] for k in range(c.ND)]
        b.ln_pass(d["x_own"], c.T, 0, 1, c.EPS_LN, xn_chunks=own, slot_c=2,
                  aq_eps=c.EPS_FFN0_AQ)
        b.ln_pass(d["x_halo"], c.WIN, 6, 7, c.EPS_LN, xn_chunks=halo)
        b.dbg_dump("dbg_xn", XN[:])
        b.phase_ffn0_stats()
        nk = c.NKT * 128
        with contextlib.ExitStack() as sa:
            b.cosT = sa.enter_context(nc.sbuf_tensor("cosT", [128, nk], F32))
            b.sinT = sa.enter_context(nc.sbuf_tensor("sinT", [128, nk], F32))
            b.vTr = sa.enter_context(nc.sbuf_tensor("vTr", [128, nk], BF16))
            b.V = sa.enter_context(nc.sbuf_tensor("Vtok", [128, c.NKT * c.KVH * 128], BF16))
            b.KRR = sa.enter_context(nc.sbuf_tensor("KRR", [128, c.KVH * nk], BF16))
            b.QR = sa.enter_context(nc.sbuf_tensor("QR", [128, c.T], BF16))
            b.Y = sa.enter_context(nc.sbuf_tensor("Yat", [128, c.H * c.T], BF16))
            b.pr = sa.enter_context(nc.sbuf_tensor("prb", [128, c.W5 * c.W5 * 128], BF16))
            b.phase_attn()
            b.dbg_dump("dbg_y", b.Y[:])
        if "dbg_s0" in b.dbg:
            def fds(sync):
                for s_ in (p.sT, p.sV, p.sS, p.sG):
                    sync.wait_ge(s_.h, s_.n)
                if p.sST.n:
                    sync.wait_ge(p.sST.h, p.sST.n)
                p.sDW.dma(sync.dma_start(out=d["dbg_s0"][:], in_=d["sT_sp"][:]))
            block.sync(fds)
    if True:
        T2 = c.T // 2
        with contextlib.ExitStack() as sf:
            QX0 = sf.enter_context(nc.sbuf_tensor("QX0", [128, c.ND * T2], BF16))
            H = sf.enter_context(nc.sbuf_tensor("H0", [128, c.NF * T2], BF16))
            b.QX = QX0
            for half in range(2):
                b.phase_ffn0(H, half * T2, T2)
                if half == 0:
                    b.dbg_dump("dbg_qh", H[:])
                b.phase_down_combine(H, d["wd0"], b.make_conv_tern_idx(2),
                                     b.bc2, 0, half * T2, T2)
                b.l1_stats(half * T2, T2)
    if "dbg_x1" in b.dbg:
        def fdx1(sync):
            for s_ in (p.sT, p.sV, p.sS, p.sG, p.sST, p.sSTG):
                if s_.n:
                    sync.wait_ge(s_.h, s_.n)
            p.sDW.dma(sync.dma_start(out=d["dbg_x1"][:], in_=d["xT_sp"][:]))
        block.sync(fdx1)

    with contextlib.ExitStack() as s1:
        RS = c.T + 32
        NSL = c.NC1 + 1
        RAWP = s1.enter_context(nc.sbuf_tensor("RAWP", [128, NSL * RS], BF16,
                                               side="right"))
        S0ST = s1.enter_context(nc.sbuf_tensor("S0ST", [128, c.NCH * c.H1 * 32],
                                               BF16, side="right"))
        b.gnw_b = s1.enter_context(nc.sbuf_tensor("gnw_b", [128, c.ZGD], F32,
                                                  side="right"))
        DT2 = s1.enter_context(nc.sbuf_tensor("DT2", [c.H1, c.T], F32, side="right"))
        b.DTR = s1.enter_context(nc.sbuf_tensor("DTR", [c.H1, c.T], F32, side="right"))
        ZGR = s1.enter_context(nc.sbuf_tensor("ZGR", [128, max(c.ZGD, c.T)], BF16,
                                              side="right"))
        with contextlib.ExitStack() as s2:
            XN1 = s2.enter_context(nc.sbuf_tensor("XN1", [128, c.ND * c.T], BF16))
            b.phase_l1_entry(XN1)
            b.dbg_dump("dbg_xn1", XN1[:])
            b.phase_l1_proj(XN1, RAWP, DT2, ZGR)
            b.dbg_dump("dbg_raw", RAWP[:])
        with contextlib.ExitStack() as s3:
            YTT = s3.enter_context(nc.sbuf_tensor("YTT", [128, c.NZ1 * c.T], BF16))
            AcsTok = s3.enter_context(nc.sbuf_tensor("AcsTok", [128, c.NCH * c.H1], F32))
            csums = s3.enter_context(nc.sbuf_tensor("csums", [c.H1, c.NCH], F32))
            DSTT = s3.enter_context(nc.sbuf_tensor("DSTT", [128, 2 * c.NCH * c.H1], F32))
            EAT = s3.enter_context(nc.sbuf_tensor("EAT", [128, max(c.H1, 32)], F32))
            XVB = s3.enter_context(nc.sbuf_tensor("XVB", [128, c.H1 * 64 + 512], BF16))
            BPB = s3.enter_context(nc.sbuf_tensor("BPB", [128, c.H1 * 64 + 512], BF16))
            SCOR = s3.enter_context(nc.sbuf_tensor("SCOR", [128, c.H1 * 32], BF16))
            S0cur = b.maxA[:, 0:c.H1 * 32]
            Sinit = b.maxB[:, 0:c.H1 * 32]
            b.paranoid = bool(int(os.environ.get("MK_PARANOID", "0")))
            b.phase_l1_ssd(RAWP, S0ST, YTT, DT2, b.DTR, AcsTok,
                           csums, DSTT, EAT, XVB, BPB, SCOR,
                           S0cur, Sinit, ZGR)
            b.paranoid = False
            b.dbg_dump("dbg_ytt", YTT[:])
            b.dbg_dump("dbg_dt2", DT2[:])
        if "dbg_s1" in b.dbg:
            def fds1(sync):
                for s_ in (p.sT, p.sV, p.sS, p.sG, p.sST, p.sSTG):
                    if s_.n:
                        sync.wait_ge(s_.h, s_.n)
                p.sDW.dma(sync.dma_start(out=d["dbg_s1"][:], in_=d["sT_sp"][:]))
            block.sync(fds1)
    with contextlib.ExitStack() as s4:
        QX1 = s4.enter_context(nc.sbuf_tensor("QX1", [128, c.ND * (c.T // 2)], BF16))
        b.QX = QX1
        b.phase_l1_xq()
        T2 = c.T // 2
        with contextlib.ExitStack() as s5:
            P1 = s5.enter_context(nc.sbuf_tensor("P1", [128, c.NF * T2], BF16))
            for half in range(2):
                b.phase_ffn1(P1, half * T2, T2)
                b.phase_down_combine(P1, d["wd1"], b.conv_sign, b.bc2, 1,
                                     half * T2, T2)

    fin = {s: s.n for s in (p.sDG, p.sDGb, p.sDW, p.sDWb, p.sDS, p.sDSb,
                            p.sST, p.sSTG, p.sCC)}

    def fend(sync):
        for s, v in fin.items():
            sync.wait_ge(s.h, v)
    block.sync(fend)
